# version 12
# baseline (speedup 1.0000x reference)
import contextlib
import numpy as np
import ml_dtypes

import concourse.bass as bass
import concourse.mybir as mybir
from concourse.bass_utils import run_bass_kernel_spmd

F32 = mybir.dt.float32
BF16 = mybir.dt.bfloat16
I32 = mybir.dt.int32
ALU = mybir.AluOpType
AF = mybir.ActivationFunctionType
BF = ml_dtypes.bfloat16

NCORES = 8
D = 1024
HID = 3072
NH = 16
DK = 64
T_CORE = 2048
B, T = 4, 4096
NCH = D // 128
HCH = HID // 128
C1 = float(np.float32(1.5 * 2.0 ** 23))
C2 = float(np.float32(1.5 * 2.0 ** 52))


def _own_tokens(core):
    p = core % 2
    idx = []
    for s in (p, p + 2, p + 4, p + 6):
        idx.extend(range(512 * s, 512 * (s + 1)))
    return np.array(idx)



class Sem:
    def __init__(self, ctx, nc, name):
        self.h = ctx.enter_context(nc.semaphore(name))
        self.n = 0

    def inc(self, instr, k=1):
        instr.then_inc(self.h, k)
        self.n += k
        return self.n

    @property
    def mark(self):
        return (self, self.n)


class Prog:

    def __init__(self):
        self.nc = bass.Bass("TRN2", target_bir_lowering=False)
        self.ctx = contextlib.ExitStack()
        self.block = None
        self.sems = {}
        self.E = {}
        self._tag = ""

    @contextlib.contextmanager
    def scoped(self, tag):
        old = self._tag
        self._tag = tag
        try:
            yield
        finally:
            self._tag = old

    def __enter__(self):
        self._blk = self.nc.Block()
        self.block = self._blk.__enter__()
        for e in ("vector", "scalar", "tensor", "gpsimd"):
            self.E[e] = self.sem("E_" + e)
        return self

    def __exit__(self, exc_type, exc, tb):
        if exc is not None:
            try:
                self._blk.__exit__(None, None, None)
            except Exception:
                pass
            try:
                self.ctx.close()
            except Exception:
                pass
            return False
        self._blk.__exit__(None, None, None)
        self.ctx.close()
        return False

    def sem(self, name):
        if name not in self.sems:
            self.sems[name] = Sem(self.ctx, self.nc, name)
        return self.sems[name]

    def on(self, eng, fn):
        getattr(self.block, eng)(fn)

    def sb(self, name, shape, dt, stack=None, side="left"):
        return (stack or self.ctx).enter_context(
            self.nc.sbuf_tensor(self._tag + name, list(shape), dt, side=side))

    def ps(self, name, shape, dt=F32, stack=None):
        return (stack or self.ctx).enter_context(
            self.nc.psum_tensor(self._tag + name, list(shape), dt))

    def dram_in(self, name, shape, dt):
        return self.nc.dram_tensor(name, list(shape), dt, kind="ExternalInput").ap()

    def dram_out(self, name, shape, dt):
        return self.nc.dram_tensor(name, list(shape), dt, kind="ExternalOutput").ap()

    def dram_tmp(self, name, shape, dt):
        return self.nc.dram_tensor(self._tag + name, list(shape), dt, kind="Internal").ap()


def wait_marks(eng, marks):
    for s, v in marks:
        if v > 0:
            eng.wait_ge(s.h, v)



def emit_ln_stream(P, tag, x_in, xln_sb, waits):
    V, S, Tn = P.E["vector"], P.E["scalar"], P.E["tensor"]
    ldx = [P.sem("lnld" + tag + str(i)) for i in range(2)]
    stk = contextlib.ExitStack()
    ones128 = P.sb("ln_ones" + tag, (128, 1), F32, stk, side="right")
    eps_t = P.sb("ln_eps" + tag, (1, 1), F32, stk, side="right")
    ones1 = P.sb("ln_ones1" + tag, (1, 128), F32, stk, side="right")
    xtile = [P.sb(f"ln_xt{i}" + tag, (128, NCH * 512), F32, stk, side="right") for i in range(2)]
    sq = [P.sb(f"ln_sq{i}" + tag, (128, 512), F32, stk, side="right") for i in range(2)]
    rows = {n: P.sb("ln_" + n + tag, (1, 512), F32, stk, side="right")
            for n in ("mu", "e2", "t1", "var", "sd", "rstd", "nmr")}
    tmp = P.sb("ln_tmp" + tag, (128, 512), F32, stk, side="right")
    mu_ps = P.ps("ln_mups" + tag, (1, 512), stack=stk)
    e2_ps = P.ps("ln_e2ps" + tag, (1, 512), stack=stk)
    repA = P.ps("ln_repA" + tag, (128, 512), stack=stk)
    repB = P.ps("ln_repB" + tag, (128, 512), stack=stk)

    def init(v):
        wait_marks(v, waits)
        v.memset(ones128[:], 1.0)
        v.memset(eps_t[:], 1e-5)
        V.inc(v.memset(ones1[:], 1.0))
    P.on("vector", init)
    init_mark = V.mark

    NT = T_CORE // 512
    tile_done = {}
    mmpair = {}
    sq_mark = {}
    for ti in range(NT):
        c0 = ti * 512
        xb = xtile[ti % 2]
        lds = ldx[ti % 2]
        def fldx(sy, ti=ti, c0=c0, xb=xb, lds=lds):
            wait_marks(sy, waits)
            if ti >= 2:
                wait_marks(sy, [tile_done[ti - 2]])
            for c in range(NCH):
                lds.inc(sy.dma_start(out=xb[:, c * 512:(c + 1) * 512],
                                     in_=x_in[c * 128:(c + 1) * 128, c0:c0 + 512]), 16)
        P.on("sync", fldx)
        ld_mark = lds.mark
        for c in range(NCH):
            buf = sq[c % 2]
            gidx = ti * NCH + c
            def fsq(sc, c=c, buf=buf, xb=xb, ld_mark=ld_mark, gidx=gidx):
                wait_marks(sc, [ld_mark, init_mark])
                if gidx >= 2:
                    wait_marks(sc, [mmpair[gidx - 2]])
                S.inc(sc.activation(buf[:], xb[:, c * 512:(c + 1) * 512], AF.Square))
                sq_mark[gidx] = S.mark
            P.on("scalar", fsq)
            def fmm(t, c=c, buf=buf, xb=xb, ld_mark=ld_mark, gidx=gidx):
                wait_marks(t, [ld_mark, sq_mark[gidx]])
                t.matmul(mu_ps[:], ones128[:], xb[:, c * 512:(c + 1) * 512],
                         start=(c == 0), stop=(c == NCH - 1))
                Tn.inc(t.matmul(e2_ps[:], ones128[:], buf[:],
                                start=(c == 0), stop=(c == NCH - 1)))
                mmpair[gidx] = Tn.mark
            P.on("tensor", fmm)
        stats_mark = Tn.mark
        def fv1(v, stats_mark=stats_mark):
            wait_marks(v, [stats_mark])
            v.wait_ge(V.h, V.n)
            V.inc(v.tensor_scalar(rows["mu"][:], mu_ps[:], 1.0 / 1024.0, None, ALU.mult))
            V.inc(v.tensor_scalar(rows["e2"][:], e2_ps[:], 1.0 / 1024.0, None, ALU.mult))
            v.wait_ge(V.h, V.n)
            V.inc(v.tensor_mul(rows["t1"][:], rows["mu"][:], rows["mu"][:]))
            v.wait_ge(V.h, V.n)
            V.inc(v.tensor_sub(rows["var"][:], rows["e2"][:], rows["t1"][:]))
        P.on("vector", fv1)
        var_mark = V.mark
        def fs1(sc, var_mark=var_mark):
            wait_marks(sc, [var_mark])
            S.inc(sc.activation(rows["sd"][:], rows["var"][:], AF.Sqrt,
                                bias=eps_t[0:1, 0:1], scale=1.0))
        P.on("scalar", fs1)
        sd_mark = S.mark
        def fv2(v, sd_mark=sd_mark):
            wait_marks(v, [sd_mark])
            v.wait_ge(V.h, V.n)
            V.inc(v.reciprocal(rows["rstd"][:], rows["sd"][:]))
            v.wait_ge(V.h, V.n)
            V.inc(v.scalar_tensor_tensor(rows["nmr"][:], rows["mu"][:], -1.0, rows["rstd"][:],
                                         ALU.mult, ALU.mult))
        P.on("vector", fv2)
        smalls_mark = V.mark
        def frep(t, smalls_mark=smalls_mark):
            wait_marks(t, [smalls_mark])
            t.matmul(repA[:], ones1[:], rows["rstd"][:], start=True, stop=True)
            Tn.inc(t.matmul(repB[:], ones1[:], rows["nmr"][:], start=True, stop=True))
        P.on("tensor", frep)
        reps_mark = Tn.mark
        def fap(v, c0=c0, reps_mark=reps_mark, xb=xb):
            wait_marks(v, [reps_mark])
            for c in range(NCH):
                src_ = xb[:, c * 512:(c + 1) * 512]
                dst = xln_sb[:, c * T_CORE + c0: c * T_CORE + c0 + 512]
                v.wait_ge(V.h, V.n)
                V.inc(v.tensor_mul(tmp[:], src_, repA[:]))
                v.wait_ge(V.h, V.n)
                V.inc(v.tensor_add(dst, tmp[:], repB[:]))
        P.on("vector", fap)
        tile_done[ti] = V.mark
        def fg(t, td=V.mark):
            wait_marks(t, [td])
        P.on("tensor", fg)
    stk.close()
    return V.mark


def emit_proj_fm(P, tag, src_sb, w_dram, w_col0, n_out_ch, evac, waits):
    V, Tn = P.E["vector"], P.E["tensor"]
    ws = [P.sem(f"pjw{tag}{i}") for i in range(2)]
    ws_base = [ws[0].n, ws[1].n]
    stk = contextlib.ExitStack()
    wbuf = [P.sb(f"pjw{tag}{i}", (128, NCH * 128), BF16, stk, side="right") for i in range(2)]
    ps = [P.ps(f"pjps{tag}{i}", (128, 512), stack=stk) for i in range(2)]
    w_r = w_dram.rearrange("(c p) o -> p c o", p=128)

    def load_w(g, oc):
        s = ws[oc % 2]
        s.inc(g.dma_start(out=wbuf[oc % 2][:].rearrange("p (c k) -> p c k", c=NCH),
                          in_=w_r[:, :, w_col0 + oc * 128: w_col0 + (oc + 1) * 128]), 16)

    def f0(g):
        wait_marks(g, waits)
        load_w(g, 0)
        if n_out_ch > 1:
            load_w(g, 1)
    P.on("gpsimd", f0)

    NT = T_CORE // 512
    evac_mark = {}
    mm_last = {}
    for oc in range(n_out_ch):
        for tt in range(NT):
            it = oc * NT + tt
            pp = it % 2
            def fmm(t, oc=oc, tt=tt, pp=pp, it=it):
                wait_marks(t, waits)
                t.wait_ge(ws[oc % 2].h, ws_base[oc % 2] + 16 * (oc // 2 + 1))
                if it >= 2:
                    wait_marks(t, [evac_mark[it - 2]])
                for c in range(NCH):
                    i = t.matmul(ps[pp][:], wbuf[oc % 2][:, c * 128:(c + 1) * 128],
                                 src_sb[:, c * T_CORE + tt * 512: c * T_CORE + tt * 512 + 512],
                                 start=(c == 0), stop=(c == NCH - 1))
                Tn.inc(i)
            P.on("tensor", fmm)
            mm_mark = Tn.mark
            mm_last[oc] = mm_mark
            def fev(v, oc=oc, tt=tt, pp=pp, mm_mark=mm_mark):
                wait_marks(v, [mm_mark])
                evac(v, ps[pp], oc, tt)
            P.on("vector", fev)
            evac_mark[it] = V.mark
        nxt = oc + 2
        if nxt < n_out_ch:
            def fpre(g, nxt=nxt, mm_mark=mm_last[oc]):
                wait_marks(g, [mm_mark])
                load_w(g, nxt)
            P.on("gpsimd", fpre)
    stk.close()
    return V.mark


def ktv(j):
    gk = j // 4
    return (gk % 2) * 16 + (gk // 2) * 4 + (j % 4)


def emit_l0attn(P, x_in, x_out, xln_sb, qT, attn_sb, wqk_d, wv_d, wproj_d, bqk_sb,
                bv_sb, bproj_sb, cos_sb, sin_sb, mask_sb, waits):
    V, S, Tn = P.E["vector"], P.E["scalar"], P.E["tensor"]
    gq = P.sem("at_gdma")
    cc = P.sem("at_cc")
    xl = [P.sem(f"at_xl{i}") for i in range(2)]
    xst = [P.sem(f"at_xst{i}") for i in range(2)]
    vstq = P.sem("at_vst")

    k_own = P.dram_tmp("k_own", (D, T_CORE), BF16)
    v_own = P.dram_tmp("v_own", (NH * T_CORE, 64), BF16)
    k_all = P.dram_tmp("k_all", (2 * D, T_CORE), BF16)
    v_all = P.dram_tmp("v_all", (2 * NH * T_CORE, 64), BF16)

    stON = contextlib.ExitStack()
    onesf = P.sb("at_ones1", (1, 128), F32, stON, side="right")
    def fvinit(v):
        wait_marks(v, waits)
        V.inc(v.memset(onesf[:], 1.0))
    P.on("vector", fvinit)
    ones_mark = V.mark

    stQK = contextlib.ExitStack()
    raw = P.sb("qkraw", (128, NCH * T_CORE), BF16, stQK, side="right")
    scat_mark = None
    prev_pj = None
    for which in ("q", "k"):
        stP = contextlib.ExitStack()
        boff = 0 if which == "q" else NCH
        def mk_evac(boff):
            def evac(v, ps_t, oc, tt):
                V.inc(v.tensor_scalar(raw[:, oc * T_CORE + tt * 512: oc * T_CORE + tt * 512 + 512],
                                      ps_t[:], bqk_sb[:, boff + oc: boff + oc + 1],
                                      None, ALU.add))
            return evac
        pj_mark = emit_proj_fm(P, which, xln_sb, wqk_d, 0 if which == "q" else D,
                               NCH, mk_evac(boff),
                               waits + ([prev_pj] if which == "k" else []))
        ropeA = P.sb("ropeA" + which, (128, 512), F32, stP, side="right")
        ropeB = P.sb("ropeB" + which, (128, 512), F32, stP, side="right")
        re_t = P.sb("re_t" + which, (128, 512), BF16, stP, side="right")
        ro_t = P.sb("ro_t" + which, (128, 512), BF16, stP, side="right")
        prev_pj = pj_mark
        if which == "k":
            kfin = P.sb("kfin", (128, 2 * T_CORE), BF16, stP, side="right")
        for m in range(4):
            for ti in range(T_CORE // 512):
                t0 = ti * 512
                def frope(v, m=m, t0=t0, pj_mark=pj_mark, scat_mark=scat_mark):
                    wait_marks(v, [pj_mark])
                    if scat_mark is not None:
                        wait_marks(v, [scat_mark])
                    e = raw[:, (2 * m) * T_CORE + t0: (2 * m) * T_CORE + t0 + 512]
                    o = raw[:, (2 * m + 1) * T_CORE + t0: (2 * m + 1) * T_CORE + t0 + 512]
                    cw = cos_sb[:, t0:t0 + 512]
                    sw = sin_sb[:, t0:t0 + 512]
                    v.wait_ge(V.h, V.n)
                    V.inc(v.tensor_mul(ropeA[:], e, cw))
                    V.inc(v.tensor_mul(ropeB[:], o, sw))
                    v.wait_ge(V.h, V.n)
                    V.inc(v.tensor_sub(re_t[:], ropeA[:], ropeB[:]))
                    v.wait_ge(V.h, V.n)
                    V.inc(v.tensor_mul(ropeA[:], e, sw))
                    V.inc(v.tensor_mul(ropeB[:], o, cw))
                    v.wait_ge(V.h, V.n)
                    V.inc(v.tensor_add(ro_t[:], ropeA[:], ropeB[:]))
                P.on("vector", frope)
                rope_mark = V.mark
                def fscat(g, which=which, m=m, t0=t0, rope_mark=rope_mark):
                    wait_marks(g, [rope_mark])
                    for hl in range(4):
                        dc = 2 * m + hl // 2
                        base = (hl % 2) * 64
                        if which == "q":
                            dst = qT[:, dc * T_CORE + t0: dc * T_CORE + t0 + 512]
                        else:
                            dst = kfin[:, (dc % 2) * T_CORE + t0: (dc % 2) * T_CORE + t0 + 512]
                        gq.inc(g.dma_start(out=dst[base:base + 32, :],
                                           in_=re_t[hl * 32:(hl + 1) * 32, :]), 16)
                        gq.inc(g.dma_start(out=dst[base + 32:base + 64, :],
                                           in_=ro_t[hl * 32:(hl + 1) * 32, :]), 16)
                P.on("gpsimd", fscat)
                scat_mark = gq.mark
            if which == "k":
                def fkst(g, m=m, scat_mark=scat_mark):
                    wait_marks(g, [scat_mark])
                    for dcl in range(2):
                        dc = 2 * m + dcl
                        gq.inc(g.dma_start(out=k_own[dc * 128:(dc + 1) * 128, :],
                                           in_=kfin[:, dcl * T_CORE:(dcl + 1) * T_CORE]), 16)
                P.on("gpsimd", fkst)
                scat_mark = gq.mark
        stP.close()
    qk_all = gq.mark
    stQK.close()

    stV = contextlib.ExitStack()
    wv_sb = P.sb("wv_sb", (128, NCH * 1024), BF16, stV, side="right")
    v_sbt = P.sb("v_sbt", (128, 512), BF16, stV, side="right")
    v_ps = [P.ps(f"v_ps{i}", (128, 512), stack=stV) for i in range(2)]
    wv_r = wv_d.rearrange("(c p) o -> p c o", p=128)

    def fvw(g):
        wait_marks(g, waits)
        wait_marks(g, [qk_all])
        gq.inc(g.dma_start(out=wv_sb[:].rearrange("p (c o) -> p c o", c=NCH), in_=wv_r), 16)
    P.on("gpsimd", fvw)
    vw_mark = gq.mark

    vev = {}
    vst = {}
    for tc in range(T_CORE // 128):
        for half in range(2):
            it = tc * 2 + half
            pp = it % 2
            def fvmm(t, tc=tc, half=half, pp=pp, it=it):
                wait_marks(t, waits)
                wait_marks(t, [vw_mark, ones_mark, qk_all])
                if it >= 2:
                    wait_marks(t, [vev[it - 2]])
                for c in range(NCH):
                    t.matmul(v_ps[pp][:], xln_sb[:, c * T_CORE + tc * 128: c * T_CORE + (tc + 1) * 128],
                             wv_sb[:, c * 1024 + half * 512: c * 1024 + half * 512 + 512],
                             start=(c == 0), stop=False)
                Tn.inc(t.matmul(v_ps[pp][:], onesf[:], bv_sb[0:1, half * 512:(half + 1) * 512],
                                start=False, stop=True))
            P.on("tensor", fvmm)
            vmm_mark = Tn.mark
            def fvev(v, pp=pp, vmm_mark=vmm_mark, it=it):
                wait_marks(v, [vmm_mark])
                if it >= 1:
                    wait_marks(v, [vst[it - 1]])
                V.inc(v.tensor_copy(v_sbt[:], v_ps[pp][:]))
            P.on("vector", fvev)
            vev[it] = V.mark
            def fvst(g, tc=tc, half=half, ev=vev[it]):
                wait_marks(g, [ev])
                vr = v_own.rearrange("(h t) d -> h t d", h=NH)
                for hh in range(8):
                    h = half * 8 + hh
                    vstq.inc(g.dma_start(out=vr[h, tc * 128:(tc + 1) * 128, :],
                                         in_=v_sbt[:, hh * 64:(hh + 1) * 64]), 16)
            P.on("gpsimd", fvst)
            vst[it] = vstq.mark
    stV.close()
    all_stored = [gq.mark, vstq.mark]

    def fcc(g):
        wait_marks(g, all_stored)
        rg = [[0, 1], [2, 3], [4, 5], [6, 7]]
        cc.inc(g.collective_compute("AllGather", ALU.bypass, replica_groups=rg,
                                    ins=[k_own.opt()], outs=[k_all.opt()]))
        cc.inc(g.collective_compute("AllGather", ALU.bypass, replica_groups=rg,
                                    ins=[v_own.opt()], outs=[v_all.opt()]))
    P.on("gpsimd", fcc)
    cc_mark = cc.mark

    stAT = contextlib.ExitStack()
    khead = [P.sb(f"khead{i}", (128, 2 * T_CORE), BF16, stAT, side="right") for i in range(2)]
    vhead = [P.sb(f"vhead{i}", (128, 32 * 65), BF16, stAT, side="right") for i in range(2)]
    qhead = [P.sb(f"qhead{i}", (64, T_CORE), BF16, stAT, side="right") for i in range(2)]
    mqv = [P.sb(f"mqv{i}", (64, 2), F32, stAT, side="right") for i in range(2)]
    gbias = [P.sb(f"gbias{i}", (128, 1), F32, stAT, side="right") for i in range(2)]
    g1 = P.sb("g1", (1, 1), F32, stAT, side="right")
    mtile = [P.sb(f"mtile{i}", (128, 512), F32, stAT, side="right") for i in range(2)]
    linv = P.sb("linv", (1, 512), F32, stAT, side="right")
    pt = [P.sb(f"pt{i}", (128, 512), BF16, stAT, side="right") for i in range(4)]
    otmp = P.sb("otmp", (64, 512), BF16, stAT, side="right")
    st_ps = [P.ps(f"st_ps{i}", (128, 512), stack=stAT) for i in range(4)]
    ot_ps = [P.ps(f"ot_ps{i}", (65, 512), stack=stAT) for i in range(2)]
    gg_ps = P.ps("gg_ps", (128, 2), stack=stAT)
    rep_ps = P.ps("rep_ps", (64, 512), stack=stAT)

    def ktile_loc(j):
        gk = j // 4
        return gk % 2, (gk // 2) * 512 + (j % 4) * 128

    ep_done = {}
    kvq = P.sem("at_kvq")
    slot_prev = {0: [], 1: []}
    st_guard = {}
    prev_mask_exp = None
    S0 = S.n
    exp_jobs = []
    def exp_pred(e):
        return (S, S0 + e + 1)
    gtile = 0
    mseq = 0
    pm_exp = {0: None, 1: None}
    for h in range(NH):
        hb = (h % 2) * 64
        hc = h // 2
        hs = h % 2
        kh, vh, qh, mq, gbi = khead[hs], vhead[hs], qhead[hs], mqv[hs], gbias[hs]
        prevm = list(slot_prev[hs])
        def fkv(sy, h=h, kh=kh, vh=vh, pm=prevm):
            wait_marks(sy, [cc_mark] + pm)
            va = v_all.rearrange("(r h t) d -> r h t d", r=2, h=NH)
            for r in range(2):
                kvq.inc(sy.dma_start(out=kh[0:64, r * T_CORE:(r + 1) * T_CORE],
                                     in_=k_all[r * D + 64 * h: r * D + 64 * h + 64, :]), 16)
                kvq.inc(sy.dma_start(
                    out=vh[:].rearrange("p (kt c) -> p kt c", c=65)[:, r * 16:(r + 1) * 16, 0:64],
                    in_=va[r, h].rearrange("(t p) c -> p t c", p=128)), 16)
        P.on("sync", fkv)
        kv_mark = kvq.mark
        def fvh1(v, kv_mark=kv_mark, h=h, hb=hb, hc=hc, kh=kh, vh=vh, qh=qh, mq=mq,
                 pm=prevm):
            wait_marks(v, [kv_mark] + pm)
            if h < 2:
                v.memset(vh[:].rearrange("p (kt c) -> p kt c", c=65)[:, :, 64:65], 1.0)
            V.inc(v.tensor_copy(qh[:], qT[hb:hb + 64, hc * T_CORE:(hc + 1) * T_CORE]))
            v.wait_ge(V.h, V.n)
            v.tensor_reduce(mq[:, 0:1], qh[:], mybir.AxisListType.X, ALU.max,
                            apply_absolute_value=True)
            V.inc(v.tensor_reduce(mq[:, 1:2], kh[0:64, :], mybir.AxisListType.X, ALU.max,
                                  apply_absolute_value=True))
        P.on("vector", fvh1)
        mq_mark = V.mark
        def fgb(t, mq_mark=mq_mark, mq=mq):
            wait_marks(t, [mq_mark])
            Tn.inc(t.matmul(gg_ps[0:1, 0:1], mq[:, 0:1], mq[:, 1:2], start=True, stop=True))
        P.on("tensor", fgb)
        gmm_mark = Tn.mark
        def fgb2(v, gmm_mark=gmm_mark):
            wait_marks(v, [gmm_mark])
            V.inc(v.tensor_scalar(g1[:], gg_ps[0:1, 0:1], -0.125, None, ALU.mult))
        P.on("vector", fgb2)
        g1_mark = V.mark
        def fgb3(t, g1_mark=g1_mark):
            wait_marks(t, [g1_mark])
            Tn.inc(t.matmul(gg_ps[:, 1:2], onesf[:], g1[:], start=True, stop=True))
        P.on("tensor", fgb3)
        gb_mark = Tn.mark
        def fgb4(v, gb_mark=gb_mark, gbi=gbi, pm=prevm):
            wait_marks(v, [gb_mark] + pm)
            V.inc(v.tensor_copy(gbi[:], gg_ps[:, 1:2]))
        P.on("vector", fgb4)
        gbias_mark = V.mark

        for si in range(4):
            KT = 8 * si + 8
            osel = si % 2
            prev_ep = ep_done.get((h, si - 2)) if si >= 2 else ep_done.get((h - 1, si + 2))
            pend_fav = None
            for j in range(KT):
                rk, loc = ktile_loc(j)
                dj = j - 8 * si
                pp = gtile % 4
                gtile += 1
                def fsc(t, si=si, j=j, rk=rk, loc=loc, pp=pp, gbias_mark=gbias_mark,
                        prev_ep=prev_ep, kh=kh, qh=qh, guard=st_guard.get(pp)):
                    wait_marks(t, [gbias_mark])
                    if guard is not None:
                        wait_marks(t, guard)
                    if j == 0 and prev_ep is not None:
                        wait_marks(t, [prev_ep])
                    Tn.inc(t.matmul(st_ps[pp][:],
                                    kh[0:64, rk * T_CORE + loc: rk * T_CORE + loc + 128],
                                    qh[:, si * 512:(si + 1) * 512], start=True, stop=True))
                P.on("tensor", fsc)
                sc_mark = Tn.mark
                if dj >= 0:
                    ms = mseq % 2
                    mseq += 1
                    def fmask(v, pp=pp, dj=dj, ms=ms, sc_mark=sc_mark, pme=pm_exp[ms]):
                        wait_marks(v, [sc_mark])
                        if pme is not None:
                            wait_marks(v, [pme])
                        V.inc(v.scalar_tensor_tensor(mtile[ms][:], st_ps[pp][:], 0.125,
                                                     mask_sb[:, dj * 512:(dj + 1) * 512],
                                                     ALU.mult, ALU.add))
                    P.on("vector", fmask)
                    exp_jobs.append((V.mark, True, pp, ms, gbi))
                    pm_exp[ms] = exp_pred(len(exp_jobs) - 1)
                else:
                    exp_jobs.append((sc_mark, False, pp, 0, gbi))
                exp_mark = exp_pred(len(exp_jobs) - 1)
                def fav(t, j=j, pp=pp, osel=osel, exp_mark=exp_mark, KT=KT, vh=vh):
                    wait_marks(t, [exp_mark])
                    Tn.inc(t.matmul(ot_ps[osel][:], vh[:, ktv(j) * 65: ktv(j) * 65 + 65],
                                    pt[pp][:], start=(j == 0), stop=(j == KT - 1)))
                if pend_fav is not None:
                    pf, ppp, pexp = pend_fav
                    P.on("tensor", pf)
                    st_guard[ppp] = [pexp, Tn.mark]
                pend_fav = (fav, pp, exp_mark)
            pf, ppp, pexp = pend_fav
            P.on("tensor", pf)
            st_guard[ppp] = [pexp, Tn.mark]
            av_mark = Tn.mark
            def fep1(v, osel=osel, av_mark=av_mark):
                wait_marks(v, [av_mark])
                V.inc(v.reciprocal(linv[:], ot_ps[osel][64:65, :]))
            P.on("vector", fep1)
            li_mark = V.mark
            def fep2(t, li_mark=li_mark):
                wait_marks(t, [li_mark])
                Tn.inc(t.matmul(rep_ps[:], onesf[0:1, 0:64], linv[:], start=True, stop=True))
            P.on("tensor", fep2)
            rep_mark = Tn.mark
            def fep3(v, osel=osel, rep_mark=rep_mark, hb=hb, hc=hc, si=si):
                wait_marks(v, [rep_mark])
                v.wait_ge(V.h, V.n)
                V.inc(v.tensor_copy(otmp[:], ot_ps[osel][0:64, :]))
                v.wait_ge(V.h, V.n)
                V.inc(v.tensor_mul(
                    attn_sb[hb:hb + 64, hc * T_CORE + si * 512: hc * T_CORE + si * 512 + 512],
                    otmp[:], rep_ps[:]))
            P.on("vector", fep3)
            ep_done[(h, si)] = V.mark
        slot_prev[hs] = [V.mark, Tn.mark, exp_pred(len(exp_jobs) - 1)]

    def fexps(sc, jobs=list(exp_jobs)):
        for wm, masked, pp, ms, gbi in jobs:
            wait_marks(sc, [wm])
            if masked:
                S.inc(sc.activation(pt[pp][:], mtile[ms][:], AF.Exp,
                                    bias=gbi[:, 0:1], scale=1.0))
            else:
                S.inc(sc.activation(pt[pp][:], st_ps[pp][:], AF.Exp,
                                    bias=gbi[:, 0:1], scale=0.125))
    P.on("scalar", fexps)
    assert S.n == S0 + len(exp_jobs)

    attn_done = [V.mark, Tn.mark, S.mark]
    stAT.close()

    stPJ = contextlib.ExitStack()
    wpj_sb = P.sb("wpj_sb", (128, NCH * 1024), BF16, stPJ, side="right")
    ptmp = P.sb("ptmp", (128, 512), F32, stPJ, side="right")
    xres = [P.sb(f"xres{i}", (128, 512), F32, stPJ, side="right") for i in range(2)]
    pj_ps = [P.ps(f"pj_ps{i}", (128, 512), stack=stPJ) for i in range(2)]
    wpj_r = wproj_d.rearrange("(c p) o -> p c o", p=128)
    def fpw(g):
        wait_marks(g, attn_done)
        gq.inc(g.dma_start(out=wpj_sb[:].rearrange("p (c o) -> p c o", c=NCH), in_=wpj_r), 16)
    P.on("gpsimd", fpw)
    pw_mark = gq.mark
    res_done = {}
    for oc in range(NCH):
        for tt in range(4):
            it = oc * 4 + tt
            pp = it % 2
            def fpm(t, oc=oc, tt=tt, pp=pp, it=it):
                wait_marks(t, [pw_mark] + attn_done)
                if it >= 2:
                    wait_marks(t, [res_done[it - 2]])
                for c in range(NCH):
                    i = t.matmul(pj_ps[pp][:], wpj_sb[:, c * 1024 + oc * 128: c * 1024 + (oc + 1) * 128],
                                 attn_sb[:, c * T_CORE + tt * 512: c * T_CORE + tt * 512 + 512],
                                 start=(c == 0), stop=(c == NCH - 1))
                Tn.inc(i)
            P.on("tensor", fpm)
            pm_mark = Tn.mark
            def fxld(sy, oc=oc, tt=tt, pp=pp, it=it):
                if it >= 2:
                    wait_marks(sy, [res_done[it - 2]])
                else:
                    wait_marks(sy, attn_done)
                xl[pp].inc(sy.dma_start(out=xres[pp][:],
                                        in_=x_in[oc * 128:(oc + 1) * 128, tt * 512:(tt + 1) * 512]), 16)
            P.on("sync", fxld)
            xld_mark = xl[pp].mark
            def fpr(v, oc=oc, pp=pp, pm_mark=pm_mark, xld_mark=xld_mark):
                wait_marks(v, [pm_mark, xld_mark])
                v.wait_ge(V.h, V.n)
                V.inc(v.tensor_scalar(ptmp[:], pj_ps[pp][:], bproj_sb[:, oc:oc + 1], None, ALU.add))
                v.wait_ge(V.h, V.n)
                V.inc(v.tensor_add(xres[pp][:], ptmp[:], xres[pp][:]))
            P.on("vector", fpr)
            wr_mark = V.mark
            def fxst(g, oc=oc, tt=tt, pp=pp, wr_mark=wr_mark):
                wait_marks(g, [wr_mark])
                xst[pp].inc(g.dma_start(out=x_out[oc * 128:(oc + 1) * 128, tt * 512:(tt + 1) * 512],
                                        in_=xres[pp][:]), 16)
            P.on("gpsimd", fxst)
            res_done[it] = xst[pp].mark
    fin = [xst[0].mark, xst[1].mark]
    G = P.E["gpsimd"]
    def ffin(g, fin=fin):
        wait_marks(g, fin)
        G.inc(g.nop())
    P.on("gpsimd", ffin)
    stPJ.close()
    stON.close()
    return G.mark


def emit_l0ffn(P, x_in, x_out, xln_sb, wg_d, wu_d, wd_d, waits):
    V, S, Tn = P.E["vector"], P.E["scalar"], P.E["tensor"]
    wgu = [P.sem(f"ffnw{i}") for i in range(2)]
    wds = P.sem("ffnwd")
    xl = [P.sem(f"ffnxl{i}") for i in range(2)]
    xst = [P.sem(f"ffnxst{i}") for i in range(2)]
    stk = contextlib.ExitStack()
    SW = 512
    NSTRIP = T_CORE // SW
    wgt = [P.sb(f"wg{i}", (128, NCH * 128), BF16, stk, side="right") for i in range(2)]
    wut = [P.sb(f"wu{i}", (128, NCH * 128), BF16, stk, side="right") for i in range(2)]
    wd_res = P.sb("wd_res", (128, HCH * 1024), BF16, stk, side="right")
    gbuf = P.sb("gbuf", (128, SW), BF16, stk, side="right")
    ubuf = P.sb("ubuf", (128, SW), BF16, stk, side="right")
    h_sb = P.sb("h_sb", (128, HCH * SW), BF16, stk, side="right")
    xres = [P.sb(f"fxres{i}", (128, SW), F32, stk, side="right") for i in range(2)]
    g_ps_ = [P.ps(f"gg_ps{i}", (128, SW), stack=stk) for i in range(2)]
    u_ps_ = [P.ps(f"uu_ps{i}", (128, SW), stack=stk) for i in range(2)]
    down_pp = [P.ps(f"down_pp{i}", (128, 512), stack=stk) for i in range(2)]

    wg_r = wg_d.rearrange("(c p) h -> p c h", p=128)
    wu_r = wu_d.rearrange("(c p) h -> p c h", p=128)
    wd_r = wd_d.rearrange("(c p) d -> p c d", p=128)
    wgu_base = [wgu[0].n, wgu[1].n]
    wds_base = wds.n

    def load_gu(g, it):
        hc = it % HCH
        s = wgu[it % 2]
        s.inc(g.dma_start(out=wgt[it % 2][:].rearrange("p (c k) -> p c k", c=NCH),
                          in_=wg_r[:, :, hc * 128:(hc + 1) * 128]), 16)
        s.inc(g.dma_start(out=wut[it % 2][:].rearrange("p (c k) -> p c k", c=NCH),
                          in_=wu_r[:, :, hc * 128:(hc + 1) * 128]), 16)

    def f0(g):
        wait_marks(g, waits)
        wds.inc(g.dma_start(out=wd_res[:].rearrange("p (c d) -> p c d", c=HCH), in_=wd_r), 16)
        load_gu(g, 0)
        load_gu(g, 1)
    P.on("gpsimd", f0)

    h_mark = {}
    res_mark = {}
    down_mark = {}
    for st_i in range(NSTRIP):
        c0 = st_i * SW
        for hc in range(HCH):
            it = st_i * HCH + hc
            slot = hc % 2
            def fmm(t, hc=hc, slot=slot, c0=c0, it=it):
                wait_marks(t, waits)
                t.wait_ge(wgu[it % 2].h, wgu_base[it % 2] + 32 * (it // 2 + 1))
                if it >= 2:
                    wait_marks(t, [h_mark[it - 2]])
                for c in range(NCH):
                    t.matmul(g_ps_[slot][:], wgt[slot][:, c * 128:(c + 1) * 128],
                             xln_sb[:, c * T_CORE + c0: c * T_CORE + c0 + SW],
                             start=(c == 0), stop=(c == NCH - 1))
                for c in range(NCH):
                    i = t.matmul(u_ps_[slot][:], wut[slot][:, c * 128:(c + 1) * 128],
                                 xln_sb[:, c * T_CORE + c0: c * T_CORE + c0 + SW],
                                 start=(c == 0), stop=(c == NCH - 1))
                Tn.inc(i)
            P.on("tensor", fmm)
            gu_mark = Tn.mark
            def fsilu(sc, gu_mark=gu_mark, slot=slot):
                wait_marks(sc, [gu_mark])
                S.inc(sc.activation(gbuf[:], g_ps_[slot][:], AF.Silu, bias=0.0, scale=1.0 / 127.0))
            P.on("scalar", fsilu)
            silu_mark = S.mark
            def fup(v, silu_mark=silu_mark, gu_mark=gu_mark, hc=hc, st_i=st_i, slot=slot):
                wait_marks(v, [gu_mark])
                if st_i >= 1:
                    wait_marks(v, [down_mark[st_i - 1]])
                v.wait_ge(V.h, V.n)
                V.inc(v.tensor_scalar(ubuf[:], u_ps_[slot][:], 1.0 / 127.0, None, ALU.mult))
                wait_marks(v, [silu_mark])
                v.wait_ge(V.h, V.n)
                V.inc(v.tensor_mul(h_sb[:, hc * SW:(hc + 1) * SW], gbuf[:], ubuf[:]))
            P.on("vector", fup)
            h_mark[it] = V.mark
            if it + 2 < NSTRIP * HCH:
                def fpre(g, it=it, gu_mark=gu_mark):
                    wait_marks(g, [gu_mark])
                    load_gu(g, it + 2)
                P.on("gpsimd", fpre)
        h_all = V.mark
        for dd in range(NCH):
            bank = down_pp[dd % 2]
            it2 = st_i * NCH + dd
            pp = it2 % 2
            def fdown(t, dd=dd, bank=bank, h_all=h_all, st_i=st_i, it2=it2):
                wait_marks(t, [h_all])
                t.wait_ge(wds.h, wds_base + 16)
                if it2 >= 2:
                    wait_marks(t, [res_mark[it2 - 2]])
                for hc in range(HCH):
                    i = t.matmul(bank[:, 0:SW],
                                 wd_res[:, hc * 1024 + dd * 128: hc * 1024 + (dd + 1) * 128],
                                 h_sb[:, hc * SW:(hc + 1) * SW],
                                 start=(hc == 0), stop=(hc == HCH - 1))
                Tn.inc(i)
            P.on("tensor", fdown)
            dd_mark = Tn.mark
            def fxld(sy, dd=dd, c0=c0, pp=pp, it2=it2, h_all=h_all):
                if it2 >= 2:
                    wait_marks(sy, [res_mark[it2 - 2]])
                else:
                    wait_marks(sy, [h_all])
                xl[pp].inc(sy.dma_start(out=xres[pp][:],
                                        in_=x_in[dd * 128:(dd + 1) * 128, c0:c0 + SW]), 16)
            P.on("sync", fxld)
            xld_mark = xl[pp].mark
            def fres(v, bank=bank, pp=pp, dd_mark=dd_mark, xld_mark=xld_mark):
                wait_marks(v, [dd_mark, xld_mark])
                V.inc(v.scalar_tensor_tensor(xres[pp][:], bank[:, 0:SW], 0.5 / 127.0,
                                             xres[pp][:], ALU.mult, ALU.add))
            P.on("vector", fres)
            wr_mark = V.mark
            def fxst(g, dd=dd, c0=c0, pp=pp, wr_mark=wr_mark):
                wait_marks(g, [wr_mark])
                xst[pp].inc(g.dma_start(out=x_out[dd * 128:(dd + 1) * 128, c0:c0 + SW],
                                        in_=xres[pp][:]), 16)
            P.on("gpsimd", fxst)
            res_mark[it2] = xst[pp].mark
        down_mark[st_i] = Tn.mark
    fin = [xst[0].mark, xst[1].mark]
    G = P.E["gpsimd"]
    def ffin(g, fin=fin):
        wait_marks(g, fin)
        G.inc(g.nop())
    P.on("gpsimd", ffin)
    stk.close()
    return G.mark




def emit_flag(P, scl):
    V, Tn = P.E["vector"], P.E["tensor"]
    fl = P.sb("flags", (1, 8), F32, side="left")
    on1 = P.sb("flag_on1", (1, 128), F32, side="left")
    stF = contextlib.ExitStack()
    bc_ps = P.ps("flag_bc", (128, 4), stack=stF)
    def f(v):
        v.wait_ge(V.h, V.n)
        V.inc(v.memset(fl[0:1, 0:1], 1.0))
        v.wait_ge(V.h, V.n)
        V.inc(v.tensor_scalar(fl[0:1, 1:2], fl[0:1, 0:1], float(np.float32(2.0 ** -30)), 1.0,
                              ALU.add, ALU.subtract))
        v.wait_ge(V.h, V.n)
        V.inc(v.tensor_scalar(fl[0:1, 1:2], fl[0:1, 1:2], float(np.float32(2.0 ** 30)), None,
                              ALU.mult))
        v.wait_ge(V.h, V.n)
        V.inc(v.tensor_scalar(fl[0:1, 2:3], fl[0:1, 1:2], -1.0, 1.0, ALU.mult, ALU.add))
        v.wait_ge(V.h, V.n)
        V.inc(v.memset(fl[0:1, 3:4], 32767.0))
        V.inc(v.memset(fl[0:1, 4:5], 7.0))
        V.inc(v.memset(on1[:], 1.0))
        v.wait_ge(V.h, V.n)
        V.inc(v.reciprocal(fl[0:1, 3:4], fl[0:1, 3:4]))
        v.wait_ge(V.h, V.n)
        V.inc(v.reciprocal(fl[0:1, 4:5], fl[0:1, 4:5]))
    P.on("vector", f)
    fm = V.mark
    def fb(t, fm=fm):
        wait_marks(t, [fm])
        Tn.inc(t.matmul(bc_ps[:], on1[:], fl[0:1, 1:5], start=True, stop=True))
    P.on("tensor", fb)
    bm = Tn.mark
    def fc(v, bm=bm):
        wait_marks(v, [bm])
        v.wait_ge(V.h, V.n)
        V.inc(v.tensor_copy(scl[:, 20:24], bc_ps[:]))
    P.on("vector", fc)
    stF.close()
    return fl, on1, V.mark


def emit_quant_tile(v, V, src, dst, inv_ap, pa_ap, pb_ap, y, r2, pool=False):
    v.wait_ge(V.h, V.n)
    V.inc(v.tensor_scalar(y, src, inv_ap, C1, ALU.mult, ALU.add))
    v.wait_ge(V.h, V.n)
    V.inc(v.tensor_scalar(r2, y, C2, C2, ALU.add, ALU.subtract))
    if pool:
        V.inc(v.tensor_scalar(y, y, C1, None, ALU.subtract))
        v.wait_ge(V.h, V.n)
        V.inc(v.tensor_scalar(y, y, pa_ap, None, ALU.mult))
        v.wait_ge(V.h, V.n)
        V.inc(v.tensor_scalar(r2, r2, C1, None, ALU.subtract))
        v.wait_ge(V.h, V.n)
        V.inc(v.tensor_scalar(r2, r2, pb_ap, None, ALU.mult))
    else:
        V.inc(v.tensor_scalar(y, y, C1, pa_ap, ALU.subtract, ALU.mult))
        v.wait_ge(V.h, V.n)
        V.inc(v.tensor_scalar(r2, r2, C1, pb_ap, ALU.subtract, ALU.mult))
    v.wait_ge(V.h, V.n)
    V.inc(v.tensor_add(dst, y, r2))


def emit_lpbq_dequant(P, tag, w_d, wdeq_d, wdeqT_d, O, I, fl, fl_mark, waits):
    V = P.E["vector"]
    gl = [P.sem("lpl" + tag + str(i)) for i in range(2)]
    gs = [P.sem("lps" + tag + str(i)) for i in range(2)]
    tq = P.sem("lpt" + tag)
    gt_ = P.sem("lpu" + tag)
    stk = contextlib.ExitStack()
    NB = I // 32
    wtile = [P.sb("lpw" + tag + str(i), (128, I), F32, stk, side="right") for i in range(2)]
    mx = P.sb("lpmx" + tag, (128, NB), F32, stk, side="right")
    sblk = P.sb("lps" + tag, (128, NB), F32, stk, side="right")
    iblk = P.sb("lpi" + tag, (128, NB), F32, stk, side="right")
    t1 = P.sb("lpt1" + tag, (128, I), F32, stk, side="right")
    r1 = P.sb("lpr1" + tag, (128, I), F32, stk, side="right")
    r2 = P.sb("lpr2" + tag, (128, I), F32, stk, side="right")
    wq = [P.sb("lpwq" + tag + str(i), (128, I), BF16, stk, side="right") for i in range(2)]
    ttile = P.sb("lptt" + tag, (128, 1024), BF16, stk, side="right")
    NC_ = O // 128
    st_mark = {}
    q_mark = {}
    for rc in range(NC_):
        wt = wtile[rc % 2]
        def fld(g, rc=rc, wt=wt):
            wait_marks(g, waits)
            if rc >= 2:
                wait_marks(g, [q_mark[rc - 2]])
            gl[rc % 2].inc(g.dma_start(out=wt[:], in_=w_d[rc * 128:(rc + 1) * 128, :]), 16)
        P.on("gpsimd", fld)
        ld_mark = gl[rc % 2].mark
        def fq(v, rc=rc, wt=wt, ld_mark=ld_mark):
            wait_marks(v, [ld_mark, fl_mark])
            if rc >= 2:
                wait_marks(v, [st_mark[rc - 2]])
            v.wait_ge(V.h, V.n)
            V.inc(v.tensor_reduce(mx[:], wt[:].rearrange("p (b k) -> p b k", k=32),
                                  mybir.AxisListType.X, ALU.max, apply_absolute_value=True))
            v.wait_ge(V.h, V.n)
            V.inc(v.tensor_scalar(sblk[:], mx[:], P._scl[:, 23:24], 1e-12, ALU.mult, ALU.max))
            v.wait_ge(V.h, V.n)
            V.inc(v.reciprocal(iblk[:], sblk[:]))
            v.wait_ge(V.h, V.n)
            V.inc(v.tensor_mul(t1[:].rearrange("p (b k) -> p b k", k=32),
                               wt[:].rearrange("p (b k) -> p b k", k=32),
                               iblk[:].broadcast_to((128, NB, 32))))
            v.wait_ge(V.h, V.n)
            V.inc(v.tensor_scalar(r1[:], t1[:], C1, C1, ALU.add, ALU.subtract))
            V.inc(v.tensor_scalar(r2[:], t1[:], C2, C2, ALU.add, ALU.subtract))
            v.wait_ge(V.h, V.n)
            V.inc(v.tensor_scalar(r1[:], r1[:], scl_flagm(P), None, ALU.mult))
            v.wait_ge(V.h, V.n)
            V.inc(v.tensor_scalar(r2[:], r2[:], scl_flag(P), None, ALU.mult))
            v.wait_ge(V.h, V.n)
            V.inc(v.tensor_add(r1[:], r1[:], r2[:]))
            v.wait_ge(V.h, V.n)
            V.inc(v.tensor_mul(wq[rc % 2][:].rearrange("p (b k) -> p b k", k=32),
                               r1[:].rearrange("p (b k) -> p b k", k=32),
                               sblk[:].broadcast_to((128, NB, 32))))
            q_mark[rc] = V.mark
        P.on("vector", fq)
        def fst(g, rc=rc):
            wait_marks(g, [q_mark[rc]])
            gs[rc % 2].inc(g.dma_start(out=wdeq_d[rc * 128:(rc + 1) * 128, :], in_=wq[rc % 2][:]), 16)
        P.on("gpsimd", fst)
        st_mark[rc] = gs[rc % 2].mark
    all_st = [gs[0].mark, gs[1].mark]
    tr_mark = {}
    for dc in range(I // 128):
        for og in range(O // 1024):
            it = dc * (O // 1024) + og
            def ftr(sc, dc=dc, og=og, it=it, all_st=all_st):
                wait_marks(sc, all_st)
                if it >= 1:
                    wait_marks(sc, [tr_mark[it - 1]])
                tq.inc(sc.dma_start_transpose(ttile[:, 0:1024],
                                              wdeq_d[og * 1024:(og + 1) * 1024,
                                                     dc * 128:(dc + 1) * 128]), 16)
            P.on("scalar", ftr)
            ld2 = tq.mark
            def fst2(g, dc=dc, og=og, ld2=ld2):
                wait_marks(g, [ld2])
                gt_.inc(g.dma_start(out=wdeqT_d[dc * 128:(dc + 1) * 128, og * 1024:(og + 1) * 1024],
                                    in_=ttile[:, 0:1024]), 16)
            P.on("gpsimd", fst2)
            tr_mark[it] = gt_.mark
    fin = gt_.mark
    stk.close()
    return fin


def scl_flag(P):
    return P._scl[:, 20:21]


def scl_flagm(P):
    return P._scl[:, 21:22]


def emit_cross_part_max(P, tag, maxbuf_ap, src_mark, amax_row, stack):
    V = P.E["vector"]
    gx = P.sem("cpm" + tag)
    col = P.sb("cpm_c" + tag, (128, 1), F32, stack, side="right")
    row = P.sb("cpm_r" + tag, (1, 128), F32, stack, side="right")
    def f1(v, src_mark=src_mark):
        wait_marks(v, src_mark if isinstance(src_mark, list) else [src_mark])
        v.wait_ge(V.h, V.n)
        V.inc(v.tensor_reduce(col[:], maxbuf_ap, mybir.AxisListType.X, ALU.max))
    P.on("vector", f1)
    m1 = V.mark
    def f2(g, m1=m1):
        wait_marks(g, [m1])
        gx.inc(g.dma_start(out=row[0:1, :], in_=col[:, 0:1]), 16)
    P.on("gpsimd", f2)
    m2 = gx.mark
    def f3(v, m2=m2):
        wait_marks(v, [m2])
        v.wait_ge(V.h, V.n)
        V.inc(v.tensor_reduce(amax_row[0:1, 0:1], row[0:1, :], mybir.AxisListType.X, ALU.max))
    P.on("vector", f3)
    return V.mark


def emit_qdq_scalars(P, tag, amax_mark, amax_row, fl, on1, flbc_mark, scl, col0, stack):
    V, Tn = P.E["vector"], P.E["tensor"]
    gq = P.sem("qsg" + tag)
    cc = P.sem("qsc" + tag)
    cc_in = P.dram_tmp("qs_in" + tag, (1, 128), F32)
    cc_out = P.dram_tmp("qs_out" + tag, (1, 128), F32)
    row = P.sb("qsr" + tag, (1, 8), F32, stack, side="right")
    zr = P.sb("qsz" + tag, (1, 128), F32, stack, side="right")
    bc_ps = P.ps("qsb" + tag, (128, 4), stack=stack)
    def fz(v, amax_mark=amax_mark):
        wait_marks(v, [amax_mark])
        v.wait_ge(V.h, V.n)
        V.inc(v.memset(zr[:], 0.0))
        v.wait_ge(V.h, V.n)
        V.inc(v.tensor_copy(zr[0:1, 0:1], amax_row[0:1, 0:1]))
    P.on("vector", fz)
    z_mark = V.mark
    def fst(g, z_mark=z_mark):
        wait_marks(g, [z_mark])
        gq.inc(g.dma_start(out=cc_in, in_=zr[:]), 16)
        g.wait_ge(gq.h, gq.n)
        cc.inc(g.collective_compute("AllReduce", ALU.max,
                                    replica_groups=[list(range(8))],
                                    ins=[cc_in.opt()], outs=[cc_out.opt()]))
        g.wait_ge(cc.h, cc.n)
        gq.inc(g.dma_start(out=row[0:1, 0:1], in_=cc_out[0:1, 0:1]), 16)
    P.on("gpsimd", fst)
    ar_mark = gq.mark
    def fsc(v, ar_mark=ar_mark):
        wait_marks(v, [ar_mark, flbc_mark])
        v.wait_ge(V.h, V.n)
        V.inc(v.tensor_scalar(row[0:1, 1:2], row[0:1, 0:1], fl[0:1, 3:4], 1e-12,
                              ALU.mult, ALU.max))
        v.wait_ge(V.h, V.n)
        V.inc(v.reciprocal(row[0:1, 2:3], row[0:1, 1:2]))
        v.wait_ge(V.h, V.n)
        V.inc(v.tensor_mul(row[0:1, 3:4], row[0:1, 1:2], fl[0:1, 2:3]))
        v.wait_ge(V.h, V.n)
        V.inc(v.tensor_mul(row[0:1, 4:5], row[0:1, 1:2], fl[0:1, 1:2]))
        v.wait_ge(V.h, V.n)
        V.inc(v.tensor_copy(row[0:1, 5:6], row[0:1, 1:2]))
    P.on("vector", fsc)
    sc_mark = V.mark
    def fbc(t, sc_mark=sc_mark):
        wait_marks(t, [sc_mark])
        Tn.inc(t.matmul(bc_ps[:], on1[:], row[0:1, 2:6], start=True, stop=True))
    P.on("tensor", fbc)
    bc_mark = Tn.mark
    def fcp(v, bc_mark=bc_mark):
        wait_marks(v, [bc_mark])
        v.wait_ge(V.h, V.n)
        V.inc(v.tensor_copy(scl[:, col0:col0 + 4], bc_ps[:]))
    P.on("vector", fcp)
    return V.mark


def emit_l1ffn(P, x_in, x_out, wg_d, wu_d, wd_d, waits, deq_cache=None):
    V, S, Tn = P.E["vector"], P.E["scalar"], P.E["tensor"]
    scl = P.sb("scl", (128, 24), F32, side="left")
    P._scl = scl
    fl, on1, fl_mark = emit_flag(P, scl)

    up_pre = P.dram_tmp("up_pre", (HID, T_CORE), BF16)
    gate_pre = P.dram_tmp("gate_pre", (HID, T_CORE), BF16)
    t_pre = P.dram_tmp("t_pre", (HID, T_CORE), BF16)
    o_pre = P.dram_tmp("o_pre", (HID, T_CORE), BF16)

    if deq_cache is None:
        wdeq_u = P.dram_tmp("wdeq_u", (HID, D), BF16)
        wdeq_g = P.dram_tmp("wdeq_g", (HID, D), BF16)
        wdeq_d2 = P.dram_tmp("wdeq_d", (D, HID), BF16)
        wdqT_u = P.dram_tmp("wdqT_u", (D, HID), BF16)
        wdqT_g = P.dram_tmp("wdqT_g", (D, HID), BF16)
        wdqT_d = P.dram_tmp("wdqT_d", (HID, D), BF16)
        dq_u = emit_lpbq_dequant(P, "u", wu_d, wdeq_u, wdqT_u, HID, D, fl, fl_mark, waits)
        dq_g = emit_lpbq_dequant(P, "g", wg_d, wdeq_g, wdqT_g, HID, D, fl, fl_mark,
                                 waits + [dq_u])
        dq_d = emit_lpbq_dequant(P, "d", wd_d, wdeq_d2, wdqT_d, D, HID, fl, fl_mark,
                                 waits + [dq_g])
    else:
        wdqT_u, wdqT_g, wdqT_d = deq_cache
        dq_u = dq_g = dq_d = fl_mark
    P._deq_cache = (wdqT_u, wdqT_g, wdqT_d)

    stLN = contextlib.ExitStack()
    xln_sb = P.sb("xln_sb", (128, NCH * T_CORE), BF16, stLN, side="right")
    xq_sb = P.sb("xq_sb", (128, NCH * T_CORE), BF16, stLN, side="right")
    ln_mark = emit_ln_stream(P, "L1", x_in, xln_sb, waits + [dq_d])
    mbx = P.sb("mbx", (128, NCH), F32, stLN, side="right")
    amax_row = P.sb("amaxr", (1, 4), F32, side="left")
    def fmx(v, ln_mark=ln_mark):
        wait_marks(v, [ln_mark])
        for c in range(NCH):
            v.wait_ge(V.h, V.n)
            V.inc(v.tensor_reduce(mbx[:, c:c + 1], xln_sb[:, c * T_CORE:(c + 1) * T_CORE],
                                  mybir.AxisListType.X, ALU.max, apply_absolute_value=True))
    P.on("vector", fmx)
    xmx_mark = emit_cross_part_max(P, "x", mbx[:], V.mark, amax_row, stLN)
    sx_mark = emit_qdq_scalars(P, "x", xmx_mark, amax_row, fl, on1, fl_mark, scl, 0, stLN)
    qy = P.sb("qy", (128, 512), F32, stLN, side="right")
    qr2 = P.sb("qr2", (128, 512), F32, stLN, side="right")
    def fqx(v, sx_mark=sx_mark):
        wait_marks(v, [sx_mark])
        for c in range(NCH):
            for tt in range(4):
                sl = slice(c * T_CORE + tt * 512, c * T_CORE + tt * 512 + 512)
                emit_quant_tile(v, V, xln_sb[:, sl], xq_sb[:, sl],
                                scl[:, 0:1], scl[:, 1:2], scl[:, 2:3], qy[:], qr2[:])
    P.on("vector", fqx)
    xq_mark = V.mark

    stA = contextlib.ExitStack()
    SW = 512
    NSTRIP = T_CORE // SW
    wub = [P.sb(f"l1wu{i}", (128, NCH * 128), BF16, stA, side="right") for i in range(2)]
    wgb = [P.sb(f"l1wg{i}", (128, NCH * 128), BF16, stA, side="right") for i in range(2)]
    uev = [P.sb(f"uev{i}", (128, SW), BF16, stA, side="right") for i in range(2)]
    gev = [P.sb(f"gev{i}", (128, SW), BF16, stA, side="right") for i in range(2)]
    mbu = P.sb("mbu", (128, HCH * NSTRIP), F32, stA, side="right")
    mbg = P.sb("mbg", (128, HCH * NSTRIP), F32, stA, side="right")
    ua_ps = [P.ps(f"ua_ps{i}", (128, 512), stack=stA) for i in range(2)]
    ga_ps = [P.ps(f"ga_ps{i}", (128, 512), stack=stA) for i in range(2)]
    wA = [P.sem(f"l1wa{i}") for i in range(2)]
    sA = [P.sem(f"l1sa{i}") for i in range(2)]
    wA_base = [wA[0].n, wA[1].n]
    wu_r = wdqT_u.rearrange("(c p) h -> p c h", p=128)
    wg_r = wdqT_g.rearrange("(c p) h -> p c h", p=128)

    def loadA(g, it):
        hc = it % HCH
        s = wA[it % 2]
        s.inc(g.dma_start(out=wub[it % 2][:].rearrange("p (c k) -> p c k", c=NCH),
                          in_=wu_r[:, :, hc * 128:(hc + 1) * 128]), 16)
        s.inc(g.dma_start(out=wgb[it % 2][:].rearrange("p (c k) -> p c k", c=NCH),
                          in_=wg_r[:, :, hc * 128:(hc + 1) * 128]), 16)

    def fA0(g, dq=[dq_u, dq_g]):
        wait_marks(g, dq)
        loadA(g, 0)
        loadA(g, 1)
    P.on("gpsimd", fA0)

    ev_mark = {}
    stq_mark = {}
    for st_i in range(NSTRIP):
        c0 = st_i * SW
        for hc in range(HCH):
            it = st_i * HCH + hc
            slot = it % 2
            def fmm(t, hc=hc, slot=slot, c0=c0, it=it):
                wait_marks(t, [xq_mark])
                t.wait_ge(wA[slot].h, wA_base[slot] + 32 * (it // 2 + 1))
                if it >= 2:
                    wait_marks(t, [ev_mark[it - 2]])
                for c in range(NCH):
                    t.matmul(ua_ps[slot][:], wub[slot][:, c * 128:(c + 1) * 128],
                             xq_sb[:, c * T_CORE + c0: c * T_CORE + c0 + SW],
                             start=(c == 0), stop=(c == NCH - 1))
                for c in range(NCH):
                    i = t.matmul(ga_ps[slot][:], wgb[slot][:, c * 128:(c + 1) * 128],
                                 xq_sb[:, c * T_CORE + c0: c * T_CORE + c0 + SW],
                                 start=(c == 0), stop=(c == NCH - 1))
                Tn.inc(i)
            P.on("tensor", fmm)
            mm_mark = Tn.mark
            def fev(v, hc=hc, slot=slot, it=it, mm_mark=mm_mark):
                wait_marks(v, [mm_mark])
                if it >= 2:
                    wait_marks(v, [stq_mark[it - 2]])
                v.wait_ge(V.h, V.n)
                V.inc(v.tensor_copy(uev[slot][:], ua_ps[slot][:]))
                V.inc(v.tensor_copy(gev[slot][:], ga_ps[slot][:]))
                v.wait_ge(V.h, V.n)
                V.inc(v.tensor_reduce(mbu[:, it:it + 1], uev[slot][:], mybir.AxisListType.X,
                                      ALU.max, apply_absolute_value=True))
                V.inc(v.tensor_reduce(mbg[:, it:it + 1], gev[slot][:], mybir.AxisListType.X,
                                      ALU.max, apply_absolute_value=True))
            P.on("vector", fev)
            ev_mark[it] = V.mark
            def fstq(g, hc=hc, slot=slot, c0=c0, ev=ev_mark[it]):
                wait_marks(g, [ev])
                sA[slot].inc(g.dma_start(out=up_pre[hc * 128:(hc + 1) * 128, c0:c0 + SW],
                                         in_=uev[slot][:]), 16)
                sA[slot].inc(g.dma_start(out=gate_pre[hc * 128:(hc + 1) * 128, c0:c0 + SW],
                                         in_=gev[slot][:]), 16)
            P.on("gpsimd", fstq)
            stq_mark[it] = sA[slot].mark
            nxt = it + 2
            if nxt < NSTRIP * HCH:
                def fpre(g, nxt=nxt, mm_mark=mm_mark):
                    wait_marks(g, [mm_mark])
                    loadA(g, nxt)
                P.on("gpsimd", fpre)
    spillA = [sA[0].mark, sA[1].mark]
    umx_mark = emit_cross_part_max(P, "u", mbu[:], ev_mark[NSTRIP * HCH - 1], amax_row, stA)
    su_mark = emit_qdq_scalars(P, "u", umx_mark, amax_row, fl, on1, fl_mark, scl, 4, stA)
    gmx_mark = emit_cross_part_max(P, "g", mbg[:], su_mark, amax_row, stA)
    sg_mark = emit_qdq_scalars(P, "g", gmx_mark, amax_row, fl, on1, fl_mark, scl, 8, stA)
    stA.close()
    stLN.close()

    stB = contextlib.ExitStack()
    G = P.E["gpsimd"]
    gt = [P.sb(f"bgt{i}", (128, 512), BF16, stB, side="right") for i in range(4)]
    btv = [P.sb(f"btv{i}", (128, 512), BF16, stB, side="right") for i in range(4)]
    br = [P.sb(f"br{i}", (128, 512), F32, stB, side="right") for i in range(2)]
    by = [P.sb(f"by{i}", (128, 512), F32, stB, side="right") for i in range(2)]
    br2 = [P.sb(f"br2{i}", (128, 512), F32, stB, side="right") for i in range(2)]
    bsig = [P.sb(f"bsig{i}", (128, 512), F32, stB, side="right") for i in range(2)]
    bsq = [P.sb(f"bsq{i}", (128, 512), F32, stB, side="right") for i in range(2)]
    mbt = P.sb("mbt", (128, HCH * 4), F32, stB, side="right")
    lB = [P.sem(f"l1lb{i}") for i in range(2)]
    sB = [P.sem(f"l1sb{i}") for i in range(2)]
    NTB = HCH * 4
    bq_mark = {}
    bst_mark = {}
    bred_mark = {}
    SB0 = S.n
    sig_jobs = []
    for ti in range(NTB):
        hc = ti // 4
        tc = ti % 4
        slot = ti % 4
        lane = ti % 2
        eng = "vector" if lane == 0 else "gpsimd"
        E = V if lane == 0 else G
        def fbl(sy, hc=hc, tc=tc, slot=slot, ti=ti, sg_mark=sg_mark, spillA=spillA):
            wait_marks(sy, [sg_mark] + spillA)
            if ti >= 4:
                wait_marks(sy, [bq_mark[ti - 4]])
            lB[ti % 2].inc(sy.dma_start(out=gt[slot][:],
                                      in_=gate_pre[hc * 128:(hc + 1) * 128,
                                                   tc * 512:(tc + 1) * 512]), 16)
        P.on("sync", fbl)
        ld_mark = lB[lane].mark
        def fbq(v, slot=slot, lane=lane, ti=ti, ld_mark=ld_mark, E=E):
            wait_marks(v, [ld_mark])
            emit_quant_tile(v, E, gt[slot][:], br[lane][:], scl[:, 8:9], scl[:, 21:22],
                            scl[:, 20:21], by[lane][:], br2[lane][:], pool=(lane == 1))
        P.on(eng, fbq)
        sig_jobs.append((E.mark, lane))
        sig_mark = (S, SB0 + ti + 1)
        def fbq2(v, slot=slot, lane=lane, ti=ti, sig_mark=sig_mark, E=E):
            wait_marks(v, [sig_mark])
            if ti >= 4:
                wait_marks(v, [bst_mark[ti - 4]])
                if ti - 4 in bred_mark:
                    wait_marks(v, [bred_mark[ti - 4]])
            v.wait_ge(E.h, E.n)
            E.inc(v.tensor_scalar(by[lane][:], bsig[lane][:], 65536.0, C1, ALU.mult, ALU.add))
            v.wait_ge(E.h, E.n)
            E.inc(v.tensor_scalar(br2[lane][:], by[lane][:], C2, C2, ALU.add, ALU.subtract))
            if lane == 1:
                E.inc(v.tensor_scalar(by[lane][:], by[lane][:], C1, None, ALU.subtract))
                v.wait_ge(E.h, E.n)
                E.inc(v.tensor_scalar(by[lane][:], by[lane][:], scl[:, 21:22], None, ALU.mult))
                v.wait_ge(E.h, E.n)
                E.inc(v.tensor_scalar(br2[lane][:], br2[lane][:], C1, None, ALU.subtract))
                v.wait_ge(E.h, E.n)
                E.inc(v.tensor_scalar(br2[lane][:], br2[lane][:], scl[:, 20:21], None, ALU.mult))
            else:
                E.inc(v.tensor_scalar(by[lane][:], by[lane][:], C1, scl[:, 21:22],
                                      ALU.subtract, ALU.mult))
                v.wait_ge(E.h, E.n)
                E.inc(v.tensor_scalar(br2[lane][:], br2[lane][:], C1, scl[:, 20:21],
                                      ALU.subtract, ALU.mult))
            v.wait_ge(E.h, E.n)
            E.inc(v.tensor_add(bsq[lane][:], by[lane][:], br2[lane][:]))
            v.wait_ge(E.h, E.n)
            E.inc(v.tensor_scalar(bsq[lane][:], bsq[lane][:], 65535.0, float(1.0 / 65536.0),
                                  ALU.min, ALU.mult))
            v.wait_ge(E.h, E.n)
            if lane == 1:
                E.inc(v.tensor_scalar(br[lane][:], br[lane][:], scl[:, 11:12], None, ALU.mult))
                v.wait_ge(E.h, E.n)
                E.inc(v.tensor_mul(by[lane][:], br[lane][:], bsq[lane][:]))
            else:
                E.inc(v.scalar_tensor_tensor(by[lane][:], br[lane][:], scl[:, 11:12],
                                             bsq[lane][:], ALU.mult, ALU.mult))
            v.wait_ge(E.h, E.n)
            E.inc(v.tensor_copy(btv[slot][:], by[lane][:]))
            if lane == 0:
                v.wait_ge(E.h, E.n)
                E.inc(v.tensor_reduce(mbt[:, ti:ti + 1], btv[slot][:], mybir.AxisListType.X,
                                      ALU.max, apply_absolute_value=True))
        P.on(eng, fbq2)
        bq_mark[ti] = E.mark
        if lane == 1:
            def fbred(v, ti=ti, slot=slot, m=bq_mark[ti]):
                wait_marks(v, [m])
                v.wait_ge(V.h, V.n)
                V.inc(v.tensor_reduce(mbt[:, ti:ti + 1], btv[slot][:], mybir.AxisListType.X,
                                      ALU.max, apply_absolute_value=True))
            P.on("vector", fbred)
            bred_mark[ti] = V.mark
        def fbst(sy, hc=hc, tc=tc, slot=slot, m=bq_mark[ti]):
            wait_marks(sy, [m])
            sB[slot % 2].inc(sy.dma_start(out=t_pre[hc * 128:(hc + 1) * 128, tc * 512:(tc + 1) * 512],
                                          in_=btv[slot][:]), 16)
        P.on("sync", fbst)
        bst_mark[ti] = sB[lane].mark
    def fsigs(sc, jobs=list(sig_jobs)):
        for rm, lane in jobs:
            wait_marks(sc, [rm])
            S.inc(sc.activation(bsig[lane][:], br[lane][:], AF.Sigmoid,
                                bias=0.0, scale=scl[:, 11:12]))
    P.on("scalar", fsigs)
    assert S.n == SB0 + NTB
    spillB = [sB[i].mark for i in range(2)]
    tmx_mark = emit_cross_part_max(P, "t", mbt[:], [bq_mark[NTB - 2], bred_mark[NTB - 1]],
                                   amax_row, stB)
    st_mark2 = emit_qdq_scalars(P, "t", tmx_mark, amax_row, fl, on1, fl_mark, scl, 12, stB)
    stB.close()

    stC = contextlib.ExitStack()
    ct = [P.sb(f"cct{i}", (128, 512), BF16, stC, side="right") for i in range(4)]
    cu = [P.sb(f"ccu{i}", (128, 512), BF16, stC, side="right") for i in range(4)]
    cr = [P.sb(f"cr{i}", (128, 512), F32, stC, side="right") for i in range(2)]
    cru = [P.sb(f"cru{i}", (128, 512), F32, stC, side="right") for i in range(2)]
    cy = [P.sb(f"cy{i}", (128, 512), F32, stC, side="right") for i in range(2)]
    cr2 = [P.sb(f"cr2{i}", (128, 512), F32, stC, side="right") for i in range(2)]
    cov = [P.sb(f"cov{i}", (128, 512), BF16, stC, side="right") for i in range(4)]
    mbo = P.sb("mbo", (128, HCH * 4), F32, stC, side="right")
    stu = P.sb("stu", (128, 1), F32, stC, side="right")
    lC = [P.sem(f"l1lc{i}") for i in range(2)]
    sC = [P.sem(f"l1sc{i}") for i in range(2)]
    def fstu(v, m=st_mark2):
        wait_marks(v, [m])
        v.wait_ge(V.h, V.n)
        V.inc(v.tensor_mul(stu[:], scl[:, 15:16], scl[:, 7:8]))
    P.on("vector", fstu)
    stu_mark = V.mark
    cq_mark = {}
    cst_mark = {}
    cred_mark = {}
    for ti in range(NTB):
        hc = ti // 4
        tc = ti % 4
        slot = ti % 4
        lane = ti % 2
        eng = "vector" if lane == 0 else "gpsimd"
        E = V if lane == 0 else G
        def fcl(sy, hc=hc, tc=tc, slot=slot, ti=ti, stu_mark=stu_mark, spillB=spillB):
            wait_marks(sy, [stu_mark] + spillB)
            if ti >= 4:
                wait_marks(sy, [cq_mark[ti - 4]])
            lC[ti % 2].inc(sy.dma_start(out=ct[slot][:],
                                        in_=t_pre[hc * 128:(hc + 1) * 128, tc * 512:(tc + 1) * 512]), 16)
            lC[ti % 2].inc(sy.dma_start(out=cu[slot][:],
                                        in_=up_pre[hc * 128:(hc + 1) * 128, tc * 512:(tc + 1) * 512]), 16)
        P.on("sync", fcl)
        ld_mark = lC[lane].mark
        def fcq(v, slot=slot, lane=lane, ti=ti, ld_mark=ld_mark, E=E):
            wait_marks(v, [ld_mark])
            if ti >= 4:
                wait_marks(v, [cst_mark[ti - 4]])
                if ti - 4 in cred_mark:
                    wait_marks(v, [cred_mark[ti - 4]])
            emit_quant_tile(v, E, ct[slot][:], cr[lane][:], scl[:, 12:13], scl[:, 21:22],
                            scl[:, 20:21], cy[lane][:], cr2[lane][:], pool=(lane == 1))
            emit_quant_tile(v, E, cu[slot][:], cru[lane][:], scl[:, 4:5], scl[:, 21:22],
                            scl[:, 20:21], cy[lane][:], cr2[lane][:], pool=(lane == 1))
            v.wait_ge(E.h, E.n)
            if lane == 1:
                E.inc(v.tensor_scalar(cr[lane][:], cr[lane][:], stu[:, 0:1], None, ALU.mult))
                v.wait_ge(E.h, E.n)
                E.inc(v.tensor_mul(cy[lane][:], cr[lane][:], cru[lane][:]))
            else:
                E.inc(v.scalar_tensor_tensor(cy[lane][:], cr[lane][:], stu[:, 0:1],
                                             cru[lane][:], ALU.mult, ALU.mult))
            v.wait_ge(E.h, E.n)
            E.inc(v.tensor_copy(cov[slot][:], cy[lane][:]))
            if lane == 0:
                v.wait_ge(E.h, E.n)
                E.inc(v.tensor_reduce(mbo[:, ti:ti + 1], cov[slot][:], mybir.AxisListType.X,
                                      ALU.max, apply_absolute_value=True))
        P.on(eng, fcq)
        cq_mark[ti] = E.mark
        if lane == 1:
            def fcred(v, ti=ti, slot=slot, m=cq_mark[ti]):
                wait_marks(v, [m])
                v.wait_ge(V.h, V.n)
                V.inc(v.tensor_reduce(mbo[:, ti:ti + 1], cov[slot][:], mybir.AxisListType.X,
                                      ALU.max, apply_absolute_value=True))
            P.on("vector", fcred)
            cred_mark[ti] = V.mark
        def fcst(sy, hc=hc, tc=tc, slot=slot, m=cq_mark[ti]):
            wait_marks(sy, [m])
            sC[slot % 2].inc(sy.dma_start(out=o_pre[hc * 128:(hc + 1) * 128, tc * 512:(tc + 1) * 512],
                                          in_=cov[slot][:]), 16)
        P.on("sync", fcst)
        cst_mark[ti] = sC[lane].mark
    spillC = [sC[i].mark for i in range(2)]
    omx_mark = emit_cross_part_max(P, "o", mbo[:], [cq_mark[NTB - 2], cred_mark[NTB - 1]],
                                   amax_row, stC)
    so_mark = emit_qdq_scalars(P, "o", omx_mark, amax_row, fl, on1, fl_mark, scl, 16, stC)
    stC.close()

    stD = contextlib.ExitStack()
    wd_res = P.sb("l1wd_res", (128, HCH * 1024), BF16, stD, side="right")
    oq_sb = P.sb("oq_sb", (128, HCH * 512), BF16, stD, side="right")
    dt = [P.sb(f"ddt{i}", (128, 512), BF16, stD, side="right") for i in range(2)]
    dy = P.sb("dy", (128, 512), F32, stD, side="right")
    dr2 = P.sb("dr2", (128, 512), F32, stD, side="right")
    dxres = [P.sb(f"dxres{i}", (128, 512), F32, stD, side="right") for i in range(2)]
    down_pp = [P.ps(f"l1dpp{i}", (128, 512), stack=stD) for i in range(2)]
    wD = P.sem("l1wd")
    lD = [P.sem(f"l1ld{i}") for i in range(2)]
    xlD = [P.sem(f"l1xl{i}") for i in range(2)]
    xstD = [P.sem(f"l1xs{i}") for i in range(2)]
    wdd_r = wdqT_d.rearrange("(c p) d -> p c d", p=128)
    def fDw(g, dq_d=dq_d):
        wait_marks(g, [dq_d, so_mark] + spillC)
        wD.inc(g.dma_start(out=wd_res[:].rearrange("p (c d) -> p c d", c=HCH), in_=wdd_r), 16)
    P.on("gpsimd", fDw)
    wd_mark = wD.mark
    SWD = 512
    oqm = {}
    res_mark = {}
    for st_i in range(T_CORE // SWD):
        c0 = st_i * SWD
        for hc in range(HCH):
            it = st_i * HCH + hc
            slot = it % 2
            def fdl(sy, hc=hc, c0=c0, slot=slot, it=it, so_mark=so_mark, spillC=spillC):
                wait_marks(sy, [so_mark] + spillC)
                if it >= 2:
                    wait_marks(sy, [oqm[it - 2]])
                lD[slot].inc(sy.dma_start(out=dt[slot][:],
                                          in_=o_pre[hc * 128:(hc + 1) * 128, c0:c0 + SWD]), 16)
            P.on("sync", fdl)
            ld_mark = lD[slot].mark
            def fdq(v, hc=hc, slot=slot, it=it, ld_mark=ld_mark, st_i=st_i):
                wait_marks(v, [ld_mark])
                if st_i >= 1:
                    wait_marks(v, [dn_all[st_i - 1]])
                emit_quant_tile(v, V, dt[slot][:], oq_sb[:, hc * SWD:(hc + 1) * SWD],
                                scl[:, 16:17], scl[:, 17:18], scl[:, 18:19], dy[:], dr2[:])
            P.on("vector", fdq)
            oqm[it] = V.mark
        oq_all = V.mark
        dn_all = dn_all if st_i > 0 else {}
        for dd in range(NCH):
            bank = down_pp[dd % 2]
            it2 = st_i * NCH + dd
            pp = it2 % 2
            def fdown(t, dd=dd, bank=bank, oq_all=oq_all, it2=it2, wd_mark=wd_mark):
                wait_marks(t, [oq_all, wd_mark])
                if it2 >= 2:
                    wait_marks(t, [res_mark[it2 - 2]])
                for hc in range(HCH):
                    i = t.matmul(bank[:, 0:SWD],
                                 wd_res[:, hc * 1024 + dd * 128: hc * 1024 + (dd + 1) * 128],
                                 oq_sb[:, hc * SWD:(hc + 1) * SWD],
                                 start=(hc == 0), stop=(hc == HCH - 1))
                Tn.inc(i)
            P.on("tensor", fdown)
            dd_mark = Tn.mark
            def fxld(sy, dd=dd, c0=c0, pp=pp, it2=it2, oq_all=oq_all):
                if it2 >= 2:
                    wait_marks(sy, [res_mark[it2 - 2]])
                else:
                    wait_marks(sy, [oq_all])
                xlD[pp].inc(sy.dma_start(out=dxres[pp][:],
                                         in_=x_in[dd * 128:(dd + 1) * 128, c0:c0 + SWD]), 16)
            P.on("sync", fxld)
            xld_mark = xlD[pp].mark
            def fres(v, bank=bank, pp=pp, dd_mark=dd_mark, xld_mark=xld_mark):
                wait_marks(v, [dd_mark, xld_mark])
                v.wait_ge(V.h, V.n)
                V.inc(v.scalar_tensor_tensor(dxres[pp][:], bank[:, 0:SWD], 0.5,
                                             dxres[pp][:], ALU.mult, ALU.add))
            P.on("vector", fres)
            wr_mark = V.mark
            def fxst(g, dd=dd, c0=c0, pp=pp, wr_mark=wr_mark):
                wait_marks(g, [wr_mark])
                xstD[pp].inc(g.dma_start(out=x_out[dd * 128:(dd + 1) * 128, c0:c0 + SWD],
                                         in_=dxres[pp][:]), 16)
            P.on("gpsimd", fxst)
            res_mark[it2] = xstD[pp].mark
        dn_all[st_i] = Tn.mark
    fin = [xstD[0].mark, xstD[1].mark]
    G = P.E["gpsimd"]
    def ffinD(g, fin=fin):
        wait_marks(g, fin)
        G.inc(g.nop())
    P.on("gpsimd", ffinD)
    stD.close()
    return G.mark


def build_l1ffn_launch():
    P = Prog()
    x_in = P.dram_in("x", (D, T_CORE), F32)
    wg = P.dram_in("l1_wg", (HID, D), F32)
    wu = P.dram_in("l1_wu", (HID, D), F32)
    wd = P.dram_in("l1_wd", (D, HID), F32)
    x_out = P.dram_out("x_next", (D, T_CORE), F32)
    with P:
        emit_l1ffn(P, x_in, x_out, wg, wu, wd, [])
    return P.nc




L1A_UPTO = "full"


def _early_out(P, x_out, marks):
    eo = P.sem("earlyout")
    def f(g):
        wait_marks(g, marks)
        for c in range(NCH):
            eo.inc(g.dma_start(out=x_out[c * 128:(c + 1) * 128, 0:512],
                               in_=x_out[c * 128:(c + 1) * 128, 0:512]), 16)
        g.wait_ge(eo.h, eo.n)
    P.on("gpsimd", f)


def emit_l1attn(P, x_in, x_out, qT, kT, vtm, wqkv_d, wo_d,
                bqk_sb, bv_sb, bo_sb, waits):
    V, S, Tn = P.E["vector"], P.E["scalar"], P.E["tensor"]
    stX = contextlib.ExitStack()
    xln_sb = P.sb("xln_sb", (128, NCH * T_CORE), BF16, stX, side="right")
    ln_mark = emit_ln_stream(P, "L1A", x_in, xln_sb, waits)
    waits = waits + [ln_mark]
    gq = P.sem("l1a_g")
    xl = [P.sem(f"l1a_xl{i}") for i in range(2)]
    xst = [P.sem(f"l1a_xs{i}") for i in range(2)]
    scq = P.sem("l1a_sc")

    stON = contextlib.ExitStack()
    onesf = P.sb("l1a_on", (1, 128), F32, stON, side="left")
    ones64 = P.sb("l1a_o64", (64, 1), BF16, stON, side="left")
    gb64 = P.sb("l1a_gb", (64, 1), F32, stON, side="left")
    grow = P.sb("l1a_gr", (1, 4), F32, stON, side="left")
    def finit(v):
        wait_marks(v, waits)
        v.wait_ge(V.h, V.n)
        V.inc(v.memset(onesf[:], 1.0))
        V.inc(v.memset(ones64[:], 1.0))
    P.on("vector", finit)
    ones_mark = V.mark

    def mk_evac(dst, boff):
        def evac(v, ps_t, oc, tt):
            V.inc(v.tensor_scalar(dst[:, oc * T_CORE + tt * 512: oc * T_CORE + tt * 512 + 512],
                                  ps_t[:], bqk_sb[:, boff + oc: boff + oc + 1],
                                  None, ALU.add))
        return evac
    pq_mark = emit_proj_fm(P, "lq", xln_sb, wqkv_d, 0, NCH, mk_evac(qT, 0), waits)
    pk_mark = emit_proj_fm(P, "lk", xln_sb, wqkv_d, D, NCH, mk_evac(kT, NCH),
                           waits + [pq_mark])

    stV = contextlib.ExitStack()
    wv_sb = P.sb("l1a_wv", (128, NCH * 1024), BF16, stV, side="right")
    v_ps = [P.ps(f"l1a_vps{i}", (128, 512), stack=stV) for i in range(2)]
    wv_r = wqkv_d.rearrange("(c p) o -> p c o", p=128)
    def fvw(g):
        wait_marks(g, waits + [pq_mark, pk_mark])
        gq.inc(g.dma_start(out=wv_sb[:].rearrange("p (c o) -> p c o", c=NCH),
                           in_=wv_r[:, :, 2 * D: 3 * D]), 16)
    P.on("gpsimd", fvw)
    vw_mark = gq.mark
    vev = {}
    for tc in range(T_CORE // 128):
        for half in range(2):
            it = tc * 2 + half
            pp = it % 2
            def fvmm(t, tc=tc, half=half, pp=pp, it=it):
                wait_marks(t, waits)
                wait_marks(t, [vw_mark, ones_mark])
                if it >= 2:
                    wait_marks(t, [vev[it - 2]])
                for c in range(NCH):
                    t.matmul(v_ps[pp][:], xln_sb[:, c * T_CORE + tc * 128: c * T_CORE + (tc + 1) * 128],
                             wv_sb[:, c * 1024 + half * 512: c * 1024 + half * 512 + 512],
                             start=(c == 0), stop=False)
                Tn.inc(t.matmul(v_ps[pp][:], onesf[:], bv_sb[0:1, half * 512:(half + 1) * 512],
                                start=False, stop=True))
            P.on("tensor", fvmm)
            vmm_mark = Tn.mark
            def fvev(v, tc=tc, half=half, pp=pp, vmm_mark=vmm_mark):
                wait_marks(v, [vmm_mark])
                v.wait_ge(V.h, V.n)
                V.inc(v.tensor_copy(vtm[:, tc * 1024 + half * 512: tc * 1024 + half * 512 + 512],
                                    v_ps[pp][:]))
            P.on("vector", fvev)
            vev[it] = V.mark
    v_mark = V.mark
    stV.close()
    stX.close()
    if L1A_UPTO == "proj":
        _early_out(P, x_out, [v_mark, pq_mark, pk_mark])
        return None

    stA2 = contextlib.ExitStack()
    attn_sb = P.sb("attn_sb", (128, NCH * T_CORE), BF16, stA2, side="right")
    stP1 = contextlib.ExitStack()
    maxb = P.sb("l1a_mx", (64, 32), F32, stP1, side="right")
    mcol = P.sb("l1a_mc", (64, 1), F32, stP1, side="right")
    mrow = P.sb("l1a_mr", (1, 64), F32, stP1, side="right")
    st_ps = [P.ps(f"l1a_sps{i}", (64, 1024), stack=stP1) for i in range(2)]
    red_mark = {}
    for m in range(32):
        ch = m // 8
        pp = m % 2
        def fs1(t, m=m, pp=pp, pq=pq_mark, pk=pk_mark, v_mark=v_mark):
            wait_marks(t, [pq, pk, v_mark])
            if m >= 2:
                wait_marks(t, [red_mark[m - 2]])
            for grp in (0, 1):
                hb = grp * 64
                for h in range(grp, NH, 2):
                    hc = h // 2
                    i = t.matmul(st_ps[pp][:, h * 64:(h + 1) * 64],
                                 kT[hb:hb + 64, hc * T_CORE + m * 64: hc * T_CORE + m * 64 + 64],
                                 qT[hb:hb + 64, hc * T_CORE + m * 64: hc * T_CORE + m * 64 + 64],
                                 start=True, stop=True)
                Tn.inc(i)
                t.wait_ge(Tn.h, Tn.n)
        P.on("tensor", fs1)
        s1_mark = Tn.mark
        def fr1(v, m=m, pp=pp, s1_mark=s1_mark):
            wait_marks(v, [s1_mark])
            v.wait_ge(V.h, V.n)
            V.inc(v.tensor_reduce(maxb[:, m:m + 1], st_ps[pp][:], mybir.AxisListType.X,
                                  ALU.max, apply_absolute_value=True))
        P.on("vector", fr1)
        red_mark[m] = V.mark
        if L1A_UPTO == "p1score" and m == 0:
            break
    if L1A_UPTO == "p1score":
        m0 = red_mark[0]
        stP1.close()
        _early_out(P, x_out, [m0])
        return None
    def fgm(v):
        wait_marks(v, [red_mark[31]])
        v.wait_ge(V.h, V.n)
        V.inc(v.tensor_reduce(mcol[:], maxb[:], mybir.AxisListType.X, ALU.max))
    P.on("vector", fgm)
    gm1 = V.mark
    def fgd(g, gm1=gm1):
        wait_marks(g, [gm1])
        scq.inc(g.dma_start(out=mrow[0:1, :], in_=mcol[0:64, 0:1]), 16)
    P.on("gpsimd", fgd)
    gm2 = scq.mark
    def fgr(v, gm2=gm2):
        wait_marks(v, [gm2])
        v.wait_ge(V.h, V.n)
        V.inc(v.tensor_reduce(grow[0:1, 0:1], mrow[0:1, :], mybir.AxisListType.X, ALU.max))
        v.wait_ge(V.h, V.n)
        V.inc(v.tensor_scalar(grow[0:1, 1:2], grow[0:1, 0:1], -0.125, None, ALU.mult))
    P.on("vector", fgr)
    gm3 = V.mark
    gb_ps = P.ps("l1a_gbps", (64, 1), stack=stP1)
    def fgb(t, gm3=gm3):
        wait_marks(t, [gm3])
        Tn.inc(t.matmul(gb_ps[:], onesf[0:1, 0:64], grow[0:1, 1:2], start=True, stop=True))
    P.on("tensor", fgb)
    gm4 = Tn.mark
    def fgc(v, gm4=gm4):
        wait_marks(v, [gm4])
        v.wait_ge(V.h, V.n)
        V.inc(v.tensor_copy(gb64[:], gb_ps[:]))
    P.on("vector", fgc)
    g_mark = V.mark
    stP1.close()
    if L1A_UPTO == "phase1":
        _early_out(P, x_out, [g_mark])
        return None

    stP2 = contextlib.ExitStack()
    vblk = P.sb("l1a_vb", (64, 1024), BF16, stP2, side="right")
    ptb = P.sb("l1a_pt", (64, 1024), BF16, stP2, side="right")
    otb = P.sb("l1a_ot", (64, 1024), BF16, stP2, side="right")
    linv = P.sb("l1a_li", (1, 1024), F32, stP2, side="right")
    s2_ps = [P.ps("l1a_s2", (64, 1024), stack=stP2)] * 2
    ot_ps = [P.ps("l1a_otp", (64, 1024), stack=stP2)] * 2
    l_ps = [P.ps(f"l1a_lps{i}", (1, 512), stack=stP2) for i in range(2)]
    rep_ps = P.ps("l1a_rep", (64, 1024), stack=stP2)
    blk_mark = {}
    scat_mark = {}
    for m in range(32):
        pp = m % 2
        def fs2(t, m=m, pp=pp, g_mark=g_mark):
            wait_marks(t, [g_mark])
            if m >= 1:
                wait_marks(t, [blk_mark[m - 1]])
            for grp in (0, 1):
                hb = grp * 64
                for h in range(grp, NH, 2):
                    hc = h // 2
                    i = t.matmul(s2_ps[pp][:, h * 64:(h + 1) * 64],
                                 kT[hb:hb + 64, hc * T_CORE + m * 64: hc * T_CORE + m * 64 + 64],
                                 qT[hb:hb + 64, hc * T_CORE + m * 64: hc * T_CORE + m * 64 + 64],
                                 start=True, stop=True)
                Tn.inc(i)
                t.wait_ge(Tn.h, Tn.n)
        P.on("tensor", fs2)
        s2_mark = Tn.mark
        def fexp(sc, pp=pp, s2_mark=s2_mark, m=m):
            wait_marks(sc, [s2_mark])
            if m >= 1:
                wait_marks(sc, [blk_mark.get(m - 1, (S, 0))])
            S.inc(sc.activation(ptb[:], s2_ps[pp][:], AF.Exp, bias=gb64[:, 0:1], scale=0.125))
        P.on("scalar", fexp)
        exp_mark = S.mark
        def fvb(v, m=m, exp_mark=exp_mark, v_mark=v_mark):
            wait_marks(v, [v_mark])
            if m >= 1:
                wait_marks(v, [blk_mark.get(m - 1, (V, 0))])
            v.wait_ge(V.h, V.n)
            V.inc(v.tensor_copy(vblk[:], vtm[(m % 2) * 64:(m % 2) * 64 + 64,
                                             (m // 2) * 1024:(m // 2) * 1024 + 1024]))
        P.on("vector", fvb)
        vb_mark = V.mark
        def fav(t, pp=pp, exp_mark=exp_mark, vb_mark=vb_mark):
            wait_marks(t, [exp_mark, vb_mark])
            for h in range(NH):
                t.matmul(ot_ps[pp][:, h * 64:(h + 1) * 64], vblk[:, h * 64:(h + 1) * 64],
                         ptb[:, h * 64:(h + 1) * 64], start=True, stop=True)
            t.matmul(l_ps[0][:], ones64[:], ptb[:, 0:512], start=True, stop=True)
            Tn.inc(t.matmul(l_ps[1][:], ones64[:], ptb[:, 512:1024], start=True, stop=True))
        P.on("tensor", fav)
        av_mark = Tn.mark
        def fep(v, pp=pp, av_mark=av_mark):
            wait_marks(v, [av_mark])
            v.wait_ge(V.h, V.n)
            V.inc(v.reciprocal(linv[0:1, 0:512], l_ps[0][:]))
            V.inc(v.reciprocal(linv[0:1, 512:1024], l_ps[1][:]))
        P.on("vector", fep)
        li_mark = V.mark
        def frep(t, li_mark=li_mark):
            wait_marks(t, [li_mark])
            t.matmul(rep_ps[:, 0:512], onesf[0:1, 0:64], linv[0:1, 0:512], start=True, stop=True)
            Tn.inc(t.matmul(rep_ps[:, 512:1024], onesf[0:1, 0:64], linv[0:1, 512:1024],
                            start=True, stop=True))
        P.on("tensor", frep)
        rep_mark = Tn.mark
        def fot(v, pp=pp, rep_mark=rep_mark, m=m):
            wait_marks(v, [rep_mark])
            if m >= 1:
                wait_marks(v, [scat_mark[m - 1]])
            v.wait_ge(V.h, V.n)
            V.inc(v.tensor_copy(otb[:], ot_ps[pp][:]))
            v.wait_ge(V.h, V.n)
            V.inc(v.tensor_mul(otb[:], otb[:], rep_ps[:]))
        P.on("vector", fot)
        blk_mark[m] = V.mark
        def fscat(g, m=m, bm=blk_mark[m]):
            wait_marks(g, [bm])
            src = otb[:].rearrange("d (ch par q) -> d ch par q", ch=8, par=2)
            for par in range(2):
                dst = attn_sb[par * 64:par * 64 + 64, :].rearrange(
                    "d (ch t) -> d ch t", ch=8)[:, :, m * 64:(m + 1) * 64]
                scq.inc(g.dma_start(out=dst, in_=src[:, :, par, :],
                                    allow_slow_non_contiguous=True), 16)
        P.on("gpsimd", fscat)
        scat_mark[m] = scq.mark
    attn_done = [scat_mark[31], blk_mark[31], Tn.mark]
    stP2.close()
    if L1A_UPTO == "phase2":
        _early_out(P, x_out, attn_done)
        return None

    stPJ = contextlib.ExitStack()
    wpj_sb = P.sb("l1a_wpj", (128, NCH * 1024), BF16, stPJ, side="right")
    ptmp = P.sb("l1a_ptmp", (128, 512), F32, stPJ, side="right")
    xres = [P.sb(f"l1a_xr{i}", (128, 512), F32, stPJ, side="right") for i in range(2)]
    pj_ps = [P.ps(f"l1a_pj{i}", (128, 512), stack=stPJ) for i in range(2)]
    wpj_r = wo_d.rearrange("(c p) o -> p c o", p=128)
    def fpw(g):
        wait_marks(g, attn_done)
        gq.inc(g.dma_start(out=wpj_sb[:].rearrange("p (c o) -> p c o", c=NCH), in_=wpj_r), 16)
    P.on("gpsimd", fpw)
    pw_mark = gq.mark
    res_done = {}
    for oc in range(NCH):
        for tt in range(4):
            it = oc * 4 + tt
            pp = it % 2
            def fpm(t, oc=oc, tt=tt, pp=pp, it=it):
                wait_marks(t, [pw_mark] + attn_done)
                if it >= 2:
                    wait_marks(t, [res_done[it - 2]])
                for c in range(NCH):
                    i = t.matmul(pj_ps[pp][:], wpj_sb[:, c * 1024 + oc * 128: c * 1024 + (oc + 1) * 128],
                                 attn_sb[:, c * T_CORE + tt * 512: c * T_CORE + tt * 512 + 512],
                                 start=(c == 0), stop=(c == NCH - 1))
                Tn.inc(i)
            P.on("tensor", fpm)
            pm_mark = Tn.mark
            def fxld(sy, oc=oc, tt=tt, pp=pp, it=it):
                if it >= 2:
                    wait_marks(sy, [res_done[it - 2]])
                else:
                    wait_marks(sy, attn_done)
                xl[pp].inc(sy.dma_start(out=xres[pp][:],
                                        in_=x_in[oc * 128:(oc + 1) * 128, tt * 512:(tt + 1) * 512]), 16)
            P.on("sync", fxld)
            xld_mark = xl[pp].mark
            def fpr(v, oc=oc, pp=pp, pm_mark=pm_mark, xld_mark=xld_mark):
                wait_marks(v, [pm_mark, xld_mark])
                v.wait_ge(V.h, V.n)
                V.inc(v.tensor_scalar(ptmp[:], pj_ps[pp][:], bo_sb[:, oc:oc + 1], None, ALU.add))
                v.wait_ge(V.h, V.n)
                V.inc(v.tensor_add(xres[pp][:], ptmp[:], xres[pp][:]))
            P.on("vector", fpr)
            wr_mark = V.mark
            def fxst(g, oc=oc, tt=tt, pp=pp, wr_mark=wr_mark):
                wait_marks(g, [wr_mark])
                xst[pp].inc(g.dma_start(out=x_out[oc * 128:(oc + 1) * 128, tt * 512:(tt + 1) * 512],
                                        in_=xres[pp][:]), 16)
            P.on("gpsimd", fxst)
            res_done[it] = xst[pp].mark
    fin = [xst[0].mark, xst[1].mark]
    def ffin(g, fin=fin):
        wait_marks(g, fin)
    P.on("gpsimd", ffin)
    stPJ.close()
    stA2.close()
    stON.close()
    return fin[0]


def build_l1attn_launch():
    P = Prog()
    x_in = P.dram_in("x", (D, T_CORE), F32)
    wqkv = P.dram_in("l1_wqkv", (D, 3 * D), F32)
    wo = P.dram_in("l1_wo", (D, D), F32)
    bqk_in = P.dram_in("l1_bqk", (128, 16), F32)
    bv_in = P.dram_in("l1_bv", (1, D), F32)
    bo_in = P.dram_in("l1_bo", (128, 8), F32)
    x_out = P.dram_out("x_next", (D, T_CORE), F32)
    bqk_sb = P.sb("bqk_sb", (128, 16), F32)
    bv_sb = P.sb("bv_sb", (1, D), F32)
    bo_sb = P.sb("bo_sb", (128, 8), F32)
    qT = P.sb("qT", (128, NCH * T_CORE), BF16, side="right")
    kT = P.sb("kT", (128, NCH * T_CORE), BF16, side="right")
    vtm = P.sb("vtm", (128, 16 * 1024), BF16, side="right")
    with P:
        ld = P.sem("smalls")
        def fsmall(sy):
            for src_, dst in ((bqk_in, bqk_sb), (bv_in, bv_sb), (bo_in, bo_sb)):
                ld.inc(sy.dma_start(out=dst[:], in_=src_), 16)
        P.on("sync", fsmall)
        emit_l1attn(P, x_in, x_out, qT, kT, vtm, wqkv, wo,
                    bqk_sb, bv_sb, bo_sb, [ld.mark])
    return P.nc


def _l1_host_prep(inputs):
    Wqkv = np.asarray(inputs["L1_attn_w_qkv"])
    bqkv = np.asarray(inputs["L1_attn_b_qkv"])
    p = np.zeros(3072, dtype=np.int64)
    for h in range(NH):
        for i in range(64):
            p[0 * 1024 + h * 64 + i] = h * 192 + i
            p[1 * 1024 + h * 64 + i] = h * 192 + 64 + i
            p[2 * 1024 + h * 64 + i] = h * 192 + 128 + i
    wqkv = np.ascontiguousarray(Wqkv[:, p])
    bp = bqkv[p]
    bqk = np.ascontiguousarray(bp[:2048].reshape(16, 128).T)
    bv = np.ascontiguousarray(bp[2048:].reshape(1, D))
    bo = np.ascontiguousarray(np.asarray(inputs["L1_attn_b_o"]).reshape(8, 128).T)
    wo = np.ascontiguousarray(np.asarray(inputs["L1_attn_w_o"]))
    return dict(wqkv=wqkv, wo=wo, bqk=bqk, bv=bv, bo=bo)



def build_l0ffn_launch():
    P = Prog()
    x_in = P.dram_in("x", (D, T_CORE), F32)
    wg = P.dram_in("l0_wgT", (D, HID), I32)
    wu = P.dram_in("l0_wuT", (D, HID), I32)
    wd = P.dram_in("l0_wdT", (HID, D), I32)
    x_out = P.dram_out("x_next", (D, T_CORE), F32)
    xln_sb = P.sb("xln_sb", (128, NCH * T_CORE), BF16)
    with P:
        ln_mark = emit_ln_stream(P, "0", x_in, xln_sb, [])
        emit_l0ffn(P, x_in, x_out, xln_sb, wg, wu, wd, [ln_mark])
    return P.nc


def build_l0attn_launch():
    P = Prog()
    x_in = P.dram_in("x", (D, T_CORE), F32)
    wqk = P.dram_in("l0_wqk", (D, 2 * D), F32)
    wv = P.dram_in("l0_wv", (D, D), F32)
    wpj = P.dram_in("l0_wproj", (D, D), F32)
    bqk_in = P.dram_in("l0_bqk", (128, 16), F32)
    bv_in = P.dram_in("l0_bv", (1, D), F32)
    bpj_in = P.dram_in("l0_bproj", (128, 8), F32)
    cos_in = P.dram_in("l0_cos", (128, T_CORE), F32)
    sin_in = P.dram_in("l0_sin", (128, T_CORE), F32)
    mask_in = P.dram_in("l0_mask", (128, 8 * 512), BF16)
    x_out = P.dram_out("x_next", (D, T_CORE), F32)
    qT = P.sb("qT", (128, NCH * T_CORE), BF16)
    attn_sb = P.sb("attn_sb", (128, NCH * T_CORE), BF16)
    cos_sb = P.sb("cos_sb", (128, T_CORE), F32)
    sin_sb = P.sb("sin_sb", (128, T_CORE), F32)
    mask_sb = P.sb("mask_sb", (128, 8 * 512), BF16)
    bqk_sb = P.sb("bqk_sb", (128, 16), F32)
    bv_sb = P.sb("bv_sb", (1, D), F32)
    bpj_sb = P.sb("bpj_sb", (128, 8), F32)
    xln_sb = P.sb("xln_sb", (128, NCH * T_CORE), BF16, side="right")
    with P:
        ld = P.sem("smalls")
        def fsmall(sy):
            for src_, dst in ((bqk_in, bqk_sb), (bv_in, bv_sb), (bpj_in, bpj_sb),
                              (cos_in, cos_sb), (sin_in, sin_sb), (mask_in, mask_sb)):
                ld.inc(sy.dma_start(out=dst[:], in_=src_), 16)
        P.on("sync", fsmall)
        ln_mark = emit_ln_stream(P, "0", x_in, xln_sb, [ld.mark])
        emit_l0attn(P, x_in, x_out, xln_sb, qT, attn_sb, wqk, wv, wpj, bqk_sb,
                    bv_sb, bpj_sb, cos_sb, sin_sb, mask_sb, [ld.mark, ln_mark])
    return P.nc


def _l0_host_prep(inputs):
    Wqkv = np.asarray(inputs["L0_attn_Wqkv"])
    bqkv = np.asarray(inputs["L0_attn_bqkv"])
    perm = np.zeros(2 * D, dtype=np.int64)
    for blk in range(2):
        for m in range(4):
            for hl in range(4):
                h = 4 * m + hl
                for i in range(32):
                    perm[blk * 1024 + (2 * m) * 128 + 32 * hl + i] = blk * 1024 + h * 64 + 2 * i
                    perm[blk * 1024 + (2 * m + 1) * 128 + 32 * hl + i] = blk * 1024 + h * 64 + 2 * i + 1
    wqk = np.ascontiguousarray(Wqkv[:, perm])
    wv = np.ascontiguousarray(Wqkv[:, 2048:])
    bqk_p = bqkv[perm]
    bqk = np.ascontiguousarray(bqk_p.reshape(16, 128).T)
    bv = np.ascontiguousarray(bqkv[2048:].reshape(1, D))
    bpj = np.ascontiguousarray(np.asarray(inputs["L0_attn_bproj"]).reshape(8, 128).T)
    wpj = np.ascontiguousarray(np.asarray(inputs["L0_attn_Wproj"]))
    inv_freq = (1.0 / (10000.0 ** (np.arange(0, DK, 2, dtype=np.float32) / DK))).astype(np.float32)
    tabs = []
    for core in range(NCORES):
        pos = _own_tokens(core).astype(np.float32)
        f = (inv_freq[:, None] * pos[None, :]).astype(np.float32)
        cos = np.tile(np.cos(f).astype(np.float32), (4, 1))
        sin = np.tile(np.sin(f).astype(np.float32), (4, 1))
        tabs.append((np.ascontiguousarray(cos), np.ascontiguousarray(sin)))
    kk = np.arange(128)[:, None]
    qq = np.arange(512)[None, :]
    def diag(dj):
        return np.where(qq >= kk + 128 * dj, np.float32(0.0), np.float32(-1e9))
    allow = np.zeros((128, 512), np.float32)
    blockm = np.full((128, 512), np.float32(-1e9))
    masks = []
    for p in range(2):
        slots = []
        for b_ in range(8):
            dj = b_ - 4 * p
            slots.append(allow if dj < 0 else (diag(dj) if dj < 4 else blockm))
        masks.append(np.ascontiguousarray(
            np.concatenate(slots, axis=1).astype(ml_dtypes.bfloat16)))
    return dict(wqk=wqk, wv=wv, wpj=wpj, bqk=bqk, bv=bv, bpj=bpj, tabs=tabs, masks=masks)



def build_fused(n_sub=6):
    P = Prog()
    x_in = P.dram_in("x", (D, T_CORE), F32)
    l0_wg = P.dram_in("l0_wgT", (D, HID), I32)
    l0_wu = P.dram_in("l0_wuT", (D, HID), I32)
    l0_wd = P.dram_in("l0_wdT", (HID, D), I32)
    wqk = P.dram_in("l0_wqk", (D, 2 * D), F32)
    wv = P.dram_in("l0_wv", (D, D), F32)
    wpj = P.dram_in("l0_wproj", (D, D), F32)
    bqk_in = P.dram_in("l0_bqk", (128, 16), F32)
    bv_in = P.dram_in("l0_bv", (1, D), F32)
    bpj_in = P.dram_in("l0_bproj", (128, 8), F32)
    cos_in = P.dram_in("l0_cos", (128, T_CORE), F32)
    sin_in = P.dram_in("l0_sin", (128, T_CORE), F32)
    mask_in = P.dram_in("l0_mask", (128, 8 * 512), BF16)
    l1_wg = P.dram_in("l1_wg", (HID, D), F32)
    l1_wu = P.dram_in("l1_wu", (HID, D), F32)
    l1_wd = P.dram_in("l1_wd", (D, HID), F32)
    wqkv1 = P.dram_in("l1_wqkv", (D, 3 * D), F32)
    wo1 = P.dram_in("l1_wo", (D, D), F32)
    bqk1_in = P.dram_in("l1_bqk", (128, 16), F32)
    bv1_in = P.dram_in("l1_bv", (1, D), F32)
    bo1_in = P.dram_in("l1_bo", (128, 8), F32)
    x_out = P.dram_out("x_next", (D, T_CORE), F32)
    xp = [P.nc.dram_tensor(f"xping{i}", [D, T_CORE], F32, kind="Internal").ap()
          for i in range(5)]
    dst = [xp[0], xp[1], xp[2], xp[3], xp[4], x_out]
    dst[n_sub - 1] = x_out
    with P:
        waits = []
        with P.scoped("s0"):
            stk = contextlib.ExitStack()
            xln = P.sb("xln", (128, NCH * T_CORE), BF16, stk, side="right")
            lnm = emit_ln_stream(P, "a", x_in, xln, waits)
            fin = emit_l0ffn(P, x_in, dst[0], xln, l0_wg, l0_wu, l0_wd, waits + [lnm])
            if n_sub == 1:
                stk.close()
                return P.nc
            stk.close()
        waits = [fin]
        with P.scoped("s1"):
            stk = contextlib.ExitStack()
            qT = P.sb("qT", (128, NCH * T_CORE), BF16, stk, side="right")
            attn_sb = P.sb("attn", (128, NCH * T_CORE), BF16, stk, side="right")
            cos_sb = P.sb("cos", (128, T_CORE), F32, stk, side="right")
            sin_sb = P.sb("sin", (128, T_CORE), F32, stk, side="right")
            mask_sb = P.sb("mask", (128, 8 * 512), BF16, stk, side="right")
            bqk_sb = P.sb("bqk", (128, 16), F32, stk, side="right")
            bv_sb = P.sb("bv", (1, D), F32, stk, side="right")
            bpj_sb = P.sb("bpj", (128, 8), F32, stk, side="right")
            xln = P.sb("xln", (128, NCH * T_CORE), BF16, stk, side="right")
            ld = P.sem("smalls")
            def fsmall(sy, waits=waits):
                wait_marks(sy, waits)
                for src_, dst in ((bqk_in, bqk_sb), (bv_in, bv_sb), (bpj_in, bpj_sb),
                                  (cos_in, cos_sb), (sin_in, sin_sb), (mask_in, mask_sb)):
                    ld.inc(sy.dma_start(out=dst[:], in_=src_), 16)
            P.on("sync", fsmall)
            waits = waits + [ld.mark]
            lnm = emit_ln_stream(P, "b", xp[0], xln, waits)
            fin = emit_l0attn(P, xp[0], dst[1], xln, qT, attn_sb, wqk, wv, wpj, bqk_sb,
                              bv_sb, bpj_sb, cos_sb, sin_sb, mask_sb, waits + [lnm])
            if n_sub == 2:
                stk.close()
                return P.nc
            stk.close()
        waits = [fin]
        with P.scoped("s2"):
            stk = contextlib.ExitStack()
            xln = P.sb("xln", (128, NCH * T_CORE), BF16, stk, side="right")
            lnm = emit_ln_stream(P, "c", xp[1], xln, waits)
            fin = emit_l0ffn(P, xp[1], dst[2], xln, l0_wg, l0_wu, l0_wd, waits + [lnm])
            if n_sub == 3:
                stk.close()
                return P.nc
            stk.close()
        waits = [fin]
        with P.scoped("s3"):
            fin = emit_l1ffn(P, xp[2], dst[3], l1_wg, l1_wu, l1_wd, waits)
            deq = P._deq_cache
            if n_sub == 4:
                return P.nc
        waits = [fin]
        with P.scoped("s4"):
            stk = contextlib.ExitStack()
            bqk_sb = P.sb("bqk", (128, 16), F32, stk, side="right")
            bv_sb = P.sb("bv", (1, D), F32, stk, side="right")
            bo_sb = P.sb("bo", (128, 8), F32, stk, side="right")
            qT = P.sb("qT", (128, NCH * T_CORE), BF16, stk, side="right")
            kT = P.sb("kT", (128, NCH * T_CORE), BF16, stk, side="right")
            vtm = P.sb("vtm", (128, 16 * 1024), BF16, stk, side="right")
            ld = P.sem("smalls")
            def fsmall1(sy, waits=waits):
                wait_marks(sy, waits)
                for src_, dst in ((bqk1_in, bqk_sb), (bv1_in, bv_sb), (bo1_in, bo_sb)):
                    ld.inc(sy.dma_start(out=dst[:], in_=src_), 16)
            P.on("sync", fsmall1)
            fin = emit_l1attn(P, xp[3], dst[4], qT, kT, vtm, wqkv1, wo1,
                              bqk_sb, bv_sb, bo_sb, waits + [ld.mark])
            if n_sub == 5:
                stk.close()
                return P.nc
            stk.close()
        waits = [fin]
        with P.scoped("s5"):
            emit_l1ffn(P, xp[4], x_out, l1_wg, l1_wu, l1_wd, waits, deq_cache=deq)
    return P.nc


def _in_maps(inputs, shards):
    wgT = np.ascontiguousarray(np.asarray(inputs["L0_ffn_w_gate"]).T)
    wuT = np.ascontiguousarray(np.asarray(inputs["L0_ffn_w_up"]).T)
    wdT = np.ascontiguousarray(np.asarray(inputs["L0_ffn_w_down"]).T)
    hp0 = _l0_host_prep(inputs)
    hp1 = _l1_host_prep(inputs)
    wg1 = np.ascontiguousarray(np.asarray(inputs["L1_ffn_w_gate"]))
    wu1 = np.ascontiguousarray(np.asarray(inputs["L1_ffn_w_up"]))
    wd1 = np.ascontiguousarray(np.asarray(inputs["L1_ffn_w_down"]))
    maps = []
    for c in range(NCORES):
        cos, sin = hp0["tabs"][c]
        maps.append({
            "x": shards[c], "l0_wgT": wgT, "l0_wuT": wuT, "l0_wdT": wdT,
            "l0_wqk": hp0["wqk"], "l0_wv": hp0["wv"], "l0_wproj": hp0["wpj"],
            "l0_bqk": hp0["bqk"], "l0_bv": hp0["bv"], "l0_bproj": hp0["bpj"],
            "l0_cos": cos, "l0_sin": sin, "l0_mask": hp0["masks"][c % 2],
            "l1_wg": wg1, "l1_wu": wu1, "l1_wd": wd1,
            "l1_wqkv": hp1["wqkv"], "l1_wo": hp1["wo"],
            "l1_bqk": hp1["bqk"], "l1_bv": hp1["bv"], "l1_bo": hp1["bo"],
        })
    return maps


def kernel(**inputs):
    x0 = np.asarray(inputs["L0_attn_x"])
    shards = []
    for core in range(NCORES):
        b = core // 2
        toks = _own_tokens(core)
        shards.append(np.ascontiguousarray(x0[b][toks].T))
    nc = build_fused()
    res = run_bass_kernel_spmd(nc, _in_maps(inputs, shards), core_ids=list(range(NCORES)))
    out = np.zeros((B, T, D), dtype=np.float32)
    for core in range(NCORES):
        b = core // 2
        toks = _own_tokens(core)
        out[b][toks] = res.results[core]["x_next"].T
    return out


# version 14
# speedup vs baseline: 1.6743x; 1.6743x over previous
import contextlib
import numpy as np
import ml_dtypes

import concourse.bass as bass
import concourse.mybir as mybir
from concourse.bass_utils import run_bass_kernel_spmd

F32 = mybir.dt.float32
BF16 = mybir.dt.bfloat16
I32 = mybir.dt.int32
ALU = mybir.AluOpType
AF = mybir.ActivationFunctionType
BF = ml_dtypes.bfloat16

NCORES = 8
D = 1024
HID = 3072
NH = 16
DK = 64
T_CORE = 2048
B, T = 4, 4096
NCH = D // 128
HCH = HID // 128
C1 = float(np.float32(1.5 * 2.0 ** 23))
C2 = float(np.float32(1.5 * 2.0 ** 52))


def _own_tokens(core):
    p = core % 2
    idx = []
    for s in (p, p + 2, p + 4, p + 6):
        idx.extend(range(512 * s, 512 * (s + 1)))
    return np.array(idx)



class Sem:
    def __init__(self, ctx, nc, name):
        self.h = ctx.enter_context(nc.semaphore(name))
        self.n = 0

    def inc(self, instr, k=1):
        instr.then_inc(self.h, k)
        self.n += k
        return self.n

    @property
    def mark(self):
        return (self, self.n)


class Prog:

    def __init__(self):
        self.nc = bass.Bass("TRN2", target_bir_lowering=False)
        self.ctx = contextlib.ExitStack()
        self.block = None
        self.sems = {}
        self.E = {}
        self._tag = ""

    @contextlib.contextmanager
    def scoped(self, tag):
        old = self._tag
        self._tag = tag
        try:
            yield
        finally:
            self._tag = old

    def __enter__(self):
        self._blk = self.nc.Block()
        self.block = self._blk.__enter__()
        for e in ("vector", "scalar", "tensor", "gpsimd"):
            self.E[e] = self.sem("E_" + e)
        return self

    def __exit__(self, exc_type, exc, tb):
        if exc is not None:
            try:
                self._blk.__exit__(None, None, None)
            except Exception:
                pass
            try:
                self.ctx.close()
            except Exception:
                pass
            return False
        self._blk.__exit__(None, None, None)
        self.ctx.close()
        return False

    def sem(self, name):
        if name not in self.sems:
            self.sems[name] = Sem(self.ctx, self.nc, name)
        return self.sems[name]

    def on(self, eng, fn):
        getattr(self.block, eng)(fn)

    def sb(self, name, shape, dt, stack=None, side="left"):
        return (stack or self.ctx).enter_context(
            self.nc.sbuf_tensor(self._tag + name, list(shape), dt, side=side))

    def ps(self, name, shape, dt=F32, stack=None):
        return (stack or self.ctx).enter_context(
            self.nc.psum_tensor(self._tag + name, list(shape), dt))

    def dram_in(self, name, shape, dt):
        return self.nc.dram_tensor(name, list(shape), dt, kind="ExternalInput").ap()

    def dram_out(self, name, shape, dt):
        return self.nc.dram_tensor(name, list(shape), dt, kind="ExternalOutput").ap()

    def dram_tmp(self, name, shape, dt):
        return self.nc.dram_tensor(self._tag + name, list(shape), dt, kind="Internal").ap()


def wait_marks(eng, marks):
    for s, v in marks:
        if v > 0:
            eng.wait_ge(s.h, v)



def emit_ln_stream(P, tag, x_in, xln_sb, waits):
    V, S, Tn = P.E["vector"], P.E["scalar"], P.E["tensor"]
    ldx = [P.sem("lnld" + tag + str(i)) for i in range(2)]
    stk = contextlib.ExitStack()
    ones128 = P.sb("ln_ones" + tag, (128, 1), F32, stk, side="right")
    eps_t = P.sb("ln_eps" + tag, (1, 1), F32, stk, side="right")
    ones1 = P.sb("ln_ones1" + tag, (1, 128), F32, stk, side="right")
    xtile = [P.sb(f"ln_xt{i}" + tag, (128, NCH * 512), F32, stk, side="right") for i in range(2)]
    sq = [P.sb(f"ln_sq{i}" + tag, (128, 512), F32, stk, side="right") for i in range(2)]
    rows = {n: P.sb("ln_" + n + tag, (1, 512), F32, stk, side="right")
            for n in ("mu", "e2", "t1", "var", "sd", "rstd", "nmr")}
    tmp = P.sb("ln_tmp" + tag, (128, 512), F32, stk, side="right")
    mu_ps = P.ps("ln_mups" + tag, (1, 512), stack=stk)
    e2_ps = P.ps("ln_e2ps" + tag, (1, 512), stack=stk)
    repA = P.ps("ln_repA" + tag, (128, 512), stack=stk)
    repB = P.ps("ln_repB" + tag, (128, 512), stack=stk)

    def init(v):
        wait_marks(v, waits)
        v.memset(ones128[:], 1.0)
        v.memset(eps_t[:], 1e-5)
        V.inc(v.memset(ones1[:], 1.0))
    P.on("vector", init)
    init_mark = V.mark

    NT = T_CORE // 512
    tile_done = {}
    mmpair = {}
    sq_mark = {}
    for ti in range(NT):
        c0 = ti * 512
        xb = xtile[ti % 2]
        lds = ldx[ti % 2]
        def fldx(sy, ti=ti, c0=c0, xb=xb, lds=lds):
            wait_marks(sy, waits)
            if ti >= 2:
                wait_marks(sy, [tile_done[ti - 2]])
            for c in range(NCH):
                lds.inc(sy.dma_start(out=xb[:, c * 512:(c + 1) * 512],
                                     in_=x_in[c * 128:(c + 1) * 128, c0:c0 + 512]), 16)
        P.on("sync", fldx)
        ld_mark = lds.mark
        for c in range(NCH):
            buf = sq[c % 2]
            gidx = ti * NCH + c
            def fsq(sc, c=c, buf=buf, xb=xb, ld_mark=ld_mark, gidx=gidx):
                wait_marks(sc, [ld_mark, init_mark])
                if gidx >= 2:
                    wait_marks(sc, [mmpair[gidx - 2]])
                S.inc(sc.activation(buf[:], xb[:, c * 512:(c + 1) * 512], AF.Square))
                sq_mark[gidx] = S.mark
            P.on("scalar", fsq)
            def fmm(t, c=c, buf=buf, xb=xb, ld_mark=ld_mark, gidx=gidx):
                wait_marks(t, [ld_mark, sq_mark[gidx]])
                t.matmul(mu_ps[:], ones128[:], xb[:, c * 512:(c + 1) * 512],
                         start=(c == 0), stop=(c == NCH - 1))
                Tn.inc(t.matmul(e2_ps[:], ones128[:], buf[:],
                                start=(c == 0), stop=(c == NCH - 1)))
                mmpair[gidx] = Tn.mark
            P.on("tensor", fmm)
        stats_mark = Tn.mark
        def fv1(v, stats_mark=stats_mark):
            wait_marks(v, [stats_mark])
            v.wait_ge(V.h, V.n)
            V.inc(v.tensor_scalar(rows["mu"][:], mu_ps[:], 1.0 / 1024.0, None, ALU.mult))
            V.inc(v.tensor_scalar(rows["e2"][:], e2_ps[:], 1.0 / 1024.0, None, ALU.mult))
            v.wait_ge(V.h, V.n)
            V.inc(v.tensor_mul(rows["t1"][:], rows["mu"][:], rows["mu"][:]))
            v.wait_ge(V.h, V.n)
            V.inc(v.tensor_sub(rows["var"][:], rows["e2"][:], rows["t1"][:]))
        P.on("vector", fv1)
        var_mark = V.mark
        def fs1(sc, var_mark=var_mark):
            wait_marks(sc, [var_mark])
            S.inc(sc.activation(rows["sd"][:], rows["var"][:], AF.Sqrt,
                                bias=eps_t[0:1, 0:1], scale=1.0))
        P.on("scalar", fs1)
        sd_mark = S.mark
        def fv2(v, sd_mark=sd_mark):
            wait_marks(v, [sd_mark])
            v.wait_ge(V.h, V.n)
            V.inc(v.reciprocal(rows["rstd"][:], rows["sd"][:]))
            v.wait_ge(V.h, V.n)
            V.inc(v.scalar_tensor_tensor(rows["nmr"][:], rows["mu"][:], -1.0, rows["rstd"][:],
                                         ALU.mult, ALU.mult))
        P.on("vector", fv2)
        smalls_mark = V.mark
        def frep(t, smalls_mark=smalls_mark):
            wait_marks(t, [smalls_mark])
            t.matmul(repA[:], ones1[:], rows["rstd"][:], start=True, stop=True)
            Tn.inc(t.matmul(repB[:], ones1[:], rows["nmr"][:], start=True, stop=True))
        P.on("tensor", frep)
        reps_mark = Tn.mark
        def fap(v, c0=c0, reps_mark=reps_mark, xb=xb):
            wait_marks(v, [reps_mark])
            for c in range(NCH):
                src_ = xb[:, c * 512:(c + 1) * 512]
                dst = xln_sb[:, c * T_CORE + c0: c * T_CORE + c0 + 512]
                v.wait_ge(V.h, V.n)
                V.inc(v.tensor_mul(tmp[:], src_, repA[:]))
                v.wait_ge(V.h, V.n)
                V.inc(v.tensor_add(dst, tmp[:], repB[:]))
        P.on("vector", fap)
        tile_done[ti] = V.mark
        def fg(t, td=V.mark):
            wait_marks(t, [td])
        P.on("tensor", fg)
    stk.close()
    return V.mark


def emit_proj_fm(P, tag, src_sb, w_dram, w_col0, n_out_ch, evac, waits):
    V, Tn = P.E["vector"], P.E["tensor"]
    ws = [P.sem(f"pjw{tag}{i}") for i in range(2)]
    ws_base = [ws[0].n, ws[1].n]
    stk = contextlib.ExitStack()
    wbuf = [P.sb(f"pjw{tag}{i}", (128, NCH * 128), BF16, stk, side="right") for i in range(2)]
    ps = [P.ps(f"pjps{tag}{i}", (128, 512), stack=stk) for i in range(2)]
    w_r = w_dram.rearrange("(c p) o -> p c o", p=128)

    def load_w(g, oc):
        s = ws[oc % 2]
        s.inc(g.dma_start(out=wbuf[oc % 2][:].rearrange("p (c k) -> p c k", c=NCH),
                          in_=w_r[:, :, w_col0 + oc * 128: w_col0 + (oc + 1) * 128]), 16)

    def f0(g):
        wait_marks(g, waits)
        load_w(g, 0)
        if n_out_ch > 1:
            load_w(g, 1)
    P.on("gpsimd", f0)

    NT = T_CORE // 512
    evac_mark = {}
    mm_last = {}
    for oc in range(n_out_ch):
        for tt in range(NT):
            it = oc * NT + tt
            pp = it % 2
            def fmm(t, oc=oc, tt=tt, pp=pp, it=it):
                wait_marks(t, waits)
                t.wait_ge(ws[oc % 2].h, ws_base[oc % 2] + 16 * (oc // 2 + 1))
                if it >= 2:
                    wait_marks(t, [evac_mark[it - 2]])
                for c in range(NCH):
                    i = t.matmul(ps[pp][:], wbuf[oc % 2][:, c * 128:(c + 1) * 128],
                                 src_sb[:, c * T_CORE + tt * 512: c * T_CORE + tt * 512 + 512],
                                 start=(c == 0), stop=(c == NCH - 1))
                Tn.inc(i)
            P.on("tensor", fmm)
            mm_mark = Tn.mark
            mm_last[oc] = mm_mark
            def fev(v, oc=oc, tt=tt, pp=pp, mm_mark=mm_mark):
                wait_marks(v, [mm_mark])
                evac(v, ps[pp], oc, tt)
            P.on("vector", fev)
            evac_mark[it] = V.mark
        nxt = oc + 2
        if nxt < n_out_ch:
            def fpre(g, nxt=nxt, mm_mark=mm_last[oc]):
                wait_marks(g, [mm_mark])
                load_w(g, nxt)
            P.on("gpsimd", fpre)
    stk.close()
    return V.mark


def ktv(j):
    gk = j // 4
    return (gk % 2) * 16 + (gk // 2) * 4 + (j % 4)


def emit_l0attn(P, x_in, x_out, xln_sb, qT, attn_sb, wqk_d, wv_d, wproj_d, bqk_sb,
                bv_sb, bproj_sb, cos_sb, sin_sb, mask_sb, waits):
    V, S, Tn = P.E["vector"], P.E["scalar"], P.E["tensor"]
    gq = P.sem("at_gdma")
    cc = P.sem("at_cc")
    xl = [P.sem(f"at_xl{i}") for i in range(2)]
    xst = [P.sem(f"at_xst{i}") for i in range(2)]
    vstq = P.sem("at_vst")

    k_own = P.dram_tmp("k_own", (D, T_CORE), BF16)
    v_own = P.dram_tmp("v_own", (NH * T_CORE, 64), BF16)
    k_all = P.dram_tmp("k_all", (2 * D, T_CORE), BF16)
    v_all = P.dram_tmp("v_all", (2 * NH * T_CORE, 64), BF16)

    stON = contextlib.ExitStack()
    onesf = P.sb("at_ones1", (1, 128), F32, stON, side="right")
    def fvinit(v):
        wait_marks(v, waits)
        V.inc(v.memset(onesf[:], 1.0))
    P.on("vector", fvinit)
    ones_mark = V.mark

    stQK = contextlib.ExitStack()
    raw = P.sb("qkraw", (128, NCH * T_CORE), BF16, stQK, side="right")
    scat_mark = None
    prev_pj = None
    for which in ("q", "k"):
        stP = contextlib.ExitStack()
        boff = 0 if which == "q" else NCH
        def mk_evac(boff):
            def evac(v, ps_t, oc, tt):
                V.inc(v.tensor_scalar(raw[:, oc * T_CORE + tt * 512: oc * T_CORE + tt * 512 + 512],
                                      ps_t[:], bqk_sb[:, boff + oc: boff + oc + 1],
                                      None, ALU.add))
            return evac
        pj_mark = emit_proj_fm(P, which, xln_sb, wqk_d, 0 if which == "q" else D,
                               NCH, mk_evac(boff),
                               waits + ([prev_pj] if which == "k" else []))
        ropeA = P.sb("ropeA" + which, (128, 512), F32, stP, side="right")
        ropeB = P.sb("ropeB" + which, (128, 512), F32, stP, side="right")
        re_t = P.sb("re_t" + which, (128, 512), BF16, stP, side="right")
        ro_t = P.sb("ro_t" + which, (128, 512), BF16, stP, side="right")
        prev_pj = pj_mark
        if which == "k":
            kfin = P.sb("kfin", (128, 2 * T_CORE), BF16, stP, side="right")
        for m in range(4):
            for ti in range(T_CORE // 512):
                t0 = ti * 512
                def frope(v, m=m, t0=t0, pj_mark=pj_mark, scat_mark=scat_mark):
                    wait_marks(v, [pj_mark])
                    if scat_mark is not None:
                        wait_marks(v, [scat_mark])
                    e = raw[:, (2 * m) * T_CORE + t0: (2 * m) * T_CORE + t0 + 512]
                    o = raw[:, (2 * m + 1) * T_CORE + t0: (2 * m + 1) * T_CORE + t0 + 512]
                    cw = cos_sb[:, t0:t0 + 512]
                    sw = sin_sb[:, t0:t0 + 512]
                    v.wait_ge(V.h, V.n)
                    V.inc(v.tensor_mul(ropeA[:], e, cw))
                    V.inc(v.tensor_mul(ropeB[:], o, sw))
                    v.wait_ge(V.h, V.n)
                    V.inc(v.tensor_sub(re_t[:], ropeA[:], ropeB[:]))
                    v.wait_ge(V.h, V.n)
                    V.inc(v.tensor_mul(ropeA[:], e, sw))
                    V.inc(v.tensor_mul(ropeB[:], o, cw))
                    v.wait_ge(V.h, V.n)
                    V.inc(v.tensor_add(ro_t[:], ropeA[:], ropeB[:]))
                P.on("vector", frope)
                rope_mark = V.mark
                def fscat(g, which=which, m=m, t0=t0, rope_mark=rope_mark):
                    wait_marks(g, [rope_mark])
                    for hl in range(4):
                        dc = 2 * m + hl // 2
                        base = (hl % 2) * 64
                        if which == "q":
                            dst = qT[:, dc * T_CORE + t0: dc * T_CORE + t0 + 512]
                        else:
                            dst = kfin[:, (dc % 2) * T_CORE + t0: (dc % 2) * T_CORE + t0 + 512]
                        gq.inc(g.dma_start(out=dst[base:base + 32, :],
                                           in_=re_t[hl * 32:(hl + 1) * 32, :]), 16)
                        gq.inc(g.dma_start(out=dst[base + 32:base + 64, :],
                                           in_=ro_t[hl * 32:(hl + 1) * 32, :]), 16)
                P.on("gpsimd", fscat)
                scat_mark = gq.mark
            if which == "k":
                def fkst(g, m=m, scat_mark=scat_mark):
                    wait_marks(g, [scat_mark])
                    for dcl in range(2):
                        dc = 2 * m + dcl
                        gq.inc(g.dma_start(out=k_own[dc * 128:(dc + 1) * 128, :],
                                           in_=kfin[:, dcl * T_CORE:(dcl + 1) * T_CORE]), 16)
                P.on("gpsimd", fkst)
                scat_mark = gq.mark
        stP.close()
    qk_all = gq.mark
    stQK.close()

    stV = contextlib.ExitStack()
    wv_sb = P.sb("wv_sb", (128, NCH * 1024), BF16, stV, side="right")
    v_sbt = P.sb("v_sbt", (128, 512), BF16, stV, side="right")
    v_ps = [P.ps(f"v_ps{i}", (128, 512), stack=stV) for i in range(2)]
    wv_r = wv_d.rearrange("(c p) o -> p c o", p=128)

    def fvw(g):
        wait_marks(g, waits)
        wait_marks(g, [qk_all])
        gq.inc(g.dma_start(out=wv_sb[:].rearrange("p (c o) -> p c o", c=NCH), in_=wv_r), 16)
    P.on("gpsimd", fvw)
    vw_mark = gq.mark

    vev = {}
    vst = {}
    for tc in range(T_CORE // 128):
        for half in range(2):
            it = tc * 2 + half
            pp = it % 2
            def fvmm(t, tc=tc, half=half, pp=pp, it=it):
                wait_marks(t, waits)
                wait_marks(t, [vw_mark, ones_mark, qk_all])
                if it >= 2:
                    wait_marks(t, [vev[it - 2]])
                for c in range(NCH):
                    t.matmul(v_ps[pp][:], xln_sb[:, c * T_CORE + tc * 128: c * T_CORE + (tc + 1) * 128],
                             wv_sb[:, c * 1024 + half * 512: c * 1024 + half * 512 + 512],
                             start=(c == 0), stop=False)
                Tn.inc(t.matmul(v_ps[pp][:], onesf[:], bv_sb[0:1, half * 512:(half + 1) * 512],
                                start=False, stop=True))
            P.on("tensor", fvmm)
            vmm_mark = Tn.mark
            def fvev(v, pp=pp, vmm_mark=vmm_mark, it=it):
                wait_marks(v, [vmm_mark])
                if it >= 1:
                    wait_marks(v, [vst[it - 1]])
                V.inc(v.tensor_copy(v_sbt[:], v_ps[pp][:]))
            P.on("vector", fvev)
            vev[it] = V.mark
            def fvst(g, tc=tc, half=half, ev=vev[it]):
                wait_marks(g, [ev])
                vr = v_own.rearrange("(h t) d -> h t d", h=NH)
                for hh in range(8):
                    h = half * 8 + hh
                    vstq.inc(g.dma_start(out=vr[h, tc * 128:(tc + 1) * 128, :],
                                         in_=v_sbt[:, hh * 64:(hh + 1) * 64]), 16)
            P.on("gpsimd", fvst)
            vst[it] = vstq.mark
    stV.close()
    all_stored = [gq.mark, vstq.mark]

    def fcc(g):
        wait_marks(g, all_stored)
        rg = [[0, 1], [2, 3], [4, 5], [6, 7]]
        cc.inc(g.collective_compute("AllGather", ALU.bypass, replica_groups=rg,
                                    ins=[k_own.opt()], outs=[k_all.opt()]))
        cc.inc(g.collective_compute("AllGather", ALU.bypass, replica_groups=rg,
                                    ins=[v_own.opt()], outs=[v_all.opt()]))
    P.on("gpsimd", fcc)
    cc_mark = cc.mark

    stAT = contextlib.ExitStack()
    khead = [P.sb(f"khead{i}", (128, 2 * T_CORE), BF16, stAT, side="right") for i in range(2)]
    vhead = [P.sb(f"vhead{i}", (128, 32 * 65), BF16, stAT, side="right") for i in range(2)]
    qhead = [P.sb(f"qhead{i}", (64, T_CORE), BF16, stAT, side="right") for i in range(2)]
    mqv = [P.sb(f"mqv{i}", (64, 2), F32, stAT, side="right") for i in range(2)]
    gbias = [P.sb(f"gbias{i}", (128, 1), F32, stAT, side="right") for i in range(2)]
    g1 = P.sb("g1", (1, 1), F32, stAT, side="right")
    mtile = [P.sb(f"mtile{i}", (128, 512), F32, stAT, side="right") for i in range(2)]
    linv = P.sb("linv", (1, 512), F32, stAT, side="right")
    pt = [P.sb(f"pt{i}", (128, 512), BF16, stAT, side="right") for i in range(4)]
    otmp = P.sb("otmp", (64, 512), BF16, stAT, side="right")
    st_ps = [P.ps(f"st_ps{i}", (128, 512), stack=stAT) for i in range(4)]
    ot_ps = [P.ps(f"ot_ps{i}", (65, 512), stack=stAT) for i in range(2)]
    gg_ps = P.ps("gg_ps", (128, 2), stack=stAT)
    rep_ps = P.ps("rep_ps", (64, 512), stack=stAT)

    def ktile_loc(j):
        gk = j // 4
        return gk % 2, (gk // 2) * 512 + (j % 4) * 128

    ep_done = {}
    kvq = P.sem("at_kvq")
    slot_prev = {0: [], 1: []}
    st_guard = {}
    prev_mask_exp = None
    S0 = S.n
    exp_jobs = []
    def exp_pred(e):
        return (S, S0 + e + 1)
    gtile = 0
    mseq = 0
    pm_exp = {0: None, 1: None}
    for h in range(NH):
        hb = (h % 2) * 64
        hc = h // 2
        hs = h % 2
        kh, vh, qh, mq, gbi = khead[hs], vhead[hs], qhead[hs], mqv[hs], gbias[hs]
        prevm = list(slot_prev[hs])
        def fkv(sy, h=h, kh=kh, vh=vh, pm=prevm):
            wait_marks(sy, [cc_mark] + pm)
            va = v_all.rearrange("(r h t) d -> r h t d", r=2, h=NH)
            for r in range(2):
                kvq.inc(sy.dma_start(out=kh[0:64, r * T_CORE:(r + 1) * T_CORE],
                                     in_=k_all[r * D + 64 * h: r * D + 64 * h + 64, :]), 16)
                kvq.inc(sy.dma_start(
                    out=vh[:].rearrange("p (kt c) -> p kt c", c=65)[:, r * 16:(r + 1) * 16, 0:64],
                    in_=va[r, h].rearrange("(t p) c -> p t c", p=128)), 16)
        P.on("sync", fkv)
        kv_mark = kvq.mark
        def fvh1(v, kv_mark=kv_mark, h=h, hb=hb, hc=hc, kh=kh, vh=vh, qh=qh, mq=mq,
                 pm=prevm):
            wait_marks(v, [kv_mark] + pm)
            if h < 2:
                v.memset(vh[:].rearrange("p (kt c) -> p kt c", c=65)[:, :, 64:65], 1.0)
            V.inc(v.tensor_copy(qh[:], qT[hb:hb + 64, hc * T_CORE:(hc + 1) * T_CORE]))
            v.wait_ge(V.h, V.n)
            v.tensor_reduce(mq[:, 0:1], qh[:], mybir.AxisListType.X, ALU.max,
                            apply_absolute_value=True)
            V.inc(v.tensor_reduce(mq[:, 1:2], kh[0:64, :], mybir.AxisListType.X, ALU.max,
                                  apply_absolute_value=True))
        P.on("vector", fvh1)
        mq_mark = V.mark
        def fgb(t, mq_mark=mq_mark, mq=mq):
            wait_marks(t, [mq_mark])
            Tn.inc(t.matmul(gg_ps[0:1, 0:1], mq[:, 0:1], mq[:, 1:2], start=True, stop=True))
        P.on("tensor", fgb)
        gmm_mark = Tn.mark
        def fgb2(v, gmm_mark=gmm_mark):
            wait_marks(v, [gmm_mark])
            V.inc(v.tensor_scalar(g1[:], gg_ps[0:1, 0:1], -0.125, None, ALU.mult))
        P.on("vector", fgb2)
        g1_mark = V.mark
        def fgb3(t, g1_mark=g1_mark):
            wait_marks(t, [g1_mark])
            Tn.inc(t.matmul(gg_ps[:, 1:2], onesf[:], g1[:], start=True, stop=True))
        P.on("tensor", fgb3)
        gb_mark = Tn.mark
        def fgb4(v, gb_mark=gb_mark, gbi=gbi, pm=prevm):
            wait_marks(v, [gb_mark] + pm)
            V.inc(v.tensor_copy(gbi[:], gg_ps[:, 1:2]))
        P.on("vector", fgb4)
        gbias_mark = V.mark

        for si in range(4):
            KT = 8 * si + 8
            osel = si % 2
            prev_ep = ep_done.get((h, si - 2)) if si >= 2 else ep_done.get((h - 1, si + 2))
            pend_fav = None
            for j in range(KT):
                rk, loc = ktile_loc(j)
                dj = j - 8 * si
                pp = gtile % 4
                gtile += 1
                def fsc(t, si=si, j=j, rk=rk, loc=loc, pp=pp, gbias_mark=gbias_mark,
                        prev_ep=prev_ep, kh=kh, qh=qh, guard=st_guard.get(pp)):
                    wait_marks(t, [gbias_mark])
                    if guard is not None:
                        wait_marks(t, guard)
                    if j == 0 and prev_ep is not None:
                        wait_marks(t, [prev_ep])
                    Tn.inc(t.matmul(st_ps[pp][:],
                                    kh[0:64, rk * T_CORE + loc: rk * T_CORE + loc + 128],
                                    qh[:, si * 512:(si + 1) * 512], start=True, stop=True))
                P.on("tensor", fsc)
                sc_mark = Tn.mark
                if dj >= 0:
                    ms = mseq % 2
                    mseq += 1
                    def fmask(v, pp=pp, dj=dj, ms=ms, sc_mark=sc_mark, pme=pm_exp[ms]):
                        wait_marks(v, [sc_mark])
                        if pme is not None:
                            wait_marks(v, [pme])
                        V.inc(v.scalar_tensor_tensor(mtile[ms][:], st_ps[pp][:], 0.125,
                                                     mask_sb[:, dj * 512:(dj + 1) * 512],
                                                     ALU.mult, ALU.add))
                    P.on("vector", fmask)
                    exp_jobs.append((V.mark, True, pp, ms, gbi))
                    pm_exp[ms] = exp_pred(len(exp_jobs) - 1)
                else:
                    exp_jobs.append((sc_mark, False, pp, 0, gbi))
                exp_mark = exp_pred(len(exp_jobs) - 1)
                def fav(t, j=j, pp=pp, osel=osel, exp_mark=exp_mark, KT=KT, vh=vh):
                    wait_marks(t, [exp_mark])
                    Tn.inc(t.matmul(ot_ps[osel][:], vh[:, ktv(j) * 65: ktv(j) * 65 + 65],
                                    pt[pp][:], start=(j == 0), stop=(j == KT - 1)))
                if pend_fav is not None:
                    pf, ppp, pexp = pend_fav
                    P.on("tensor", pf)
                    st_guard[ppp] = [pexp, Tn.mark]
                pend_fav = (fav, pp, exp_mark)
            pf, ppp, pexp = pend_fav
            P.on("tensor", pf)
            st_guard[ppp] = [pexp, Tn.mark]
            av_mark = Tn.mark
            def fep1(v, osel=osel, av_mark=av_mark):
                wait_marks(v, [av_mark])
                V.inc(v.reciprocal(linv[:], ot_ps[osel][64:65, :]))
            P.on("vector", fep1)
            li_mark = V.mark
            def fep2(t, li_mark=li_mark):
                wait_marks(t, [li_mark])
                Tn.inc(t.matmul(rep_ps[:], onesf[0:1, 0:64], linv[:], start=True, stop=True))
            P.on("tensor", fep2)
            rep_mark = Tn.mark
            def fep3(v, osel=osel, rep_mark=rep_mark, hb=hb, hc=hc, si=si):
                wait_marks(v, [rep_mark])
                v.wait_ge(V.h, V.n)
                V.inc(v.tensor_copy(otmp[:], ot_ps[osel][0:64, :]))
                v.wait_ge(V.h, V.n)
                V.inc(v.tensor_mul(
                    attn_sb[hb:hb + 64, hc * T_CORE + si * 512: hc * T_CORE + si * 512 + 512],
                    otmp[:], rep_ps[:]))
            P.on("vector", fep3)
            ep_done[(h, si)] = V.mark
        slot_prev[hs] = [V.mark, Tn.mark, exp_pred(len(exp_jobs) - 1)]

    def fexps(sc, jobs=list(exp_jobs)):
        for wm, masked, pp, ms, gbi in jobs:
            wait_marks(sc, [wm])
            if masked:
                S.inc(sc.activation(pt[pp][:], mtile[ms][:], AF.Exp,
                                    bias=gbi[:, 0:1], scale=1.0))
            else:
                S.inc(sc.activation(pt[pp][:], st_ps[pp][:], AF.Exp,
                                    bias=gbi[:, 0:1], scale=0.125))
    P.on("scalar", fexps)
    assert S.n == S0 + len(exp_jobs)

    attn_done = [V.mark, Tn.mark, S.mark]
    stAT.close()

    stPJ = contextlib.ExitStack()
    wpj_sb = P.sb("wpj_sb", (128, NCH * 1024), BF16, stPJ, side="right")
    ptmp = P.sb("ptmp", (128, 512), F32, stPJ, side="right")
    xres = [P.sb(f"xres{i}", (128, 512), F32, stPJ, side="right") for i in range(2)]
    pj_ps = [P.ps(f"pj_ps{i}", (128, 512), stack=stPJ) for i in range(2)]
    wpj_r = wproj_d.rearrange("(c p) o -> p c o", p=128)
    def fpw(g):
        wait_marks(g, attn_done)
        gq.inc(g.dma_start(out=wpj_sb[:].rearrange("p (c o) -> p c o", c=NCH), in_=wpj_r), 16)
    P.on("gpsimd", fpw)
    pw_mark = gq.mark
    res_done = {}
    for oc in range(NCH):
        for tt in range(4):
            it = oc * 4 + tt
            pp = it % 2
            def fpm(t, oc=oc, tt=tt, pp=pp, it=it):
                wait_marks(t, [pw_mark] + attn_done)
                if it >= 2:
                    wait_marks(t, [res_done[it - 2]])
                for c in range(NCH):
                    i = t.matmul(pj_ps[pp][:], wpj_sb[:, c * 1024 + oc * 128: c * 1024 + (oc + 1) * 128],
                                 attn_sb[:, c * T_CORE + tt * 512: c * T_CORE + tt * 512 + 512],
                                 start=(c == 0), stop=(c == NCH - 1))
                Tn.inc(i)
            P.on("tensor", fpm)
            pm_mark = Tn.mark
            def fxld(sy, oc=oc, tt=tt, pp=pp, it=it):
                if it >= 2:
                    wait_marks(sy, [res_done[it - 2]])
                else:
                    wait_marks(sy, attn_done)
                xl[pp].inc(sy.dma_start(out=xres[pp][:],
                                        in_=x_in[oc * 128:(oc + 1) * 128, tt * 512:(tt + 1) * 512]), 16)
            P.on("sync", fxld)
            xld_mark = xl[pp].mark
            def fpr(v, oc=oc, pp=pp, pm_mark=pm_mark, xld_mark=xld_mark):
                wait_marks(v, [pm_mark, xld_mark])
                v.wait_ge(V.h, V.n)
                V.inc(v.tensor_scalar(ptmp[:], pj_ps[pp][:], bproj_sb[:, oc:oc + 1], None, ALU.add))
                v.wait_ge(V.h, V.n)
                V.inc(v.tensor_add(xres[pp][:], ptmp[:], xres[pp][:]))
            P.on("vector", fpr)
            wr_mark = V.mark
            def fxst(g, oc=oc, tt=tt, pp=pp, wr_mark=wr_mark):
                wait_marks(g, [wr_mark])
                xst[pp].inc(g.dma_start(out=x_out[oc * 128:(oc + 1) * 128, tt * 512:(tt + 1) * 512],
                                        in_=xres[pp][:]), 16)
            P.on("gpsimd", fxst)
            res_done[it] = xst[pp].mark
    fin = [xst[0].mark, xst[1].mark]
    G = P.E["gpsimd"]
    def ffin(g, fin=fin):
        wait_marks(g, fin)
        G.inc(g.nop())
    P.on("gpsimd", ffin)
    stPJ.close()
    stON.close()
    return G.mark


def emit_l0ffn(P, x_in, x_out, xln_sb, wg_d, wu_d, wd_d, waits):
    V, S, Tn = P.E["vector"], P.E["scalar"], P.E["tensor"]
    wgu = [P.sem(f"ffnw{i}") for i in range(2)]
    wds = P.sem("ffnwd")
    xl = [P.sem(f"ffnxl{i}") for i in range(2)]
    xst = [P.sem(f"ffnxst{i}") for i in range(2)]
    stk = contextlib.ExitStack()
    SW = 512
    NSTRIP = T_CORE // SW
    wgt = [P.sb(f"wg{i}", (128, NCH * 128), BF16, stk, side="right") for i in range(2)]
    wut = [P.sb(f"wu{i}", (128, NCH * 128), BF16, stk, side="right") for i in range(2)]
    wd_res = P.sb("wd_res", (128, HCH * 1024), BF16, stk, side="right")
    gbuf = P.sb("gbuf", (128, SW), BF16, stk, side="right")
    ubuf = P.sb("ubuf", (128, SW), BF16, stk, side="right")
    h_sb = P.sb("h_sb", (128, HCH * SW), BF16, stk, side="right")
    xres = [P.sb(f"fxres{i}", (128, SW), F32, stk, side="right") for i in range(2)]
    g_ps_ = [P.ps(f"gg_ps{i}", (128, SW), stack=stk) for i in range(2)]
    u_ps_ = [P.ps(f"uu_ps{i}", (128, SW), stack=stk) for i in range(2)]
    down_pp = [P.ps(f"down_pp{i}", (128, 512), stack=stk) for i in range(2)]

    wg_r = wg_d.rearrange("(c p) h -> p c h", p=128)
    wu_r = wu_d.rearrange("(c p) h -> p c h", p=128)
    wd_r = wd_d.rearrange("(c p) d -> p c d", p=128)
    wgu_base = [wgu[0].n, wgu[1].n]
    wds_base = wds.n

    def load_gu(g, it):
        hc = it % HCH
        s = wgu[it % 2]
        s.inc(g.dma_start(out=wgt[it % 2][:].rearrange("p (c k) -> p c k", c=NCH),
                          in_=wg_r[:, :, hc * 128:(hc + 1) * 128]), 16)
        s.inc(g.dma_start(out=wut[it % 2][:].rearrange("p (c k) -> p c k", c=NCH),
                          in_=wu_r[:, :, hc * 128:(hc + 1) * 128]), 16)

    def f0(g):
        wait_marks(g, waits)
        wds.inc(g.dma_start(out=wd_res[:].rearrange("p (c d) -> p c d", c=HCH), in_=wd_r), 16)
        load_gu(g, 0)
        load_gu(g, 1)
    P.on("gpsimd", f0)

    h_mark = {}
    res_mark = {}
    down_mark = {}
    for st_i in range(NSTRIP):
        c0 = st_i * SW
        for hc in range(HCH):
            it = st_i * HCH + hc
            slot = hc % 2
            def fmm(t, hc=hc, slot=slot, c0=c0, it=it):
                wait_marks(t, waits)
                t.wait_ge(wgu[it % 2].h, wgu_base[it % 2] + 32 * (it // 2 + 1))
                if it >= 2:
                    wait_marks(t, [h_mark[it - 2]])
                for c in range(NCH):
                    t.matmul(g_ps_[slot][:], wgt[slot][:, c * 128:(c + 1) * 128],
                             xln_sb[:, c * T_CORE + c0: c * T_CORE + c0 + SW],
                             start=(c == 0), stop=(c == NCH - 1))
                for c in range(NCH):
                    i = t.matmul(u_ps_[slot][:], wut[slot][:, c * 128:(c + 1) * 128],
                                 xln_sb[:, c * T_CORE + c0: c * T_CORE + c0 + SW],
                                 start=(c == 0), stop=(c == NCH - 1))
                Tn.inc(i)
            P.on("tensor", fmm)
            gu_mark = Tn.mark
            def fsilu(sc, gu_mark=gu_mark, slot=slot):
                wait_marks(sc, [gu_mark])
                S.inc(sc.activation(gbuf[:], g_ps_[slot][:], AF.Silu, bias=0.0, scale=1.0 / 127.0))
            P.on("scalar", fsilu)
            silu_mark = S.mark
            def fup(v, silu_mark=silu_mark, gu_mark=gu_mark, hc=hc, st_i=st_i, slot=slot):
                wait_marks(v, [gu_mark])
                if st_i >= 1:
                    wait_marks(v, [down_mark[st_i - 1]])
                v.wait_ge(V.h, V.n)
                V.inc(v.tensor_scalar(ubuf[:], u_ps_[slot][:], 1.0 / 127.0, None, ALU.mult))
                wait_marks(v, [silu_mark])
                v.wait_ge(V.h, V.n)
                V.inc(v.tensor_mul(h_sb[:, hc * SW:(hc + 1) * SW], gbuf[:], ubuf[:]))
            P.on("vector", fup)
            h_mark[it] = V.mark
            if it + 2 < NSTRIP * HCH:
                def fpre(g, it=it, gu_mark=gu_mark):
                    wait_marks(g, [gu_mark])
                    load_gu(g, it + 2)
                P.on("gpsimd", fpre)
        h_all = V.mark
        for dd in range(NCH):
            bank = down_pp[dd % 2]
            it2 = st_i * NCH + dd
            pp = it2 % 2
            def fdown(t, dd=dd, bank=bank, h_all=h_all, st_i=st_i, it2=it2):
                wait_marks(t, [h_all])
                t.wait_ge(wds.h, wds_base + 16)
                if it2 >= 2:
                    wait_marks(t, [res_mark[it2 - 2]])
                for hc in range(HCH):
                    i = t.matmul(bank[:, 0:SW],
                                 wd_res[:, hc * 1024 + dd * 128: hc * 1024 + (dd + 1) * 128],
                                 h_sb[:, hc * SW:(hc + 1) * SW],
                                 start=(hc == 0), stop=(hc == HCH - 1))
                Tn.inc(i)
            P.on("tensor", fdown)
            dd_mark = Tn.mark
            def fxld(sy, dd=dd, c0=c0, pp=pp, it2=it2, h_all=h_all):
                if it2 >= 2:
                    wait_marks(sy, [res_mark[it2 - 2]])
                else:
                    wait_marks(sy, [h_all])
                xl[pp].inc(sy.dma_start(out=xres[pp][:],
                                        in_=x_in[dd * 128:(dd + 1) * 128, c0:c0 + SW]), 16)
            P.on("sync", fxld)
            xld_mark = xl[pp].mark
            def fres(v, bank=bank, pp=pp, dd_mark=dd_mark, xld_mark=xld_mark):
                wait_marks(v, [dd_mark, xld_mark])
                V.inc(v.scalar_tensor_tensor(xres[pp][:], bank[:, 0:SW], 0.5 / 127.0,
                                             xres[pp][:], ALU.mult, ALU.add))
            P.on("vector", fres)
            wr_mark = V.mark
            def fxst(g, dd=dd, c0=c0, pp=pp, wr_mark=wr_mark):
                wait_marks(g, [wr_mark])
                xst[pp].inc(g.dma_start(out=x_out[dd * 128:(dd + 1) * 128, c0:c0 + SW],
                                        in_=xres[pp][:]), 16)
            P.on("gpsimd", fxst)
            res_mark[it2] = xst[pp].mark
        down_mark[st_i] = Tn.mark
    fin = [xst[0].mark, xst[1].mark]
    G = P.E["gpsimd"]
    def ffin(g, fin=fin):
        wait_marks(g, fin)
        G.inc(g.nop())
    P.on("gpsimd", ffin)
    stk.close()
    return G.mark




def emit_flag(P, scl):
    V, Tn = P.E["vector"], P.E["tensor"]
    fl = P.sb("flags", (1, 8), F32, side="left")
    on1 = P.sb("flag_on1", (1, 128), F32, side="left")
    stF = contextlib.ExitStack()
    bc_ps = P.ps("flag_bc", (128, 4), stack=stF)
    def f(v):
        v.wait_ge(V.h, V.n)
        V.inc(v.memset(fl[0:1, 0:1], 1.0))
        v.wait_ge(V.h, V.n)
        V.inc(v.tensor_scalar(fl[0:1, 1:2], fl[0:1, 0:1], float(np.float32(2.0 ** -30)), 1.0,
                              ALU.add, ALU.subtract))
        v.wait_ge(V.h, V.n)
        V.inc(v.tensor_scalar(fl[0:1, 1:2], fl[0:1, 1:2], float(np.float32(2.0 ** 30)), None,
                              ALU.mult))
        v.wait_ge(V.h, V.n)
        V.inc(v.tensor_scalar(fl[0:1, 2:3], fl[0:1, 1:2], -1.0, 1.0, ALU.mult, ALU.add))
        v.wait_ge(V.h, V.n)
        V.inc(v.memset(fl[0:1, 3:4], 32767.0))
        V.inc(v.memset(fl[0:1, 4:5], 7.0))
        V.inc(v.memset(on1[:], 1.0))
        v.wait_ge(V.h, V.n)
        V.inc(v.reciprocal(fl[0:1, 3:4], fl[0:1, 3:4]))
        v.wait_ge(V.h, V.n)
        V.inc(v.reciprocal(fl[0:1, 4:5], fl[0:1, 4:5]))
    P.on("vector", f)
    fm = V.mark
    def fb(t, fm=fm):
        wait_marks(t, [fm])
        Tn.inc(t.matmul(bc_ps[:], on1[:], fl[0:1, 1:5], start=True, stop=True))
    P.on("tensor", fb)
    bm = Tn.mark
    def fc(v, bm=bm):
        wait_marks(v, [bm])
        v.wait_ge(V.h, V.n)
        V.inc(v.tensor_copy(scl[:, 20:24], bc_ps[:]))
    P.on("vector", fc)
    stF.close()
    return fl, on1, V.mark


def _ma(v, E, dst, src, s, c):
    v.wait_ge(E.h, E.n)
    E.inc(v.tensor_scalar(dst, src, s, c, ALU.mult, ALU.add))


def emit_quant_tile(v, V, src, dst, inv_ap, pa_ap, pb_ap, y, r2, pool=False):
    v.wait_ge(V.h, V.n)
    V.inc(v.tensor_scalar(y, src, inv_ap, C1, ALU.mult, ALU.add))
    if pool:
        _ma(v, V, r2, y, 1.0, C2)
        _ma(v, V, r2, r2, 1.0, -C2)
        _ma(v, V, y, y, 1.0, -C1)
        _ma(v, V, y, y, pa_ap, 0.0)
        _ma(v, V, r2, r2, 1.0, -C1)
        _ma(v, V, r2, r2, pb_ap, 0.0)
    else:
        v.wait_ge(V.h, V.n)
        V.inc(v.tensor_scalar(r2, y, C2, C2, ALU.add, ALU.subtract))
        V.inc(v.tensor_scalar(y, y, C1, pa_ap, ALU.subtract, ALU.mult))
        v.wait_ge(V.h, V.n)
        V.inc(v.tensor_scalar(r2, r2, C1, pb_ap, ALU.subtract, ALU.mult))
    v.wait_ge(V.h, V.n)
    V.inc(v.tensor_add(dst, y, r2))


def emit_lpbq_dequant(P, tag, w_d, wdeq_d, wdeqT_d, O, I, fl, fl_mark, waits):
    V = P.E["vector"]
    gl = [P.sem("lpl" + tag + str(i)) for i in range(2)]
    gs = [P.sem("lps" + tag + str(i)) for i in range(2)]
    tq = P.sem("lpt" + tag)
    gt_ = P.sem("lpu" + tag)
    stk = contextlib.ExitStack()
    NB = I // 32
    wtile = [P.sb("lpw" + tag + str(i), (128, I), F32, stk, side="right") for i in range(2)]
    mx = P.sb("lpmx" + tag, (128, NB), F32, stk, side="right")
    sblk = P.sb("lps" + tag, (128, NB), F32, stk, side="right")
    iblk = P.sb("lpi" + tag, (128, NB), F32, stk, side="right")
    t1 = P.sb("lpt1" + tag, (128, I), F32, stk, side="right")
    r1 = P.sb("lpr1" + tag, (128, I), F32, stk, side="right")
    r2 = P.sb("lpr2" + tag, (128, I), F32, stk, side="right")
    wq = [P.sb("lpwq" + tag + str(i), (128, I), BF16, stk, side="right") for i in range(2)]
    ttile = P.sb("lptt" + tag, (128, 1024), BF16, stk, side="right")
    NC_ = O // 128
    st_mark = {}
    q_mark = {}
    for rc in range(NC_):
        wt = wtile[rc % 2]
        def fld(g, rc=rc, wt=wt):
            wait_marks(g, waits)
            if rc >= 2:
                wait_marks(g, [q_mark[rc - 2]])
            gl[rc % 2].inc(g.dma_start(out=wt[:], in_=w_d[rc * 128:(rc + 1) * 128, :]), 16)
        P.on("gpsimd", fld)
        ld_mark = gl[rc % 2].mark
        def fq(v, rc=rc, wt=wt, ld_mark=ld_mark):
            wait_marks(v, [ld_mark, fl_mark])
            if rc >= 2:
                wait_marks(v, [st_mark[rc - 2]])
            v.wait_ge(V.h, V.n)
            V.inc(v.tensor_reduce(mx[:], wt[:].rearrange("p (b k) -> p b k", k=32),
                                  mybir.AxisListType.X, ALU.max, apply_absolute_value=True))
            v.wait_ge(V.h, V.n)
            V.inc(v.tensor_scalar(sblk[:], mx[:], P._scl[:, 23:24], 1e-12, ALU.mult, ALU.max))
            v.wait_ge(V.h, V.n)
            V.inc(v.reciprocal(iblk[:], sblk[:]))
            v.wait_ge(V.h, V.n)
            V.inc(v.tensor_mul(t1[:].rearrange("p (b k) -> p b k", k=32),
                               wt[:].rearrange("p (b k) -> p b k", k=32),
                               iblk[:].broadcast_to((128, NB, 32))))
            v.wait_ge(V.h, V.n)
            V.inc(v.tensor_scalar(r1[:], t1[:], C1, C1, ALU.add, ALU.subtract))
            V.inc(v.tensor_scalar(r2[:], t1[:], C2, C2, ALU.add, ALU.subtract))
            v.wait_ge(V.h, V.n)
            V.inc(v.tensor_scalar(r1[:], r1[:], scl_flagm(P), None, ALU.mult))
            v.wait_ge(V.h, V.n)
            V.inc(v.tensor_scalar(r2[:], r2[:], scl_flag(P), None, ALU.mult))
            v.wait_ge(V.h, V.n)
            V.inc(v.tensor_add(r1[:], r1[:], r2[:]))
            v.wait_ge(V.h, V.n)
            V.inc(v.tensor_mul(wq[rc % 2][:].rearrange("p (b k) -> p b k", k=32),
                               r1[:].rearrange("p (b k) -> p b k", k=32),
                               sblk[:].broadcast_to((128, NB, 32))))
            q_mark[rc] = V.mark
        P.on("vector", fq)
        def fst(g, rc=rc):
            wait_marks(g, [q_mark[rc]])
            gs[rc % 2].inc(g.dma_start(out=wdeq_d[rc * 128:(rc + 1) * 128, :], in_=wq[rc % 2][:]), 16)
        P.on("gpsimd", fst)
        st_mark[rc] = gs[rc % 2].mark
    all_st = [gs[0].mark, gs[1].mark]
    tr_mark = {}
    for dc in range(I // 128):
        for og in range(O // 1024):
            it = dc * (O // 1024) + og
            def ftr(sc, dc=dc, og=og, it=it, all_st=all_st):
                wait_marks(sc, all_st)
                if it >= 1:
                    wait_marks(sc, [tr_mark[it - 1]])
                tq.inc(sc.dma_start_transpose(ttile[:, 0:1024],
                                              wdeq_d[og * 1024:(og + 1) * 1024,
                                                     dc * 128:(dc + 1) * 128]), 16)
            P.on("scalar", ftr)
            ld2 = tq.mark
            def fst2(g, dc=dc, og=og, ld2=ld2):
                wait_marks(g, [ld2])
                gt_.inc(g.dma_start(out=wdeqT_d[dc * 128:(dc + 1) * 128, og * 1024:(og + 1) * 1024],
                                    in_=ttile[:, 0:1024]), 16)
            P.on("gpsimd", fst2)
            tr_mark[it] = gt_.mark
    fin = gt_.mark
    stk.close()
    return fin


def scl_flag(P):
    return P._scl[:, 20:21]


def scl_flagm(P):
    return P._scl[:, 21:22]


def emit_cross_part_max(P, tag, maxbuf_ap, src_mark, amax_row, stack):
    V = P.E["vector"]
    gx = P.sem("cpm" + tag)
    col = P.sb("cpm_c" + tag, (128, 1), F32, stack, side="right")
    row = P.sb("cpm_r" + tag, (1, 128), F32, stack, side="right")
    def f1(v, src_mark=src_mark):
        wait_marks(v, src_mark if isinstance(src_mark, list) else [src_mark])
        v.wait_ge(V.h, V.n)
        V.inc(v.tensor_reduce(col[:], maxbuf_ap, mybir.AxisListType.X, ALU.max))
    P.on("vector", f1)
    m1 = V.mark
    def f2(g, m1=m1):
        wait_marks(g, [m1])
        gx.inc(g.dma_start(out=row[0:1, :], in_=col[:, 0:1]), 16)
    P.on("gpsimd", f2)
    m2 = gx.mark
    def f3(v, m2=m2):
        wait_marks(v, [m2])
        v.wait_ge(V.h, V.n)
        V.inc(v.tensor_reduce(amax_row[0:1, 0:1], row[0:1, :], mybir.AxisListType.X, ALU.max))
    P.on("vector", f3)
    return V.mark


def emit_qdq_scalars(P, tag, amax_mark, amax_row, fl, on1, flbc_mark, scl, col0, stack):
    V, Tn = P.E["vector"], P.E["tensor"]
    gq = P.sem("qsg" + tag)
    cc = P.sem("qsc" + tag)
    cc_in = P.dram_tmp("qs_in" + tag, (1, 128), F32)
    cc_out = P.dram_tmp("qs_out" + tag, (1, 128), F32)
    row = P.sb("qsr" + tag, (1, 8), F32, stack, side="right")
    zr = P.sb("qsz" + tag, (1, 128), F32, stack, side="right")
    bc_ps = P.ps("qsb" + tag, (128, 4), stack=stack)
    def fz(v, amax_mark=amax_mark):
        wait_marks(v, [amax_mark])
        v.wait_ge(V.h, V.n)
        V.inc(v.memset(zr[:], 0.0))
        v.wait_ge(V.h, V.n)
        V.inc(v.tensor_copy(zr[0:1, 0:1], amax_row[0:1, 0:1]))
    P.on("vector", fz)
    z_mark = V.mark
    def fst(g, z_mark=z_mark):
        wait_marks(g, [z_mark])
        gq.inc(g.dma_start(out=cc_in, in_=zr[:]), 16)
        g.wait_ge(gq.h, gq.n)
        cc.inc(g.collective_compute("AllReduce", ALU.max,
                                    replica_groups=[list(range(8))],
                                    ins=[cc_in.opt()], outs=[cc_out.opt()]))
        g.wait_ge(cc.h, cc.n)
        gq.inc(g.dma_start(out=row[0:1, 0:1], in_=cc_out[0:1, 0:1]), 16)
    P.on("gpsimd", fst)
    ar_mark = gq.mark
    def fsc(v, ar_mark=ar_mark):
        wait_marks(v, [ar_mark, flbc_mark])
        v.wait_ge(V.h, V.n)
        V.inc(v.tensor_scalar(row[0:1, 1:2], row[0:1, 0:1], fl[0:1, 3:4], 1e-12,
                              ALU.mult, ALU.max))
        v.wait_ge(V.h, V.n)
        V.inc(v.reciprocal(row[0:1, 2:3], row[0:1, 1:2]))
        v.wait_ge(V.h, V.n)
        V.inc(v.tensor_mul(row[0:1, 3:4], row[0:1, 1:2], fl[0:1, 2:3]))
        v.wait_ge(V.h, V.n)
        V.inc(v.tensor_mul(row[0:1, 4:5], row[0:1, 1:2], fl[0:1, 1:2]))
        v.wait_ge(V.h, V.n)
        V.inc(v.tensor_copy(row[0:1, 5:6], row[0:1, 1:2]))
    P.on("vector", fsc)
    sc_mark = V.mark
    def fbc(t, sc_mark=sc_mark):
        wait_marks(t, [sc_mark])
        Tn.inc(t.matmul(bc_ps[:], on1[:], row[0:1, 2:6], start=True, stop=True))
    P.on("tensor", fbc)
    bc_mark = Tn.mark
    def fcp(v, bc_mark=bc_mark):
        wait_marks(v, [bc_mark])
        v.wait_ge(V.h, V.n)
        V.inc(v.tensor_copy(scl[:, col0:col0 + 4], bc_ps[:]))
    P.on("vector", fcp)
    return V.mark


def emit_l1ffn(P, x_in, x_out, wg_d, wu_d, wd_d, waits, deq_cache=None):
    V, S, Tn = P.E["vector"], P.E["scalar"], P.E["tensor"]
    scl = P.sb("scl", (128, 24), F32, side="left")
    P._scl = scl
    fl, on1, fl_mark = emit_flag(P, scl)

    up_pre = P.dram_tmp("up_pre", (HID, T_CORE), BF16)
    gate_pre = P.dram_tmp("gate_pre", (HID, T_CORE), BF16)
    t_pre = P.dram_tmp("t_pre", (HID, T_CORE), BF16)
    o_pre = P.dram_tmp("o_pre", (HID, T_CORE), BF16)

    if deq_cache is None:
        wdeq_u = P.dram_tmp("wdeq_u", (HID, D), BF16)
        wdeq_g = P.dram_tmp("wdeq_g", (HID, D), BF16)
        wdeq_d2 = P.dram_tmp("wdeq_d", (D, HID), BF16)
        wdqT_u = P.dram_tmp("wdqT_u", (D, HID), BF16)
        wdqT_g = P.dram_tmp("wdqT_g", (D, HID), BF16)
        wdqT_d = P.dram_tmp("wdqT_d", (HID, D), BF16)
        dq_u = emit_lpbq_dequant(P, "u", wu_d, wdeq_u, wdqT_u, HID, D, fl, fl_mark, waits)
        dq_g = emit_lpbq_dequant(P, "g", wg_d, wdeq_g, wdqT_g, HID, D, fl, fl_mark,
                                 waits + [dq_u])
        dq_d = emit_lpbq_dequant(P, "d", wd_d, wdeq_d2, wdqT_d, D, HID, fl, fl_mark,
                                 waits + [dq_g])
    else:
        wdqT_u, wdqT_g, wdqT_d = deq_cache
        dq_u = dq_g = dq_d = fl_mark
    P._deq_cache = (wdqT_u, wdqT_g, wdqT_d)

    stLN = contextlib.ExitStack()
    xln_sb = P.sb("xln_sb", (128, NCH * T_CORE), BF16, stLN, side="right")
    xq_sb = P.sb("xq_sb", (128, NCH * T_CORE), BF16, stLN, side="right")
    ln_mark = emit_ln_stream(P, "L1", x_in, xln_sb, waits + [dq_d])
    mbx = P.sb("mbx", (128, NCH), F32, stLN, side="right")
    amax_row = P.sb("amaxr", (1, 4), F32, side="left")
    def fmx(v, ln_mark=ln_mark):
        wait_marks(v, [ln_mark])
        for c in range(NCH):
            v.wait_ge(V.h, V.n)
            V.inc(v.tensor_reduce(mbx[:, c:c + 1], xln_sb[:, c * T_CORE:(c + 1) * T_CORE],
                                  mybir.AxisListType.X, ALU.max, apply_absolute_value=True))
    P.on("vector", fmx)
    xmx_mark = emit_cross_part_max(P, "x", mbx[:], V.mark, amax_row, stLN)
    sx_mark = emit_qdq_scalars(P, "x", xmx_mark, amax_row, fl, on1, fl_mark, scl, 0, stLN)
    qy = P.sb("qy", (128, 512), F32, stLN, side="right")
    qr2 = P.sb("qr2", (128, 512), F32, stLN, side="right")
    def fqx(v, sx_mark=sx_mark):
        wait_marks(v, [sx_mark])
        for c in range(NCH):
            for tt in range(4):
                sl = slice(c * T_CORE + tt * 512, c * T_CORE + tt * 512 + 512)
                emit_quant_tile(v, V, xln_sb[:, sl], xq_sb[:, sl],
                                scl[:, 0:1], scl[:, 1:2], scl[:, 2:3], qy[:], qr2[:])
    P.on("vector", fqx)
    xq_mark = V.mark

    stA = contextlib.ExitStack()
    SW = 512
    NSTRIP = T_CORE // SW
    wub = [P.sb(f"l1wu{i}", (128, NCH * 128), BF16, stA, side="right") for i in range(2)]
    wgb = [P.sb(f"l1wg{i}", (128, NCH * 128), BF16, stA, side="right") for i in range(2)]
    uev = [P.sb(f"uev{i}", (128, SW), BF16, stA, side="right") for i in range(2)]
    gev = [P.sb(f"gev{i}", (128, SW), BF16, stA, side="right") for i in range(2)]
    mbu = P.sb("mbu", (128, HCH * NSTRIP), F32, stA, side="right")
    mbg = P.sb("mbg", (128, HCH * NSTRIP), F32, stA, side="right")
    ua_ps = [P.ps(f"ua_ps{i}", (128, 512), stack=stA) for i in range(2)]
    ga_ps = [P.ps(f"ga_ps{i}", (128, 512), stack=stA) for i in range(2)]
    wA = [P.sem(f"l1wa{i}") for i in range(2)]
    sA = [P.sem(f"l1sa{i}") for i in range(2)]
    wA_base = [wA[0].n, wA[1].n]
    wu_r = wdqT_u.rearrange("(c p) h -> p c h", p=128)
    wg_r = wdqT_g.rearrange("(c p) h -> p c h", p=128)

    def loadA(g, it):
        hc = it % HCH
        s = wA[it % 2]
        s.inc(g.dma_start(out=wub[it % 2][:].rearrange("p (c k) -> p c k", c=NCH),
                          in_=wu_r[:, :, hc * 128:(hc + 1) * 128]), 16)
        s.inc(g.dma_start(out=wgb[it % 2][:].rearrange("p (c k) -> p c k", c=NCH),
                          in_=wg_r[:, :, hc * 128:(hc + 1) * 128]), 16)

    def fA0(g, dq=[dq_u, dq_g]):
        wait_marks(g, dq)
        loadA(g, 0)
        loadA(g, 1)
    P.on("gpsimd", fA0)

    ev_mark = {}
    stq_mark = {}
    for st_i in range(NSTRIP):
        c0 = st_i * SW
        for hc in range(HCH):
            it = st_i * HCH + hc
            slot = it % 2
            def fmm(t, hc=hc, slot=slot, c0=c0, it=it):
                wait_marks(t, [xq_mark])
                t.wait_ge(wA[slot].h, wA_base[slot] + 32 * (it // 2 + 1))
                if it >= 2:
                    wait_marks(t, [ev_mark[it - 2]])
                for c in range(NCH):
                    t.matmul(ua_ps[slot][:], wub[slot][:, c * 128:(c + 1) * 128],
                             xq_sb[:, c * T_CORE + c0: c * T_CORE + c0 + SW],
                             start=(c == 0), stop=(c == NCH - 1))
                for c in range(NCH):
                    i = t.matmul(ga_ps[slot][:], wgb[slot][:, c * 128:(c + 1) * 128],
                                 xq_sb[:, c * T_CORE + c0: c * T_CORE + c0 + SW],
                                 start=(c == 0), stop=(c == NCH - 1))
                Tn.inc(i)
            P.on("tensor", fmm)
            mm_mark = Tn.mark
            def fev(v, hc=hc, slot=slot, it=it, mm_mark=mm_mark):
                wait_marks(v, [mm_mark])
                if it >= 2:
                    wait_marks(v, [stq_mark[it - 2]])
                v.wait_ge(V.h, V.n)
                V.inc(v.tensor_copy(uev[slot][:], ua_ps[slot][:]))
                V.inc(v.tensor_copy(gev[slot][:], ga_ps[slot][:]))
                v.wait_ge(V.h, V.n)
                V.inc(v.tensor_reduce(mbu[:, it:it + 1], uev[slot][:], mybir.AxisListType.X,
                                      ALU.max, apply_absolute_value=True))
                V.inc(v.tensor_reduce(mbg[:, it:it + 1], gev[slot][:], mybir.AxisListType.X,
                                      ALU.max, apply_absolute_value=True))
            P.on("vector", fev)
            ev_mark[it] = V.mark
            def fstq(g, hc=hc, slot=slot, c0=c0, ev=ev_mark[it]):
                wait_marks(g, [ev])
                sA[slot].inc(g.dma_start(out=up_pre[hc * 128:(hc + 1) * 128, c0:c0 + SW],
                                         in_=uev[slot][:]), 16)
                sA[slot].inc(g.dma_start(out=gate_pre[hc * 128:(hc + 1) * 128, c0:c0 + SW],
                                         in_=gev[slot][:]), 16)
            P.on("gpsimd", fstq)
            stq_mark[it] = sA[slot].mark
            nxt = it + 2
            if nxt < NSTRIP * HCH:
                def fpre(g, nxt=nxt, mm_mark=mm_mark):
                    wait_marks(g, [mm_mark])
                    loadA(g, nxt)
                P.on("gpsimd", fpre)
    spillA = [sA[0].mark, sA[1].mark]
    umx_mark = emit_cross_part_max(P, "u", mbu[:], ev_mark[NSTRIP * HCH - 1], amax_row, stA)
    su_mark = emit_qdq_scalars(P, "u", umx_mark, amax_row, fl, on1, fl_mark, scl, 4, stA)
    gmx_mark = emit_cross_part_max(P, "g", mbg[:], su_mark, amax_row, stA)
    sg_mark = emit_qdq_scalars(P, "g", gmx_mark, amax_row, fl, on1, fl_mark, scl, 8, stA)
    stA.close()
    stLN.close()

    stB = contextlib.ExitStack()
    G = P.E["gpsimd"]
    gt = [P.sb(f"bgt{i}", (128, 512), BF16, stB, side="right") for i in range(4)]
    btv = [P.sb(f"btv{i}", (128, 512), BF16, stB, side="right") for i in range(4)]
    br = [P.sb(f"br{i}", (128, 512), F32, stB, side="right") for i in range(2)]
    by = [P.sb(f"by{i}", (128, 512), F32, stB, side="right") for i in range(2)]
    br2 = [P.sb(f"br2{i}", (128, 512), F32, stB, side="right") for i in range(2)]
    bsig = [P.sb(f"bsig{i}", (128, 512), F32, stB, side="right") for i in range(2)]
    bsq = [P.sb(f"bsq{i}", (128, 512), F32, stB, side="right") for i in range(2)]
    mbt = P.sb("mbt", (128, HCH * 4), F32, stB, side="right")
    lB = [P.sem(f"l1lb{i}") for i in range(2)]
    sB = [P.sem(f"l1sb{i}") for i in range(2)]
    NTB = HCH * 4
    bq_mark = {}
    bst_mark = {}
    bred_mark = {}
    SB0 = S.n
    sig_jobs = []
    for ti in range(NTB):
        hc = ti // 4
        tc = ti % 4
        slot = ti % 4
        lane = ti % 2
        eng = "vector" if lane == 0 else "gpsimd"
        E = V if lane == 0 else G
        def fbl(sy, hc=hc, tc=tc, slot=slot, ti=ti, sg_mark=sg_mark, spillA=spillA):
            wait_marks(sy, [sg_mark] + spillA)
            if ti >= 4:
                wait_marks(sy, [bq_mark[ti - 4]])
            lB[ti % 2].inc(sy.dma_start(out=gt[slot][:],
                                      in_=gate_pre[hc * 128:(hc + 1) * 128,
                                                   tc * 512:(tc + 1) * 512]), 16)
        P.on("sync", fbl)
        ld_mark = lB[lane].mark
        def fbq(v, slot=slot, lane=lane, ti=ti, ld_mark=ld_mark, E=E):
            wait_marks(v, [ld_mark])
            emit_quant_tile(v, E, gt[slot][:], br[lane][:], scl[:, 8:9], scl[:, 21:22],
                            scl[:, 20:21], by[lane][:], br2[lane][:], pool=(lane == 1))
        P.on(eng, fbq)
        sig_jobs.append((E.mark, lane))
        sig_mark = (S, SB0 + ti + 1)
        def fbq2(v, slot=slot, lane=lane, ti=ti, sig_mark=sig_mark, E=E):
            wait_marks(v, [sig_mark])
            if ti >= 4:
                wait_marks(v, [bst_mark[ti - 4]])
                if ti - 4 in bred_mark:
                    wait_marks(v, [bred_mark[ti - 4]])
            v.wait_ge(E.h, E.n)
            E.inc(v.tensor_scalar(by[lane][:], bsig[lane][:], 65536.0, C1, ALU.mult, ALU.add))
            if lane == 1:
                _ma(v, E, br2[lane][:], by[lane][:], 1.0, C2)
                _ma(v, E, br2[lane][:], br2[lane][:], 1.0, -C2)
                _ma(v, E, by[lane][:], by[lane][:], 1.0, -C1)
                _ma(v, E, by[lane][:], by[lane][:], scl[:, 21:22], 0.0)
                _ma(v, E, br2[lane][:], br2[lane][:], 1.0, -C1)
                _ma(v, E, br2[lane][:], br2[lane][:], scl[:, 20:21], 0.0)
                v.wait_ge(E.h, E.n)
                E.inc(v.tensor_add(bsq[lane][:], by[lane][:], br2[lane][:]))
                _ma(v, E, br2[lane][:], bsq[lane][:], 1.0, -65535.0)
                v.wait_ge(E.h, E.n)
                E.inc(v.tensor_relu(by[lane][:], br2[lane][:]))
                _ma(v, E, by[lane][:], by[lane][:], -1.0, 0.0)
                v.wait_ge(E.h, E.n)
                E.inc(v.tensor_add(bsq[lane][:], bsq[lane][:], by[lane][:]))
                _ma(v, E, bsq[lane][:], bsq[lane][:], float(1.0 / 65536.0), 0.0)
                _ma(v, E, br[lane][:], br[lane][:], scl[:, 11:12], 0.0)
                v.wait_ge(E.h, E.n)
                E.inc(v.tensor_mul(by[lane][:], br[lane][:], bsq[lane][:]))
            else:
                v.wait_ge(E.h, E.n)
                E.inc(v.tensor_scalar(br2[lane][:], by[lane][:], C2, C2, ALU.add, ALU.subtract))
                E.inc(v.tensor_scalar(by[lane][:], by[lane][:], C1, scl[:, 21:22],
                                      ALU.subtract, ALU.mult))
                v.wait_ge(E.h, E.n)
                E.inc(v.tensor_scalar(br2[lane][:], br2[lane][:], C1, scl[:, 20:21],
                                      ALU.subtract, ALU.mult))
                v.wait_ge(E.h, E.n)
                E.inc(v.tensor_add(bsq[lane][:], by[lane][:], br2[lane][:]))
                v.wait_ge(E.h, E.n)
                E.inc(v.tensor_scalar(bsq[lane][:], bsq[lane][:], 65535.0, float(1.0 / 65536.0),
                                      ALU.min, ALU.mult))
                v.wait_ge(E.h, E.n)
                E.inc(v.scalar_tensor_tensor(by[lane][:], br[lane][:], scl[:, 11:12],
                                             bsq[lane][:], ALU.mult, ALU.mult))
            if lane == 1:
                _ma(v, E, btv[slot][:], by[lane][:], 1.0, 0.0)
            else:
                v.wait_ge(E.h, E.n)
                E.inc(v.tensor_copy(btv[slot][:], by[lane][:]))
            if lane == 0:
                v.wait_ge(E.h, E.n)
                E.inc(v.tensor_reduce(mbt[:, ti:ti + 1], btv[slot][:], mybir.AxisListType.X,
                                      ALU.max, apply_absolute_value=True))
        P.on(eng, fbq2)
        bq_mark[ti] = E.mark
        if lane == 1:
            def fbred(v, ti=ti, slot=slot, m=bq_mark[ti]):
                wait_marks(v, [m])
                v.wait_ge(V.h, V.n)
                V.inc(v.tensor_reduce(mbt[:, ti:ti + 1], btv[slot][:], mybir.AxisListType.X,
                                      ALU.max, apply_absolute_value=True))
            P.on("vector", fbred)
            bred_mark[ti] = V.mark
        def fbst(sy, hc=hc, tc=tc, slot=slot, m=bq_mark[ti]):
            wait_marks(sy, [m])
            sB[slot % 2].inc(sy.dma_start(out=t_pre[hc * 128:(hc + 1) * 128, tc * 512:(tc + 1) * 512],
                                          in_=btv[slot][:]), 16)
        P.on("sync", fbst)
        bst_mark[ti] = sB[lane].mark
    def fsigs(sc, jobs=list(sig_jobs)):
        for rm, lane in jobs:
            wait_marks(sc, [rm])
            S.inc(sc.activation(bsig[lane][:], br[lane][:], AF.Sigmoid,
                                bias=0.0, scale=scl[:, 11:12]))
    P.on("scalar", fsigs)
    assert S.n == SB0 + NTB
    spillB = [sB[i].mark for i in range(2)]
    tmx_mark = emit_cross_part_max(P, "t", mbt[:], [bq_mark[NTB - 2], bred_mark[NTB - 1]],
                                   amax_row, stB)
    st_mark2 = emit_qdq_scalars(P, "t", tmx_mark, amax_row, fl, on1, fl_mark, scl, 12, stB)
    stB.close()

    stC = contextlib.ExitStack()
    ct = [P.sb(f"cct{i}", (128, 512), BF16, stC, side="right") for i in range(4)]
    cu = [P.sb(f"ccu{i}", (128, 512), BF16, stC, side="right") for i in range(4)]
    cr = [P.sb(f"cr{i}", (128, 512), F32, stC, side="right") for i in range(2)]
    cru = [P.sb(f"cru{i}", (128, 512), F32, stC, side="right") for i in range(2)]
    cy = [P.sb(f"cy{i}", (128, 512), F32, stC, side="right") for i in range(2)]
    cr2 = [P.sb(f"cr2{i}", (128, 512), F32, stC, side="right") for i in range(2)]
    cov = [P.sb(f"cov{i}", (128, 512), BF16, stC, side="right") for i in range(4)]
    mbo = P.sb("mbo", (128, HCH * 4), F32, stC, side="right")
    stu = P.sb("stu", (128, 1), F32, stC, side="right")
    lC = [P.sem(f"l1lc{i}") for i in range(2)]
    sC = [P.sem(f"l1sc{i}") for i in range(2)]
    def fstu(v, m=st_mark2):
        wait_marks(v, [m])
        v.wait_ge(V.h, V.n)
        V.inc(v.tensor_mul(stu[:], scl[:, 15:16], scl[:, 7:8]))
    P.on("vector", fstu)
    stu_mark = V.mark
    cq_mark = {}
    cst_mark = {}
    cred_mark = {}
    for ti in range(NTB):
        hc = ti // 4
        tc = ti % 4
        slot = ti % 4
        lane = ti % 2
        eng = "vector" if lane == 0 else "gpsimd"
        E = V if lane == 0 else G
        def fcl(sy, hc=hc, tc=tc, slot=slot, ti=ti, stu_mark=stu_mark, spillB=spillB):
            wait_marks(sy, [stu_mark] + spillB)
            if ti >= 4:
                wait_marks(sy, [cq_mark[ti - 4]])
            lC[ti % 2].inc(sy.dma_start(out=ct[slot][:],
                                        in_=t_pre[hc * 128:(hc + 1) * 128, tc * 512:(tc + 1) * 512]), 16)
            lC[ti % 2].inc(sy.dma_start(out=cu[slot][:],
                                        in_=up_pre[hc * 128:(hc + 1) * 128, tc * 512:(tc + 1) * 512]), 16)
        P.on("sync", fcl)
        ld_mark = lC[lane].mark
        def fcq(v, slot=slot, lane=lane, ti=ti, ld_mark=ld_mark, E=E):
            wait_marks(v, [ld_mark])
            if ti >= 4:
                wait_marks(v, [cst_mark[ti - 4]])
                if ti - 4 in cred_mark:
                    wait_marks(v, [cred_mark[ti - 4]])
            emit_quant_tile(v, E, ct[slot][:], cr[lane][:], scl[:, 12:13], scl[:, 21:22],
                            scl[:, 20:21], cy[lane][:], cr2[lane][:], pool=(lane == 1))
            emit_quant_tile(v, E, cu[slot][:], cru[lane][:], scl[:, 4:5], scl[:, 21:22],
                            scl[:, 20:21], cy[lane][:], cr2[lane][:], pool=(lane == 1))
            if lane == 1:
                _ma(v, E, cr[lane][:], cr[lane][:], stu[:, 0:1], 0.0)
                v.wait_ge(E.h, E.n)
                E.inc(v.tensor_mul(cy[lane][:], cr[lane][:], cru[lane][:]))
            else:
                v.wait_ge(E.h, E.n)
                E.inc(v.scalar_tensor_tensor(cy[lane][:], cr[lane][:], stu[:, 0:1],
                                             cru[lane][:], ALU.mult, ALU.mult))
            if lane == 1:
                _ma(v, E, cov[slot][:], cy[lane][:], 1.0, 0.0)
            else:
                v.wait_ge(E.h, E.n)
                E.inc(v.tensor_copy(cov[slot][:], cy[lane][:]))
            if lane == 0:
                v.wait_ge(E.h, E.n)
                E.inc(v.tensor_reduce(mbo[:, ti:ti + 1], cov[slot][:], mybir.AxisListType.X,
                                      ALU.max, apply_absolute_value=True))
        P.on(eng, fcq)
        cq_mark[ti] = E.mark
        if lane == 1:
            def fcred(v, ti=ti, slot=slot, m=cq_mark[ti]):
                wait_marks(v, [m])
                v.wait_ge(V.h, V.n)
                V.inc(v.tensor_reduce(mbo[:, ti:ti + 1], cov[slot][:], mybir.AxisListType.X,
                                      ALU.max, apply_absolute_value=True))
            P.on("vector", fcred)
            cred_mark[ti] = V.mark
        def fcst(sy, hc=hc, tc=tc, slot=slot, m=cq_mark[ti]):
            wait_marks(sy, [m])
            sC[slot % 2].inc(sy.dma_start(out=o_pre[hc * 128:(hc + 1) * 128, tc * 512:(tc + 1) * 512],
                                          in_=cov[slot][:]), 16)
        P.on("sync", fcst)
        cst_mark[ti] = sC[lane].mark
    spillC = [sC[i].mark for i in range(2)]
    omx_mark = emit_cross_part_max(P, "o", mbo[:], [cq_mark[NTB - 2], cred_mark[NTB - 1]],
                                   amax_row, stC)
    so_mark = emit_qdq_scalars(P, "o", omx_mark, amax_row, fl, on1, fl_mark, scl, 16, stC)
    stC.close()

    stD = contextlib.ExitStack()
    wd_res = P.sb("l1wd_res", (128, HCH * 1024), BF16, stD, side="right")
    oq_sb = P.sb("oq_sb", (128, HCH * 512), BF16, stD, side="right")
    dt = [P.sb(f"ddt{i}", (128, 512), BF16, stD, side="right") for i in range(2)]
    dy = P.sb("dy", (128, 512), F32, stD, side="right")
    dr2 = P.sb("dr2", (128, 512), F32, stD, side="right")
    dxres = [P.sb(f"dxres{i}", (128, 512), F32, stD, side="right") for i in range(2)]
    down_pp = [P.ps(f"l1dpp{i}", (128, 512), stack=stD) for i in range(2)]
    wD = P.sem("l1wd")
    lD = [P.sem(f"l1ld{i}") for i in range(2)]
    xlD = [P.sem(f"l1xl{i}") for i in range(2)]
    xstD = [P.sem(f"l1xs{i}") for i in range(2)]
    wdd_r = wdqT_d.rearrange("(c p) d -> p c d", p=128)
    def fDw(g, dq_d=dq_d):
        wait_marks(g, [dq_d, so_mark] + spillC)
        wD.inc(g.dma_start(out=wd_res[:].rearrange("p (c d) -> p c d", c=HCH), in_=wdd_r), 16)
    P.on("gpsimd", fDw)
    wd_mark = wD.mark
    SWD = 512
    oqm = {}
    res_mark = {}
    for st_i in range(T_CORE // SWD):
        c0 = st_i * SWD
        for hc in range(HCH):
            it = st_i * HCH + hc
            slot = it % 2
            def fdl(sy, hc=hc, c0=c0, slot=slot, it=it, so_mark=so_mark, spillC=spillC):
                wait_marks(sy, [so_mark] + spillC)
                if it >= 2:
                    wait_marks(sy, [oqm[it - 2]])
                lD[slot].inc(sy.dma_start(out=dt[slot][:],
                                          in_=o_pre[hc * 128:(hc + 1) * 128, c0:c0 + SWD]), 16)
            P.on("sync", fdl)
            ld_mark = lD[slot].mark
            def fdq(v, hc=hc, slot=slot, it=it, ld_mark=ld_mark, st_i=st_i):
                wait_marks(v, [ld_mark])
                if st_i >= 1:
                    wait_marks(v, [dn_all[st_i - 1]])
                emit_quant_tile(v, V, dt[slot][:], oq_sb[:, hc * SWD:(hc + 1) * SWD],
                                scl[:, 16:17], scl[:, 17:18], scl[:, 18:19], dy[:], dr2[:])
            P.on("vector", fdq)
            oqm[it] = V.mark
        oq_all = V.mark
        dn_all = dn_all if st_i > 0 else {}
        for dd in range(NCH):
            bank = down_pp[dd % 2]
            it2 = st_i * NCH + dd
            pp = it2 % 2
            def fdown(t, dd=dd, bank=bank, oq_all=oq_all, it2=it2, wd_mark=wd_mark):
                wait_marks(t, [oq_all, wd_mark])
                if it2 >= 2:
                    wait_marks(t, [res_mark[it2 - 2]])
                for hc in range(HCH):
                    i = t.matmul(bank[:, 0:SWD],
                                 wd_res[:, hc * 1024 + dd * 128: hc * 1024 + (dd + 1) * 128],
                                 oq_sb[:, hc * SWD:(hc + 1) * SWD],
                                 start=(hc == 0), stop=(hc == HCH - 1))
                Tn.inc(i)
            P.on("tensor", fdown)
            dd_mark = Tn.mark
            def fxld(sy, dd=dd, c0=c0, pp=pp, it2=it2, oq_all=oq_all):
                if it2 >= 2:
                    wait_marks(sy, [res_mark[it2 - 2]])
                else:
                    wait_marks(sy, [oq_all])
                xlD[pp].inc(sy.dma_start(out=dxres[pp][:],
                                         in_=x_in[dd * 128:(dd + 1) * 128, c0:c0 + SWD]), 16)
            P.on("sync", fxld)
            xld_mark = xlD[pp].mark
            def fres(v, bank=bank, pp=pp, dd_mark=dd_mark, xld_mark=xld_mark):
                wait_marks(v, [dd_mark, xld_mark])
                v.wait_ge(V.h, V.n)
                V.inc(v.scalar_tensor_tensor(dxres[pp][:], bank[:, 0:SWD], 0.5,
                                             dxres[pp][:], ALU.mult, ALU.add))
            P.on("vector", fres)
            wr_mark = V.mark
            def fxst(g, dd=dd, c0=c0, pp=pp, wr_mark=wr_mark):
                wait_marks(g, [wr_mark])
                xstD[pp].inc(g.dma_start(out=x_out[dd * 128:(dd + 1) * 128, c0:c0 + SWD],
                                         in_=dxres[pp][:]), 16)
            P.on("gpsimd", fxst)
            res_mark[it2] = xstD[pp].mark
        dn_all[st_i] = Tn.mark
    fin = [xstD[0].mark, xstD[1].mark]
    G = P.E["gpsimd"]
    def ffinD(g, fin=fin):
        wait_marks(g, fin)
        G.inc(g.nop())
    P.on("gpsimd", ffinD)
    stD.close()
    return G.mark


def build_l1ffn_launch():
    P = Prog()
    x_in = P.dram_in("x", (D, T_CORE), F32)
    wg = P.dram_in("l1_wg", (HID, D), F32)
    wu = P.dram_in("l1_wu", (HID, D), F32)
    wd = P.dram_in("l1_wd", (D, HID), F32)
    x_out = P.dram_out("x_next", (D, T_CORE), F32)
    with P:
        emit_l1ffn(P, x_in, x_out, wg, wu, wd, [])
    return P.nc




L1A_UPTO = "full"


def _early_out(P, x_out, marks):
    eo = P.sem("earlyout")
    def f(g):
        wait_marks(g, marks)
        for c in range(NCH):
            eo.inc(g.dma_start(out=x_out[c * 128:(c + 1) * 128, 0:512],
                               in_=x_out[c * 128:(c + 1) * 128, 0:512]), 16)
        g.wait_ge(eo.h, eo.n)
    P.on("gpsimd", f)


def emit_l1attn(P, x_in, x_out, qT, kT, vtm, wqkv_d, wo_d,
                bqk_sb, bv_sb, bo_sb, waits):
    V, S, Tn = P.E["vector"], P.E["scalar"], P.E["tensor"]
    stX = contextlib.ExitStack()
    xln_sb = P.sb("xln_sb", (128, NCH * T_CORE), BF16, stX, side="right")
    ln_mark = emit_ln_stream(P, "L1A", x_in, xln_sb, waits)
    waits = waits + [ln_mark]
    gq = P.sem("l1a_g")
    xl = [P.sem(f"l1a_xl{i}") for i in range(2)]
    xst = [P.sem(f"l1a_xs{i}") for i in range(2)]
    scq = P.sem("l1a_sc")

    stON = contextlib.ExitStack()
    onesf = P.sb("l1a_on", (1, 128), F32, stON, side="left")
    ones64 = P.sb("l1a_o64", (64, 1), BF16, stON, side="left")
    gb64 = P.sb("l1a_gb", (64, 1), F32, stON, side="left")
    grow = P.sb("l1a_gr", (1, 4), F32, stON, side="left")
    def finit(v):
        wait_marks(v, waits)
        v.wait_ge(V.h, V.n)
        V.inc(v.memset(onesf[:], 1.0))
        V.inc(v.memset(ones64[:], 1.0))
    P.on("vector", finit)
    ones_mark = V.mark

    def mk_evac(dst, boff):
        def evac(v, ps_t, oc, tt):
            V.inc(v.tensor_scalar(dst[:, oc * T_CORE + tt * 512: oc * T_CORE + tt * 512 + 512],
                                  ps_t[:], bqk_sb[:, boff + oc: boff + oc + 1],
                                  None, ALU.add))
        return evac
    pq_mark = emit_proj_fm(P, "lq", xln_sb, wqkv_d, 0, NCH, mk_evac(qT, 0), waits)
    pk_mark = emit_proj_fm(P, "lk", xln_sb, wqkv_d, D, NCH, mk_evac(kT, NCH),
                           waits + [pq_mark])

    stV = contextlib.ExitStack()
    wv_sb = P.sb("l1a_wv", (128, NCH * 1024), BF16, stV, side="right")
    v_ps = [P.ps(f"l1a_vps{i}", (128, 512), stack=stV) for i in range(2)]
    wv_r = wqkv_d.rearrange("(c p) o -> p c o", p=128)
    def fvw(g):
        wait_marks(g, waits + [pq_mark, pk_mark])
        gq.inc(g.dma_start(out=wv_sb[:].rearrange("p (c o) -> p c o", c=NCH),
                           in_=wv_r[:, :, 2 * D: 3 * D]), 16)
    P.on("gpsimd", fvw)
    vw_mark = gq.mark
    vev = {}
    for tc in range(T_CORE // 128):
        for half in range(2):
            it = tc * 2 + half
            pp = it % 2
            def fvmm(t, tc=tc, half=half, pp=pp, it=it):
                wait_marks(t, waits)
                wait_marks(t, [vw_mark, ones_mark])
                if it >= 2:
                    wait_marks(t, [vev[it - 2]])
                for c in range(NCH):
                    t.matmul(v_ps[pp][:], xln_sb[:, c * T_CORE + tc * 128: c * T_CORE + (tc + 1) * 128],
                             wv_sb[:, c * 1024 + half * 512: c * 1024 + half * 512 + 512],
                             start=(c == 0), stop=False)
                Tn.inc(t.matmul(v_ps[pp][:], onesf[:], bv_sb[0:1, half * 512:(half + 1) * 512],
                                start=False, stop=True))
            P.on("tensor", fvmm)
            vmm_mark = Tn.mark
            def fvev(v, tc=tc, half=half, pp=pp, vmm_mark=vmm_mark):
                wait_marks(v, [vmm_mark])
                v.wait_ge(V.h, V.n)
                V.inc(v.tensor_copy(vtm[:, tc * 1024 + half * 512: tc * 1024 + half * 512 + 512],
                                    v_ps[pp][:]))
            P.on("vector", fvev)
            vev[it] = V.mark
    v_mark = V.mark
    stV.close()
    stX.close()
    if L1A_UPTO == "proj":
        _early_out(P, x_out, [v_mark, pq_mark, pk_mark])
        return None

    stA2 = contextlib.ExitStack()
    attn_sb = P.sb("attn_sb", (128, NCH * T_CORE), BF16, stA2, side="right")
    stP1 = contextlib.ExitStack()
    maxb = P.sb("l1a_mx", (64, 32), F32, stP1, side="right")
    mcol = P.sb("l1a_mc", (64, 1), F32, stP1, side="right")
    mrow = P.sb("l1a_mr", (1, 64), F32, stP1, side="right")
    st_ps = [P.ps(f"l1a_sps{i}", (64, 1024), stack=stP1) for i in range(2)]
    red_mark = {}
    for m in range(32):
        ch = m // 8
        pp = m % 2
        def fs1(t, m=m, pp=pp, pq=pq_mark, pk=pk_mark, v_mark=v_mark):
            wait_marks(t, [pq, pk, v_mark])
            if m >= 2:
                wait_marks(t, [red_mark[m - 2]])
            for grp in (0, 1):
                hb = grp * 64
                for h in range(grp, NH, 2):
                    hc = h // 2
                    i = t.matmul(st_ps[pp][:, h * 64:(h + 1) * 64],
                                 kT[hb:hb + 64, hc * T_CORE + m * 64: hc * T_CORE + m * 64 + 64],
                                 qT[hb:hb + 64, hc * T_CORE + m * 64: hc * T_CORE + m * 64 + 64],
                                 start=True, stop=True)
                Tn.inc(i)
                t.wait_ge(Tn.h, Tn.n)
        P.on("tensor", fs1)
        s1_mark = Tn.mark
        def fr1(v, m=m, pp=pp, s1_mark=s1_mark):
            wait_marks(v, [s1_mark])
            v.wait_ge(V.h, V.n)
            V.inc(v.tensor_reduce(maxb[:, m:m + 1], st_ps[pp][:], mybir.AxisListType.X,
                                  ALU.max, apply_absolute_value=True))
        P.on("vector", fr1)
        red_mark[m] = V.mark
        if L1A_UPTO == "p1score" and m == 0:
            break
    if L1A_UPTO == "p1score":
        m0 = red_mark[0]
        stP1.close()
        _early_out(P, x_out, [m0])
        return None
    def fgm(v):
        wait_marks(v, [red_mark[31]])
        v.wait_ge(V.h, V.n)
        V.inc(v.tensor_reduce(mcol[:], maxb[:], mybir.AxisListType.X, ALU.max))
    P.on("vector", fgm)
    gm1 = V.mark
    def fgd(g, gm1=gm1):
        wait_marks(g, [gm1])
        scq.inc(g.dma_start(out=mrow[0:1, :], in_=mcol[0:64, 0:1]), 16)
    P.on("gpsimd", fgd)
    gm2 = scq.mark
    def fgr(v, gm2=gm2):
        wait_marks(v, [gm2])
        v.wait_ge(V.h, V.n)
        V.inc(v.tensor_reduce(grow[0:1, 0:1], mrow[0:1, :], mybir.AxisListType.X, ALU.max))
        v.wait_ge(V.h, V.n)
        V.inc(v.tensor_scalar(grow[0:1, 1:2], grow[0:1, 0:1], -0.125, None, ALU.mult))
    P.on("vector", fgr)
    gm3 = V.mark
    gb_ps = P.ps("l1a_gbps", (64, 1), stack=stP1)
    def fgb(t, gm3=gm3):
        wait_marks(t, [gm3])
        Tn.inc(t.matmul(gb_ps[:], onesf[0:1, 0:64], grow[0:1, 1:2], start=True, stop=True))
    P.on("tensor", fgb)
    gm4 = Tn.mark
    def fgc(v, gm4=gm4):
        wait_marks(v, [gm4])
        v.wait_ge(V.h, V.n)
        V.inc(v.tensor_copy(gb64[:], gb_ps[:]))
    P.on("vector", fgc)
    g_mark = V.mark
    stP1.close()
    if L1A_UPTO == "phase1":
        _early_out(P, x_out, [g_mark])
        return None

    stP2 = contextlib.ExitStack()
    vblk = P.sb("l1a_vb", (64, 1024), BF16, stP2, side="right")
    ptb = P.sb("l1a_pt", (64, 1024), BF16, stP2, side="right")
    otb = P.sb("l1a_ot", (64, 1024), BF16, stP2, side="right")
    linv = P.sb("l1a_li", (1, 1024), F32, stP2, side="right")
    s2_ps = [P.ps("l1a_s2", (64, 1024), stack=stP2)] * 2
    ot_ps = [P.ps("l1a_otp", (64, 1024), stack=stP2)] * 2
    l_ps = [P.ps(f"l1a_lps{i}", (1, 512), stack=stP2) for i in range(2)]
    rep_ps = P.ps("l1a_rep", (64, 1024), stack=stP2)
    blk_mark = {}
    scat_mark = {}
    for m in range(32):
        pp = m % 2
        def fs2(t, m=m, pp=pp, g_mark=g_mark):
            wait_marks(t, [g_mark])
            if m >= 1:
                wait_marks(t, [blk_mark[m - 1]])
            for grp in (0, 1):
                hb = grp * 64
                for h in range(grp, NH, 2):
                    hc = h // 2
                    i = t.matmul(s2_ps[pp][:, h * 64:(h + 1) * 64],
                                 kT[hb:hb + 64, hc * T_CORE + m * 64: hc * T_CORE + m * 64 + 64],
                                 qT[hb:hb + 64, hc * T_CORE + m * 64: hc * T_CORE + m * 64 + 64],
                                 start=True, stop=True)
                Tn.inc(i)
                t.wait_ge(Tn.h, Tn.n)
        P.on("tensor", fs2)
        s2_mark = Tn.mark
        def fexp(sc, pp=pp, s2_mark=s2_mark, m=m):
            wait_marks(sc, [s2_mark])
            if m >= 1:
                wait_marks(sc, [blk_mark.get(m - 1, (S, 0))])
            S.inc(sc.activation(ptb[:], s2_ps[pp][:], AF.Exp, bias=gb64[:, 0:1], scale=0.125))
        P.on("scalar", fexp)
        exp_mark = S.mark
        def fvb(v, m=m, exp_mark=exp_mark, v_mark=v_mark):
            wait_marks(v, [v_mark])
            if m >= 1:
                wait_marks(v, [blk_mark.get(m - 1, (V, 0))])
            v.wait_ge(V.h, V.n)
            V.inc(v.tensor_copy(vblk[:], vtm[(m % 2) * 64:(m % 2) * 64 + 64,
                                             (m // 2) * 1024:(m // 2) * 1024 + 1024]))
        P.on("vector", fvb)
        vb_mark = V.mark
        def fav(t, pp=pp, exp_mark=exp_mark, vb_mark=vb_mark):
            wait_marks(t, [exp_mark, vb_mark])
            for h in range(NH):
                t.matmul(ot_ps[pp][:, h * 64:(h + 1) * 64], vblk[:, h * 64:(h + 1) * 64],
                         ptb[:, h * 64:(h + 1) * 64], start=True, stop=True)
            t.matmul(l_ps[0][:], ones64[:], ptb[:, 0:512], start=True, stop=True)
            Tn.inc(t.matmul(l_ps[1][:], ones64[:], ptb[:, 512:1024], start=True, stop=True))
        P.on("tensor", fav)
        av_mark = Tn.mark
        def fep(v, pp=pp, av_mark=av_mark):
            wait_marks(v, [av_mark])
            v.wait_ge(V.h, V.n)
            V.inc(v.reciprocal(linv[0:1, 0:512], l_ps[0][:]))
            V.inc(v.reciprocal(linv[0:1, 512:1024], l_ps[1][:]))
        P.on("vector", fep)
        li_mark = V.mark
        def frep(t, li_mark=li_mark):
            wait_marks(t, [li_mark])
            t.matmul(rep_ps[:, 0:512], onesf[0:1, 0:64], linv[0:1, 0:512], start=True, stop=True)
            Tn.inc(t.matmul(rep_ps[:, 512:1024], onesf[0:1, 0:64], linv[0:1, 512:1024],
                            start=True, stop=True))
        P.on("tensor", frep)
        rep_mark = Tn.mark
        def fot(v, pp=pp, rep_mark=rep_mark, m=m):
            wait_marks(v, [rep_mark])
            if m >= 1:
                wait_marks(v, [scat_mark[m - 1]])
            v.wait_ge(V.h, V.n)
            V.inc(v.tensor_copy(otb[:], ot_ps[pp][:]))
            v.wait_ge(V.h, V.n)
            V.inc(v.tensor_mul(otb[:], otb[:], rep_ps[:]))
        P.on("vector", fot)
        blk_mark[m] = V.mark
        def fscat(g, m=m, bm=blk_mark[m]):
            wait_marks(g, [bm])
            src = otb[:].rearrange("d (ch par q) -> d ch par q", ch=8, par=2)
            for par in range(2):
                dst = attn_sb[par * 64:par * 64 + 64, :].rearrange(
                    "d (ch t) -> d ch t", ch=8)[:, :, m * 64:(m + 1) * 64]
                scq.inc(g.dma_start(out=dst, in_=src[:, :, par, :],
                                    allow_slow_non_contiguous=True), 16)
        P.on("gpsimd", fscat)
        scat_mark[m] = scq.mark
    attn_done = [scat_mark[31], blk_mark[31], Tn.mark]
    stP2.close()
    if L1A_UPTO == "phase2":
        _early_out(P, x_out, attn_done)
        return None

    stPJ = contextlib.ExitStack()
    wpj_sb = P.sb("l1a_wpj", (128, NCH * 1024), BF16, stPJ, side="right")
    ptmp = P.sb("l1a_ptmp", (128, 512), F32, stPJ, side="right")
    xres = [P.sb(f"l1a_xr{i}", (128, 512), F32, stPJ, side="right") for i in range(2)]
    pj_ps = [P.ps(f"l1a_pj{i}", (128, 512), stack=stPJ) for i in range(2)]
    wpj_r = wo_d.rearrange("(c p) o -> p c o", p=128)
    def fpw(g):
        wait_marks(g, attn_done)
        gq.inc(g.dma_start(out=wpj_sb[:].rearrange("p (c o) -> p c o", c=NCH), in_=wpj_r), 16)
    P.on("gpsimd", fpw)
    pw_mark = gq.mark
    res_done = {}
    for oc in range(NCH):
        for tt in range(4):
            it = oc * 4 + tt
            pp = it % 2
            def fpm(t, oc=oc, tt=tt, pp=pp, it=it):
                wait_marks(t, [pw_mark] + attn_done)
                if it >= 2:
                    wait_marks(t, [res_done[it - 2]])
                for c in range(NCH):
                    i = t.matmul(pj_ps[pp][:], wpj_sb[:, c * 1024 + oc * 128: c * 1024 + (oc + 1) * 128],
                                 attn_sb[:, c * T_CORE + tt * 512: c * T_CORE + tt * 512 + 512],
                                 start=(c == 0), stop=(c == NCH - 1))
                Tn.inc(i)
            P.on("tensor", fpm)
            pm_mark = Tn.mark
            def fxld(sy, oc=oc, tt=tt, pp=pp, it=it):
                if it >= 2:
                    wait_marks(sy, [res_done[it - 2]])
                else:
                    wait_marks(sy, attn_done)
                xl[pp].inc(sy.dma_start(out=xres[pp][:],
                                        in_=x_in[oc * 128:(oc + 1) * 128, tt * 512:(tt + 1) * 512]), 16)
            P.on("sync", fxld)
            xld_mark = xl[pp].mark
            def fpr(v, oc=oc, pp=pp, pm_mark=pm_mark, xld_mark=xld_mark):
                wait_marks(v, [pm_mark, xld_mark])
                v.wait_ge(V.h, V.n)
                V.inc(v.tensor_scalar(ptmp[:], pj_ps[pp][:], bo_sb[:, oc:oc + 1], None, ALU.add))
                v.wait_ge(V.h, V.n)
                V.inc(v.tensor_add(xres[pp][:], ptmp[:], xres[pp][:]))
            P.on("vector", fpr)
            wr_mark = V.mark
            def fxst(g, oc=oc, tt=tt, pp=pp, wr_mark=wr_mark):
                wait_marks(g, [wr_mark])
                xst[pp].inc(g.dma_start(out=x_out[oc * 128:(oc + 1) * 128, tt * 512:(tt + 1) * 512],
                                        in_=xres[pp][:]), 16)
            P.on("gpsimd", fxst)
            res_done[it] = xst[pp].mark
    fin = [xst[0].mark, xst[1].mark]
    def ffin(g, fin=fin):
        wait_marks(g, fin)
    P.on("gpsimd", ffin)
    stPJ.close()
    stA2.close()
    stON.close()
    return fin[0]


def build_l1attn_launch():
    P = Prog()
    x_in = P.dram_in("x", (D, T_CORE), F32)
    wqkv = P.dram_in("l1_wqkv", (D, 3 * D), F32)
    wo = P.dram_in("l1_wo", (D, D), F32)
    bqk_in = P.dram_in("l1_bqk", (128, 16), F32)
    bv_in = P.dram_in("l1_bv", (1, D), F32)
    bo_in = P.dram_in("l1_bo", (128, 8), F32)
    x_out = P.dram_out("x_next", (D, T_CORE), F32)
    bqk_sb = P.sb("bqk_sb", (128, 16), F32)
    bv_sb = P.sb("bv_sb", (1, D), F32)
    bo_sb = P.sb("bo_sb", (128, 8), F32)
    qT = P.sb("qT", (128, NCH * T_CORE), BF16, side="right")
    kT = P.sb("kT", (128, NCH * T_CORE), BF16, side="right")
    vtm = P.sb("vtm", (128, 16 * 1024), BF16, side="right")
    with P:
        ld = P.sem("smalls")
        def fsmall(sy):
            for src_, dst in ((bqk_in, bqk_sb), (bv_in, bv_sb), (bo_in, bo_sb)):
                ld.inc(sy.dma_start(out=dst[:], in_=src_), 16)
        P.on("sync", fsmall)
        emit_l1attn(P, x_in, x_out, qT, kT, vtm, wqkv, wo,
                    bqk_sb, bv_sb, bo_sb, [ld.mark])
    return P.nc


def _l1_host_prep(inputs):
    Wqkv = np.asarray(inputs["L1_attn_w_qkv"])
    bqkv = np.asarray(inputs["L1_attn_b_qkv"])
    p = np.zeros(3072, dtype=np.int64)
    for h in range(NH):
        for i in range(64):
            p[0 * 1024 + h * 64 + i] = h * 192 + i
            p[1 * 1024 + h * 64 + i] = h * 192 + 64 + i
            p[2 * 1024 + h * 64 + i] = h * 192 + 128 + i
    wqkv = np.ascontiguousarray(Wqkv[:, p])
    bp = bqkv[p]
    bqk = np.ascontiguousarray(bp[:2048].reshape(16, 128).T)
    bv = np.ascontiguousarray(bp[2048:].reshape(1, D))
    bo = np.ascontiguousarray(np.asarray(inputs["L1_attn_b_o"]).reshape(8, 128).T)
    wo = np.ascontiguousarray(np.asarray(inputs["L1_attn_w_o"]))
    return dict(wqkv=wqkv, wo=wo, bqk=bqk, bv=bv, bo=bo)



def build_l0ffn_launch():
    P = Prog()
    x_in = P.dram_in("x", (D, T_CORE), F32)
    wg = P.dram_in("l0_wgT", (D, HID), I32)
    wu = P.dram_in("l0_wuT", (D, HID), I32)
    wd = P.dram_in("l0_wdT", (HID, D), I32)
    x_out = P.dram_out("x_next", (D, T_CORE), F32)
    xln_sb = P.sb("xln_sb", (128, NCH * T_CORE), BF16)
    with P:
        ln_mark = emit_ln_stream(P, "0", x_in, xln_sb, [])
        emit_l0ffn(P, x_in, x_out, xln_sb, wg, wu, wd, [ln_mark])
    return P.nc


def build_l0attn_launch():
    P = Prog()
    x_in = P.dram_in("x", (D, T_CORE), F32)
    wqk = P.dram_in("l0_wqk", (D, 2 * D), F32)
    wv = P.dram_in("l0_wv", (D, D), F32)
    wpj = P.dram_in("l0_wproj", (D, D), F32)
    bqk_in = P.dram_in("l0_bqk", (128, 16), F32)
    bv_in = P.dram_in("l0_bv", (1, D), F32)
    bpj_in = P.dram_in("l0_bproj", (128, 8), F32)
    cos_in = P.dram_in("l0_cos", (128, T_CORE), F32)
    sin_in = P.dram_in("l0_sin", (128, T_CORE), F32)
    mask_in = P.dram_in("l0_mask", (128, 8 * 512), BF16)
    x_out = P.dram_out("x_next", (D, T_CORE), F32)
    qT = P.sb("qT", (128, NCH * T_CORE), BF16)
    attn_sb = P.sb("attn_sb", (128, NCH * T_CORE), BF16)
    cos_sb = P.sb("cos_sb", (128, T_CORE), F32)
    sin_sb = P.sb("sin_sb", (128, T_CORE), F32)
    mask_sb = P.sb("mask_sb", (128, 8 * 512), BF16)
    bqk_sb = P.sb("bqk_sb", (128, 16), F32)
    bv_sb = P.sb("bv_sb", (1, D), F32)
    bpj_sb = P.sb("bpj_sb", (128, 8), F32)
    xln_sb = P.sb("xln_sb", (128, NCH * T_CORE), BF16, side="right")
    with P:
        ld = P.sem("smalls")
        def fsmall(sy):
            for src_, dst in ((bqk_in, bqk_sb), (bv_in, bv_sb), (bpj_in, bpj_sb),
                              (cos_in, cos_sb), (sin_in, sin_sb), (mask_in, mask_sb)):
                ld.inc(sy.dma_start(out=dst[:], in_=src_), 16)
        P.on("sync", fsmall)
        ln_mark = emit_ln_stream(P, "0", x_in, xln_sb, [ld.mark])
        emit_l0attn(P, x_in, x_out, xln_sb, qT, attn_sb, wqk, wv, wpj, bqk_sb,
                    bv_sb, bpj_sb, cos_sb, sin_sb, mask_sb, [ld.mark, ln_mark])
    return P.nc


def _l0_host_prep(inputs):
    Wqkv = np.asarray(inputs["L0_attn_Wqkv"])
    bqkv = np.asarray(inputs["L0_attn_bqkv"])
    perm = np.zeros(2 * D, dtype=np.int64)
    for blk in range(2):
        for m in range(4):
            for hl in range(4):
                h = 4 * m + hl
                for i in range(32):
                    perm[blk * 1024 + (2 * m) * 128 + 32 * hl + i] = blk * 1024 + h * 64 + 2 * i
                    perm[blk * 1024 + (2 * m + 1) * 128 + 32 * hl + i] = blk * 1024 + h * 64 + 2 * i + 1
    wqk = np.ascontiguousarray(Wqkv[:, perm])
    wv = np.ascontiguousarray(Wqkv[:, 2048:])
    bqk_p = bqkv[perm]
    bqk = np.ascontiguousarray(bqk_p.reshape(16, 128).T)
    bv = np.ascontiguousarray(bqkv[2048:].reshape(1, D))
    bpj = np.ascontiguousarray(np.asarray(inputs["L0_attn_bproj"]).reshape(8, 128).T)
    wpj = np.ascontiguousarray(np.asarray(inputs["L0_attn_Wproj"]))
    inv_freq = (1.0 / (10000.0 ** (np.arange(0, DK, 2, dtype=np.float32) / DK))).astype(np.float32)
    tabs = []
    for core in range(NCORES):
        pos = _own_tokens(core).astype(np.float32)
        f = (inv_freq[:, None] * pos[None, :]).astype(np.float32)
        cos = np.tile(np.cos(f).astype(np.float32), (4, 1))
        sin = np.tile(np.sin(f).astype(np.float32), (4, 1))
        tabs.append((np.ascontiguousarray(cos), np.ascontiguousarray(sin)))
    kk = np.arange(128)[:, None]
    qq = np.arange(512)[None, :]
    def diag(dj):
        return np.where(qq >= kk + 128 * dj, np.float32(0.0), np.float32(-1e9))
    allow = np.zeros((128, 512), np.float32)
    blockm = np.full((128, 512), np.float32(-1e9))
    masks = []
    for p in range(2):
        slots = []
        for b_ in range(8):
            dj = b_ - 4 * p
            slots.append(allow if dj < 0 else (diag(dj) if dj < 4 else blockm))
        masks.append(np.ascontiguousarray(
            np.concatenate(slots, axis=1).astype(ml_dtypes.bfloat16)))
    return dict(wqk=wqk, wv=wv, wpj=wpj, bqk=bqk, bv=bv, bpj=bpj, tabs=tabs, masks=masks)



def build_fused(n_sub=6):
    P = Prog()
    x_in = P.dram_in("x", (D, T_CORE), F32)
    l0_wg = P.dram_in("l0_wgT", (D, HID), I32)
    l0_wu = P.dram_in("l0_wuT", (D, HID), I32)
    l0_wd = P.dram_in("l0_wdT", (HID, D), I32)
    wqk = P.dram_in("l0_wqk", (D, 2 * D), F32)
    wv = P.dram_in("l0_wv", (D, D), F32)
    wpj = P.dram_in("l0_wproj", (D, D), F32)
    bqk_in = P.dram_in("l0_bqk", (128, 16), F32)
    bv_in = P.dram_in("l0_bv", (1, D), F32)
    bpj_in = P.dram_in("l0_bproj", (128, 8), F32)
    cos_in = P.dram_in("l0_cos", (128, T_CORE), F32)
    sin_in = P.dram_in("l0_sin", (128, T_CORE), F32)
    mask_in = P.dram_in("l0_mask", (128, 8 * 512), BF16)
    l1_wg = P.dram_in("l1_wg", (HID, D), F32)
    l1_wu = P.dram_in("l1_wu", (HID, D), F32)
    l1_wd = P.dram_in("l1_wd", (D, HID), F32)
    wqkv1 = P.dram_in("l1_wqkv", (D, 3 * D), F32)
    wo1 = P.dram_in("l1_wo", (D, D), F32)
    bqk1_in = P.dram_in("l1_bqk", (128, 16), F32)
    bv1_in = P.dram_in("l1_bv", (1, D), F32)
    bo1_in = P.dram_in("l1_bo", (128, 8), F32)
    x_out = P.dram_out("x_next", (D, T_CORE), F32)
    xp = [P.nc.dram_tensor(f"xping{i}", [D, T_CORE], F32, kind="Internal").ap()
          for i in range(5)]
    dst = [xp[0], xp[1], xp[2], xp[3], xp[4], x_out]
    dst[n_sub - 1] = x_out
    with P:
        waits = []
        with P.scoped("s0"):
            stk = contextlib.ExitStack()
            xln = P.sb("xln", (128, NCH * T_CORE), BF16, stk, side="right")
            lnm = emit_ln_stream(P, "a", x_in, xln, waits)
            fin = emit_l0ffn(P, x_in, dst[0], xln, l0_wg, l0_wu, l0_wd, waits + [lnm])
            if n_sub == 1:
                stk.close()
                return P.nc
            stk.close()
        waits = [fin]
        with P.scoped("s1"):
            stk = contextlib.ExitStack()
            qT = P.sb("qT", (128, NCH * T_CORE), BF16, stk, side="right")
            attn_sb = P.sb("attn", (128, NCH * T_CORE), BF16, stk, side="right")
            cos_sb = P.sb("cos", (128, T_CORE), F32, stk, side="right")
            sin_sb = P.sb("sin", (128, T_CORE), F32, stk, side="right")
            mask_sb = P.sb("mask", (128, 8 * 512), BF16, stk, side="right")
            bqk_sb = P.sb("bqk", (128, 16), F32, stk, side="right")
            bv_sb = P.sb("bv", (1, D), F32, stk, side="right")
            bpj_sb = P.sb("bpj", (128, 8), F32, stk, side="right")
            xln = P.sb("xln", (128, NCH * T_CORE), BF16, stk, side="right")
            ld = P.sem("smalls")
            def fsmall(sy, waits=waits):
                wait_marks(sy, waits)
                for src_, dst in ((bqk_in, bqk_sb), (bv_in, bv_sb), (bpj_in, bpj_sb),
                                  (cos_in, cos_sb), (sin_in, sin_sb), (mask_in, mask_sb)):
                    ld.inc(sy.dma_start(out=dst[:], in_=src_), 16)
            P.on("sync", fsmall)
            waits = waits + [ld.mark]
            lnm = emit_ln_stream(P, "b", xp[0], xln, waits)
            fin = emit_l0attn(P, xp[0], dst[1], xln, qT, attn_sb, wqk, wv, wpj, bqk_sb,
                              bv_sb, bpj_sb, cos_sb, sin_sb, mask_sb, waits + [lnm])
            if n_sub == 2:
                stk.close()
                return P.nc
            stk.close()
        waits = [fin]
        with P.scoped("s2"):
            stk = contextlib.ExitStack()
            xln = P.sb("xln", (128, NCH * T_CORE), BF16, stk, side="right")
            lnm = emit_ln_stream(P, "c", xp[1], xln, waits)
            fin = emit_l0ffn(P, xp[1], dst[2], xln, l0_wg, l0_wu, l0_wd, waits + [lnm])
            if n_sub == 3:
                stk.close()
                return P.nc
            stk.close()
        waits = [fin]
        with P.scoped("s3"):
            fin = emit_l1ffn(P, xp[2], dst[3], l1_wg, l1_wu, l1_wd, waits)
            deq = P._deq_cache
            if n_sub == 4:
                return P.nc
        waits = [fin]
        with P.scoped("s4"):
            stk = contextlib.ExitStack()
            bqk_sb = P.sb("bqk", (128, 16), F32, stk, side="right")
            bv_sb = P.sb("bv", (1, D), F32, stk, side="right")
            bo_sb = P.sb("bo", (128, 8), F32, stk, side="right")
            qT = P.sb("qT", (128, NCH * T_CORE), BF16, stk, side="right")
            kT = P.sb("kT", (128, NCH * T_CORE), BF16, stk, side="right")
            vtm = P.sb("vtm", (128, 16 * 1024), BF16, stk, side="right")
            ld = P.sem("smalls")
            def fsmall1(sy, waits=waits):
                wait_marks(sy, waits)
                for src_, dst in ((bqk1_in, bqk_sb), (bv1_in, bv_sb), (bo1_in, bo_sb)):
                    ld.inc(sy.dma_start(out=dst[:], in_=src_), 16)
            P.on("sync", fsmall1)
            fin = emit_l1attn(P, xp[3], dst[4], qT, kT, vtm, wqkv1, wo1,
                              bqk_sb, bv_sb, bo_sb, waits + [ld.mark])
            if n_sub == 5:
                stk.close()
                return P.nc
            stk.close()
        waits = [fin]
        with P.scoped("s5"):
            emit_l1ffn(P, xp[4], x_out, l1_wg, l1_wu, l1_wd, waits, deq_cache=deq)
    return P.nc


def _in_maps(inputs, shards):
    wgT = np.ascontiguousarray(np.asarray(inputs["L0_ffn_w_gate"]).T)
    wuT = np.ascontiguousarray(np.asarray(inputs["L0_ffn_w_up"]).T)
    wdT = np.ascontiguousarray(np.asarray(inputs["L0_ffn_w_down"]).T)
    hp0 = _l0_host_prep(inputs)
    hp1 = _l1_host_prep(inputs)
    wg1 = np.ascontiguousarray(np.asarray(inputs["L1_ffn_w_gate"]))
    wu1 = np.ascontiguousarray(np.asarray(inputs["L1_ffn_w_up"]))
    wd1 = np.ascontiguousarray(np.asarray(inputs["L1_ffn_w_down"]))
    maps = []
    for c in range(NCORES):
        cos, sin = hp0["tabs"][c]
        maps.append({
            "x": shards[c], "l0_wgT": wgT, "l0_wuT": wuT, "l0_wdT": wdT,
            "l0_wqk": hp0["wqk"], "l0_wv": hp0["wv"], "l0_wproj": hp0["wpj"],
            "l0_bqk": hp0["bqk"], "l0_bv": hp0["bv"], "l0_bproj": hp0["bpj"],
            "l0_cos": cos, "l0_sin": sin, "l0_mask": hp0["masks"][c % 2],
            "l1_wg": wg1, "l1_wu": wu1, "l1_wd": wd1,
            "l1_wqkv": hp1["wqkv"], "l1_wo": hp1["wo"],
            "l1_bqk": hp1["bqk"], "l1_bv": hp1["bv"], "l1_bo": hp1["bo"],
        })
    return maps


def kernel(**inputs):
    x0 = np.asarray(inputs["L0_attn_x"])
    shards = []
    for core in range(NCORES):
        b = core // 2
        toks = _own_tokens(core)
        shards.append(np.ascontiguousarray(x0[b][toks].T))
    nc = build_fused()
    res = run_bass_kernel_spmd(nc, _in_maps(inputs, shards), core_ids=list(range(NCORES)))
    out = np.zeros((B, T, D), dtype=np.float32)
    for core in range(NCORES):
        b = core // 2
        toks = _own_tokens(core)
        out[b][toks] = res.results[core]["x_next"].T
    return out


# version 15
# speedup vs baseline: 1.9423x; 1.1600x over previous
import contextlib
import numpy as np
import ml_dtypes

import concourse.bass as bass
import concourse.mybir as mybir
from concourse.bass_utils import run_bass_kernel_spmd

F32 = mybir.dt.float32
BF16 = mybir.dt.bfloat16
I32 = mybir.dt.int32
ALU = mybir.AluOpType
AF = mybir.ActivationFunctionType
BF = ml_dtypes.bfloat16

NCORES = 8
D = 1024
HID = 3072
NH = 16
DK = 64
T_CORE = 2048
B, T = 4, 4096
NCH = D // 128
HCH = HID // 128
C1 = float(np.float32(1.5 * 2.0 ** 23))
C2 = float(np.float32(1.5 * 2.0 ** 52))


def _own_tokens(core):
    p = core % 2
    idx = []
    for s in (p, p + 2, p + 4, p + 6):
        idx.extend(range(512 * s, 512 * (s + 1)))
    return np.array(idx)



class Sem:
    def __init__(self, ctx, nc, name):
        self.h = ctx.enter_context(nc.semaphore(name))
        self.n = 0

    def inc(self, instr, k=1):
        instr.then_inc(self.h, k)
        self.n += k
        return self.n

    @property
    def mark(self):
        return (self, self.n)


class Prog:

    def __init__(self):
        self.nc = bass.Bass("TRN2", target_bir_lowering=False)
        self.ctx = contextlib.ExitStack()
        self.block = None
        self.sems = {}
        self.E = {}
        self._tag = ""

    @contextlib.contextmanager
    def scoped(self, tag):
        old = self._tag
        self._tag = tag
        try:
            yield
        finally:
            self._tag = old

    def __enter__(self):
        self._blk = self.nc.Block()
        self.block = self._blk.__enter__()
        for e in ("vector", "scalar", "tensor", "gpsimd"):
            self.E[e] = self.sem("E_" + e)
        return self

    def __exit__(self, exc_type, exc, tb):
        if exc is not None:
            try:
                self._blk.__exit__(None, None, None)
            except Exception:
                pass
            try:
                self.ctx.close()
            except Exception:
                pass
            return False
        self._blk.__exit__(None, None, None)
        self.ctx.close()
        return False

    def sem(self, name):
        if name not in self.sems:
            self.sems[name] = Sem(self.ctx, self.nc, name)
        return self.sems[name]

    def on(self, eng, fn):
        getattr(self.block, eng)(fn)

    def sb(self, name, shape, dt, stack=None, side="left"):
        return (stack or self.ctx).enter_context(
            self.nc.sbuf_tensor(self._tag + name, list(shape), dt, side=side))

    def ps(self, name, shape, dt=F32, stack=None):
        return (stack or self.ctx).enter_context(
            self.nc.psum_tensor(self._tag + name, list(shape), dt))

    def dram_in(self, name, shape, dt):
        return self.nc.dram_tensor(name, list(shape), dt, kind="ExternalInput").ap()

    def dram_out(self, name, shape, dt):
        return self.nc.dram_tensor(name, list(shape), dt, kind="ExternalOutput").ap()

    def dram_tmp(self, name, shape, dt):
        return self.nc.dram_tensor(self._tag + name, list(shape), dt, kind="Internal").ap()


def wait_marks(eng, marks):
    for s, v in marks:
        if v > 0:
            eng.wait_ge(s.h, v)



def emit_ln_stream(P, tag, x_in, xln_sb, waits):
    V, S, Tn = P.E["vector"], P.E["scalar"], P.E["tensor"]
    ldx = [P.sem("lnld" + tag + str(i)) for i in range(2)]
    stk = contextlib.ExitStack()
    ones128 = P.sb("ln_ones" + tag, (128, 1), F32, stk, side="right")
    eps_t = P.sb("ln_eps" + tag, (1, 1), F32, stk, side="right")
    ones1 = P.sb("ln_ones1" + tag, (1, 128), F32, stk, side="right")
    xtile = [P.sb(f"ln_xt{i}" + tag, (128, NCH * 512), F32, stk, side="right") for i in range(2)]
    sq = [P.sb(f"ln_sq{i}" + tag, (128, 512), F32, stk, side="right") for i in range(2)]
    rows = {n: P.sb("ln_" + n + tag, (1, 512), F32, stk, side="right")
            for n in ("mu", "e2", "t1", "var", "sd", "rstd", "nmr")}
    tmp = P.sb("ln_tmp" + tag, (128, 512), F32, stk, side="right")
    mu_ps = P.ps("ln_mups" + tag, (1, 512), stack=stk)
    e2_ps = P.ps("ln_e2ps" + tag, (1, 512), stack=stk)
    repA = P.ps("ln_repA" + tag, (128, 512), stack=stk)
    repB = P.ps("ln_repB" + tag, (128, 512), stack=stk)

    def init(v):
        wait_marks(v, waits)
        v.memset(ones128[:], 1.0)
        v.memset(eps_t[:], 1e-5)
        V.inc(v.memset(ones1[:], 1.0))
    P.on("vector", init)
    init_mark = V.mark

    NT = T_CORE // 512
    tile_done = {}
    mmpair = {}
    sq_mark = {}
    for ti in range(NT):
        c0 = ti * 512
        xb = xtile[ti % 2]
        lds = ldx[ti % 2]
        def fldx(sy, ti=ti, c0=c0, xb=xb, lds=lds):
            wait_marks(sy, waits)
            if ti >= 2:
                wait_marks(sy, [tile_done[ti - 2]])
            for c in range(NCH):
                lds.inc(sy.dma_start(out=xb[:, c * 512:(c + 1) * 512],
                                     in_=x_in[c * 128:(c + 1) * 128, c0:c0 + 512]), 16)
        P.on("sync", fldx)
        ld_mark = lds.mark
        for c in range(NCH):
            buf = sq[c % 2]
            gidx = ti * NCH + c
            def fsq(sc, c=c, buf=buf, xb=xb, ld_mark=ld_mark, gidx=gidx):
                wait_marks(sc, [ld_mark, init_mark])
                if gidx >= 2:
                    wait_marks(sc, [mmpair[gidx - 2]])
                S.inc(sc.activation(buf[:], xb[:, c * 512:(c + 1) * 512], AF.Square))
                sq_mark[gidx] = S.mark
            P.on("scalar", fsq)
            def fmm(t, c=c, buf=buf, xb=xb, ld_mark=ld_mark, gidx=gidx):
                wait_marks(t, [ld_mark, sq_mark[gidx]])
                t.matmul(mu_ps[:], ones128[:], xb[:, c * 512:(c + 1) * 512],
                         start=(c == 0), stop=(c == NCH - 1))
                Tn.inc(t.matmul(e2_ps[:], ones128[:], buf[:],
                                start=(c == 0), stop=(c == NCH - 1)))
                mmpair[gidx] = Tn.mark
            P.on("tensor", fmm)
        stats_mark = Tn.mark
        def fv1(v, stats_mark=stats_mark):
            wait_marks(v, [stats_mark])
            v.wait_ge(V.h, V.n)
            V.inc(v.tensor_scalar(rows["mu"][:], mu_ps[:], 1.0 / 1024.0, None, ALU.mult))
            V.inc(v.tensor_scalar(rows["e2"][:], e2_ps[:], 1.0 / 1024.0, None, ALU.mult))
            v.wait_ge(V.h, V.n)
            V.inc(v.tensor_mul(rows["t1"][:], rows["mu"][:], rows["mu"][:]))
            v.wait_ge(V.h, V.n)
            V.inc(v.tensor_sub(rows["var"][:], rows["e2"][:], rows["t1"][:]))
        P.on("vector", fv1)
        var_mark = V.mark
        def fs1(sc, var_mark=var_mark):
            wait_marks(sc, [var_mark])
            S.inc(sc.activation(rows["sd"][:], rows["var"][:], AF.Sqrt,
                                bias=eps_t[0:1, 0:1], scale=1.0))
        P.on("scalar", fs1)
        sd_mark = S.mark
        def fv2(v, sd_mark=sd_mark):
            wait_marks(v, [sd_mark])
            v.wait_ge(V.h, V.n)
            V.inc(v.reciprocal(rows["rstd"][:], rows["sd"][:]))
            v.wait_ge(V.h, V.n)
            V.inc(v.scalar_tensor_tensor(rows["nmr"][:], rows["mu"][:], -1.0, rows["rstd"][:],
                                         ALU.mult, ALU.mult))
        P.on("vector", fv2)
        smalls_mark = V.mark
        def frep(t, smalls_mark=smalls_mark):
            wait_marks(t, [smalls_mark])
            t.matmul(repA[:], ones1[:], rows["rstd"][:], start=True, stop=True)
            Tn.inc(t.matmul(repB[:], ones1[:], rows["nmr"][:], start=True, stop=True))
        P.on("tensor", frep)
        reps_mark = Tn.mark
        def fap(v, c0=c0, reps_mark=reps_mark, xb=xb):
            wait_marks(v, [reps_mark])
            for c in range(NCH):
                src_ = xb[:, c * 512:(c + 1) * 512]
                dst = xln_sb[:, c * T_CORE + c0: c * T_CORE + c0 + 512]
                v.wait_ge(V.h, V.n)
                V.inc(v.tensor_mul(tmp[:], src_, repA[:]))
                v.wait_ge(V.h, V.n)
                V.inc(v.tensor_add(dst, tmp[:], repB[:]))
        P.on("vector", fap)
        tile_done[ti] = V.mark
        def fg(t, td=V.mark):
            wait_marks(t, [td])
        P.on("tensor", fg)
    stk.close()
    return V.mark


def emit_proj_fm(P, tag, src_sb, w_dram, w_col0, n_out_ch, evac, waits):
    V, Tn = P.E["vector"], P.E["tensor"]
    ws = [P.sem(f"pjw{tag}{i}") for i in range(2)]
    ws_base = [ws[0].n, ws[1].n]
    stk = contextlib.ExitStack()
    wbuf = [P.sb(f"pjw{tag}{i}", (128, NCH * 128), BF16, stk, side="right") for i in range(2)]
    ps = [P.ps(f"pjps{tag}{i}", (128, 512), stack=stk) for i in range(2)]
    w_r = w_dram.rearrange("(c p) o -> p c o", p=128)

    def load_w(g, oc):
        s = ws[oc % 2]
        s.inc(g.dma_start(out=wbuf[oc % 2][:].rearrange("p (c k) -> p c k", c=NCH),
                          in_=w_r[:, :, w_col0 + oc * 128: w_col0 + (oc + 1) * 128]), 16)

    def f0(g):
        wait_marks(g, waits)
        load_w(g, 0)
        if n_out_ch > 1:
            load_w(g, 1)
    P.on("gpsimd", f0)

    NT = T_CORE // 512
    evac_mark = {}
    mm_last = {}
    for oc in range(n_out_ch):
        for tt in range(NT):
            it = oc * NT + tt
            pp = it % 2
            def fmm(t, oc=oc, tt=tt, pp=pp, it=it):
                wait_marks(t, waits)
                t.wait_ge(ws[oc % 2].h, ws_base[oc % 2] + 16 * (oc // 2 + 1))
                if it >= 2:
                    wait_marks(t, [evac_mark[it - 2]])
                for c in range(NCH):
                    i = t.matmul(ps[pp][:], wbuf[oc % 2][:, c * 128:(c + 1) * 128],
                                 src_sb[:, c * T_CORE + tt * 512: c * T_CORE + tt * 512 + 512],
                                 start=(c == 0), stop=(c == NCH - 1))
                Tn.inc(i)
            P.on("tensor", fmm)
            mm_mark = Tn.mark
            mm_last[oc] = mm_mark
            def fev(v, oc=oc, tt=tt, pp=pp, mm_mark=mm_mark):
                wait_marks(v, [mm_mark])
                evac(v, ps[pp], oc, tt)
            P.on("vector", fev)
            evac_mark[it] = V.mark
        nxt = oc + 2
        if nxt < n_out_ch:
            def fpre(g, nxt=nxt, mm_mark=mm_last[oc]):
                wait_marks(g, [mm_mark])
                load_w(g, nxt)
            P.on("gpsimd", fpre)
    stk.close()
    return V.mark


def ktv(j):
    gk = j // 4
    return (gk % 2) * 16 + (gk // 2) * 4 + (j % 4)


def emit_l0attn(P, x_in, x_out, xln_sb, qT, attn_sb, wqk_d, wv_d, wproj_d, bqk_sb,
                bv_sb, bproj_sb, cos_sb, sin_sb, mask_sb, waits):
    V, S, Tn = P.E["vector"], P.E["scalar"], P.E["tensor"]
    gq = P.sem("at_gdma")
    cc = P.sem("at_cc")
    xl = [P.sem(f"at_xl{i}") for i in range(2)]
    xst = [P.sem(f"at_xst{i}") for i in range(2)]
    vstq = P.sem("at_vst")

    k_own = P.dram_tmp("k_own", (D, T_CORE), BF16)
    v_own = P.dram_tmp("v_own", (NH * T_CORE, 64), BF16)
    k_all = P.dram_tmp("k_all", (2 * D, T_CORE), BF16)
    v_all = P.dram_tmp("v_all", (2 * NH * T_CORE, 64), BF16)

    stON = contextlib.ExitStack()
    onesf = P.sb("at_ones1", (1, 128), F32, stON, side="right")
    def fvinit(v):
        wait_marks(v, waits)
        V.inc(v.memset(onesf[:], 1.0))
    P.on("vector", fvinit)
    ones_mark = V.mark

    stQK = contextlib.ExitStack()
    raw = P.sb("qkraw", (128, NCH * T_CORE), BF16, stQK, side="right")
    scat_mark = None
    prev_pj = None
    for which in ("q", "k"):
        stP = contextlib.ExitStack()
        boff = 0 if which == "q" else NCH
        def mk_evac(boff):
            def evac(v, ps_t, oc, tt):
                V.inc(v.tensor_scalar(raw[:, oc * T_CORE + tt * 512: oc * T_CORE + tt * 512 + 512],
                                      ps_t[:], bqk_sb[:, boff + oc: boff + oc + 1],
                                      None, ALU.add))
            return evac
        pj_mark = emit_proj_fm(P, which, xln_sb, wqk_d, 0 if which == "q" else D,
                               NCH, mk_evac(boff),
                               waits + ([prev_pj] if which == "k" else []))
        ropeA = P.sb("ropeA" + which, (128, 512), F32, stP, side="right")
        ropeB = P.sb("ropeB" + which, (128, 512), F32, stP, side="right")
        re_t = P.sb("re_t" + which, (128, 512), BF16, stP, side="right")
        ro_t = P.sb("ro_t" + which, (128, 512), BF16, stP, side="right")
        prev_pj = pj_mark
        if which == "k":
            kfin = P.sb("kfin", (128, 2 * T_CORE), BF16, stP, side="right")
        for m in range(4):
            for ti in range(T_CORE // 512):
                t0 = ti * 512
                def frope(v, m=m, t0=t0, pj_mark=pj_mark, scat_mark=scat_mark):
                    wait_marks(v, [pj_mark])
                    if scat_mark is not None:
                        wait_marks(v, [scat_mark])
                    e = raw[:, (2 * m) * T_CORE + t0: (2 * m) * T_CORE + t0 + 512]
                    o = raw[:, (2 * m + 1) * T_CORE + t0: (2 * m + 1) * T_CORE + t0 + 512]
                    cw = cos_sb[:, t0:t0 + 512]
                    sw = sin_sb[:, t0:t0 + 512]
                    v.wait_ge(V.h, V.n)
                    V.inc(v.tensor_mul(ropeA[:], e, cw))
                    V.inc(v.tensor_mul(ropeB[:], o, sw))
                    v.wait_ge(V.h, V.n)
                    V.inc(v.tensor_sub(re_t[:], ropeA[:], ropeB[:]))
                    v.wait_ge(V.h, V.n)
                    V.inc(v.tensor_mul(ropeA[:], e, sw))
                    V.inc(v.tensor_mul(ropeB[:], o, cw))
                    v.wait_ge(V.h, V.n)
                    V.inc(v.tensor_add(ro_t[:], ropeA[:], ropeB[:]))
                P.on("vector", frope)
                rope_mark = V.mark
                def fscat(g, which=which, m=m, t0=t0, rope_mark=rope_mark):
                    wait_marks(g, [rope_mark])
                    for hl in range(4):
                        dc = 2 * m + hl // 2
                        base = (hl % 2) * 64
                        if which == "q":
                            dst = qT[:, dc * T_CORE + t0: dc * T_CORE + t0 + 512]
                        else:
                            dst = kfin[:, (dc % 2) * T_CORE + t0: (dc % 2) * T_CORE + t0 + 512]
                        gq.inc(g.dma_start(out=dst[base:base + 32, :],
                                           in_=re_t[hl * 32:(hl + 1) * 32, :]), 16)
                        gq.inc(g.dma_start(out=dst[base + 32:base + 64, :],
                                           in_=ro_t[hl * 32:(hl + 1) * 32, :]), 16)
                P.on("gpsimd", fscat)
                scat_mark = gq.mark
            if which == "k":
                def fkst(g, m=m, scat_mark=scat_mark):
                    wait_marks(g, [scat_mark])
                    for dcl in range(2):
                        dc = 2 * m + dcl
                        gq.inc(g.dma_start(out=k_own[dc * 128:(dc + 1) * 128, :],
                                           in_=kfin[:, dcl * T_CORE:(dcl + 1) * T_CORE]), 16)
                P.on("gpsimd", fkst)
                scat_mark = gq.mark
        stP.close()
    qk_all = gq.mark
    stQK.close()

    stV = contextlib.ExitStack()
    wv_sb = P.sb("wv_sb", (128, NCH * 1024), BF16, stV, side="right")
    v_sbt = P.sb("v_sbt", (128, 512), BF16, stV, side="right")
    v_ps = [P.ps(f"v_ps{i}", (128, 512), stack=stV) for i in range(2)]
    wv_r = wv_d.rearrange("(c p) o -> p c o", p=128)

    def fvw(g):
        wait_marks(g, waits)
        wait_marks(g, [qk_all])
        gq.inc(g.dma_start(out=wv_sb[:].rearrange("p (c o) -> p c o", c=NCH), in_=wv_r), 16)
    P.on("gpsimd", fvw)
    vw_mark = gq.mark

    vev = {}
    vst = {}
    for tc in range(T_CORE // 128):
        for half in range(2):
            it = tc * 2 + half
            pp = it % 2
            def fvmm(t, tc=tc, half=half, pp=pp, it=it):
                wait_marks(t, waits)
                wait_marks(t, [vw_mark, ones_mark, qk_all])
                if it >= 2:
                    wait_marks(t, [vev[it - 2]])
                for c in range(NCH):
                    t.matmul(v_ps[pp][:], xln_sb[:, c * T_CORE + tc * 128: c * T_CORE + (tc + 1) * 128],
                             wv_sb[:, c * 1024 + half * 512: c * 1024 + half * 512 + 512],
                             start=(c == 0), stop=False)
                Tn.inc(t.matmul(v_ps[pp][:], onesf[:], bv_sb[0:1, half * 512:(half + 1) * 512],
                                start=False, stop=True))
            P.on("tensor", fvmm)
            vmm_mark = Tn.mark
            def fvev(v, pp=pp, vmm_mark=vmm_mark, it=it):
                wait_marks(v, [vmm_mark])
                if it >= 1:
                    wait_marks(v, [vst[it - 1]])
                V.inc(v.tensor_copy(v_sbt[:], v_ps[pp][:]))
            P.on("vector", fvev)
            vev[it] = V.mark
            def fvst(g, tc=tc, half=half, ev=vev[it]):
                wait_marks(g, [ev])
                vr = v_own.rearrange("(h t) d -> h t d", h=NH)
                for hh in range(8):
                    h = half * 8 + hh
                    vstq.inc(g.dma_start(out=vr[h, tc * 128:(tc + 1) * 128, :],
                                         in_=v_sbt[:, hh * 64:(hh + 1) * 64]), 16)
            P.on("gpsimd", fvst)
            vst[it] = vstq.mark
    stV.close()
    all_stored = [gq.mark, vstq.mark]

    def fcc(g):
        wait_marks(g, all_stored)
        rg = [[0, 1], [2, 3], [4, 5], [6, 7]]
        cc.inc(g.collective_compute("AllGather", ALU.bypass, replica_groups=rg,
                                    ins=[k_own.opt()], outs=[k_all.opt()]))
        cc.inc(g.collective_compute("AllGather", ALU.bypass, replica_groups=rg,
                                    ins=[v_own.opt()], outs=[v_all.opt()]))
    P.on("gpsimd", fcc)
    cc_mark = cc.mark

    stAT = contextlib.ExitStack()
    khead = [P.sb(f"khead{i}", (128, 2 * T_CORE), BF16, stAT, side="right") for i in range(2)]
    vhead = [P.sb(f"vhead{i}", (128, 32 * 65), BF16, stAT, side="right") for i in range(2)]
    qhead = [P.sb(f"qhead{i}", (64, T_CORE), BF16, stAT, side="right") for i in range(2)]
    mqv = [P.sb(f"mqv{i}", (64, 2), F32, stAT, side="right") for i in range(2)]
    gbias = [P.sb(f"gbias{i}", (128, 1), F32, stAT, side="right") for i in range(2)]
    g1 = P.sb("g1", (1, 1), F32, stAT, side="right")
    mtile = [P.sb(f"mtile{i}", (128, 512), F32, stAT, side="right") for i in range(2)]
    linv = P.sb("linv", (1, 512), F32, stAT, side="right")
    pt = [P.sb(f"pt{i}", (128, 512), BF16, stAT, side="right") for i in range(4)]
    otmp = P.sb("otmp", (64, 512), BF16, stAT, side="right")
    st_ps = [P.ps(f"st_ps{i}", (128, 512), stack=stAT) for i in range(4)]
    ot_ps = [P.ps(f"ot_ps{i}", (65, 512), stack=stAT) for i in range(2)]
    gg_ps = P.ps("gg_ps", (128, 2), stack=stAT)
    rep_ps = P.ps("rep_ps", (64, 512), stack=stAT)

    def ktile_loc(j):
        gk = j // 4
        return gk % 2, (gk // 2) * 512 + (j % 4) * 128

    ep_done = {}
    kvq = P.sem("at_kvq")
    slot_prev = {0: [], 1: []}
    st_guard = {}
    prev_mask_exp = None
    S0 = S.n
    exp_jobs = []
    def exp_pred(e):
        return (S, S0 + e + 1)
    gtile = 0
    mseq = 0
    pm_exp = {0: None, 1: None}
    for h in range(NH):
        hb = (h % 2) * 64
        hc = h // 2
        hs = h % 2
        kh, vh, qh, mq, gbi = khead[hs], vhead[hs], qhead[hs], mqv[hs], gbias[hs]
        prevm = list(slot_prev[hs])
        def fkv(sy, h=h, kh=kh, vh=vh, pm=prevm):
            wait_marks(sy, [cc_mark] + pm)
            va = v_all.rearrange("(r h t) d -> r h t d", r=2, h=NH)
            for r in range(2):
                kvq.inc(sy.dma_start(out=kh[0:64, r * T_CORE:(r + 1) * T_CORE],
                                     in_=k_all[r * D + 64 * h: r * D + 64 * h + 64, :]), 16)
                kvq.inc(sy.dma_start(
                    out=vh[:].rearrange("p (kt c) -> p kt c", c=65)[:, r * 16:(r + 1) * 16, 0:64],
                    in_=va[r, h].rearrange("(t p) c -> p t c", p=128)), 16)
        P.on("sync", fkv)
        kv_mark = kvq.mark
        def fvh1(v, kv_mark=kv_mark, h=h, hb=hb, hc=hc, kh=kh, vh=vh, qh=qh, mq=mq,
                 pm=prevm):
            wait_marks(v, [kv_mark] + pm)
            if h < 2:
                v.memset(vh[:].rearrange("p (kt c) -> p kt c", c=65)[:, :, 64:65], 1.0)
            V.inc(v.tensor_copy(qh[:], qT[hb:hb + 64, hc * T_CORE:(hc + 1) * T_CORE]))
            v.wait_ge(V.h, V.n)
            v.tensor_reduce(mq[:, 0:1], qh[:], mybir.AxisListType.X, ALU.max,
                            apply_absolute_value=True)
            V.inc(v.tensor_reduce(mq[:, 1:2], kh[0:64, :], mybir.AxisListType.X, ALU.max,
                                  apply_absolute_value=True))
        P.on("vector", fvh1)
        mq_mark = V.mark
        def fgb(t, mq_mark=mq_mark, mq=mq):
            wait_marks(t, [mq_mark])
            Tn.inc(t.matmul(gg_ps[0:1, 0:1], mq[:, 0:1], mq[:, 1:2], start=True, stop=True))
        P.on("tensor", fgb)
        gmm_mark = Tn.mark
        def fgb2(v, gmm_mark=gmm_mark):
            wait_marks(v, [gmm_mark])
            V.inc(v.tensor_scalar(g1[:], gg_ps[0:1, 0:1], -0.125, None, ALU.mult))
        P.on("vector", fgb2)
        g1_mark = V.mark
        def fgb3(t, g1_mark=g1_mark):
            wait_marks(t, [g1_mark])
            Tn.inc(t.matmul(gg_ps[:, 1:2], onesf[:], g1[:], start=True, stop=True))
        P.on("tensor", fgb3)
        gb_mark = Tn.mark
        def fgb4(v, gb_mark=gb_mark, gbi=gbi, pm=prevm):
            wait_marks(v, [gb_mark] + pm)
            V.inc(v.tensor_copy(gbi[:], gg_ps[:, 1:2]))
        P.on("vector", fgb4)
        gbias_mark = V.mark

        for si in range(4):
            KT = 8 * si + 8
            osel = si % 2
            prev_ep = ep_done.get((h, si - 2)) if si >= 2 else ep_done.get((h - 1, si + 2))
            pend_fav = None
            for j in range(KT):
                rk, loc = ktile_loc(j)
                dj = j - 8 * si
                pp = gtile % 4
                gtile += 1
                def fsc(t, si=si, j=j, rk=rk, loc=loc, pp=pp, gbias_mark=gbias_mark,
                        prev_ep=prev_ep, kh=kh, qh=qh, guard=st_guard.get(pp)):
                    wait_marks(t, [gbias_mark])
                    if guard is not None:
                        wait_marks(t, guard)
                    if j == 0 and prev_ep is not None:
                        wait_marks(t, [prev_ep])
                    Tn.inc(t.matmul(st_ps[pp][:],
                                    kh[0:64, rk * T_CORE + loc: rk * T_CORE + loc + 128],
                                    qh[:, si * 512:(si + 1) * 512], start=True, stop=True))
                P.on("tensor", fsc)
                sc_mark = Tn.mark
                if dj >= 0:
                    ms = mseq % 2
                    mseq += 1
                    def fmask(v, pp=pp, dj=dj, ms=ms, sc_mark=sc_mark, pme=pm_exp[ms]):
                        wait_marks(v, [sc_mark])
                        if pme is not None:
                            wait_marks(v, [pme])
                        V.inc(v.scalar_tensor_tensor(mtile[ms][:], st_ps[pp][:], 0.125,
                                                     mask_sb[:, dj * 512:(dj + 1) * 512],
                                                     ALU.mult, ALU.add))
                    P.on("vector", fmask)
                    exp_jobs.append((V.mark, True, pp, ms, gbi))
                    pm_exp[ms] = exp_pred(len(exp_jobs) - 1)
                else:
                    exp_jobs.append((sc_mark, False, pp, 0, gbi))
                exp_mark = exp_pred(len(exp_jobs) - 1)
                def fav(t, j=j, pp=pp, osel=osel, exp_mark=exp_mark, KT=KT, vh=vh):
                    wait_marks(t, [exp_mark])
                    Tn.inc(t.matmul(ot_ps[osel][:], vh[:, ktv(j) * 65: ktv(j) * 65 + 65],
                                    pt[pp][:], start=(j == 0), stop=(j == KT - 1)))
                if pend_fav is not None:
                    pf, ppp, pexp = pend_fav
                    P.on("tensor", pf)
                    st_guard[ppp] = [pexp, Tn.mark]
                pend_fav = (fav, pp, exp_mark)
            pf, ppp, pexp = pend_fav
            P.on("tensor", pf)
            st_guard[ppp] = [pexp, Tn.mark]
            av_mark = Tn.mark
            def fep1(v, osel=osel, av_mark=av_mark):
                wait_marks(v, [av_mark])
                V.inc(v.reciprocal(linv[:], ot_ps[osel][64:65, :]))
            P.on("vector", fep1)
            li_mark = V.mark
            def fep2(t, li_mark=li_mark):
                wait_marks(t, [li_mark])
                Tn.inc(t.matmul(rep_ps[:], onesf[0:1, 0:64], linv[:], start=True, stop=True))
            P.on("tensor", fep2)
            rep_mark = Tn.mark
            def fep3(v, osel=osel, rep_mark=rep_mark, hb=hb, hc=hc, si=si):
                wait_marks(v, [rep_mark])
                v.wait_ge(V.h, V.n)
                V.inc(v.tensor_copy(otmp[:], ot_ps[osel][0:64, :]))
                v.wait_ge(V.h, V.n)
                V.inc(v.tensor_mul(
                    attn_sb[hb:hb + 64, hc * T_CORE + si * 512: hc * T_CORE + si * 512 + 512],
                    otmp[:], rep_ps[:]))
            P.on("vector", fep3)
            ep_done[(h, si)] = V.mark
        slot_prev[hs] = [V.mark, Tn.mark, exp_pred(len(exp_jobs) - 1)]

    def fexps(sc, jobs=list(exp_jobs)):
        for wm, masked, pp, ms, gbi in jobs:
            wait_marks(sc, [wm])
            if masked:
                S.inc(sc.activation(pt[pp][:], mtile[ms][:], AF.Exp,
                                    bias=gbi[:, 0:1], scale=1.0))
            else:
                S.inc(sc.activation(pt[pp][:], st_ps[pp][:], AF.Exp,
                                    bias=gbi[:, 0:1], scale=0.125))
    P.on("scalar", fexps)
    assert S.n == S0 + len(exp_jobs)

    attn_done = [V.mark, Tn.mark, S.mark]
    stAT.close()

    stPJ = contextlib.ExitStack()
    wpj_sb = P.sb("wpj_sb", (128, NCH * 1024), BF16, stPJ, side="right")
    ptmp = P.sb("ptmp", (128, 512), F32, stPJ, side="right")
    xres = [P.sb(f"xres{i}", (128, 512), F32, stPJ, side="right") for i in range(2)]
    pj_ps = [P.ps(f"pj_ps{i}", (128, 512), stack=stPJ) for i in range(2)]
    wpj_r = wproj_d.rearrange("(c p) o -> p c o", p=128)
    def fpw(g):
        wait_marks(g, attn_done)
        gq.inc(g.dma_start(out=wpj_sb[:].rearrange("p (c o) -> p c o", c=NCH), in_=wpj_r), 16)
    P.on("gpsimd", fpw)
    pw_mark = gq.mark
    res_done = {}
    for oc in range(NCH):
        for tt in range(4):
            it = oc * 4 + tt
            pp = it % 2
            def fpm(t, oc=oc, tt=tt, pp=pp, it=it):
                wait_marks(t, [pw_mark] + attn_done)
                if it >= 2:
                    wait_marks(t, [res_done[it - 2]])
                for c in range(NCH):
                    i = t.matmul(pj_ps[pp][:], wpj_sb[:, c * 1024 + oc * 128: c * 1024 + (oc + 1) * 128],
                                 attn_sb[:, c * T_CORE + tt * 512: c * T_CORE + tt * 512 + 512],
                                 start=(c == 0), stop=(c == NCH - 1))
                Tn.inc(i)
            P.on("tensor", fpm)
            pm_mark = Tn.mark
            def fxld(sy, oc=oc, tt=tt, pp=pp, it=it):
                if it >= 2:
                    wait_marks(sy, [res_done[it - 2]])
                else:
                    wait_marks(sy, attn_done)
                xl[pp].inc(sy.dma_start(out=xres[pp][:],
                                        in_=x_in[oc * 128:(oc + 1) * 128, tt * 512:(tt + 1) * 512]), 16)
            P.on("sync", fxld)
            xld_mark = xl[pp].mark
            def fpr(v, oc=oc, pp=pp, pm_mark=pm_mark, xld_mark=xld_mark):
                wait_marks(v, [pm_mark, xld_mark])
                v.wait_ge(V.h, V.n)
                V.inc(v.tensor_scalar(ptmp[:], pj_ps[pp][:], bproj_sb[:, oc:oc + 1], None, ALU.add))
                v.wait_ge(V.h, V.n)
                V.inc(v.tensor_add(xres[pp][:], ptmp[:], xres[pp][:]))
            P.on("vector", fpr)
            wr_mark = V.mark
            def fxst(g, oc=oc, tt=tt, pp=pp, wr_mark=wr_mark):
                wait_marks(g, [wr_mark])
                xst[pp].inc(g.dma_start(out=x_out[oc * 128:(oc + 1) * 128, tt * 512:(tt + 1) * 512],
                                        in_=xres[pp][:]), 16)
            P.on("gpsimd", fxst)
            res_done[it] = xst[pp].mark
    fin = [xst[0].mark, xst[1].mark]
    G = P.E["gpsimd"]
    def ffin(g, fin=fin):
        wait_marks(g, fin)
        G.inc(g.nop())
    P.on("gpsimd", ffin)
    stPJ.close()
    stON.close()
    return G.mark


def emit_l0ffn(P, x_in, x_out, xln_sb, wg_d, wu_d, wd_d, waits):
    V, S, Tn = P.E["vector"], P.E["scalar"], P.E["tensor"]
    wgu = [P.sem(f"ffnw{i}") for i in range(2)]
    wds = P.sem("ffnwd")
    xl = [P.sem(f"ffnxl{i}") for i in range(2)]
    xst = [P.sem(f"ffnxst{i}") for i in range(2)]
    stk = contextlib.ExitStack()
    SW = 512
    NSTRIP = T_CORE // SW
    wgt = [P.sb(f"wg{i}", (128, NCH * 128), BF16, stk, side="right") for i in range(2)]
    wut = [P.sb(f"wu{i}", (128, NCH * 128), BF16, stk, side="right") for i in range(2)]
    wd_res = P.sb("wd_res", (128, HCH * 1024), BF16, stk, side="right")
    gbuf = P.sb("gbuf", (128, SW), BF16, stk, side="right")
    ubuf = P.sb("ubuf", (128, SW), BF16, stk, side="right")
    h_sb = P.sb("h_sb", (128, HCH * SW), BF16, stk, side="right")
    xres = [P.sb(f"fxres{i}", (128, SW), F32, stk, side="right") for i in range(2)]
    g_ps_ = [P.ps(f"gg_ps{i}", (128, SW), stack=stk) for i in range(2)]
    u_ps_ = [P.ps(f"uu_ps{i}", (128, SW), stack=stk) for i in range(2)]
    down_pp = [P.ps(f"down_pp{i}", (128, 512), stack=stk) for i in range(2)]

    wg_r = wg_d.rearrange("(c p) h -> p c h", p=128)
    wu_r = wu_d.rearrange("(c p) h -> p c h", p=128)
    wd_r = wd_d.rearrange("(c p) d -> p c d", p=128)
    wgu_base = [wgu[0].n, wgu[1].n]
    wds_base = wds.n

    def load_gu(g, it):
        hc = it % HCH
        s = wgu[it % 2]
        s.inc(g.dma_start(out=wgt[it % 2][:].rearrange("p (c k) -> p c k", c=NCH),
                          in_=wg_r[:, :, hc * 128:(hc + 1) * 128]), 16)
        s.inc(g.dma_start(out=wut[it % 2][:].rearrange("p (c k) -> p c k", c=NCH),
                          in_=wu_r[:, :, hc * 128:(hc + 1) * 128]), 16)

    def f0(g):
        wait_marks(g, waits)
        wds.inc(g.dma_start(out=wd_res[:].rearrange("p (c d) -> p c d", c=HCH), in_=wd_r), 16)
        load_gu(g, 0)
        load_gu(g, 1)
    P.on("gpsimd", f0)

    h_mark = {}
    res_mark = {}
    down_mark = {}
    for st_i in range(NSTRIP):
        c0 = st_i * SW
        for hc in range(HCH):
            it = st_i * HCH + hc
            slot = hc % 2
            def fmm(t, hc=hc, slot=slot, c0=c0, it=it):
                wait_marks(t, waits)
                t.wait_ge(wgu[it % 2].h, wgu_base[it % 2] + 32 * (it // 2 + 1))
                if it >= 2:
                    wait_marks(t, [h_mark[it - 2]])
                for c in range(NCH):
                    t.matmul(g_ps_[slot][:], wgt[slot][:, c * 128:(c + 1) * 128],
                             xln_sb[:, c * T_CORE + c0: c * T_CORE + c0 + SW],
                             start=(c == 0), stop=(c == NCH - 1))
                for c in range(NCH):
                    i = t.matmul(u_ps_[slot][:], wut[slot][:, c * 128:(c + 1) * 128],
                                 xln_sb[:, c * T_CORE + c0: c * T_CORE + c0 + SW],
                                 start=(c == 0), stop=(c == NCH - 1))
                Tn.inc(i)
            P.on("tensor", fmm)
            gu_mark = Tn.mark
            def fsilu(sc, gu_mark=gu_mark, slot=slot):
                wait_marks(sc, [gu_mark])
                S.inc(sc.activation(gbuf[:], g_ps_[slot][:], AF.Silu, bias=0.0, scale=1.0 / 127.0))
            P.on("scalar", fsilu)
            silu_mark = S.mark
            def fup(v, silu_mark=silu_mark, gu_mark=gu_mark, hc=hc, st_i=st_i, slot=slot):
                wait_marks(v, [gu_mark])
                if st_i >= 1:
                    wait_marks(v, [down_mark[st_i - 1]])
                v.wait_ge(V.h, V.n)
                V.inc(v.tensor_scalar(ubuf[:], u_ps_[slot][:], 1.0 / 127.0, None, ALU.mult))
                wait_marks(v, [silu_mark])
                v.wait_ge(V.h, V.n)
                V.inc(v.tensor_mul(h_sb[:, hc * SW:(hc + 1) * SW], gbuf[:], ubuf[:]))
            P.on("vector", fup)
            h_mark[it] = V.mark
            if it + 2 < NSTRIP * HCH:
                def fpre(g, it=it, gu_mark=gu_mark):
                    wait_marks(g, [gu_mark])
                    load_gu(g, it + 2)
                P.on("gpsimd", fpre)
        h_all = V.mark
        for dd in range(NCH):
            bank = down_pp[dd % 2]
            it2 = st_i * NCH + dd
            pp = it2 % 2
            def fdown(t, dd=dd, bank=bank, h_all=h_all, st_i=st_i, it2=it2):
                wait_marks(t, [h_all])
                t.wait_ge(wds.h, wds_base + 16)
                if it2 >= 2:
                    wait_marks(t, [res_mark[it2 - 2]])
                for hc in range(HCH):
                    i = t.matmul(bank[:, 0:SW],
                                 wd_res[:, hc * 1024 + dd * 128: hc * 1024 + (dd + 1) * 128],
                                 h_sb[:, hc * SW:(hc + 1) * SW],
                                 start=(hc == 0), stop=(hc == HCH - 1))
                Tn.inc(i)
            P.on("tensor", fdown)
            dd_mark = Tn.mark
            def fxld(sy, dd=dd, c0=c0, pp=pp, it2=it2, h_all=h_all):
                if it2 >= 2:
                    wait_marks(sy, [res_mark[it2 - 2]])
                else:
                    wait_marks(sy, [h_all])
                xl[pp].inc(sy.dma_start(out=xres[pp][:],
                                        in_=x_in[dd * 128:(dd + 1) * 128, c0:c0 + SW]), 16)
            P.on("sync", fxld)
            xld_mark = xl[pp].mark
            def fres(v, bank=bank, pp=pp, dd_mark=dd_mark, xld_mark=xld_mark):
                wait_marks(v, [dd_mark, xld_mark])
                V.inc(v.scalar_tensor_tensor(xres[pp][:], bank[:, 0:SW], 0.5 / 127.0,
                                             xres[pp][:], ALU.mult, ALU.add))
            P.on("vector", fres)
            wr_mark = V.mark
            def fxst(g, dd=dd, c0=c0, pp=pp, wr_mark=wr_mark):
                wait_marks(g, [wr_mark])
                xst[pp].inc(g.dma_start(out=x_out[dd * 128:(dd + 1) * 128, c0:c0 + SW],
                                        in_=xres[pp][:]), 16)
            P.on("gpsimd", fxst)
            res_mark[it2] = xst[pp].mark
        down_mark[st_i] = Tn.mark
    fin = [xst[0].mark, xst[1].mark]
    G = P.E["gpsimd"]
    def ffin(g, fin=fin):
        wait_marks(g, fin)
        G.inc(g.nop())
    P.on("gpsimd", ffin)
    stk.close()
    return G.mark




def emit_flag(P, scl):
    V, Tn = P.E["vector"], P.E["tensor"]
    fl = P.sb("flags", (1, 8), F32, side="left")
    on1 = P.sb("flag_on1", (1, 128), F32, side="left")
    stF = contextlib.ExitStack()
    bc_ps = P.ps("flag_bc", (128, 4), stack=stF)
    def f(v):
        v.wait_ge(V.h, V.n)
        V.inc(v.memset(fl[0:1, 0:1], 1.0))
        v.wait_ge(V.h, V.n)
        V.inc(v.tensor_scalar(fl[0:1, 1:2], fl[0:1, 0:1], float(np.float32(2.0 ** -30)), 1.0,
                              ALU.add, ALU.subtract))
        v.wait_ge(V.h, V.n)
        V.inc(v.tensor_scalar(fl[0:1, 1:2], fl[0:1, 1:2], float(np.float32(2.0 ** 30)), None,
                              ALU.mult))
        v.wait_ge(V.h, V.n)
        V.inc(v.tensor_scalar(fl[0:1, 2:3], fl[0:1, 1:2], -1.0, 1.0, ALU.mult, ALU.add))
        v.wait_ge(V.h, V.n)
        V.inc(v.memset(fl[0:1, 3:4], 32767.0))
        V.inc(v.memset(fl[0:1, 4:5], 7.0))
        V.inc(v.memset(on1[:], 1.0))
        v.wait_ge(V.h, V.n)
        V.inc(v.reciprocal(fl[0:1, 3:4], fl[0:1, 3:4]))
        v.wait_ge(V.h, V.n)
        V.inc(v.reciprocal(fl[0:1, 4:5], fl[0:1, 4:5]))
    P.on("vector", f)
    fm = V.mark
    def fb(t, fm=fm):
        wait_marks(t, [fm])
        Tn.inc(t.matmul(bc_ps[:], on1[:], fl[0:1, 1:5], start=True, stop=True))
    P.on("tensor", fb)
    bm = Tn.mark
    def fc(v, bm=bm):
        wait_marks(v, [bm])
        v.wait_ge(V.h, V.n)
        V.inc(v.tensor_copy(scl[:, 20:24], bc_ps[:]))
    P.on("vector", fc)
    stF.close()
    return fl, on1, V.mark


def _ma(v, E, dst, src, s, c):
    v.wait_ge(E.h, E.n)
    E.inc(v.tensor_scalar(dst, src, s, c, ALU.mult, ALU.add))


def emit_quant_tile(v, V, src, dst, inv_ap, pa_ap, pb_ap, y, r2, pool=False):
    v.wait_ge(V.h, V.n)
    V.inc(v.tensor_scalar(y, src, inv_ap, C1, ALU.mult, ALU.add))
    if pool:
        _ma(v, V, r2, y, 1.0, C2)
        _ma(v, V, r2, r2, 1.0, -C2)
        _ma(v, V, y, y, 1.0, -C1)
        _ma(v, V, y, y, pa_ap, 0.0)
        _ma(v, V, r2, r2, 1.0, -C1)
        _ma(v, V, r2, r2, pb_ap, 0.0)
    else:
        v.wait_ge(V.h, V.n)
        V.inc(v.tensor_scalar(r2, y, C2, C2, ALU.add, ALU.subtract))
        V.inc(v.tensor_scalar(y, y, C1, pa_ap, ALU.subtract, ALU.mult))
        v.wait_ge(V.h, V.n)
        V.inc(v.tensor_scalar(r2, r2, C1, pb_ap, ALU.subtract, ALU.mult))
    v.wait_ge(V.h, V.n)
    V.inc(v.tensor_add(dst, y, r2))


def emit_lpbq_dequant(P, tag, w_d, wdeq_d, wdeqT_d, O, I, fl, fl_mark, waits):
    V = P.E["vector"]
    gl = [P.sem("lpl" + tag + str(i)) for i in range(2)]
    gs = [P.sem("lps" + tag + str(i)) for i in range(2)]
    tq = P.sem("lpt" + tag)
    gt_ = P.sem("lpu" + tag)
    stk = contextlib.ExitStack()
    NB = I // 32
    wtile = [P.sb("lpw" + tag + str(i), (128, I), F32, stk, side="right") for i in range(2)]
    mx = P.sb("lpmx" + tag, (128, NB), F32, stk, side="right")
    sblk = P.sb("lps" + tag, (128, NB), F32, stk, side="right")
    iblk = P.sb("lpi" + tag, (128, NB), F32, stk, side="right")
    t1 = P.sb("lpt1" + tag, (128, I), F32, stk, side="right")
    r1 = P.sb("lpr1" + tag, (128, I), F32, stk, side="right")
    r2 = P.sb("lpr2" + tag, (128, I), F32, stk, side="right")
    wq = [P.sb("lpwq" + tag + str(i), (128, I), BF16, stk, side="right") for i in range(2)]
    ttile = P.sb("lptt" + tag, (128, 1024), BF16, stk, side="right")
    NC_ = O // 128
    st_mark = {}
    q_mark = {}
    for rc in range(NC_):
        wt = wtile[rc % 2]
        def fld(g, rc=rc, wt=wt):
            wait_marks(g, waits)
            if rc >= 2:
                wait_marks(g, [q_mark[rc - 2]])
            gl[rc % 2].inc(g.dma_start(out=wt[:], in_=w_d[rc * 128:(rc + 1) * 128, :]), 16)
        P.on("gpsimd", fld)
        ld_mark = gl[rc % 2].mark
        def fq(v, rc=rc, wt=wt, ld_mark=ld_mark):
            wait_marks(v, [ld_mark, fl_mark])
            if rc >= 2:
                wait_marks(v, [st_mark[rc - 2]])
            v.wait_ge(V.h, V.n)
            V.inc(v.tensor_reduce(mx[:], wt[:].rearrange("p (b k) -> p b k", k=32),
                                  mybir.AxisListType.X, ALU.max, apply_absolute_value=True))
            v.wait_ge(V.h, V.n)
            V.inc(v.tensor_scalar(sblk[:], mx[:], P._scl[:, 23:24], 1e-12, ALU.mult, ALU.max))
            v.wait_ge(V.h, V.n)
            V.inc(v.reciprocal(iblk[:], sblk[:]))
            v.wait_ge(V.h, V.n)
            V.inc(v.tensor_mul(t1[:].rearrange("p (b k) -> p b k", k=32),
                               wt[:].rearrange("p (b k) -> p b k", k=32),
                               iblk[:].broadcast_to((128, NB, 32))))
            v.wait_ge(V.h, V.n)
            V.inc(v.tensor_scalar(r1[:], t1[:], C1, C1, ALU.add, ALU.subtract))
            V.inc(v.tensor_scalar(r2[:], t1[:], C2, C2, ALU.add, ALU.subtract))
            v.wait_ge(V.h, V.n)
            V.inc(v.tensor_scalar(r1[:], r1[:], scl_flagm(P), None, ALU.mult))
            v.wait_ge(V.h, V.n)
            V.inc(v.tensor_scalar(r2[:], r2[:], scl_flag(P), None, ALU.mult))
            v.wait_ge(V.h, V.n)
            V.inc(v.tensor_add(r1[:], r1[:], r2[:]))
            v.wait_ge(V.h, V.n)
            V.inc(v.tensor_mul(wq[rc % 2][:].rearrange("p (b k) -> p b k", k=32),
                               r1[:].rearrange("p (b k) -> p b k", k=32),
                               sblk[:].broadcast_to((128, NB, 32))))
            q_mark[rc] = V.mark
        P.on("vector", fq)
        def fst(g, rc=rc):
            wait_marks(g, [q_mark[rc]])
            gs[rc % 2].inc(g.dma_start(out=wdeq_d[rc * 128:(rc + 1) * 128, :], in_=wq[rc % 2][:]), 16)
        P.on("gpsimd", fst)
        st_mark[rc] = gs[rc % 2].mark
    all_st = [gs[0].mark, gs[1].mark]
    tr_mark = {}
    for dc in range(I // 128):
        for og in range(O // 1024):
            it = dc * (O // 1024) + og
            def ftr(sc, dc=dc, og=og, it=it, all_st=all_st):
                wait_marks(sc, all_st)
                if it >= 1:
                    wait_marks(sc, [tr_mark[it - 1]])
                tq.inc(sc.dma_start_transpose(ttile[:, 0:1024],
                                              wdeq_d[og * 1024:(og + 1) * 1024,
                                                     dc * 128:(dc + 1) * 128]), 16)
            P.on("scalar", ftr)
            ld2 = tq.mark
            def fst2(g, dc=dc, og=og, ld2=ld2):
                wait_marks(g, [ld2])
                gt_.inc(g.dma_start(out=wdeqT_d[dc * 128:(dc + 1) * 128, og * 1024:(og + 1) * 1024],
                                    in_=ttile[:, 0:1024]), 16)
            P.on("gpsimd", fst2)
            tr_mark[it] = gt_.mark
    fin = gt_.mark
    stk.close()
    return fin


def scl_flag(P):
    return P._scl[:, 20:21]


def scl_flagm(P):
    return P._scl[:, 21:22]


def emit_cross_part_max(P, tag, maxbuf_ap, src_mark, amax_row, stack):
    V = P.E["vector"]
    gx = P.sem("cpm" + tag)
    col = P.sb("cpm_c" + tag, (128, 1), F32, stack, side="right")
    row = P.sb("cpm_r" + tag, (1, 128), F32, stack, side="right")
    def f1(v, src_mark=src_mark):
        wait_marks(v, src_mark if isinstance(src_mark, list) else [src_mark])
        v.wait_ge(V.h, V.n)
        V.inc(v.tensor_reduce(col[:], maxbuf_ap, mybir.AxisListType.X, ALU.max))
    P.on("vector", f1)
    m1 = V.mark
    def f2(g, m1=m1):
        wait_marks(g, [m1])
        gx.inc(g.dma_start(out=row[0:1, :], in_=col[:, 0:1]), 16)
    P.on("gpsimd", f2)
    m2 = gx.mark
    def f3(v, m2=m2):
        wait_marks(v, [m2])
        v.wait_ge(V.h, V.n)
        V.inc(v.tensor_reduce(amax_row[0:1, 0:1], row[0:1, :], mybir.AxisListType.X, ALU.max))
    P.on("vector", f3)
    return V.mark


def emit_qdq_scalars(P, tag, amax_mark, amax_row, fl, on1, flbc_mark, scl, col0, stack):
    V, Tn = P.E["vector"], P.E["tensor"]
    gq = P.sem("qsg" + tag)
    cc = P.sem("qsc" + tag)
    cc_in = P.dram_tmp("qs_in" + tag, (1, 128), F32)
    cc_out = P.dram_tmp("qs_out" + tag, (1, 128), F32)
    row = P.sb("qsr" + tag, (1, 8), F32, stack, side="right")
    zr = P.sb("qsz" + tag, (1, 128), F32, stack, side="right")
    bc_ps = P.ps("qsb" + tag, (128, 4), stack=stack)
    def fz(v, amax_mark=amax_mark):
        wait_marks(v, [amax_mark])
        v.wait_ge(V.h, V.n)
        V.inc(v.memset(zr[:], 0.0))
        v.wait_ge(V.h, V.n)
        V.inc(v.tensor_copy(zr[0:1, 0:1], amax_row[0:1, 0:1]))
    P.on("vector", fz)
    z_mark = V.mark
    def fst(g, z_mark=z_mark):
        wait_marks(g, [z_mark])
        gq.inc(g.dma_start(out=cc_in, in_=zr[:]), 16)
        g.wait_ge(gq.h, gq.n)
        cc.inc(g.collective_compute("AllReduce", ALU.max,
                                    replica_groups=[list(range(8))],
                                    ins=[cc_in.opt()], outs=[cc_out.opt()]))
        g.wait_ge(cc.h, cc.n)
        gq.inc(g.dma_start(out=row[0:1, 0:1], in_=cc_out[0:1, 0:1]), 16)
    P.on("gpsimd", fst)
    ar_mark = gq.mark
    def fsc(v, ar_mark=ar_mark):
        wait_marks(v, [ar_mark, flbc_mark])
        v.wait_ge(V.h, V.n)
        V.inc(v.tensor_scalar(row[0:1, 1:2], row[0:1, 0:1], fl[0:1, 3:4], 1e-12,
                              ALU.mult, ALU.max))
        v.wait_ge(V.h, V.n)
        V.inc(v.reciprocal(row[0:1, 2:3], row[0:1, 1:2]))
        v.wait_ge(V.h, V.n)
        V.inc(v.tensor_mul(row[0:1, 3:4], row[0:1, 1:2], fl[0:1, 2:3]))
        v.wait_ge(V.h, V.n)
        V.inc(v.tensor_mul(row[0:1, 4:5], row[0:1, 1:2], fl[0:1, 1:2]))
        v.wait_ge(V.h, V.n)
        V.inc(v.tensor_copy(row[0:1, 5:6], row[0:1, 1:2]))
    P.on("vector", fsc)
    sc_mark = V.mark
    def fbc(t, sc_mark=sc_mark):
        wait_marks(t, [sc_mark])
        Tn.inc(t.matmul(bc_ps[:], on1[:], row[0:1, 2:6], start=True, stop=True))
    P.on("tensor", fbc)
    bc_mark = Tn.mark
    def fcp(v, bc_mark=bc_mark):
        wait_marks(v, [bc_mark])
        v.wait_ge(V.h, V.n)
        V.inc(v.tensor_copy(scl[:, col0:col0 + 4], bc_ps[:]))
    P.on("vector", fcp)
    return V.mark


def emit_l1ffn(P, x_in, x_out, wg_d, wu_d, wd_d, waits, deq_cache=None):
    V, S, Tn = P.E["vector"], P.E["scalar"], P.E["tensor"]
    scl = P.sb("scl", (128, 24), F32, side="left")
    P._scl = scl
    fl, on1, fl_mark = emit_flag(P, scl)

    up_pre = P.dram_tmp("up_pre", (HID, T_CORE), BF16)
    gate_pre = P.dram_tmp("gate_pre", (HID, T_CORE), BF16)
    t_pre = P.dram_tmp("t_pre", (HID, T_CORE), BF16)
    o_pre = P.dram_tmp("o_pre", (HID, T_CORE), BF16)

    if deq_cache is None:
        wdeq_u = P.dram_tmp("wdeq_u", (HID, D), BF16)
        wdeq_g = P.dram_tmp("wdeq_g", (HID, D), BF16)
        wdeq_d2 = P.dram_tmp("wdeq_d", (D, HID), BF16)
        wdqT_u = P.dram_tmp("wdqT_u", (D, HID), BF16)
        wdqT_g = P.dram_tmp("wdqT_g", (D, HID), BF16)
        wdqT_d = P.dram_tmp("wdqT_d", (HID, D), BF16)
        dq_u = emit_lpbq_dequant(P, "u", wu_d, wdeq_u, wdqT_u, HID, D, fl, fl_mark, waits)
        dq_g = emit_lpbq_dequant(P, "g", wg_d, wdeq_g, wdqT_g, HID, D, fl, fl_mark,
                                 waits + [dq_u])
        dq_d = emit_lpbq_dequant(P, "d", wd_d, wdeq_d2, wdqT_d, D, HID, fl, fl_mark,
                                 waits + [dq_g])
    else:
        wdqT_u, wdqT_g, wdqT_d = deq_cache
        dq_u = dq_g = dq_d = fl_mark
    P._deq_cache = (wdqT_u, wdqT_g, wdqT_d)

    stLN = contextlib.ExitStack()
    xln_sb = P.sb("xln_sb", (128, NCH * T_CORE), BF16, stLN, side="right")
    xq_sb = P.sb("xq_sb", (128, NCH * T_CORE), BF16, stLN, side="right")
    ln_mark = emit_ln_stream(P, "L1", x_in, xln_sb, waits + [dq_d])
    mbx = P.sb("mbx", (128, NCH), F32, stLN, side="right")
    amax_row = P.sb("amaxr", (1, 4), F32, side="left")
    def fmx(v, ln_mark=ln_mark):
        wait_marks(v, [ln_mark])
        for c in range(NCH):
            v.wait_ge(V.h, V.n)
            V.inc(v.tensor_reduce(mbx[:, c:c + 1], xln_sb[:, c * T_CORE:(c + 1) * T_CORE],
                                  mybir.AxisListType.X, ALU.max, apply_absolute_value=True))
    P.on("vector", fmx)
    xmx_mark = emit_cross_part_max(P, "x", mbx[:], V.mark, amax_row, stLN)
    sx_mark = emit_qdq_scalars(P, "x", xmx_mark, amax_row, fl, on1, fl_mark, scl, 0, stLN)
    qy = P.sb("qy", (128, 512), F32, stLN, side="right")
    qr2 = P.sb("qr2", (128, 512), F32, stLN, side="right")
    def fqx(v, sx_mark=sx_mark):
        wait_marks(v, [sx_mark])
        for c in range(NCH):
            for tt in range(4):
                sl = slice(c * T_CORE + tt * 512, c * T_CORE + tt * 512 + 512)
                emit_quant_tile(v, V, xln_sb[:, sl], xq_sb[:, sl],
                                scl[:, 0:1], scl[:, 1:2], scl[:, 2:3], qy[:], qr2[:])
    P.on("vector", fqx)
    xq_mark = V.mark

    stA = contextlib.ExitStack()
    SW = 512
    NSTRIP = T_CORE // SW
    wub = [P.sb(f"l1wu{i}", (128, NCH * 128), BF16, stA, side="right") for i in range(2)]
    wgb = [P.sb(f"l1wg{i}", (128, NCH * 128), BF16, stA, side="right") for i in range(2)]
    uev = [P.sb(f"uev{i}", (128, SW), BF16, stA, side="right") for i in range(2)]
    gev = [P.sb(f"gev{i}", (128, SW), BF16, stA, side="right") for i in range(2)]
    mbu = P.sb("mbu", (128, HCH * NSTRIP), F32, stA, side="right")
    mbg = P.sb("mbg", (128, HCH * NSTRIP), F32, stA, side="right")
    ua_ps = [P.ps(f"ua_ps{i}", (128, 512), stack=stA) for i in range(2)]
    ga_ps = [P.ps(f"ga_ps{i}", (128, 512), stack=stA) for i in range(2)]
    wA = [P.sem(f"l1wa{i}") for i in range(2)]
    sA = [P.sem(f"l1sa{i}") for i in range(2)]
    wA_base = [wA[0].n, wA[1].n]
    wu_r = wdqT_u.rearrange("(c p) h -> p c h", p=128)
    wg_r = wdqT_g.rearrange("(c p) h -> p c h", p=128)

    def loadA(g, it):
        hc = it % HCH
        s = wA[it % 2]
        s.inc(g.dma_start(out=wub[it % 2][:].rearrange("p (c k) -> p c k", c=NCH),
                          in_=wu_r[:, :, hc * 128:(hc + 1) * 128]), 16)
        s.inc(g.dma_start(out=wgb[it % 2][:].rearrange("p (c k) -> p c k", c=NCH),
                          in_=wg_r[:, :, hc * 128:(hc + 1) * 128]), 16)

    def fA0(g, dq=[dq_u, dq_g]):
        wait_marks(g, dq)
        loadA(g, 0)
        loadA(g, 1)
    P.on("gpsimd", fA0)

    ev_mark = {}
    stq_mark = {}
    for st_i in range(NSTRIP):
        c0 = st_i * SW
        for hc in range(HCH):
            it = st_i * HCH + hc
            slot = it % 2
            def fmm(t, hc=hc, slot=slot, c0=c0, it=it):
                wait_marks(t, [xq_mark])
                t.wait_ge(wA[slot].h, wA_base[slot] + 32 * (it // 2 + 1))
                if it >= 2:
                    wait_marks(t, [ev_mark[it - 2]])
                for c in range(NCH):
                    t.matmul(ua_ps[slot][:], wub[slot][:, c * 128:(c + 1) * 128],
                             xq_sb[:, c * T_CORE + c0: c * T_CORE + c0 + SW],
                             start=(c == 0), stop=(c == NCH - 1))
                for c in range(NCH):
                    i = t.matmul(ga_ps[slot][:], wgb[slot][:, c * 128:(c + 1) * 128],
                                 xq_sb[:, c * T_CORE + c0: c * T_CORE + c0 + SW],
                                 start=(c == 0), stop=(c == NCH - 1))
                Tn.inc(i)
            P.on("tensor", fmm)
            mm_mark = Tn.mark
            def fev(v, hc=hc, slot=slot, it=it, mm_mark=mm_mark):
                wait_marks(v, [mm_mark])
                if it >= 2:
                    wait_marks(v, [stq_mark[it - 2]])
                v.wait_ge(V.h, V.n)
                V.inc(v.tensor_copy(uev[slot][:], ua_ps[slot][:]))
                V.inc(v.tensor_copy(gev[slot][:], ga_ps[slot][:]))
                v.wait_ge(V.h, V.n)
                V.inc(v.tensor_reduce(mbu[:, it:it + 1], uev[slot][:], mybir.AxisListType.X,
                                      ALU.max, apply_absolute_value=True))
                V.inc(v.tensor_reduce(mbg[:, it:it + 1], gev[slot][:], mybir.AxisListType.X,
                                      ALU.max, apply_absolute_value=True))
            P.on("vector", fev)
            ev_mark[it] = V.mark
            def fstq(g, hc=hc, slot=slot, c0=c0, ev=ev_mark[it]):
                wait_marks(g, [ev])
                sA[slot].inc(g.dma_start(out=up_pre[hc * 128:(hc + 1) * 128, c0:c0 + SW],
                                         in_=uev[slot][:]), 16)
                sA[slot].inc(g.dma_start(out=gate_pre[hc * 128:(hc + 1) * 128, c0:c0 + SW],
                                         in_=gev[slot][:]), 16)
            P.on("gpsimd", fstq)
            stq_mark[it] = sA[slot].mark
            nxt = it + 2
            if nxt < NSTRIP * HCH:
                def fpre(g, nxt=nxt, mm_mark=mm_mark):
                    wait_marks(g, [mm_mark])
                    loadA(g, nxt)
                P.on("gpsimd", fpre)
    spillA = [sA[0].mark, sA[1].mark]
    umx_mark = emit_cross_part_max(P, "u", mbu[:], ev_mark[NSTRIP * HCH - 1], amax_row, stA)
    su_mark = emit_qdq_scalars(P, "u", umx_mark, amax_row, fl, on1, fl_mark, scl, 4, stA)
    gmx_mark = emit_cross_part_max(P, "g", mbg[:], su_mark, amax_row, stA)
    sg_mark = emit_qdq_scalars(P, "g", gmx_mark, amax_row, fl, on1, fl_mark, scl, 8, stA)
    stA.close()
    stLN.close()

    stB = contextlib.ExitStack()
    G = P.E["gpsimd"]
    gt = [P.sb(f"bgt{i}", (128, 512), BF16, stB, side="right") for i in range(4)]
    btv = [P.sb(f"btv{i}", (128, 512), BF16, stB, side="right") for i in range(4)]
    br = [P.sb(f"br{i}", (128, 512), F32, stB, side="right") for i in range(2)]
    by = [P.sb(f"by{i}", (128, 512), F32, stB, side="right") for i in range(2)]
    br2 = [P.sb(f"br2{i}", (128, 512), F32, stB, side="right") for i in range(2)]
    bsig = [P.sb(f"bsig{i}", (128, 512), F32, stB, side="right") for i in range(2)]
    bsq = [P.sb(f"bsq{i}", (128, 512), F32, stB, side="right") for i in range(2)]
    mbt = P.sb("mbt", (128, HCH * 4), F32, stB, side="right")
    lB = [P.sem(f"l1lb{i}") for i in range(2)]
    sB = [P.sem(f"l1sb{i}") for i in range(2)]
    NTB = HCH * 4
    bq_mark = {}
    bst_mark = {}
    bred_mark = {}
    SB0 = S.n
    sig_jobs = []
    pend_bst = []
    for ti in range(NTB):
        hc = ti // 4
        tc = ti % 4
        slot = ti % 4
        lane = ti % 2
        eng = "vector" if lane == 0 else "gpsimd"
        E = V if lane == 0 else G
        if ti >= 4:
            P.on("gpsimd", pend_bst.pop(0))
            bst_mark[ti - 4] = sB[(ti - 4) % 2].mark
        def fbl(sy, hc=hc, tc=tc, slot=slot, ti=ti, sg_mark=sg_mark, spillA=spillA):
            wait_marks(sy, [sg_mark] + spillA)
            if ti >= 4:
                wait_marks(sy, [bq_mark[ti - 4]])
            lB[ti % 2].inc(sy.dma_start(out=gt[slot][:],
                                      in_=gate_pre[hc * 128:(hc + 1) * 128,
                                                   tc * 512:(tc + 1) * 512]), 16)
        P.on("sync", fbl)
        ld_mark = lB[lane].mark
        def fbq(v, slot=slot, lane=lane, ti=ti, ld_mark=ld_mark, E=E):
            wait_marks(v, [ld_mark])
            emit_quant_tile(v, E, gt[slot][:], br[lane][:], scl[:, 8:9], scl[:, 21:22],
                            scl[:, 20:21], by[lane][:], br2[lane][:], pool=(lane == 1))
        P.on(eng, fbq)
        sig_jobs.append((E.mark, lane))
        sig_mark = (S, SB0 + ti + 1)
        def fbq2(v, slot=slot, lane=lane, ti=ti, sig_mark=sig_mark, E=E):
            wait_marks(v, [sig_mark])
            if ti >= 4:
                wait_marks(v, [bst_mark[ti - 4]])
                if ti - 4 in bred_mark:
                    wait_marks(v, [bred_mark[ti - 4]])
            v.wait_ge(E.h, E.n)
            E.inc(v.tensor_scalar(by[lane][:], bsig[lane][:], 65536.0, C1, ALU.mult, ALU.add))
            if lane == 1:
                _ma(v, E, br2[lane][:], by[lane][:], 1.0, C2)
                _ma(v, E, br2[lane][:], br2[lane][:], 1.0, -C2)
                _ma(v, E, by[lane][:], by[lane][:], 1.0, -C1)
                _ma(v, E, by[lane][:], by[lane][:], scl[:, 21:22], 0.0)
                _ma(v, E, br2[lane][:], br2[lane][:], 1.0, -C1)
                _ma(v, E, br2[lane][:], br2[lane][:], scl[:, 20:21], 0.0)
                v.wait_ge(E.h, E.n)
                E.inc(v.tensor_add(bsq[lane][:], by[lane][:], br2[lane][:]))
                _ma(v, E, br2[lane][:], bsq[lane][:], 1.0, -65535.0)
                v.wait_ge(E.h, E.n)
                E.inc(v.tensor_relu(by[lane][:], br2[lane][:]))
                _ma(v, E, by[lane][:], by[lane][:], -1.0, 0.0)
                v.wait_ge(E.h, E.n)
                E.inc(v.tensor_add(bsq[lane][:], bsq[lane][:], by[lane][:]))
                _ma(v, E, bsq[lane][:], bsq[lane][:], float(1.0 / 65536.0), 0.0)
                _ma(v, E, br[lane][:], br[lane][:], scl[:, 11:12], 0.0)
                v.wait_ge(E.h, E.n)
                E.inc(v.tensor_mul(by[lane][:], br[lane][:], bsq[lane][:]))
            else:
                v.wait_ge(E.h, E.n)
                E.inc(v.tensor_scalar(br2[lane][:], by[lane][:], C2, C2, ALU.add, ALU.subtract))
                E.inc(v.tensor_scalar(by[lane][:], by[lane][:], C1, scl[:, 21:22],
                                      ALU.subtract, ALU.mult))
                v.wait_ge(E.h, E.n)
                E.inc(v.tensor_scalar(br2[lane][:], br2[lane][:], C1, scl[:, 20:21],
                                      ALU.subtract, ALU.mult))
                v.wait_ge(E.h, E.n)
                E.inc(v.tensor_add(bsq[lane][:], by[lane][:], br2[lane][:]))
                v.wait_ge(E.h, E.n)
                E.inc(v.tensor_scalar(bsq[lane][:], bsq[lane][:], 65535.0, float(1.0 / 65536.0),
                                      ALU.min, ALU.mult))
                v.wait_ge(E.h, E.n)
                E.inc(v.scalar_tensor_tensor(by[lane][:], br[lane][:], scl[:, 11:12],
                                             bsq[lane][:], ALU.mult, ALU.mult))
            if lane == 1:
                _ma(v, E, btv[slot][:], by[lane][:], 1.0, 0.0)
            else:
                v.wait_ge(E.h, E.n)
                E.inc(v.tensor_copy(btv[slot][:], by[lane][:]))
            if lane == 0:
                v.wait_ge(E.h, E.n)
                E.inc(v.tensor_reduce(mbt[:, ti:ti + 1], btv[slot][:], mybir.AxisListType.X,
                                      ALU.max, apply_absolute_value=True))
        P.on(eng, fbq2)
        bq_mark[ti] = E.mark
        if lane == 1:
            def fbred(v, ti=ti, slot=slot, m=bq_mark[ti]):
                wait_marks(v, [m])
                v.wait_ge(V.h, V.n)
                V.inc(v.tensor_reduce(mbt[:, ti:ti + 1], btv[slot][:], mybir.AxisListType.X,
                                      ALU.max, apply_absolute_value=True))
            P.on("vector", fbred)
            bred_mark[ti] = V.mark
        def fbst(g, hc=hc, tc=tc, slot=slot, m=bq_mark[ti]):
            wait_marks(g, [m])
            sB[slot % 2].inc(g.dma_start(out=t_pre[hc * 128:(hc + 1) * 128, tc * 512:(tc + 1) * 512],
                                         in_=btv[slot][:]), 16)
        pend_bst.append(fbst)
    for f_ in pend_bst:
        P.on("gpsimd", f_)
    pend_bst = []
    def fsigs(sc, jobs=list(sig_jobs)):
        for rm, lane in jobs:
            wait_marks(sc, [rm])
            S.inc(sc.activation(bsig[lane][:], br[lane][:], AF.Sigmoid,
                                bias=0.0, scale=scl[:, 11:12]))
    P.on("scalar", fsigs)
    assert S.n == SB0 + NTB
    spillB = [sB[i].mark for i in range(2)]
    tmx_mark = emit_cross_part_max(P, "t", mbt[:], [bq_mark[NTB - 2], bred_mark[NTB - 1]],
                                   amax_row, stB)
    st_mark2 = emit_qdq_scalars(P, "t", tmx_mark, amax_row, fl, on1, fl_mark, scl, 12, stB)
    stB.close()

    stC = contextlib.ExitStack()
    ct = [P.sb(f"cct{i}", (128, 512), BF16, stC, side="right") for i in range(4)]
    cu = [P.sb(f"ccu{i}", (128, 512), BF16, stC, side="right") for i in range(4)]
    cr = [P.sb(f"cr{i}", (128, 512), F32, stC, side="right") for i in range(2)]
    cru = [P.sb(f"cru{i}", (128, 512), F32, stC, side="right") for i in range(2)]
    cy = [P.sb(f"cy{i}", (128, 512), F32, stC, side="right") for i in range(2)]
    cr2 = [P.sb(f"cr2{i}", (128, 512), F32, stC, side="right") for i in range(2)]
    cov = [P.sb(f"cov{i}", (128, 512), BF16, stC, side="right") for i in range(4)]
    mbo = P.sb("mbo", (128, HCH * 4), F32, stC, side="right")
    stu = P.sb("stu", (128, 1), F32, stC, side="right")
    lC = [P.sem(f"l1lc{i}") for i in range(2)]
    lCu = P.sem("l1lcu")
    sC = [P.sem(f"l1sc{i}") for i in range(2)]
    def fstu(v, m=st_mark2):
        wait_marks(v, [m])
        v.wait_ge(V.h, V.n)
        V.inc(v.tensor_mul(stu[:], scl[:, 15:16], scl[:, 7:8]))
    P.on("vector", fstu)
    stu_mark = V.mark
    cq_mark = {}
    cst_mark = {}
    cred_mark = {}
    pend_cst = []
    for ti in range(NTB):
        hc = ti // 4
        tc = ti % 4
        slot = ti % 4
        lane = ti % 2
        eng = "vector" if lane == 0 else "gpsimd"
        E = V if lane == 0 else G
        if ti >= 4:
            P.on("gpsimd", pend_cst.pop(0))
            cst_mark[ti - 4] = sC[(ti - 4) % 2].mark
        def fcl(sy, hc=hc, tc=tc, slot=slot, ti=ti, stu_mark=stu_mark, spillB=spillB):
            wait_marks(sy, [stu_mark] + spillB)
            if ti >= 4:
                wait_marks(sy, [cq_mark[ti - 4]])
            lC[ti % 2].inc(sy.dma_start(out=ct[slot][:],
                                        in_=t_pre[hc * 128:(hc + 1) * 128, tc * 512:(tc + 1) * 512]), 16)
        P.on("sync", fcl)
        def fclu(sc, hc=hc, tc=tc, slot=slot, ti=ti, stu_mark=stu_mark, spillB=spillB):
            wait_marks(sc, [stu_mark] + spillB)
            if ti >= 4:
                wait_marks(sc, [cq_mark[ti - 4]])
            lCu.inc(sc.dma_start(out=cu[slot][:],
                                 in_=up_pre[hc * 128:(hc + 1) * 128, tc * 512:(tc + 1) * 512]), 16)
        P.on("scalar", fclu)
        ld_mark = [lC[lane].mark, lCu.mark]
        def fcq(v, slot=slot, lane=lane, ti=ti, ld_mark=ld_mark, E=E):
            wait_marks(v, ld_mark)
            if ti >= 4:
                wait_marks(v, [cst_mark[ti - 4]])
                if ti - 4 in cred_mark:
                    wait_marks(v, [cred_mark[ti - 4]])
            emit_quant_tile(v, E, ct[slot][:], cr[lane][:], scl[:, 12:13], scl[:, 21:22],
                            scl[:, 20:21], cy[lane][:], cr2[lane][:], pool=(lane == 1))
            emit_quant_tile(v, E, cu[slot][:], cru[lane][:], scl[:, 4:5], scl[:, 21:22],
                            scl[:, 20:21], cy[lane][:], cr2[lane][:], pool=(lane == 1))
            if lane == 1:
                _ma(v, E, cr[lane][:], cr[lane][:], stu[:, 0:1], 0.0)
                v.wait_ge(E.h, E.n)
                E.inc(v.tensor_mul(cy[lane][:], cr[lane][:], cru[lane][:]))
            else:
                v.wait_ge(E.h, E.n)
                E.inc(v.scalar_tensor_tensor(cy[lane][:], cr[lane][:], stu[:, 0:1],
                                             cru[lane][:], ALU.mult, ALU.mult))
            if lane == 1:
                _ma(v, E, cov[slot][:], cy[lane][:], 1.0, 0.0)
            else:
                v.wait_ge(E.h, E.n)
                E.inc(v.tensor_copy(cov[slot][:], cy[lane][:]))
            if lane == 0:
                v.wait_ge(E.h, E.n)
                E.inc(v.tensor_reduce(mbo[:, ti:ti + 1], cov[slot][:], mybir.AxisListType.X,
                                      ALU.max, apply_absolute_value=True))
        P.on(eng, fcq)
        cq_mark[ti] = E.mark
        if lane == 1:
            def fcred(v, ti=ti, slot=slot, m=cq_mark[ti]):
                wait_marks(v, [m])
                v.wait_ge(V.h, V.n)
                V.inc(v.tensor_reduce(mbo[:, ti:ti + 1], cov[slot][:], mybir.AxisListType.X,
                                      ALU.max, apply_absolute_value=True))
            P.on("vector", fcred)
            cred_mark[ti] = V.mark
        def fcst(g, hc=hc, tc=tc, slot=slot, m=cq_mark[ti]):
            wait_marks(g, [m])
            sC[slot % 2].inc(g.dma_start(out=o_pre[hc * 128:(hc + 1) * 128, tc * 512:(tc + 1) * 512],
                                         in_=cov[slot][:]), 16)
        pend_cst.append(fcst)
    for f_ in pend_cst:
        P.on("gpsimd", f_)
    pend_cst = []
    spillC = [sC[i].mark for i in range(2)]
    omx_mark = emit_cross_part_max(P, "o", mbo[:], [cq_mark[NTB - 2], cred_mark[NTB - 1]],
                                   amax_row, stC)
    so_mark = emit_qdq_scalars(P, "o", omx_mark, amax_row, fl, on1, fl_mark, scl, 16, stC)
    stC.close()

    stD = contextlib.ExitStack()
    wd_res = P.sb("l1wd_res", (128, HCH * 1024), BF16, stD, side="right")
    oq_sb = P.sb("oq_sb", (128, HCH * 512), BF16, stD, side="right")
    dt = [P.sb(f"ddt{i}", (128, 512), BF16, stD, side="right") for i in range(2)]
    dy = P.sb("dy", (128, 512), F32, stD, side="right")
    dr2 = P.sb("dr2", (128, 512), F32, stD, side="right")
    dxres = [P.sb(f"dxres{i}", (128, 512), F32, stD, side="right") for i in range(2)]
    down_pp = [P.ps(f"l1dpp{i}", (128, 512), stack=stD) for i in range(2)]
    wD = P.sem("l1wd")
    lD = [P.sem(f"l1ld{i}") for i in range(2)]
    xlD = [P.sem(f"l1xl{i}") for i in range(2)]
    xstD = [P.sem(f"l1xs{i}") for i in range(2)]
    wdd_r = wdqT_d.rearrange("(c p) d -> p c d", p=128)
    def fDw(g, dq_d=dq_d):
        wait_marks(g, [dq_d, so_mark] + spillC)
        wD.inc(g.dma_start(out=wd_res[:].rearrange("p (c d) -> p c d", c=HCH), in_=wdd_r), 16)
    P.on("gpsimd", fDw)
    wd_mark = wD.mark
    SWD = 512
    oqm = {}
    res_mark = {}
    for st_i in range(T_CORE // SWD):
        c0 = st_i * SWD
        for hc in range(HCH):
            it = st_i * HCH + hc
            slot = it % 2
            def fdl(sy, hc=hc, c0=c0, slot=slot, it=it, so_mark=so_mark, spillC=spillC):
                wait_marks(sy, [so_mark] + spillC)
                if it >= 2:
                    wait_marks(sy, [oqm[it - 2]])
                lD[slot].inc(sy.dma_start(out=dt[slot][:],
                                          in_=o_pre[hc * 128:(hc + 1) * 128, c0:c0 + SWD]), 16)
            P.on("sync", fdl)
            ld_mark = lD[slot].mark
            def fdq(v, hc=hc, slot=slot, it=it, ld_mark=ld_mark, st_i=st_i):
                wait_marks(v, [ld_mark])
                if st_i >= 1:
                    wait_marks(v, [dn_all[st_i - 1]])
                emit_quant_tile(v, V, dt[slot][:], oq_sb[:, hc * SWD:(hc + 1) * SWD],
                                scl[:, 16:17], scl[:, 17:18], scl[:, 18:19], dy[:], dr2[:])
            P.on("vector", fdq)
            oqm[it] = V.mark
        oq_all = V.mark
        dn_all = dn_all if st_i > 0 else {}
        for dd in range(NCH):
            bank = down_pp[dd % 2]
            it2 = st_i * NCH + dd
            pp = it2 % 2
            def fdown(t, dd=dd, bank=bank, oq_all=oq_all, it2=it2, wd_mark=wd_mark):
                wait_marks(t, [oq_all, wd_mark])
                if it2 >= 2:
                    wait_marks(t, [res_mark[it2 - 2]])
                for hc in range(HCH):
                    i = t.matmul(bank[:, 0:SWD],
                                 wd_res[:, hc * 1024 + dd * 128: hc * 1024 + (dd + 1) * 128],
                                 oq_sb[:, hc * SWD:(hc + 1) * SWD],
                                 start=(hc == 0), stop=(hc == HCH - 1))
                Tn.inc(i)
            P.on("tensor", fdown)
            dd_mark = Tn.mark
            def fxld(sy, dd=dd, c0=c0, pp=pp, it2=it2, oq_all=oq_all):
                if it2 >= 2:
                    wait_marks(sy, [res_mark[it2 - 2]])
                else:
                    wait_marks(sy, [oq_all])
                xlD[pp].inc(sy.dma_start(out=dxres[pp][:],
                                         in_=x_in[dd * 128:(dd + 1) * 128, c0:c0 + SWD]), 16)
            P.on("sync", fxld)
            xld_mark = xlD[pp].mark
            def fres(v, bank=bank, pp=pp, dd_mark=dd_mark, xld_mark=xld_mark):
                wait_marks(v, [dd_mark, xld_mark])
                v.wait_ge(V.h, V.n)
                V.inc(v.scalar_tensor_tensor(dxres[pp][:], bank[:, 0:SWD], 0.5,
                                             dxres[pp][:], ALU.mult, ALU.add))
            P.on("vector", fres)
            wr_mark = V.mark
            def fxst(g, dd=dd, c0=c0, pp=pp, wr_mark=wr_mark):
                wait_marks(g, [wr_mark])
                xstD[pp].inc(g.dma_start(out=x_out[dd * 128:(dd + 1) * 128, c0:c0 + SWD],
                                         in_=dxres[pp][:]), 16)
            P.on("gpsimd", fxst)
            res_mark[it2] = xstD[pp].mark
        dn_all[st_i] = Tn.mark
    fin = [xstD[0].mark, xstD[1].mark]
    G = P.E["gpsimd"]
    def ffinD(g, fin=fin):
        wait_marks(g, fin)
        G.inc(g.nop())
    P.on("gpsimd", ffinD)
    stD.close()
    return G.mark


def build_l1ffn_launch():
    P = Prog()
    x_in = P.dram_in("x", (D, T_CORE), F32)
    wg = P.dram_in("l1_wg", (HID, D), F32)
    wu = P.dram_in("l1_wu", (HID, D), F32)
    wd = P.dram_in("l1_wd", (D, HID), F32)
    x_out = P.dram_out("x_next", (D, T_CORE), F32)
    with P:
        emit_l1ffn(P, x_in, x_out, wg, wu, wd, [])
    return P.nc




L1A_UPTO = "full"


def _early_out(P, x_out, marks):
    eo = P.sem("earlyout")
    def f(g):
        wait_marks(g, marks)
        for c in range(NCH):
            eo.inc(g.dma_start(out=x_out[c * 128:(c + 1) * 128, 0:512],
                               in_=x_out[c * 128:(c + 1) * 128, 0:512]), 16)
        g.wait_ge(eo.h, eo.n)
    P.on("gpsimd", f)


def emit_l1attn(P, x_in, x_out, qT, kT, vtm, wqkv_d, wo_d,
                bqk_sb, bv_sb, bo_sb, waits):
    V, S, Tn = P.E["vector"], P.E["scalar"], P.E["tensor"]
    stX = contextlib.ExitStack()
    xln_sb = P.sb("xln_sb", (128, NCH * T_CORE), BF16, stX, side="right")
    ln_mark = emit_ln_stream(P, "L1A", x_in, xln_sb, waits)
    waits = waits + [ln_mark]
    gq = P.sem("l1a_g")
    xl = [P.sem(f"l1a_xl{i}") for i in range(2)]
    xst = [P.sem(f"l1a_xs{i}") for i in range(2)]
    scq = P.sem("l1a_sc")

    stON = contextlib.ExitStack()
    onesf = P.sb("l1a_on", (1, 128), F32, stON, side="left")
    ones64 = P.sb("l1a_o64", (64, 1), BF16, stON, side="left")
    gb64 = P.sb("l1a_gb", (64, 1), F32, stON, side="left")
    grow = P.sb("l1a_gr", (1, 4), F32, stON, side="left")
    def finit(v):
        wait_marks(v, waits)
        v.wait_ge(V.h, V.n)
        V.inc(v.memset(onesf[:], 1.0))
        V.inc(v.memset(ones64[:], 1.0))
    P.on("vector", finit)
    ones_mark = V.mark

    def mk_evac(dst, boff):
        def evac(v, ps_t, oc, tt):
            V.inc(v.tensor_scalar(dst[:, oc * T_CORE + tt * 512: oc * T_CORE + tt * 512 + 512],
                                  ps_t[:], bqk_sb[:, boff + oc: boff + oc + 1],
                                  None, ALU.add))
        return evac
    pq_mark = emit_proj_fm(P, "lq", xln_sb, wqkv_d, 0, NCH, mk_evac(qT, 0), waits)
    pk_mark = emit_proj_fm(P, "lk", xln_sb, wqkv_d, D, NCH, mk_evac(kT, NCH),
                           waits + [pq_mark])

    stV = contextlib.ExitStack()
    wv_sb = P.sb("l1a_wv", (128, NCH * 1024), BF16, stV, side="right")
    v_ps = [P.ps(f"l1a_vps{i}", (128, 512), stack=stV) for i in range(2)]
    wv_r = wqkv_d.rearrange("(c p) o -> p c o", p=128)
    def fvw(g):
        wait_marks(g, waits + [pq_mark, pk_mark])
        gq.inc(g.dma_start(out=wv_sb[:].rearrange("p (c o) -> p c o", c=NCH),
                           in_=wv_r[:, :, 2 * D: 3 * D]), 16)
    P.on("gpsimd", fvw)
    vw_mark = gq.mark
    vev = {}
    for tc in range(T_CORE // 128):
        for half in range(2):
            it = tc * 2 + half
            pp = it % 2
            def fvmm(t, tc=tc, half=half, pp=pp, it=it):
                wait_marks(t, waits)
                wait_marks(t, [vw_mark, ones_mark])
                if it >= 2:
                    wait_marks(t, [vev[it - 2]])
                for c in range(NCH):
                    t.matmul(v_ps[pp][:], xln_sb[:, c * T_CORE + tc * 128: c * T_CORE + (tc + 1) * 128],
                             wv_sb[:, c * 1024 + half * 512: c * 1024 + half * 512 + 512],
                             start=(c == 0), stop=False)
                Tn.inc(t.matmul(v_ps[pp][:], onesf[:], bv_sb[0:1, half * 512:(half + 1) * 512],
                                start=False, stop=True))
            P.on("tensor", fvmm)
            vmm_mark = Tn.mark
            def fvev(v, tc=tc, half=half, pp=pp, vmm_mark=vmm_mark):
                wait_marks(v, [vmm_mark])
                v.wait_ge(V.h, V.n)
                V.inc(v.tensor_copy(vtm[:, tc * 1024 + half * 512: tc * 1024 + half * 512 + 512],
                                    v_ps[pp][:]))
            P.on("vector", fvev)
            vev[it] = V.mark
    v_mark = V.mark
    stV.close()
    stX.close()
    if L1A_UPTO == "proj":
        _early_out(P, x_out, [v_mark, pq_mark, pk_mark])
        return None

    stA2 = contextlib.ExitStack()
    attn_sb = P.sb("attn_sb", (128, NCH * T_CORE), BF16, stA2, side="right")
    stP1 = contextlib.ExitStack()
    maxb = P.sb("l1a_mx", (64, 32), F32, stP1, side="right")
    mcol = P.sb("l1a_mc", (64, 1), F32, stP1, side="right")
    mrow = P.sb("l1a_mr", (1, 64), F32, stP1, side="right")
    st_ps = [P.ps(f"l1a_sps{i}", (64, 1024), stack=stP1) for i in range(2)]
    red_mark = {}
    for m in range(32):
        ch = m // 8
        pp = m % 2
        def fs1(t, m=m, pp=pp, pq=pq_mark, pk=pk_mark, v_mark=v_mark):
            wait_marks(t, [pq, pk, v_mark])
            if m >= 2:
                wait_marks(t, [red_mark[m - 2]])
            for grp in (0, 1):
                hb = grp * 64
                for h in range(grp, NH, 2):
                    hc = h // 2
                    i = t.matmul(st_ps[pp][:, h * 64:(h + 1) * 64],
                                 kT[hb:hb + 64, hc * T_CORE + m * 64: hc * T_CORE + m * 64 + 64],
                                 qT[hb:hb + 64, hc * T_CORE + m * 64: hc * T_CORE + m * 64 + 64],
                                 start=True, stop=True)
                Tn.inc(i)
                t.wait_ge(Tn.h, Tn.n)
        P.on("tensor", fs1)
        s1_mark = Tn.mark
        def fr1(v, m=m, pp=pp, s1_mark=s1_mark):
            wait_marks(v, [s1_mark])
            v.wait_ge(V.h, V.n)
            V.inc(v.tensor_reduce(maxb[:, m:m + 1], st_ps[pp][:], mybir.AxisListType.X,
                                  ALU.max, apply_absolute_value=True))
        P.on("vector", fr1)
        red_mark[m] = V.mark
        if L1A_UPTO == "p1score" and m == 0:
            break
    if L1A_UPTO == "p1score":
        m0 = red_mark[0]
        stP1.close()
        _early_out(P, x_out, [m0])
        return None
    def fgm(v):
        wait_marks(v, [red_mark[31]])
        v.wait_ge(V.h, V.n)
        V.inc(v.tensor_reduce(mcol[:], maxb[:], mybir.AxisListType.X, ALU.max))
    P.on("vector", fgm)
    gm1 = V.mark
    def fgd(g, gm1=gm1):
        wait_marks(g, [gm1])
        scq.inc(g.dma_start(out=mrow[0:1, :], in_=mcol[0:64, 0:1]), 16)
    P.on("gpsimd", fgd)
    gm2 = scq.mark
    def fgr(v, gm2=gm2):
        wait_marks(v, [gm2])
        v.wait_ge(V.h, V.n)
        V.inc(v.tensor_reduce(grow[0:1, 0:1], mrow[0:1, :], mybir.AxisListType.X, ALU.max))
        v.wait_ge(V.h, V.n)
        V.inc(v.tensor_scalar(grow[0:1, 1:2], grow[0:1, 0:1], -0.125, None, ALU.mult))
    P.on("vector", fgr)
    gm3 = V.mark
    gb_ps = P.ps("l1a_gbps", (64, 1), stack=stP1)
    def fgb(t, gm3=gm3):
        wait_marks(t, [gm3])
        Tn.inc(t.matmul(gb_ps[:], onesf[0:1, 0:64], grow[0:1, 1:2], start=True, stop=True))
    P.on("tensor", fgb)
    gm4 = Tn.mark
    def fgc(v, gm4=gm4):
        wait_marks(v, [gm4])
        v.wait_ge(V.h, V.n)
        V.inc(v.tensor_copy(gb64[:], gb_ps[:]))
    P.on("vector", fgc)
    g_mark = V.mark
    stP1.close()
    if L1A_UPTO == "phase1":
        _early_out(P, x_out, [g_mark])
        return None

    stP2 = contextlib.ExitStack()
    vblk = P.sb("l1a_vb", (64, 1024), BF16, stP2, side="right")
    ptb = P.sb("l1a_pt", (64, 1024), BF16, stP2, side="right")
    otb = P.sb("l1a_ot", (64, 1024), BF16, stP2, side="right")
    linv = P.sb("l1a_li", (1, 1024), F32, stP2, side="right")
    s2_ps = [P.ps("l1a_s2", (64, 1024), stack=stP2)] * 2
    ot_ps = [P.ps("l1a_otp", (64, 1024), stack=stP2)] * 2
    l_ps = [P.ps(f"l1a_lps{i}", (1, 512), stack=stP2) for i in range(2)]
    rep_ps = P.ps("l1a_rep", (64, 1024), stack=stP2)
    blk_mark = {}
    scat_mark = {}
    for m in range(32):
        pp = m % 2
        def fs2(t, m=m, pp=pp, g_mark=g_mark):
            wait_marks(t, [g_mark])
            if m >= 1:
                wait_marks(t, [blk_mark[m - 1]])
            for grp in (0, 1):
                hb = grp * 64
                for h in range(grp, NH, 2):
                    hc = h // 2
                    i = t.matmul(s2_ps[pp][:, h * 64:(h + 1) * 64],
                                 kT[hb:hb + 64, hc * T_CORE + m * 64: hc * T_CORE + m * 64 + 64],
                                 qT[hb:hb + 64, hc * T_CORE + m * 64: hc * T_CORE + m * 64 + 64],
                                 start=True, stop=True)
                Tn.inc(i)
                t.wait_ge(Tn.h, Tn.n)
        P.on("tensor", fs2)
        s2_mark = Tn.mark
        def fexp(sc, pp=pp, s2_mark=s2_mark, m=m):
            wait_marks(sc, [s2_mark])
            if m >= 1:
                wait_marks(sc, [blk_mark.get(m - 1, (S, 0))])
            S.inc(sc.activation(ptb[:], s2_ps[pp][:], AF.Exp, bias=gb64[:, 0:1], scale=0.125))
        P.on("scalar", fexp)
        exp_mark = S.mark
        def fvb(v, m=m, exp_mark=exp_mark, v_mark=v_mark):
            wait_marks(v, [v_mark])
            if m >= 1:
                wait_marks(v, [blk_mark.get(m - 1, (V, 0))])
            v.wait_ge(V.h, V.n)
            V.inc(v.tensor_copy(vblk[:], vtm[(m % 2) * 64:(m % 2) * 64 + 64,
                                             (m // 2) * 1024:(m // 2) * 1024 + 1024]))
        P.on("vector", fvb)
        vb_mark = V.mark
        def fav(t, pp=pp, exp_mark=exp_mark, vb_mark=vb_mark):
            wait_marks(t, [exp_mark, vb_mark])
            for h in range(NH):
                t.matmul(ot_ps[pp][:, h * 64:(h + 1) * 64], vblk[:, h * 64:(h + 1) * 64],
                         ptb[:, h * 64:(h + 1) * 64], start=True, stop=True)
            t.matmul(l_ps[0][:], ones64[:], ptb[:, 0:512], start=True, stop=True)
            Tn.inc(t.matmul(l_ps[1][:], ones64[:], ptb[:, 512:1024], start=True, stop=True))
        P.on("tensor", fav)
        av_mark = Tn.mark
        def fep(v, pp=pp, av_mark=av_mark):
            wait_marks(v, [av_mark])
            v.wait_ge(V.h, V.n)
            V.inc(v.reciprocal(linv[0:1, 0:512], l_ps[0][:]))
            V.inc(v.reciprocal(linv[0:1, 512:1024], l_ps[1][:]))
        P.on("vector", fep)
        li_mark = V.mark
        def frep(t, li_mark=li_mark):
            wait_marks(t, [li_mark])
            t.matmul(rep_ps[:, 0:512], onesf[0:1, 0:64], linv[0:1, 0:512], start=True, stop=True)
            Tn.inc(t.matmul(rep_ps[:, 512:1024], onesf[0:1, 0:64], linv[0:1, 512:1024],
                            start=True, stop=True))
        P.on("tensor", frep)
        rep_mark = Tn.mark
        def fot(v, pp=pp, rep_mark=rep_mark, m=m):
            wait_marks(v, [rep_mark])
            if m >= 1:
                wait_marks(v, [scat_mark[m - 1]])
            v.wait_ge(V.h, V.n)
            V.inc(v.tensor_copy(otb[:], ot_ps[pp][:]))
            v.wait_ge(V.h, V.n)
            V.inc(v.tensor_mul(otb[:], otb[:], rep_ps[:]))
        P.on("vector", fot)
        blk_mark[m] = V.mark
        def fscat(g, m=m, bm=blk_mark[m]):
            wait_marks(g, [bm])
            src = otb[:].rearrange("d (ch par q) -> d ch par q", ch=8, par=2)
            for par in range(2):
                dst = attn_sb[par * 64:par * 64 + 64, :].rearrange(
                    "d (ch t) -> d ch t", ch=8)[:, :, m * 64:(m + 1) * 64]
                scq.inc(g.dma_start(out=dst, in_=src[:, :, par, :],
                                    allow_slow_non_contiguous=True), 16)
        P.on("gpsimd", fscat)
        scat_mark[m] = scq.mark
    attn_done = [scat_mark[31], blk_mark[31], Tn.mark]
    stP2.close()
    if L1A_UPTO == "phase2":
        _early_out(P, x_out, attn_done)
        return None

    stPJ = contextlib.ExitStack()
    wpj_sb = P.sb("l1a_wpj", (128, NCH * 1024), BF16, stPJ, side="right")
    ptmp = P.sb("l1a_ptmp", (128, 512), F32, stPJ, side="right")
    xres = [P.sb(f"l1a_xr{i}", (128, 512), F32, stPJ, side="right") for i in range(2)]
    pj_ps = [P.ps(f"l1a_pj{i}", (128, 512), stack=stPJ) for i in range(2)]
    wpj_r = wo_d.rearrange("(c p) o -> p c o", p=128)
    def fpw(g):
        wait_marks(g, attn_done)
        gq.inc(g.dma_start(out=wpj_sb[:].rearrange("p (c o) -> p c o", c=NCH), in_=wpj_r), 16)
    P.on("gpsimd", fpw)
    pw_mark = gq.mark
    res_done = {}
    for oc in range(NCH):
        for tt in range(4):
            it = oc * 4 + tt
            pp = it % 2
            def fpm(t, oc=oc, tt=tt, pp=pp, it=it):
                wait_marks(t, [pw_mark] + attn_done)
                if it >= 2:
                    wait_marks(t, [res_done[it - 2]])
                for c in range(NCH):
                    i = t.matmul(pj_ps[pp][:], wpj_sb[:, c * 1024 + oc * 128: c * 1024 + (oc + 1) * 128],
                                 attn_sb[:, c * T_CORE + tt * 512: c * T_CORE + tt * 512 + 512],
                                 start=(c == 0), stop=(c == NCH - 1))
                Tn.inc(i)
            P.on("tensor", fpm)
            pm_mark = Tn.mark
            def fxld(sy, oc=oc, tt=tt, pp=pp, it=it):
                if it >= 2:
                    wait_marks(sy, [res_done[it - 2]])
                else:
                    wait_marks(sy, attn_done)
                xl[pp].inc(sy.dma_start(out=xres[pp][:],
                                        in_=x_in[oc * 128:(oc + 1) * 128, tt * 512:(tt + 1) * 512]), 16)
            P.on("sync", fxld)
            xld_mark = xl[pp].mark
            def fpr(v, oc=oc, pp=pp, pm_mark=pm_mark, xld_mark=xld_mark):
                wait_marks(v, [pm_mark, xld_mark])
                v.wait_ge(V.h, V.n)
                V.inc(v.tensor_scalar(ptmp[:], pj_ps[pp][:], bo_sb[:, oc:oc + 1], None, ALU.add))
                v.wait_ge(V.h, V.n)
                V.inc(v.tensor_add(xres[pp][:], ptmp[:], xres[pp][:]))
            P.on("vector", fpr)
            wr_mark = V.mark
            def fxst(g, oc=oc, tt=tt, pp=pp, wr_mark=wr_mark):
                wait_marks(g, [wr_mark])
                xst[pp].inc(g.dma_start(out=x_out[oc * 128:(oc + 1) * 128, tt * 512:(tt + 1) * 512],
                                        in_=xres[pp][:]), 16)
            P.on("gpsimd", fxst)
            res_done[it] = xst[pp].mark
    fin = [xst[0].mark, xst[1].mark]
    def ffin(g, fin=fin):
        wait_marks(g, fin)
    P.on("gpsimd", ffin)
    stPJ.close()
    stA2.close()
    stON.close()
    return fin[0]


def build_l1attn_launch():
    P = Prog()
    x_in = P.dram_in("x", (D, T_CORE), F32)
    wqkv = P.dram_in("l1_wqkv", (D, 3 * D), F32)
    wo = P.dram_in("l1_wo", (D, D), F32)
    bqk_in = P.dram_in("l1_bqk", (128, 16), F32)
    bv_in = P.dram_in("l1_bv", (1, D), F32)
    bo_in = P.dram_in("l1_bo", (128, 8), F32)
    x_out = P.dram_out("x_next", (D, T_CORE), F32)
    bqk_sb = P.sb("bqk_sb", (128, 16), F32)
    bv_sb = P.sb("bv_sb", (1, D), F32)
    bo_sb = P.sb("bo_sb", (128, 8), F32)
    qT = P.sb("qT", (128, NCH * T_CORE), BF16, side="right")
    kT = P.sb("kT", (128, NCH * T_CORE), BF16, side="right")
    vtm = P.sb("vtm", (128, 16 * 1024), BF16, side="right")
    with P:
        ld = P.sem("smalls")
        def fsmall(sy):
            for src_, dst in ((bqk_in, bqk_sb), (bv_in, bv_sb), (bo_in, bo_sb)):
                ld.inc(sy.dma_start(out=dst[:], in_=src_), 16)
        P.on("sync", fsmall)
        emit_l1attn(P, x_in, x_out, qT, kT, vtm, wqkv, wo,
                    bqk_sb, bv_sb, bo_sb, [ld.mark])
    return P.nc


def _l1_host_prep(inputs):
    Wqkv = np.asarray(inputs["L1_attn_w_qkv"])
    bqkv = np.asarray(inputs["L1_attn_b_qkv"])
    p = np.zeros(3072, dtype=np.int64)
    for h in range(NH):
        for i in range(64):
            p[0 * 1024 + h * 64 + i] = h * 192 + i
            p[1 * 1024 + h * 64 + i] = h * 192 + 64 + i
            p[2 * 1024 + h * 64 + i] = h * 192 + 128 + i
    wqkv = np.ascontiguousarray(Wqkv[:, p])
    bp = bqkv[p]
    bqk = np.ascontiguousarray(bp[:2048].reshape(16, 128).T)
    bv = np.ascontiguousarray(bp[2048:].reshape(1, D))
    bo = np.ascontiguousarray(np.asarray(inputs["L1_attn_b_o"]).reshape(8, 128).T)
    wo = np.ascontiguousarray(np.asarray(inputs["L1_attn_w_o"]))
    return dict(wqkv=wqkv, wo=wo, bqk=bqk, bv=bv, bo=bo)



def build_l0ffn_launch():
    P = Prog()
    x_in = P.dram_in("x", (D, T_CORE), F32)
    wg = P.dram_in("l0_wgT", (D, HID), I32)
    wu = P.dram_in("l0_wuT", (D, HID), I32)
    wd = P.dram_in("l0_wdT", (HID, D), I32)
    x_out = P.dram_out("x_next", (D, T_CORE), F32)
    xln_sb = P.sb("xln_sb", (128, NCH * T_CORE), BF16)
    with P:
        ln_mark = emit_ln_stream(P, "0", x_in, xln_sb, [])
        emit_l0ffn(P, x_in, x_out, xln_sb, wg, wu, wd, [ln_mark])
    return P.nc


def build_l0attn_launch():
    P = Prog()
    x_in = P.dram_in("x", (D, T_CORE), F32)
    wqk = P.dram_in("l0_wqk", (D, 2 * D), F32)
    wv = P.dram_in("l0_wv", (D, D), F32)
    wpj = P.dram_in("l0_wproj", (D, D), F32)
    bqk_in = P.dram_in("l0_bqk", (128, 16), F32)
    bv_in = P.dram_in("l0_bv", (1, D), F32)
    bpj_in = P.dram_in("l0_bproj", (128, 8), F32)
    cos_in = P.dram_in("l0_cos", (128, T_CORE), F32)
    sin_in = P.dram_in("l0_sin", (128, T_CORE), F32)
    mask_in = P.dram_in("l0_mask", (128, 8 * 512), BF16)
    x_out = P.dram_out("x_next", (D, T_CORE), F32)
    qT = P.sb("qT", (128, NCH * T_CORE), BF16)
    attn_sb = P.sb("attn_sb", (128, NCH * T_CORE), BF16)
    cos_sb = P.sb("cos_sb", (128, T_CORE), F32)
    sin_sb = P.sb("sin_sb", (128, T_CORE), F32)
    mask_sb = P.sb("mask_sb", (128, 8 * 512), BF16)
    bqk_sb = P.sb("bqk_sb", (128, 16), F32)
    bv_sb = P.sb("bv_sb", (1, D), F32)
    bpj_sb = P.sb("bpj_sb", (128, 8), F32)
    xln_sb = P.sb("xln_sb", (128, NCH * T_CORE), BF16, side="right")
    with P:
        ld = P.sem("smalls")
        def fsmall(sy):
            for src_, dst in ((bqk_in, bqk_sb), (bv_in, bv_sb), (bpj_in, bpj_sb),
                              (cos_in, cos_sb), (sin_in, sin_sb), (mask_in, mask_sb)):
                ld.inc(sy.dma_start(out=dst[:], in_=src_), 16)
        P.on("sync", fsmall)
        ln_mark = emit_ln_stream(P, "0", x_in, xln_sb, [ld.mark])
        emit_l0attn(P, x_in, x_out, xln_sb, qT, attn_sb, wqk, wv, wpj, bqk_sb,
                    bv_sb, bpj_sb, cos_sb, sin_sb, mask_sb, [ld.mark, ln_mark])
    return P.nc


def _l0_host_prep(inputs):
    Wqkv = np.asarray(inputs["L0_attn_Wqkv"])
    bqkv = np.asarray(inputs["L0_attn_bqkv"])
    perm = np.zeros(2 * D, dtype=np.int64)
    for blk in range(2):
        for m in range(4):
            for hl in range(4):
                h = 4 * m + hl
                for i in range(32):
                    perm[blk * 1024 + (2 * m) * 128 + 32 * hl + i] = blk * 1024 + h * 64 + 2 * i
                    perm[blk * 1024 + (2 * m + 1) * 128 + 32 * hl + i] = blk * 1024 + h * 64 + 2 * i + 1
    wqk = np.ascontiguousarray(Wqkv[:, perm])
    wv = np.ascontiguousarray(Wqkv[:, 2048:])
    bqk_p = bqkv[perm]
    bqk = np.ascontiguousarray(bqk_p.reshape(16, 128).T)
    bv = np.ascontiguousarray(bqkv[2048:].reshape(1, D))
    bpj = np.ascontiguousarray(np.asarray(inputs["L0_attn_bproj"]).reshape(8, 128).T)
    wpj = np.ascontiguousarray(np.asarray(inputs["L0_attn_Wproj"]))
    inv_freq = (1.0 / (10000.0 ** (np.arange(0, DK, 2, dtype=np.float32) / DK))).astype(np.float32)
    tabs = []
    for core in range(NCORES):
        pos = _own_tokens(core).astype(np.float32)
        f = (inv_freq[:, None] * pos[None, :]).astype(np.float32)
        cos = np.tile(np.cos(f).astype(np.float32), (4, 1))
        sin = np.tile(np.sin(f).astype(np.float32), (4, 1))
        tabs.append((np.ascontiguousarray(cos), np.ascontiguousarray(sin)))
    kk = np.arange(128)[:, None]
    qq = np.arange(512)[None, :]
    def diag(dj):
        return np.where(qq >= kk + 128 * dj, np.float32(0.0), np.float32(-1e9))
    allow = np.zeros((128, 512), np.float32)
    blockm = np.full((128, 512), np.float32(-1e9))
    masks = []
    for p in range(2):
        slots = []
        for b_ in range(8):
            dj = b_ - 4 * p
            slots.append(allow if dj < 0 else (diag(dj) if dj < 4 else blockm))
        masks.append(np.ascontiguousarray(
            np.concatenate(slots, axis=1).astype(ml_dtypes.bfloat16)))
    return dict(wqk=wqk, wv=wv, wpj=wpj, bqk=bqk, bv=bv, bpj=bpj, tabs=tabs, masks=masks)



def build_fused(n_sub=6):
    P = Prog()
    x_in = P.dram_in("x", (D, T_CORE), F32)
    l0_wg = P.dram_in("l0_wgT", (D, HID), I32)
    l0_wu = P.dram_in("l0_wuT", (D, HID), I32)
    l0_wd = P.dram_in("l0_wdT", (HID, D), I32)
    wqk = P.dram_in("l0_wqk", (D, 2 * D), F32)
    wv = P.dram_in("l0_wv", (D, D), F32)
    wpj = P.dram_in("l0_wproj", (D, D), F32)
    bqk_in = P.dram_in("l0_bqk", (128, 16), F32)
    bv_in = P.dram_in("l0_bv", (1, D), F32)
    bpj_in = P.dram_in("l0_bproj", (128, 8), F32)
    cos_in = P.dram_in("l0_cos", (128, T_CORE), F32)
    sin_in = P.dram_in("l0_sin", (128, T_CORE), F32)
    mask_in = P.dram_in("l0_mask", (128, 8 * 512), BF16)
    l1_wg = P.dram_in("l1_wg", (HID, D), F32)
    l1_wu = P.dram_in("l1_wu", (HID, D), F32)
    l1_wd = P.dram_in("l1_wd", (D, HID), F32)
    wqkv1 = P.dram_in("l1_wqkv", (D, 3 * D), F32)
    wo1 = P.dram_in("l1_wo", (D, D), F32)
    bqk1_in = P.dram_in("l1_bqk", (128, 16), F32)
    bv1_in = P.dram_in("l1_bv", (1, D), F32)
    bo1_in = P.dram_in("l1_bo", (128, 8), F32)
    x_out = P.dram_out("x_next", (D, T_CORE), F32)
    xp = [P.nc.dram_tensor(f"xping{i}", [D, T_CORE], F32, kind="Internal").ap()
          for i in range(5)]
    dst = [xp[0], xp[1], xp[2], xp[3], xp[4], x_out]
    dst[n_sub - 1] = x_out
    with P:
        waits = []
        with P.scoped("s0"):
            stk = contextlib.ExitStack()
            xln = P.sb("xln", (128, NCH * T_CORE), BF16, stk, side="right")
            lnm = emit_ln_stream(P, "a", x_in, xln, waits)
            fin = emit_l0ffn(P, x_in, dst[0], xln, l0_wg, l0_wu, l0_wd, waits + [lnm])
            if n_sub == 1:
                stk.close()
                return P.nc
            stk.close()
        waits = [fin]
        with P.scoped("s1"):
            stk = contextlib.ExitStack()
            qT = P.sb("qT", (128, NCH * T_CORE), BF16, stk, side="right")
            attn_sb = P.sb("attn", (128, NCH * T_CORE), BF16, stk, side="right")
            cos_sb = P.sb("cos", (128, T_CORE), F32, stk, side="right")
            sin_sb = P.sb("sin", (128, T_CORE), F32, stk, side="right")
            mask_sb = P.sb("mask", (128, 8 * 512), BF16, stk, side="right")
            bqk_sb = P.sb("bqk", (128, 16), F32, stk, side="right")
            bv_sb = P.sb("bv", (1, D), F32, stk, side="right")
            bpj_sb = P.sb("bpj", (128, 8), F32, stk, side="right")
            xln = P.sb("xln", (128, NCH * T_CORE), BF16, stk, side="right")
            ld = P.sem("smalls")
            def fsmall(sy, waits=waits):
                wait_marks(sy, waits)
                for src_, dst in ((bqk_in, bqk_sb), (bv_in, bv_sb), (bpj_in, bpj_sb),
                                  (cos_in, cos_sb), (sin_in, sin_sb), (mask_in, mask_sb)):
                    ld.inc(sy.dma_start(out=dst[:], in_=src_), 16)
            P.on("sync", fsmall)
            waits = waits + [ld.mark]
            lnm = emit_ln_stream(P, "b", xp[0], xln, waits)
            fin = emit_l0attn(P, xp[0], dst[1], xln, qT, attn_sb, wqk, wv, wpj, bqk_sb,
                              bv_sb, bpj_sb, cos_sb, sin_sb, mask_sb, waits + [lnm])
            if n_sub == 2:
                stk.close()
                return P.nc
            stk.close()
        waits = [fin]
        with P.scoped("s2"):
            stk = contextlib.ExitStack()
            xln = P.sb("xln", (128, NCH * T_CORE), BF16, stk, side="right")
            lnm = emit_ln_stream(P, "c", xp[1], xln, waits)
            fin = emit_l0ffn(P, xp[1], dst[2], xln, l0_wg, l0_wu, l0_wd, waits + [lnm])
            if n_sub == 3:
                stk.close()
                return P.nc
            stk.close()
        waits = [fin]
        with P.scoped("s3"):
            fin = emit_l1ffn(P, xp[2], dst[3], l1_wg, l1_wu, l1_wd, waits)
            deq = P._deq_cache
            if n_sub == 4:
                return P.nc
        waits = [fin]
        with P.scoped("s4"):
            stk = contextlib.ExitStack()
            bqk_sb = P.sb("bqk", (128, 16), F32, stk, side="right")
            bv_sb = P.sb("bv", (1, D), F32, stk, side="right")
            bo_sb = P.sb("bo", (128, 8), F32, stk, side="right")
            qT = P.sb("qT", (128, NCH * T_CORE), BF16, stk, side="right")
            kT = P.sb("kT", (128, NCH * T_CORE), BF16, stk, side="right")
            vtm = P.sb("vtm", (128, 16 * 1024), BF16, stk, side="right")
            ld = P.sem("smalls")
            def fsmall1(sy, waits=waits):
                wait_marks(sy, waits)
                for src_, dst in ((bqk1_in, bqk_sb), (bv1_in, bv_sb), (bo1_in, bo_sb)):
                    ld.inc(sy.dma_start(out=dst[:], in_=src_), 16)
            P.on("sync", fsmall1)
            fin = emit_l1attn(P, xp[3], dst[4], qT, kT, vtm, wqkv1, wo1,
                              bqk_sb, bv_sb, bo_sb, waits + [ld.mark])
            if n_sub == 5:
                stk.close()
                return P.nc
            stk.close()
        waits = [fin]
        with P.scoped("s5"):
            emit_l1ffn(P, xp[4], x_out, l1_wg, l1_wu, l1_wd, waits, deq_cache=deq)
    return P.nc


def _in_maps(inputs, shards):
    wgT = np.ascontiguousarray(np.asarray(inputs["L0_ffn_w_gate"]).T)
    wuT = np.ascontiguousarray(np.asarray(inputs["L0_ffn_w_up"]).T)
    wdT = np.ascontiguousarray(np.asarray(inputs["L0_ffn_w_down"]).T)
    hp0 = _l0_host_prep(inputs)
    hp1 = _l1_host_prep(inputs)
    wg1 = np.ascontiguousarray(np.asarray(inputs["L1_ffn_w_gate"]))
    wu1 = np.ascontiguousarray(np.asarray(inputs["L1_ffn_w_up"]))
    wd1 = np.ascontiguousarray(np.asarray(inputs["L1_ffn_w_down"]))
    maps = []
    for c in range(NCORES):
        cos, sin = hp0["tabs"][c]
        maps.append({
            "x": shards[c], "l0_wgT": wgT, "l0_wuT": wuT, "l0_wdT": wdT,
            "l0_wqk": hp0["wqk"], "l0_wv": hp0["wv"], "l0_wproj": hp0["wpj"],
            "l0_bqk": hp0["bqk"], "l0_bv": hp0["bv"], "l0_bproj": hp0["bpj"],
            "l0_cos": cos, "l0_sin": sin, "l0_mask": hp0["masks"][c % 2],
            "l1_wg": wg1, "l1_wu": wu1, "l1_wd": wd1,
            "l1_wqkv": hp1["wqkv"], "l1_wo": hp1["wo"],
            "l1_bqk": hp1["bqk"], "l1_bv": hp1["bv"], "l1_bo": hp1["bo"],
        })
    return maps


def kernel(**inputs):
    x0 = np.asarray(inputs["L0_attn_x"])
    shards = []
    for core in range(NCORES):
        b = core // 2
        toks = _own_tokens(core)
        shards.append(np.ascontiguousarray(x0[b][toks].T))
    nc = build_fused()
    res = run_bass_kernel_spmd(nc, _in_maps(inputs, shards), core_ids=list(range(NCORES)))
    out = np.zeros((B, T, D), dtype=np.float32)
    for core in range(NCORES):
        b = core // 2
        toks = _own_tokens(core)
        out[b][toks] = res.results[core]["x_next"].T
    return out


# version 16 (final)
# speedup vs baseline: 2.1327x; 1.0980x over previous
import contextlib
import numpy as np
import ml_dtypes

import concourse.bass as bass
import concourse.mybir as mybir
from concourse.bass_utils import run_bass_kernel_spmd

F32 = mybir.dt.float32
BF16 = mybir.dt.bfloat16
I32 = mybir.dt.int32
ALU = mybir.AluOpType
AF = mybir.ActivationFunctionType
BF = ml_dtypes.bfloat16

NCORES = 8
D = 1024
HID = 3072
NH = 16
DK = 64
T_CORE = 2048
B, T = 4, 4096
NCH = D // 128
HCH = HID // 128
C1 = float(np.float32(1.5 * 2.0 ** 23))
C2 = float(np.float32(1.5 * 2.0 ** 52))


def _own_tokens(core):
    p = core % 2
    idx = []
    for s in (p, p + 2, p + 4, p + 6):
        idx.extend(range(512 * s, 512 * (s + 1)))
    return np.array(idx)



class Sem:
    def __init__(self, ctx, nc, name):
        self.h = ctx.enter_context(nc.semaphore(name))
        self.n = 0

    def inc(self, instr, k=1):
        instr.then_inc(self.h, k)
        self.n += k
        return self.n

    @property
    def mark(self):
        return (self, self.n)


class Prog:

    def __init__(self):
        self.nc = bass.Bass("TRN2", target_bir_lowering=False)
        self.ctx = contextlib.ExitStack()
        self.block = None
        self.sems = {}
        self.E = {}
        self._tag = ""

    @contextlib.contextmanager
    def scoped(self, tag):
        old = self._tag
        self._tag = tag
        try:
            yield
        finally:
            self._tag = old

    def __enter__(self):
        self._blk = self.nc.Block()
        self.block = self._blk.__enter__()
        for e in ("vector", "scalar", "tensor", "gpsimd"):
            self.E[e] = self.sem("E_" + e)
        return self

    def __exit__(self, exc_type, exc, tb):
        if exc is not None:
            try:
                self._blk.__exit__(None, None, None)
            except Exception:
                pass
            try:
                self.ctx.close()
            except Exception:
                pass
            return False
        self._blk.__exit__(None, None, None)
        self.ctx.close()
        return False

    def sem(self, name):
        if name not in self.sems:
            self.sems[name] = Sem(self.ctx, self.nc, name)
        return self.sems[name]

    def on(self, eng, fn):
        getattr(self.block, eng)(fn)

    def sb(self, name, shape, dt, stack=None, side="left"):
        return (stack or self.ctx).enter_context(
            self.nc.sbuf_tensor(self._tag + name, list(shape), dt, side=side))

    def ps(self, name, shape, dt=F32, stack=None):
        return (stack or self.ctx).enter_context(
            self.nc.psum_tensor(self._tag + name, list(shape), dt))

    def dram_in(self, name, shape, dt):
        return self.nc.dram_tensor(name, list(shape), dt, kind="ExternalInput").ap()

    def dram_out(self, name, shape, dt):
        return self.nc.dram_tensor(name, list(shape), dt, kind="ExternalOutput").ap()

    def dram_tmp(self, name, shape, dt):
        return self.nc.dram_tensor(self._tag + name, list(shape), dt, kind="Internal").ap()


def wait_marks(eng, marks):
    for s, v in marks:
        if v > 0:
            eng.wait_ge(s.h, v)



def emit_ln_stream(P, tag, x_in, xln_sb, waits):
    V, S, Tn = P.E["vector"], P.E["scalar"], P.E["tensor"]
    ldx = [P.sem("lnld" + tag + str(i)) for i in range(2)]
    stk = contextlib.ExitStack()
    ones128 = P.sb("ln_ones" + tag, (128, 1), F32, stk, side="right")
    eps_t = P.sb("ln_eps" + tag, (1, 1), F32, stk, side="right")
    ones1 = P.sb("ln_ones1" + tag, (1, 128), F32, stk, side="right")
    xtile = [P.sb(f"ln_xt{i}" + tag, (128, NCH * 512), F32, stk, side="right") for i in range(2)]
    sq = [P.sb(f"ln_sq{i}" + tag, (128, 512), F32, stk, side="right") for i in range(2)]
    rows = {n: P.sb("ln_" + n + tag, (1, 512), F32, stk, side="right")
            for n in ("mu", "e2", "t1", "var", "sd", "rstd", "nmr")}
    tmp = P.sb("ln_tmp" + tag, (128, 512), F32, stk, side="right")
    mu_ps = P.ps("ln_mups" + tag, (1, 512), stack=stk)
    e2_ps = P.ps("ln_e2ps" + tag, (1, 512), stack=stk)
    repA = P.ps("ln_repA" + tag, (128, 512), stack=stk)
    repB = P.ps("ln_repB" + tag, (128, 512), stack=stk)

    def init(v):
        wait_marks(v, waits)
        v.memset(ones128[:], 1.0)
        v.memset(eps_t[:], 1e-5)
        V.inc(v.memset(ones1[:], 1.0))
    P.on("vector", init)
    init_mark = V.mark

    NT = T_CORE // 512
    tile_done = {}
    mmpair = {}
    sq_mark = {}
    for ti in range(NT):
        c0 = ti * 512
        xb = xtile[ti % 2]
        lds = ldx[ti % 2]
        def fldx(sy, ti=ti, c0=c0, xb=xb, lds=lds):
            wait_marks(sy, waits)
            if ti >= 2:
                wait_marks(sy, [tile_done[ti - 2]])
            for c in range(NCH):
                lds.inc(sy.dma_start(out=xb[:, c * 512:(c + 1) * 512],
                                     in_=x_in[c * 128:(c + 1) * 128, c0:c0 + 512]), 16)
        P.on("sync", fldx)
        ld_mark = lds.mark
        for c in range(NCH):
            buf = sq[c % 2]
            gidx = ti * NCH + c
            def fsq(sc, c=c, buf=buf, xb=xb, ld_mark=ld_mark, gidx=gidx):
                wait_marks(sc, [ld_mark, init_mark])
                if gidx >= 2:
                    wait_marks(sc, [mmpair[gidx - 2]])
                S.inc(sc.activation(buf[:], xb[:, c * 512:(c + 1) * 512], AF.Square))
                sq_mark[gidx] = S.mark
            P.on("scalar", fsq)
            def fmm(t, c=c, buf=buf, xb=xb, ld_mark=ld_mark, gidx=gidx):
                wait_marks(t, [ld_mark, sq_mark[gidx]])
                t.matmul(mu_ps[:], ones128[:], xb[:, c * 512:(c + 1) * 512],
                         start=(c == 0), stop=(c == NCH - 1))
                Tn.inc(t.matmul(e2_ps[:], ones128[:], buf[:],
                                start=(c == 0), stop=(c == NCH - 1)))
                mmpair[gidx] = Tn.mark
            P.on("tensor", fmm)
        stats_mark = Tn.mark
        def fv1(v, stats_mark=stats_mark):
            wait_marks(v, [stats_mark])
            v.wait_ge(V.h, V.n)
            V.inc(v.tensor_scalar(rows["mu"][:], mu_ps[:], 1.0 / 1024.0, None, ALU.mult))
            V.inc(v.tensor_scalar(rows["e2"][:], e2_ps[:], 1.0 / 1024.0, None, ALU.mult))
            v.wait_ge(V.h, V.n)
            V.inc(v.tensor_mul(rows["t1"][:], rows["mu"][:], rows["mu"][:]))
            v.wait_ge(V.h, V.n)
            V.inc(v.tensor_sub(rows["var"][:], rows["e2"][:], rows["t1"][:]))
        P.on("vector", fv1)
        var_mark = V.mark
        def fs1(sc, var_mark=var_mark):
            wait_marks(sc, [var_mark])
            S.inc(sc.activation(rows["sd"][:], rows["var"][:], AF.Sqrt,
                                bias=eps_t[0:1, 0:1], scale=1.0))
        P.on("scalar", fs1)
        sd_mark = S.mark
        def fv2(v, sd_mark=sd_mark):
            wait_marks(v, [sd_mark])
            v.wait_ge(V.h, V.n)
            V.inc(v.reciprocal(rows["rstd"][:], rows["sd"][:]))
            v.wait_ge(V.h, V.n)
            V.inc(v.scalar_tensor_tensor(rows["nmr"][:], rows["mu"][:], -1.0, rows["rstd"][:],
                                         ALU.mult, ALU.mult))
        P.on("vector", fv2)
        smalls_mark = V.mark
        def frep(t, smalls_mark=smalls_mark):
            wait_marks(t, [smalls_mark])
            t.matmul(repA[:], ones1[:], rows["rstd"][:], start=True, stop=True)
            Tn.inc(t.matmul(repB[:], ones1[:], rows["nmr"][:], start=True, stop=True))
        P.on("tensor", frep)
        reps_mark = Tn.mark
        def fap(v, c0=c0, reps_mark=reps_mark, xb=xb):
            wait_marks(v, [reps_mark])
            for c in range(NCH):
                src_ = xb[:, c * 512:(c + 1) * 512]
                dst = xln_sb[:, c * T_CORE + c0: c * T_CORE + c0 + 512]
                v.wait_ge(V.h, V.n)
                V.inc(v.tensor_mul(tmp[:], src_, repA[:]))
                v.wait_ge(V.h, V.n)
                V.inc(v.tensor_add(dst, tmp[:], repB[:]))
        P.on("vector", fap)
        tile_done[ti] = V.mark
        def fg(t, td=V.mark):
            wait_marks(t, [td])
        P.on("tensor", fg)
    stk.close()
    return V.mark


def emit_proj_fm(P, tag, src_sb, w_dram, w_col0, n_out_ch, evac, waits):
    V, Tn = P.E["vector"], P.E["tensor"]
    ws = [P.sem(f"pjw{tag}{i}") for i in range(2)]
    ws_base = [ws[0].n, ws[1].n]
    stk = contextlib.ExitStack()
    wbuf = [P.sb(f"pjw{tag}{i}", (128, NCH * 128), BF16, stk, side="right") for i in range(2)]
    ps = [P.ps(f"pjps{tag}{i}", (128, 512), stack=stk) for i in range(2)]
    w_r = w_dram.rearrange("(c p) o -> p c o", p=128)

    def load_w(g, oc):
        s = ws[oc % 2]
        s.inc(g.dma_start(out=wbuf[oc % 2][:].rearrange("p (c k) -> p c k", c=NCH),
                          in_=w_r[:, :, w_col0 + oc * 128: w_col0 + (oc + 1) * 128]), 16)

    def f0(g):
        wait_marks(g, waits)
        load_w(g, 0)
        if n_out_ch > 1:
            load_w(g, 1)
    P.on("gpsimd", f0)

    NT = T_CORE // 512
    evac_mark = {}
    mm_last = {}
    for oc in range(n_out_ch):
        for tt in range(NT):
            it = oc * NT + tt
            pp = it % 2
            def fmm(t, oc=oc, tt=tt, pp=pp, it=it):
                wait_marks(t, waits)
                t.wait_ge(ws[oc % 2].h, ws_base[oc % 2] + 16 * (oc // 2 + 1))
                if it >= 2:
                    wait_marks(t, [evac_mark[it - 2]])
                for c in range(NCH):
                    i = t.matmul(ps[pp][:], wbuf[oc % 2][:, c * 128:(c + 1) * 128],
                                 src_sb[:, c * T_CORE + tt * 512: c * T_CORE + tt * 512 + 512],
                                 start=(c == 0), stop=(c == NCH - 1))
                Tn.inc(i)
            P.on("tensor", fmm)
            mm_mark = Tn.mark
            mm_last[oc] = mm_mark
            def fev(v, oc=oc, tt=tt, pp=pp, mm_mark=mm_mark):
                wait_marks(v, [mm_mark])
                evac(v, ps[pp], oc, tt)
            P.on("vector", fev)
            evac_mark[it] = V.mark
        nxt = oc + 2
        if nxt < n_out_ch:
            def fpre(g, nxt=nxt, mm_mark=mm_last[oc]):
                wait_marks(g, [mm_mark])
                load_w(g, nxt)
            P.on("gpsimd", fpre)
    stk.close()
    return V.mark


def ktv(j):
    gk = j // 4
    return (gk % 2) * 16 + (gk // 2) * 4 + (j % 4)


def emit_l0attn(P, x_in, x_out, xln_sb, qT, attn_sb, wqk_d, wv_d, wproj_d, bqk_sb,
                bv_sb, bproj_sb, cos_sb, sin_sb, mask_sb, waits):
    V, S, Tn = P.E["vector"], P.E["scalar"], P.E["tensor"]
    gq = P.sem("at_gdma")
    cc = P.sem("at_cc")
    xl = [P.sem(f"at_xl{i}") for i in range(2)]
    xst = [P.sem(f"at_xst{i}") for i in range(2)]
    vstq = P.sem("at_vst")

    k_own = P.dram_tmp("k_own", (D, T_CORE), BF16)
    v_own = P.dram_tmp("v_own", (NH * T_CORE, 64), BF16)
    k_all = P.dram_tmp("k_all", (2 * D, T_CORE), BF16)
    v_all = P.dram_tmp("v_all", (2 * NH * T_CORE, 64), BF16)

    stON = contextlib.ExitStack()
    onesf = P.sb("at_ones1", (1, 128), F32, stON, side="right")
    def fvinit(v):
        wait_marks(v, waits)
        V.inc(v.memset(onesf[:], 1.0))
    P.on("vector", fvinit)
    ones_mark = V.mark

    stQK = contextlib.ExitStack()
    raw = P.sb("qkraw", (128, NCH * T_CORE), BF16, stQK, side="right")
    scat_mark = None
    prev_pj = None
    for which in ("q", "k"):
        stP = contextlib.ExitStack()
        boff = 0 if which == "q" else NCH
        def mk_evac(boff):
            def evac(v, ps_t, oc, tt):
                V.inc(v.tensor_scalar(raw[:, oc * T_CORE + tt * 512: oc * T_CORE + tt * 512 + 512],
                                      ps_t[:], bqk_sb[:, boff + oc: boff + oc + 1],
                                      None, ALU.add))
            return evac
        pj_mark = emit_proj_fm(P, which, xln_sb, wqk_d, 0 if which == "q" else D,
                               NCH, mk_evac(boff),
                               waits + ([prev_pj] if which == "k" else []))
        ropeA = P.sb("ropeA" + which, (128, 512), F32, stP, side="right")
        ropeB = P.sb("ropeB" + which, (128, 512), F32, stP, side="right")
        re_t = P.sb("re_t" + which, (128, 512), BF16, stP, side="right")
        ro_t = P.sb("ro_t" + which, (128, 512), BF16, stP, side="right")
        prev_pj = pj_mark
        if which == "k":
            kfin = P.sb("kfin", (128, 2 * T_CORE), BF16, stP, side="right")
        for m in range(4):
            for ti in range(T_CORE // 512):
                t0 = ti * 512
                def frope(v, m=m, t0=t0, pj_mark=pj_mark, scat_mark=scat_mark):
                    wait_marks(v, [pj_mark])
                    if scat_mark is not None:
                        wait_marks(v, [scat_mark])
                    e = raw[:, (2 * m) * T_CORE + t0: (2 * m) * T_CORE + t0 + 512]
                    o = raw[:, (2 * m + 1) * T_CORE + t0: (2 * m + 1) * T_CORE + t0 + 512]
                    cw = cos_sb[:, t0:t0 + 512]
                    sw = sin_sb[:, t0:t0 + 512]
                    v.wait_ge(V.h, V.n)
                    V.inc(v.tensor_mul(ropeA[:], e, cw))
                    V.inc(v.tensor_mul(ropeB[:], o, sw))
                    v.wait_ge(V.h, V.n)
                    V.inc(v.tensor_sub(re_t[:], ropeA[:], ropeB[:]))
                    v.wait_ge(V.h, V.n)
                    V.inc(v.tensor_mul(ropeA[:], e, sw))
                    V.inc(v.tensor_mul(ropeB[:], o, cw))
                    v.wait_ge(V.h, V.n)
                    V.inc(v.tensor_add(ro_t[:], ropeA[:], ropeB[:]))
                P.on("vector", frope)
                rope_mark = V.mark
                def fscat(g, which=which, m=m, t0=t0, rope_mark=rope_mark):
                    wait_marks(g, [rope_mark])
                    for hl in range(4):
                        dc = 2 * m + hl // 2
                        base = (hl % 2) * 64
                        if which == "q":
                            dst = qT[:, dc * T_CORE + t0: dc * T_CORE + t0 + 512]
                        else:
                            dst = kfin[:, (dc % 2) * T_CORE + t0: (dc % 2) * T_CORE + t0 + 512]
                        gq.inc(g.dma_start(out=dst[base:base + 32, :],
                                           in_=re_t[hl * 32:(hl + 1) * 32, :]), 16)
                        gq.inc(g.dma_start(out=dst[base + 32:base + 64, :],
                                           in_=ro_t[hl * 32:(hl + 1) * 32, :]), 16)
                P.on("gpsimd", fscat)
                scat_mark = gq.mark
            if which == "k":
                def fkst(g, m=m, scat_mark=scat_mark):
                    wait_marks(g, [scat_mark])
                    for dcl in range(2):
                        dc = 2 * m + dcl
                        gq.inc(g.dma_start(out=k_own[dc * 128:(dc + 1) * 128, :],
                                           in_=kfin[:, dcl * T_CORE:(dcl + 1) * T_CORE]), 16)
                P.on("gpsimd", fkst)
                scat_mark = gq.mark
        stP.close()
    qk_all = gq.mark
    stQK.close()

    stV = contextlib.ExitStack()
    wv_sb = P.sb("wv_sb", (128, NCH * 1024), BF16, stV, side="right")
    v_sbt = P.sb("v_sbt", (128, 512), BF16, stV, side="right")
    v_ps = [P.ps(f"v_ps{i}", (128, 512), stack=stV) for i in range(2)]
    wv_r = wv_d.rearrange("(c p) o -> p c o", p=128)

    def fvw(g):
        wait_marks(g, waits)
        wait_marks(g, [qk_all])
        gq.inc(g.dma_start(out=wv_sb[:].rearrange("p (c o) -> p c o", c=NCH), in_=wv_r), 16)
    P.on("gpsimd", fvw)
    vw_mark = gq.mark

    vev = {}
    vst = {}
    for tc in range(T_CORE // 128):
        for half in range(2):
            it = tc * 2 + half
            pp = it % 2
            def fvmm(t, tc=tc, half=half, pp=pp, it=it):
                wait_marks(t, waits)
                wait_marks(t, [vw_mark, ones_mark, qk_all])
                if it >= 2:
                    wait_marks(t, [vev[it - 2]])
                for c in range(NCH):
                    t.matmul(v_ps[pp][:], xln_sb[:, c * T_CORE + tc * 128: c * T_CORE + (tc + 1) * 128],
                             wv_sb[:, c * 1024 + half * 512: c * 1024 + half * 512 + 512],
                             start=(c == 0), stop=False)
                Tn.inc(t.matmul(v_ps[pp][:], onesf[:], bv_sb[0:1, half * 512:(half + 1) * 512],
                                start=False, stop=True))
            P.on("tensor", fvmm)
            vmm_mark = Tn.mark
            def fvev(v, pp=pp, vmm_mark=vmm_mark, it=it):
                wait_marks(v, [vmm_mark])
                if it >= 1:
                    wait_marks(v, [vst[it - 1]])
                V.inc(v.tensor_copy(v_sbt[:], v_ps[pp][:]))
            P.on("vector", fvev)
            vev[it] = V.mark
            def fvst(g, tc=tc, half=half, ev=vev[it]):
                wait_marks(g, [ev])
                vr = v_own.rearrange("(h t) d -> h t d", h=NH)
                for hh in range(8):
                    h = half * 8 + hh
                    vstq.inc(g.dma_start(out=vr[h, tc * 128:(tc + 1) * 128, :],
                                         in_=v_sbt[:, hh * 64:(hh + 1) * 64]), 16)
            P.on("gpsimd", fvst)
            vst[it] = vstq.mark
    stV.close()
    all_stored = [gq.mark, vstq.mark]

    def fcc(g):
        wait_marks(g, all_stored)
        rg = [[0, 1], [2, 3], [4, 5], [6, 7]]
        cc.inc(g.collective_compute("AllGather", ALU.bypass, replica_groups=rg,
                                    ins=[k_own.opt()], outs=[k_all.opt()]))
        cc.inc(g.collective_compute("AllGather", ALU.bypass, replica_groups=rg,
                                    ins=[v_own.opt()], outs=[v_all.opt()]))
    P.on("gpsimd", fcc)
    cc_mark = cc.mark

    stAT = contextlib.ExitStack()
    khead = [P.sb(f"khead{i}", (128, 2 * T_CORE), BF16, stAT, side="right") for i in range(2)]
    vhead = [P.sb(f"vhead{i}", (128, 32 * 65), BF16, stAT, side="right") for i in range(2)]
    qhead = [P.sb(f"qhead{i}", (64, T_CORE), BF16, stAT, side="right") for i in range(2)]
    mqv = [P.sb(f"mqv{i}", (64, 2), F32, stAT, side="right") for i in range(2)]
    gbias = [P.sb(f"gbias{i}", (128, 1), F32, stAT, side="right") for i in range(2)]
    g1 = P.sb("g1", (1, 1), F32, stAT, side="right")
    mtile = [P.sb(f"mtile{i}", (128, 512), F32, stAT, side="right") for i in range(2)]
    linv = P.sb("linv", (1, 512), F32, stAT, side="right")
    pt = [P.sb(f"pt{i}", (128, 512), BF16, stAT, side="right") for i in range(4)]
    otmp = P.sb("otmp", (64, 512), BF16, stAT, side="right")
    st_ps = [P.ps(f"st_ps{i}", (128, 512), stack=stAT) for i in range(4)]
    ot_ps = [P.ps(f"ot_ps{i}", (65, 512), stack=stAT) for i in range(2)]
    gg_ps = P.ps("gg_ps", (128, 2), stack=stAT)
    rep_ps = P.ps("rep_ps", (64, 512), stack=stAT)

    def ktile_loc(j):
        gk = j // 4
        return gk % 2, (gk // 2) * 512 + (j % 4) * 128

    ep_done = {}
    kvq = P.sem("at_kvq")
    slot_prev = {0: [], 1: []}
    st_guard = {}
    prev_mask_exp = None
    S0 = S.n
    exp_jobs = []
    def exp_pred(e):
        return (S, S0 + e + 1)
    gtile = 0
    mseq = 0
    pm_exp = {0: None, 1: None}
    for h in range(NH):
        hb = (h % 2) * 64
        hc = h // 2
        hs = h % 2
        kh, vh, qh, mq, gbi = khead[hs], vhead[hs], qhead[hs], mqv[hs], gbias[hs]
        prevm = list(slot_prev[hs])
        def fkv(sy, h=h, kh=kh, vh=vh, pm=prevm):
            wait_marks(sy, [cc_mark] + pm)
            va = v_all.rearrange("(r h t) d -> r h t d", r=2, h=NH)
            for r in range(2):
                kvq.inc(sy.dma_start(out=kh[0:64, r * T_CORE:(r + 1) * T_CORE],
                                     in_=k_all[r * D + 64 * h: r * D + 64 * h + 64, :]), 16)
                kvq.inc(sy.dma_start(
                    out=vh[:].rearrange("p (kt c) -> p kt c", c=65)[:, r * 16:(r + 1) * 16, 0:64],
                    in_=va[r, h].rearrange("(t p) c -> p t c", p=128)), 16)
        P.on("sync", fkv)
        kv_mark = kvq.mark
        def fvh1(v, kv_mark=kv_mark, h=h, hb=hb, hc=hc, kh=kh, vh=vh, qh=qh, mq=mq,
                 pm=prevm):
            wait_marks(v, [kv_mark] + pm)
            if h < 2:
                v.memset(vh[:].rearrange("p (kt c) -> p kt c", c=65)[:, :, 64:65], 1.0)
            V.inc(v.tensor_copy(qh[:], qT[hb:hb + 64, hc * T_CORE:(hc + 1) * T_CORE]))
            v.wait_ge(V.h, V.n)
            v.tensor_reduce(mq[:, 0:1], qh[:], mybir.AxisListType.X, ALU.max,
                            apply_absolute_value=True)
            V.inc(v.tensor_reduce(mq[:, 1:2], kh[0:64, :], mybir.AxisListType.X, ALU.max,
                                  apply_absolute_value=True))
        P.on("vector", fvh1)
        mq_mark = V.mark
        def fgb(t, mq_mark=mq_mark, mq=mq):
            wait_marks(t, [mq_mark])
            Tn.inc(t.matmul(gg_ps[0:1, 0:1], mq[:, 0:1], mq[:, 1:2], start=True, stop=True))
        P.on("tensor", fgb)
        gmm_mark = Tn.mark
        def fgb2(v, gmm_mark=gmm_mark):
            wait_marks(v, [gmm_mark])
            V.inc(v.tensor_scalar(g1[:], gg_ps[0:1, 0:1], -0.125, None, ALU.mult))
        P.on("vector", fgb2)
        g1_mark = V.mark
        def fgb3(t, g1_mark=g1_mark):
            wait_marks(t, [g1_mark])
            Tn.inc(t.matmul(gg_ps[:, 1:2], onesf[:], g1[:], start=True, stop=True))
        P.on("tensor", fgb3)
        gb_mark = Tn.mark
        def fgb4(v, gb_mark=gb_mark, gbi=gbi, pm=prevm):
            wait_marks(v, [gb_mark] + pm)
            V.inc(v.tensor_copy(gbi[:], gg_ps[:, 1:2]))
        P.on("vector", fgb4)
        gbias_mark = V.mark

        for si in range(4):
            KT = 8 * si + 8
            osel = si % 2
            prev_ep = ep_done.get((h, si - 2)) if si >= 2 else ep_done.get((h - 1, si + 2))
            pend_fav = None
            for j in range(KT):
                rk, loc = ktile_loc(j)
                dj = j - 8 * si
                pp = gtile % 4
                gtile += 1
                def fsc(t, si=si, j=j, rk=rk, loc=loc, pp=pp, gbias_mark=gbias_mark,
                        prev_ep=prev_ep, kh=kh, qh=qh, guard=st_guard.get(pp)):
                    wait_marks(t, [gbias_mark])
                    if guard is not None:
                        wait_marks(t, guard)
                    if j == 0 and prev_ep is not None:
                        wait_marks(t, [prev_ep])
                    Tn.inc(t.matmul(st_ps[pp][:],
                                    kh[0:64, rk * T_CORE + loc: rk * T_CORE + loc + 128],
                                    qh[:, si * 512:(si + 1) * 512], start=True, stop=True))
                P.on("tensor", fsc)
                sc_mark = Tn.mark
                if dj >= 0:
                    ms = mseq % 2
                    mseq += 1
                    def fmask(v, pp=pp, dj=dj, ms=ms, sc_mark=sc_mark, pme=pm_exp[ms]):
                        wait_marks(v, [sc_mark])
                        if pme is not None:
                            wait_marks(v, [pme])
                        V.inc(v.scalar_tensor_tensor(mtile[ms][:], st_ps[pp][:], 0.125,
                                                     mask_sb[:, dj * 512:(dj + 1) * 512],
                                                     ALU.mult, ALU.add))
                    P.on("vector", fmask)
                    exp_jobs.append((V.mark, True, pp, ms, gbi))
                    pm_exp[ms] = exp_pred(len(exp_jobs) - 1)
                else:
                    exp_jobs.append((sc_mark, False, pp, 0, gbi))
                exp_mark = exp_pred(len(exp_jobs) - 1)
                def fav(t, j=j, pp=pp, osel=osel, exp_mark=exp_mark, KT=KT, vh=vh):
                    wait_marks(t, [exp_mark])
                    Tn.inc(t.matmul(ot_ps[osel][:], vh[:, ktv(j) * 65: ktv(j) * 65 + 65],
                                    pt[pp][:], start=(j == 0), stop=(j == KT - 1)))
                if pend_fav is not None:
                    pf, ppp, pexp = pend_fav
                    P.on("tensor", pf)
                    st_guard[ppp] = [pexp, Tn.mark]
                pend_fav = (fav, pp, exp_mark)
            pf, ppp, pexp = pend_fav
            P.on("tensor", pf)
            st_guard[ppp] = [pexp, Tn.mark]
            av_mark = Tn.mark
            def fep1(v, osel=osel, av_mark=av_mark):
                wait_marks(v, [av_mark])
                V.inc(v.reciprocal(linv[:], ot_ps[osel][64:65, :]))
            P.on("vector", fep1)
            li_mark = V.mark
            def fep2(t, li_mark=li_mark):
                wait_marks(t, [li_mark])
                Tn.inc(t.matmul(rep_ps[:], onesf[0:1, 0:64], linv[:], start=True, stop=True))
            P.on("tensor", fep2)
            rep_mark = Tn.mark
            def fep3(v, osel=osel, rep_mark=rep_mark, hb=hb, hc=hc, si=si):
                wait_marks(v, [rep_mark])
                v.wait_ge(V.h, V.n)
                V.inc(v.tensor_copy(otmp[:], ot_ps[osel][0:64, :]))
                v.wait_ge(V.h, V.n)
                V.inc(v.tensor_mul(
                    attn_sb[hb:hb + 64, hc * T_CORE + si * 512: hc * T_CORE + si * 512 + 512],
                    otmp[:], rep_ps[:]))
            P.on("vector", fep3)
            ep_done[(h, si)] = V.mark
        slot_prev[hs] = [V.mark, Tn.mark, exp_pred(len(exp_jobs) - 1)]

    def fexps(sc, jobs=list(exp_jobs)):
        for wm, masked, pp, ms, gbi in jobs:
            wait_marks(sc, [wm])
            if masked:
                S.inc(sc.activation(pt[pp][:], mtile[ms][:], AF.Exp,
                                    bias=gbi[:, 0:1], scale=1.0))
            else:
                S.inc(sc.activation(pt[pp][:], st_ps[pp][:], AF.Exp,
                                    bias=gbi[:, 0:1], scale=0.125))
    P.on("scalar", fexps)
    assert S.n == S0 + len(exp_jobs)

    attn_done = [V.mark, Tn.mark, S.mark]
    stAT.close()

    stPJ = contextlib.ExitStack()
    wpj_sb = P.sb("wpj_sb", (128, NCH * 1024), BF16, stPJ, side="right")
    ptmp = P.sb("ptmp", (128, 512), F32, stPJ, side="right")
    xres = [P.sb(f"xres{i}", (128, 512), F32, stPJ, side="right") for i in range(2)]
    pj_ps = [P.ps(f"pj_ps{i}", (128, 512), stack=stPJ) for i in range(2)]
    wpj_r = wproj_d.rearrange("(c p) o -> p c o", p=128)
    def fpw(g):
        wait_marks(g, attn_done)
        gq.inc(g.dma_start(out=wpj_sb[:].rearrange("p (c o) -> p c o", c=NCH), in_=wpj_r), 16)
    P.on("gpsimd", fpw)
    pw_mark = gq.mark
    res_done = {}
    for oc in range(NCH):
        for tt in range(4):
            it = oc * 4 + tt
            pp = it % 2
            def fpm(t, oc=oc, tt=tt, pp=pp, it=it):
                wait_marks(t, [pw_mark] + attn_done)
                if it >= 2:
                    wait_marks(t, [res_done[it - 2]])
                for c in range(NCH):
                    i = t.matmul(pj_ps[pp][:], wpj_sb[:, c * 1024 + oc * 128: c * 1024 + (oc + 1) * 128],
                                 attn_sb[:, c * T_CORE + tt * 512: c * T_CORE + tt * 512 + 512],
                                 start=(c == 0), stop=(c == NCH - 1))
                Tn.inc(i)
            P.on("tensor", fpm)
            pm_mark = Tn.mark
            def fxld(sy, oc=oc, tt=tt, pp=pp, it=it):
                if it >= 2:
                    wait_marks(sy, [res_done[it - 2]])
                else:
                    wait_marks(sy, attn_done)
                xl[pp].inc(sy.dma_start(out=xres[pp][:],
                                        in_=x_in[oc * 128:(oc + 1) * 128, tt * 512:(tt + 1) * 512]), 16)
            P.on("sync", fxld)
            xld_mark = xl[pp].mark
            def fpr(v, oc=oc, pp=pp, pm_mark=pm_mark, xld_mark=xld_mark):
                wait_marks(v, [pm_mark, xld_mark])
                v.wait_ge(V.h, V.n)
                V.inc(v.tensor_scalar(ptmp[:], pj_ps[pp][:], bproj_sb[:, oc:oc + 1], None, ALU.add))
                v.wait_ge(V.h, V.n)
                V.inc(v.tensor_add(xres[pp][:], ptmp[:], xres[pp][:]))
            P.on("vector", fpr)
            wr_mark = V.mark
            def fxst(g, oc=oc, tt=tt, pp=pp, wr_mark=wr_mark):
                wait_marks(g, [wr_mark])
                xst[pp].inc(g.dma_start(out=x_out[oc * 128:(oc + 1) * 128, tt * 512:(tt + 1) * 512],
                                        in_=xres[pp][:]), 16)
            P.on("gpsimd", fxst)
            res_done[it] = xst[pp].mark
    fin = [xst[0].mark, xst[1].mark]
    G = P.E["gpsimd"]
    def ffin(g, fin=fin):
        wait_marks(g, fin)
        G.inc(g.nop())
    P.on("gpsimd", ffin)
    stPJ.close()
    stON.close()
    return G.mark


def emit_l0ffn(P, x_in, x_out, xln_sb, wg_d, wu_d, wd_d, waits):
    V, S, Tn = P.E["vector"], P.E["scalar"], P.E["tensor"]
    wgu = [P.sem(f"ffnw{i}") for i in range(2)]
    wds = P.sem("ffnwd")
    xl = [P.sem(f"ffnxl{i}") for i in range(2)]
    xst = [P.sem(f"ffnxst{i}") for i in range(2)]
    stk = contextlib.ExitStack()
    SW = 512
    NSTRIP = T_CORE // SW
    wgt = [P.sb(f"wg{i}", (128, NCH * 128), BF16, stk, side="right") for i in range(2)]
    wut = [P.sb(f"wu{i}", (128, NCH * 128), BF16, stk, side="right") for i in range(2)]
    wd_res = P.sb("wd_res", (128, HCH * 1024), BF16, stk, side="right")
    gbuf = [P.sb(f"gbuf{i}", (128, SW), BF16, stk, side="right") for i in range(2)]
    ubuf = P.sb("ubuf", (128, SW), BF16, stk, side="right")
    h_sb = P.sb("h_sb", (128, HCH * SW), BF16, stk, side="right")
    xres = [P.sb(f"fxres{i}", (128, SW), F32, stk, side="right") for i in range(2)]
    g_ps_ = [P.ps(f"gg_ps{i}", (128, SW), stack=stk) for i in range(2)]
    u_ps_ = [P.ps(f"uu_ps{i}", (128, SW), stack=stk) for i in range(2)]
    down_pp = [P.ps(f"down_pp{i}", (128, 512), stack=stk) for i in range(2)]

    wg_r = wg_d.rearrange("(c p) h -> p c h", p=128)
    wu_r = wu_d.rearrange("(c p) h -> p c h", p=128)
    wd_r = wd_d.rearrange("(c p) d -> p c d", p=128)
    wgu_base = [wgu[0].n, wgu[1].n]
    wds_base = wds.n

    def load_gu(g, it):
        hc = it % HCH
        s = wgu[it % 2]
        s.inc(g.dma_start(out=wgt[it % 2][:].rearrange("p (c k) -> p c k", c=NCH),
                          in_=wg_r[:, :, hc * 128:(hc + 1) * 128]), 16)
        s.inc(g.dma_start(out=wut[it % 2][:].rearrange("p (c k) -> p c k", c=NCH),
                          in_=wu_r[:, :, hc * 128:(hc + 1) * 128]), 16)

    def f0(g):
        wait_marks(g, waits)
        wds.inc(g.dma_start(out=wd_res[:].rearrange("p (c d) -> p c d", c=HCH), in_=wd_r), 16)
        load_gu(g, 0)
        load_gu(g, 1)
    P.on("gpsimd", f0)

    h_mark = {}
    res_mark = {}
    down_mark = {}
    S0f = S.n
    silu_jobs = []
    for st_i in range(NSTRIP):
        c0 = st_i * SW
        for hc in range(HCH):
            it = st_i * HCH + hc
            slot = hc % 2
            def fmm(t, hc=hc, slot=slot, c0=c0, it=it):
                wait_marks(t, waits)
                t.wait_ge(wgu[it % 2].h, wgu_base[it % 2] + 32 * (it // 2 + 1))
                if it >= 2:
                    wait_marks(t, [h_mark[it - 2]])
                for c in range(NCH):
                    t.matmul(g_ps_[slot][:], wgt[slot][:, c * 128:(c + 1) * 128],
                             xln_sb[:, c * T_CORE + c0: c * T_CORE + c0 + SW],
                             start=(c == 0), stop=(c == NCH - 1))
                for c in range(NCH):
                    i = t.matmul(u_ps_[slot][:], wut[slot][:, c * 128:(c + 1) * 128],
                                 xln_sb[:, c * T_CORE + c0: c * T_CORE + c0 + SW],
                                 start=(c == 0), stop=(c == NCH - 1))
                Tn.inc(i)
            P.on("tensor", fmm)
            gu_mark = Tn.mark
            silu_jobs.append((gu_mark, slot))
            silu_mark = (S, S0f + it + 1)
            def fup(v, silu_mark=silu_mark, gu_mark=gu_mark, hc=hc, st_i=st_i, slot=slot):
                wait_marks(v, [gu_mark])
                if st_i >= 1:
                    wait_marks(v, [down_mark[st_i - 1]])
                v.wait_ge(V.h, V.n)
                V.inc(v.tensor_scalar(ubuf[:], u_ps_[slot][:], 1.0 / 127.0, None, ALU.mult))
                wait_marks(v, [silu_mark])
                v.wait_ge(V.h, V.n)
                V.inc(v.tensor_mul(h_sb[:, hc * SW:(hc + 1) * SW], gbuf[slot][:], ubuf[:]))
            P.on("vector", fup)
            h_mark[it] = V.mark
            if it + 2 < NSTRIP * HCH:
                def fpre(g, it=it, gu_mark=gu_mark):
                    wait_marks(g, [gu_mark])
                    load_gu(g, it + 2)
                P.on("gpsimd", fpre)
        h_all = V.mark
        for dd in range(NCH):
            bank = down_pp[dd % 2]
            it2 = st_i * NCH + dd
            pp = it2 % 2
            def fdown(t, dd=dd, bank=bank, h_all=h_all, st_i=st_i, it2=it2):
                wait_marks(t, [h_all])
                t.wait_ge(wds.h, wds_base + 16)
                if it2 >= 2:
                    wait_marks(t, [res_mark[it2 - 2]])
                for hc in range(HCH):
                    i = t.matmul(bank[:, 0:SW],
                                 wd_res[:, hc * 1024 + dd * 128: hc * 1024 + (dd + 1) * 128],
                                 h_sb[:, hc * SW:(hc + 1) * SW],
                                 start=(hc == 0), stop=(hc == HCH - 1))
                Tn.inc(i)
            P.on("tensor", fdown)
            dd_mark = Tn.mark
            def fxld(sy, dd=dd, c0=c0, pp=pp, it2=it2, h_all=h_all):
                if it2 >= 2:
                    wait_marks(sy, [res_mark[it2 - 2]])
                else:
                    wait_marks(sy, [h_all])
                xl[pp].inc(sy.dma_start(out=xres[pp][:],
                                        in_=x_in[dd * 128:(dd + 1) * 128, c0:c0 + SW]), 16)
            P.on("sync", fxld)
            xld_mark = xl[pp].mark
            def fres(v, bank=bank, pp=pp, dd_mark=dd_mark, xld_mark=xld_mark):
                wait_marks(v, [dd_mark, xld_mark])
                V.inc(v.scalar_tensor_tensor(xres[pp][:], bank[:, 0:SW], 0.5 / 127.0,
                                             xres[pp][:], ALU.mult, ALU.add))
            P.on("vector", fres)
            wr_mark = V.mark
            def fxst(g, dd=dd, c0=c0, pp=pp, wr_mark=wr_mark):
                wait_marks(g, [wr_mark])
                xst[pp].inc(g.dma_start(out=x_out[dd * 128:(dd + 1) * 128, c0:c0 + SW],
                                        in_=xres[pp][:]), 16)
            P.on("gpsimd", fxst)
            res_mark[it2] = xst[pp].mark
        down_mark[st_i] = Tn.mark
    def fsilus(sc, jobs=list(silu_jobs)):
        for gm, slot in jobs:
            wait_marks(sc, [gm])
            S.inc(sc.activation(gbuf[slot][:], g_ps_[slot][:], AF.Silu,
                                bias=0.0, scale=1.0 / 127.0))
    P.on("scalar", fsilus)
    assert S.n == S0f + len(silu_jobs)
    fin = [xst[0].mark, xst[1].mark]
    G = P.E["gpsimd"]
    def ffin(g, fin=fin):
        wait_marks(g, fin)
        G.inc(g.nop())
    P.on("gpsimd", ffin)
    stk.close()
    return G.mark




def emit_flag(P, scl):
    V, Tn = P.E["vector"], P.E["tensor"]
    fl = P.sb("flags", (1, 8), F32, side="left")
    on1 = P.sb("flag_on1", (1, 128), F32, side="left")
    stF = contextlib.ExitStack()
    bc_ps = P.ps("flag_bc", (128, 4), stack=stF)
    def f(v):
        v.wait_ge(V.h, V.n)
        V.inc(v.memset(fl[0:1, 0:1], 1.0))
        v.wait_ge(V.h, V.n)
        V.inc(v.tensor_scalar(fl[0:1, 1:2], fl[0:1, 0:1], float(np.float32(2.0 ** -30)), 1.0,
                              ALU.add, ALU.subtract))
        v.wait_ge(V.h, V.n)
        V.inc(v.tensor_scalar(fl[0:1, 1:2], fl[0:1, 1:2], float(np.float32(2.0 ** 30)), None,
                              ALU.mult))
        v.wait_ge(V.h, V.n)
        V.inc(v.tensor_scalar(fl[0:1, 2:3], fl[0:1, 1:2], -1.0, 1.0, ALU.mult, ALU.add))
        v.wait_ge(V.h, V.n)
        V.inc(v.memset(fl[0:1, 3:4], 32767.0))
        V.inc(v.memset(fl[0:1, 4:5], 7.0))
        V.inc(v.memset(on1[:], 1.0))
        v.wait_ge(V.h, V.n)
        V.inc(v.reciprocal(fl[0:1, 3:4], fl[0:1, 3:4]))
        v.wait_ge(V.h, V.n)
        V.inc(v.reciprocal(fl[0:1, 4:5], fl[0:1, 4:5]))
    P.on("vector", f)
    fm = V.mark
    def fb(t, fm=fm):
        wait_marks(t, [fm])
        Tn.inc(t.matmul(bc_ps[:], on1[:], fl[0:1, 1:5], start=True, stop=True))
    P.on("tensor", fb)
    bm = Tn.mark
    def fc(v, bm=bm):
        wait_marks(v, [bm])
        v.wait_ge(V.h, V.n)
        V.inc(v.tensor_copy(scl[:, 20:24], bc_ps[:]))
    P.on("vector", fc)
    stF.close()
    return fl, on1, V.mark


def emit_quant_tile(v, V, src, dst, inv_ap, pa_ap, pb_ap, y, r2):
    v.wait_ge(V.h, V.n)
    V.inc(v.tensor_scalar(y, src, inv_ap, C1, ALU.mult, ALU.add))
    v.wait_ge(V.h, V.n)
    V.inc(v.tensor_scalar(r2, y, C2, C2, ALU.add, ALU.subtract))
    V.inc(v.tensor_scalar(y, y, C1, pa_ap, ALU.subtract, ALU.mult))
    v.wait_ge(V.h, V.n)
    V.inc(v.tensor_scalar(r2, r2, C1, pb_ap, ALU.subtract, ALU.mult))
    v.wait_ge(V.h, V.n)
    V.inc(v.tensor_add(dst, y, r2))


def emit_lpbq_dequant(P, tag, w_d, wdeq_d, wdeqT_d, O, I, fl, fl_mark, waits):
    V = P.E["vector"]
    gl = [P.sem("lpl" + tag + str(i)) for i in range(2)]
    gs = [P.sem("lps" + tag + str(i)) for i in range(2)]
    tq = P.sem("lpt" + tag)
    gt_ = P.sem("lpu" + tag)
    stk = contextlib.ExitStack()
    NB = I // 32
    wtile = [P.sb("lpw" + tag + str(i), (128, I), F32, stk, side="right") for i in range(2)]
    mx = P.sb("lpmx" + tag, (128, NB), F32, stk, side="right")
    sblk = P.sb("lps" + tag, (128, NB), F32, stk, side="right")
    iblk = P.sb("lpi" + tag, (128, NB), F32, stk, side="right")
    t1 = P.sb("lpt1" + tag, (128, I), F32, stk, side="right")
    r1 = P.sb("lpr1" + tag, (128, I), F32, stk, side="right")
    r2 = P.sb("lpr2" + tag, (128, I), F32, stk, side="right")
    wq = [P.sb("lpwq" + tag + str(i), (128, I), BF16, stk, side="right") for i in range(2)]
    ttile = P.sb("lptt" + tag, (128, 1024), BF16, stk, side="right")
    NC_ = O // 128
    st_mark = {}
    q_mark = {}
    for rc in range(NC_):
        wt = wtile[rc % 2]
        def fld(g, rc=rc, wt=wt):
            wait_marks(g, waits)
            if rc >= 2:
                wait_marks(g, [q_mark[rc - 2]])
            gl[rc % 2].inc(g.dma_start(out=wt[:], in_=w_d[rc * 128:(rc + 1) * 128, :]), 16)
        P.on("gpsimd", fld)
        ld_mark = gl[rc % 2].mark
        def fq(v, rc=rc, wt=wt, ld_mark=ld_mark):
            wait_marks(v, [ld_mark, fl_mark])
            if rc >= 2:
                wait_marks(v, [st_mark[rc - 2]])
            v.wait_ge(V.h, V.n)
            V.inc(v.tensor_reduce(mx[:], wt[:].rearrange("p (b k) -> p b k", k=32),
                                  mybir.AxisListType.X, ALU.max, apply_absolute_value=True))
            v.wait_ge(V.h, V.n)
            V.inc(v.tensor_scalar(sblk[:], mx[:], P._scl[:, 23:24], 1e-12, ALU.mult, ALU.max))
            v.wait_ge(V.h, V.n)
            V.inc(v.reciprocal(iblk[:], sblk[:]))
            v.wait_ge(V.h, V.n)
            V.inc(v.tensor_mul(t1[:].rearrange("p (b k) -> p b k", k=32),
                               wt[:].rearrange("p (b k) -> p b k", k=32),
                               iblk[:].broadcast_to((128, NB, 32))))
            v.wait_ge(V.h, V.n)
            V.inc(v.tensor_scalar(r1[:], t1[:], C1, C1, ALU.add, ALU.subtract))
            V.inc(v.tensor_scalar(r2[:], t1[:], C2, C2, ALU.add, ALU.subtract))
            v.wait_ge(V.h, V.n)
            V.inc(v.tensor_scalar(r1[:], r1[:], scl_flagm(P), None, ALU.mult))
            v.wait_ge(V.h, V.n)
            V.inc(v.tensor_scalar(r2[:], r2[:], scl_flag(P), None, ALU.mult))
            v.wait_ge(V.h, V.n)
            V.inc(v.tensor_add(r1[:], r1[:], r2[:]))
            v.wait_ge(V.h, V.n)
            V.inc(v.tensor_mul(wq[rc % 2][:].rearrange("p (b k) -> p b k", k=32),
                               r1[:].rearrange("p (b k) -> p b k", k=32),
                               sblk[:].broadcast_to((128, NB, 32))))
            q_mark[rc] = V.mark
        P.on("vector", fq)
        def fst(g, rc=rc):
            wait_marks(g, [q_mark[rc]])
            gs[rc % 2].inc(g.dma_start(out=wdeq_d[rc * 128:(rc + 1) * 128, :], in_=wq[rc % 2][:]), 16)
        P.on("gpsimd", fst)
        st_mark[rc] = gs[rc % 2].mark
    all_st = [gs[0].mark, gs[1].mark]
    tr_mark = {}
    for dc in range(I // 128):
        for og in range(O // 1024):
            it = dc * (O // 1024) + og
            def ftr(sc, dc=dc, og=og, it=it, all_st=all_st):
                wait_marks(sc, all_st)
                if it >= 1:
                    wait_marks(sc, [tr_mark[it - 1]])
                tq.inc(sc.dma_start_transpose(ttile[:, 0:1024],
                                              wdeq_d[og * 1024:(og + 1) * 1024,
                                                     dc * 128:(dc + 1) * 128]), 16)
            P.on("scalar", ftr)
            ld2 = tq.mark
            def fst2(g, dc=dc, og=og, ld2=ld2):
                wait_marks(g, [ld2])
                gt_.inc(g.dma_start(out=wdeqT_d[dc * 128:(dc + 1) * 128, og * 1024:(og + 1) * 1024],
                                    in_=ttile[:, 0:1024]), 16)
            P.on("gpsimd", fst2)
            tr_mark[it] = gt_.mark
    fin = gt_.mark
    stk.close()
    return fin


def scl_flag(P):
    return P._scl[:, 20:21]


def scl_flagm(P):
    return P._scl[:, 21:22]


def emit_cross_part_max(P, tag, maxbuf_ap, src_mark, amax_row, stack):
    V = P.E["vector"]
    gx = P.sem("cpm" + tag)
    col = P.sb("cpm_c" + tag, (128, 1), F32, stack, side="right")
    row = P.sb("cpm_r" + tag, (1, 128), F32, stack, side="right")
    def f1(v, src_mark=src_mark):
        wait_marks(v, [src_mark])
        v.wait_ge(V.h, V.n)
        V.inc(v.tensor_reduce(col[:], maxbuf_ap, mybir.AxisListType.X, ALU.max))
    P.on("vector", f1)
    m1 = V.mark
    def f2(g, m1=m1):
        wait_marks(g, [m1])
        gx.inc(g.dma_start(out=row[0:1, :], in_=col[:, 0:1]), 16)
    P.on("gpsimd", f2)
    m2 = gx.mark
    def f3(v, m2=m2):
        wait_marks(v, [m2])
        v.wait_ge(V.h, V.n)
        V.inc(v.tensor_reduce(amax_row[0:1, 0:1], row[0:1, :], mybir.AxisListType.X, ALU.max))
    P.on("vector", f3)
    return V.mark


def emit_qdq_scalars(P, tag, amax_mark, amax_row, fl, on1, flbc_mark, scl, col0, stack):
    V, Tn = P.E["vector"], P.E["tensor"]
    gq = P.sem("qsg" + tag)
    cc = P.sem("qsc" + tag)
    cc_in = P.dram_tmp("qs_in" + tag, (1, 128), F32)
    cc_out = P.dram_tmp("qs_out" + tag, (1, 128), F32)
    row = P.sb("qsr" + tag, (1, 8), F32, stack, side="right")
    zr = P.sb("qsz" + tag, (1, 128), F32, stack, side="right")
    bc_ps = P.ps("qsb" + tag, (128, 4), stack=stack)
    def fz(v, amax_mark=amax_mark):
        wait_marks(v, [amax_mark])
        v.wait_ge(V.h, V.n)
        V.inc(v.memset(zr[:], 0.0))
        v.wait_ge(V.h, V.n)
        V.inc(v.tensor_copy(zr[0:1, 0:1], amax_row[0:1, 0:1]))
    P.on("vector", fz)
    z_mark = V.mark
    def fst(g, z_mark=z_mark):
        wait_marks(g, [z_mark])
        gq.inc(g.dma_start(out=cc_in, in_=zr[:]), 16)
        g.wait_ge(gq.h, gq.n)
        cc.inc(g.collective_compute("AllReduce", ALU.max,
                                    replica_groups=[list(range(8))],
                                    ins=[cc_in.opt()], outs=[cc_out.opt()]))
        g.wait_ge(cc.h, cc.n)
        gq.inc(g.dma_start(out=row[0:1, 0:1], in_=cc_out[0:1, 0:1]), 16)
    P.on("gpsimd", fst)
    ar_mark = gq.mark
    def fsc(v, ar_mark=ar_mark):
        wait_marks(v, [ar_mark, flbc_mark])
        v.wait_ge(V.h, V.n)
        V.inc(v.tensor_scalar(row[0:1, 1:2], row[0:1, 0:1], fl[0:1, 3:4], 1e-12,
                              ALU.mult, ALU.max))
        v.wait_ge(V.h, V.n)
        V.inc(v.reciprocal(row[0:1, 2:3], row[0:1, 1:2]))
        v.wait_ge(V.h, V.n)
        V.inc(v.tensor_mul(row[0:1, 3:4], row[0:1, 1:2], fl[0:1, 2:3]))
        v.wait_ge(V.h, V.n)
        V.inc(v.tensor_mul(row[0:1, 4:5], row[0:1, 1:2], fl[0:1, 1:2]))
        v.wait_ge(V.h, V.n)
        V.inc(v.tensor_copy(row[0:1, 5:6], row[0:1, 1:2]))
    P.on("vector", fsc)
    sc_mark = V.mark
    def fbc(t, sc_mark=sc_mark):
        wait_marks(t, [sc_mark])
        Tn.inc(t.matmul(bc_ps[:], on1[:], row[0:1, 2:6], start=True, stop=True))
    P.on("tensor", fbc)
    bc_mark = Tn.mark
    def fcp(v, bc_mark=bc_mark):
        wait_marks(v, [bc_mark])
        v.wait_ge(V.h, V.n)
        V.inc(v.tensor_copy(scl[:, col0:col0 + 4], bc_ps[:]))
    P.on("vector", fcp)
    return V.mark


def emit_l1ffn(P, x_in, x_out, wg_d, wu_d, wd_d, waits, deq_cache=None):
    V, S, Tn = P.E["vector"], P.E["scalar"], P.E["tensor"]
    scl = P.sb("scl", (128, 24), F32, side="left")
    P._scl = scl
    fl, on1, fl_mark = emit_flag(P, scl)

    up_pre = P.dram_tmp("up_pre", (HID, T_CORE), BF16)
    gate_pre = P.dram_tmp("gate_pre", (HID, T_CORE), BF16)
    t_pre = P.dram_tmp("t_pre", (HID, T_CORE), BF16)
    o_pre = P.dram_tmp("o_pre", (HID, T_CORE), BF16)

    if deq_cache is None:
        wdeq_u = P.dram_tmp("wdeq_u", (HID, D), BF16)
        wdeq_g = P.dram_tmp("wdeq_g", (HID, D), BF16)
        wdeq_d2 = P.dram_tmp("wdeq_d", (D, HID), BF16)
        wdqT_u = P.dram_tmp("wdqT_u", (D, HID), BF16)
        wdqT_g = P.dram_tmp("wdqT_g", (D, HID), BF16)
        wdqT_d = P.dram_tmp("wdqT_d", (HID, D), BF16)
        dq_u = emit_lpbq_dequant(P, "u", wu_d, wdeq_u, wdqT_u, HID, D, fl, fl_mark, waits)
        dq_g = emit_lpbq_dequant(P, "g", wg_d, wdeq_g, wdqT_g, HID, D, fl, fl_mark,
                                 waits + [dq_u])
        dq_d = emit_lpbq_dequant(P, "d", wd_d, wdeq_d2, wdqT_d, D, HID, fl, fl_mark,
                                 waits + [dq_g])
    else:
        wdqT_u, wdqT_g, wdqT_d = deq_cache
        dq_u = dq_g = dq_d = fl_mark
    P._deq_cache = (wdqT_u, wdqT_g, wdqT_d)

    stLN = contextlib.ExitStack()
    xln_sb = P.sb("xln_sb", (128, NCH * T_CORE), BF16, stLN, side="right")
    xq_sb = P.sb("xq_sb", (128, NCH * T_CORE), BF16, stLN, side="right")
    ln_mark = emit_ln_stream(P, "L1", x_in, xln_sb, waits + [dq_d])
    mbx = P.sb("mbx", (128, NCH), F32, stLN, side="right")
    amax_row = P.sb("amaxr", (1, 4), F32, side="left")
    def fmx(v, ln_mark=ln_mark):
        wait_marks(v, [ln_mark])
        for c in range(NCH):
            v.wait_ge(V.h, V.n)
            V.inc(v.tensor_reduce(mbx[:, c:c + 1], xln_sb[:, c * T_CORE:(c + 1) * T_CORE],
                                  mybir.AxisListType.X, ALU.max, apply_absolute_value=True))
    P.on("vector", fmx)
    xmx_mark = emit_cross_part_max(P, "x", mbx[:], V.mark, amax_row, stLN)
    sx_mark = emit_qdq_scalars(P, "x", xmx_mark, amax_row, fl, on1, fl_mark, scl, 0, stLN)
    qy = P.sb("qy", (128, 512), F32, stLN, side="right")
    qr2 = P.sb("qr2", (128, 512), F32, stLN, side="right")
    def fqx(v, sx_mark=sx_mark):
        wait_marks(v, [sx_mark])
        for c in range(NCH):
            for tt in range(4):
                sl = slice(c * T_CORE + tt * 512, c * T_CORE + tt * 512 + 512)
                emit_quant_tile(v, V, xln_sb[:, sl], xq_sb[:, sl],
                                scl[:, 0:1], scl[:, 1:2], scl[:, 2:3], qy[:], qr2[:])
    P.on("vector", fqx)
    xq_mark = V.mark

    stA = contextlib.ExitStack()
    SW = 512
    NSTRIP = T_CORE // SW
    wub = [P.sb(f"l1wu{i}", (128, NCH * 128), BF16, stA, side="right") for i in range(2)]
    wgb = [P.sb(f"l1wg{i}", (128, NCH * 128), BF16, stA, side="right") for i in range(2)]
    uev = [P.sb(f"uev{i}", (128, SW), BF16, stA, side="right") for i in range(2)]
    gev = [P.sb(f"gev{i}", (128, SW), BF16, stA, side="right") for i in range(2)]
    mbu = P.sb("mbu", (128, HCH * NSTRIP), F32, stA, side="right")
    mbg = P.sb("mbg", (128, HCH * NSTRIP), F32, stA, side="right")
    ua_ps = [P.ps(f"ua_ps{i}", (128, 512), stack=stA) for i in range(2)]
    ga_ps = [P.ps(f"ga_ps{i}", (128, 512), stack=stA) for i in range(2)]
    wA = [P.sem(f"l1wa{i}") for i in range(2)]
    sA = [P.sem(f"l1sa{i}") for i in range(2)]
    wA_base = [wA[0].n, wA[1].n]
    wu_r = wdqT_u.rearrange("(c p) h -> p c h", p=128)
    wg_r = wdqT_g.rearrange("(c p) h -> p c h", p=128)

    def loadA(g, it):
        hc = it % HCH
        s = wA[it % 2]
        s.inc(g.dma_start(out=wub[it % 2][:].rearrange("p (c k) -> p c k", c=NCH),
                          in_=wu_r[:, :, hc * 128:(hc + 1) * 128]), 16)
        s.inc(g.dma_start(out=wgb[it % 2][:].rearrange("p (c k) -> p c k", c=NCH),
                          in_=wg_r[:, :, hc * 128:(hc + 1) * 128]), 16)

    def fA0(g, dq=[dq_u, dq_g]):
        wait_marks(g, dq)
        loadA(g, 0)
        loadA(g, 1)
    P.on("gpsimd", fA0)

    ev_mark = {}
    stq_mark = {}
    for st_i in range(NSTRIP):
        c0 = st_i * SW
        for hc in range(HCH):
            it = st_i * HCH + hc
            slot = it % 2
            def fmm(t, hc=hc, slot=slot, c0=c0, it=it):
                wait_marks(t, [xq_mark])
                t.wait_ge(wA[slot].h, wA_base[slot] + 32 * (it // 2 + 1))
                if it >= 2:
                    wait_marks(t, [ev_mark[it - 2]])
                for c in range(NCH):
                    t.matmul(ua_ps[slot][:], wub[slot][:, c * 128:(c + 1) * 128],
                             xq_sb[:, c * T_CORE + c0: c * T_CORE + c0 + SW],
                             start=(c == 0), stop=(c == NCH - 1))
                for c in range(NCH):
                    i = t.matmul(ga_ps[slot][:], wgb[slot][:, c * 128:(c + 1) * 128],
                                 xq_sb[:, c * T_CORE + c0: c * T_CORE + c0 + SW],
                                 start=(c == 0), stop=(c == NCH - 1))
                Tn.inc(i)
            P.on("tensor", fmm)
            mm_mark = Tn.mark
            def fev(v, hc=hc, slot=slot, it=it, mm_mark=mm_mark):
                wait_marks(v, [mm_mark])
                if it >= 2:
                    wait_marks(v, [stq_mark[it - 2]])
                v.wait_ge(V.h, V.n)
                V.inc(v.tensor_copy(uev[slot][:], ua_ps[slot][:]))
                V.inc(v.tensor_copy(gev[slot][:], ga_ps[slot][:]))
                v.wait_ge(V.h, V.n)
                V.inc(v.tensor_reduce(mbu[:, it:it + 1], uev[slot][:], mybir.AxisListType.X,
                                      ALU.max, apply_absolute_value=True))
                V.inc(v.tensor_reduce(mbg[:, it:it + 1], gev[slot][:], mybir.AxisListType.X,
                                      ALU.max, apply_absolute_value=True))
            P.on("vector", fev)
            ev_mark[it] = V.mark
            def fstq(g, hc=hc, slot=slot, c0=c0, ev=ev_mark[it]):
                wait_marks(g, [ev])
                sA[slot].inc(g.dma_start(out=up_pre[hc * 128:(hc + 1) * 128, c0:c0 + SW],
                                         in_=uev[slot][:]), 16)
                sA[slot].inc(g.dma_start(out=gate_pre[hc * 128:(hc + 1) * 128, c0:c0 + SW],
                                         in_=gev[slot][:]), 16)
            P.on("gpsimd", fstq)
            stq_mark[it] = sA[slot].mark
            nxt = it + 2
            if nxt < NSTRIP * HCH:
                def fpre(g, nxt=nxt, mm_mark=mm_mark):
                    wait_marks(g, [mm_mark])
                    loadA(g, nxt)
                P.on("gpsimd", fpre)
    spillA = [sA[0].mark, sA[1].mark]
    umx_mark = emit_cross_part_max(P, "u", mbu[:], ev_mark[NSTRIP * HCH - 1], amax_row, stA)
    su_mark = emit_qdq_scalars(P, "u", umx_mark, amax_row, fl, on1, fl_mark, scl, 4, stA)
    gmx_mark = emit_cross_part_max(P, "g", mbg[:], su_mark, amax_row, stA)
    sg_mark = emit_qdq_scalars(P, "g", gmx_mark, amax_row, fl, on1, fl_mark, scl, 8, stA)
    stA.close()
    stLN.close()

    stB = contextlib.ExitStack()
    gt = [P.sb(f"bgt{i}", (128, 512), BF16, stB, side="right") for i in range(2)]
    br = P.sb("br", (128, 512), F32, stB, side="right")
    by = P.sb("by", (128, 512), F32, stB, side="right")
    br2 = P.sb("br2", (128, 512), F32, stB, side="right")
    bsig = P.sb("bsig", (128, 512), F32, stB, side="right")
    bsq = P.sb("bsq", (128, 512), F32, stB, side="right")
    btv = [P.sb(f"btv{i}", (128, 512), BF16, stB, side="right") for i in range(2)]
    mbt = P.sb("mbt", (128, HCH * 4), F32, stB, side="right")
    lB = [P.sem(f"l1lb{i}") for i in range(2)]
    sB = [P.sem(f"l1sb{i}") for i in range(2)]
    NTB = HCH * 4
    bq_mark = {}
    bst_mark = {}
    for ti in range(NTB):
        hc = ti // 4
        tc = ti % 4
        slot = ti % 2
        def fbl(sy, hc=hc, tc=tc, slot=slot, ti=ti, sg_mark=sg_mark, spillA=spillA):
            wait_marks(sy, [sg_mark] + spillA)
            if ti >= 2:
                wait_marks(sy, [bq_mark[ti - 2]])
            lB[slot].inc(sy.dma_start(out=gt[slot][:],
                                      in_=gate_pre[hc * 128:(hc + 1) * 128,
                                                   tc * 512:(tc + 1) * 512]), 16)
        P.on("sync", fbl)
        ld_mark = lB[slot].mark
        def fbq(v, slot=slot, ti=ti, ld_mark=ld_mark):
            wait_marks(v, [ld_mark])
            if ti >= 2:
                wait_marks(v, [bst_mark[ti - 2]])
            emit_quant_tile(v, V, gt[slot][:], br[:], scl[:, 8:9], scl[:, 21:22],
                            scl[:, 20:21], by[:], br2[:])
        P.on("vector", fbq)
        r_mark = V.mark
        def fbs(sc, r_mark=r_mark):
            wait_marks(sc, [r_mark])
            S.inc(sc.activation(bsig[:], br[:], AF.Sigmoid, bias=0.0, scale=scl[:, 11:12]))
        P.on("scalar", fbs)
        sig_mark = S.mark
        def fbq2(v, slot=slot, ti=ti, sig_mark=sig_mark):
            wait_marks(v, [sig_mark])
            v.wait_ge(V.h, V.n)
            V.inc(v.tensor_scalar(by[:], bsig[:], 65536.0, C1, ALU.mult, ALU.add))
            v.wait_ge(V.h, V.n)
            V.inc(v.tensor_scalar(br2[:], by[:], C2, C2, ALU.add, ALU.subtract))
            V.inc(v.tensor_scalar(by[:], by[:], C1, scl[:, 21:22], ALU.subtract, ALU.mult))
            v.wait_ge(V.h, V.n)
            V.inc(v.tensor_scalar(br2[:], br2[:], C1, scl[:, 20:21], ALU.subtract, ALU.mult))
            v.wait_ge(V.h, V.n)
            V.inc(v.tensor_add(bsq[:], by[:], br2[:]))
            v.wait_ge(V.h, V.n)
            V.inc(v.tensor_scalar(bsq[:], bsq[:], 65535.0, float(1.0 / 65536.0),
                                  ALU.min, ALU.mult))
            v.wait_ge(V.h, V.n)
            V.inc(v.scalar_tensor_tensor(by[:], br[:], scl[:, 11:12], bsq[:],
                                         ALU.mult, ALU.mult))
            v.wait_ge(V.h, V.n)
            V.inc(v.tensor_copy(btv[slot][:], by[:]))
            v.wait_ge(V.h, V.n)
            V.inc(v.tensor_reduce(mbt[:, ti:ti + 1], btv[slot][:], mybir.AxisListType.X,
                                  ALU.max, apply_absolute_value=True))
        P.on("vector", fbq2)
        bq_mark[ti] = V.mark
        def fbst(g, hc=hc, tc=tc, slot=slot, m=bq_mark[ti]):
            wait_marks(g, [m])
            sB[slot].inc(g.dma_start(out=t_pre[hc * 128:(hc + 1) * 128, tc * 512:(tc + 1) * 512],
                                     in_=btv[slot][:]), 16)
        P.on("gpsimd", fbst)
        bst_mark[ti] = sB[slot].mark
    spillB = [sB[0].mark, sB[1].mark]
    tmx_mark = emit_cross_part_max(P, "t", mbt[:], bq_mark[NTB - 1], amax_row, stB)
    st_mark2 = emit_qdq_scalars(P, "t", tmx_mark, amax_row, fl, on1, fl_mark, scl, 12, stB)
    stB.close()

    stC = contextlib.ExitStack()
    ct = [P.sb(f"cct{i}", (128, 512), BF16, stC, side="right") for i in range(2)]
    cu = [P.sb(f"ccu{i}", (128, 512), BF16, stC, side="right") for i in range(2)]
    cr = P.sb("cr", (128, 512), F32, stC, side="right")
    cru = P.sb("cru", (128, 512), F32, stC, side="right")
    cy = P.sb("cy", (128, 512), F32, stC, side="right")
    cr2 = P.sb("cr2", (128, 512), F32, stC, side="right")
    cov = [P.sb(f"cov{i}", (128, 512), BF16, stC, side="right") for i in range(2)]
    mbo = P.sb("mbo", (128, HCH * 4), F32, stC, side="right")
    stu = P.sb("stu", (128, 1), F32, stC, side="right")
    lC = [P.sem(f"l1lc{i}") for i in range(2)]
    sC = [P.sem(f"l1sc{i}") for i in range(2)]
    def fstu(v, m=st_mark2):
        wait_marks(v, [m])
        v.wait_ge(V.h, V.n)
        V.inc(v.tensor_mul(stu[:], scl[:, 15:16], scl[:, 7:8]))
    P.on("vector", fstu)
    stu_mark = V.mark
    cq_mark = {}
    cst_mark = {}
    for ti in range(NTB):
        hc = ti // 4
        tc = ti % 4
        slot = ti % 2
        def fcl(sy, hc=hc, tc=tc, slot=slot, ti=ti, stu_mark=stu_mark, spillB=spillB):
            wait_marks(sy, [stu_mark] + spillB)
            if ti >= 2:
                wait_marks(sy, [cq_mark[ti - 2]])
            lC[slot].inc(sy.dma_start(out=ct[slot][:],
                                      in_=t_pre[hc * 128:(hc + 1) * 128, tc * 512:(tc + 1) * 512]), 16)
            lC[slot].inc(sy.dma_start(out=cu[slot][:],
                                      in_=up_pre[hc * 128:(hc + 1) * 128, tc * 512:(tc + 1) * 512]), 16)
        P.on("sync", fcl)
        ld_mark = lC[slot].mark
        def fcq(v, slot=slot, ti=ti, ld_mark=ld_mark):
            wait_marks(v, [ld_mark])
            if ti >= 2:
                wait_marks(v, [cst_mark[ti - 2]])
            emit_quant_tile(v, V, ct[slot][:], cr[:], scl[:, 12:13], scl[:, 21:22],
                            scl[:, 20:21], cy[:], cr2[:])
            emit_quant_tile(v, V, cu[slot][:], cru[:], scl[:, 4:5], scl[:, 21:22],
                            scl[:, 20:21], cy[:], cr2[:])
            v.wait_ge(V.h, V.n)
            V.inc(v.scalar_tensor_tensor(cy[:], cr[:], stu[:, 0:1], cru[:],
                                         ALU.mult, ALU.mult))
            v.wait_ge(V.h, V.n)
            V.inc(v.tensor_copy(cov[slot][:], cy[:]))
            v.wait_ge(V.h, V.n)
            V.inc(v.tensor_reduce(mbo[:, ti:ti + 1], cov[slot][:], mybir.AxisListType.X,
                                  ALU.max, apply_absolute_value=True))
        P.on("vector", fcq)
        cq_mark[ti] = V.mark
        def fcst(g, hc=hc, tc=tc, slot=slot, m=cq_mark[ti]):
            wait_marks(g, [m])
            sC[slot].inc(g.dma_start(out=o_pre[hc * 128:(hc + 1) * 128, tc * 512:(tc + 1) * 512],
                                     in_=cov[slot][:]), 16)
        P.on("gpsimd", fcst)
        cst_mark[ti] = sC[slot].mark
    spillC = [sC[0].mark, sC[1].mark]
    omx_mark = emit_cross_part_max(P, "o", mbo[:], cq_mark[NTB - 1], amax_row, stC)
    so_mark = emit_qdq_scalars(P, "o", omx_mark, amax_row, fl, on1, fl_mark, scl, 16, stC)
    stC.close()

    stD = contextlib.ExitStack()
    wd_res = P.sb("l1wd_res", (128, HCH * 1024), BF16, stD, side="right")
    oq_sb = P.sb("oq_sb", (128, HCH * 512), BF16, stD, side="right")
    dt = [P.sb(f"ddt{i}", (128, 512), BF16, stD, side="right") for i in range(2)]
    dy = P.sb("dy", (128, 512), F32, stD, side="right")
    dr2 = P.sb("dr2", (128, 512), F32, stD, side="right")
    dxres = [P.sb(f"dxres{i}", (128, 512), F32, stD, side="right") for i in range(2)]
    down_pp = [P.ps(f"l1dpp{i}", (128, 512), stack=stD) for i in range(2)]
    wD = P.sem("l1wd")
    lD = [P.sem(f"l1ld{i}") for i in range(2)]
    xlD = [P.sem(f"l1xl{i}") for i in range(2)]
    xstD = [P.sem(f"l1xs{i}") for i in range(2)]
    wdd_r = wdqT_d.rearrange("(c p) d -> p c d", p=128)
    def fDw(g, dq_d=dq_d):
        wait_marks(g, [dq_d, so_mark] + spillC)
        wD.inc(g.dma_start(out=wd_res[:].rearrange("p (c d) -> p c d", c=HCH), in_=wdd_r), 16)
    P.on("gpsimd", fDw)
    wd_mark = wD.mark
    SWD = 512
    oqm = {}
    res_mark = {}
    for st_i in range(T_CORE // SWD):
        c0 = st_i * SWD
        for hc in range(HCH):
            it = st_i * HCH + hc
            slot = it % 2
            def fdl(sy, hc=hc, c0=c0, slot=slot, it=it, so_mark=so_mark, spillC=spillC):
                wait_marks(sy, [so_mark] + spillC)
                if it >= 2:
                    wait_marks(sy, [oqm[it - 2]])
                lD[slot].inc(sy.dma_start(out=dt[slot][:],
                                          in_=o_pre[hc * 128:(hc + 1) * 128, c0:c0 + SWD]), 16)
            P.on("sync", fdl)
            ld_mark = lD[slot].mark
            def fdq(v, hc=hc, slot=slot, it=it, ld_mark=ld_mark, st_i=st_i):
                wait_marks(v, [ld_mark])
                if st_i >= 1:
                    wait_marks(v, [dn_all[st_i - 1]])
                emit_quant_tile(v, V, dt[slot][:], oq_sb[:, hc * SWD:(hc + 1) * SWD],
                                scl[:, 16:17], scl[:, 17:18], scl[:, 18:19], dy[:], dr2[:])
            P.on("vector", fdq)
            oqm[it] = V.mark
        oq_all = V.mark
        dn_all = dn_all if st_i > 0 else {}
        for dd in range(NCH):
            bank = down_pp[dd % 2]
            it2 = st_i * NCH + dd
            pp = it2 % 2
            def fdown(t, dd=dd, bank=bank, oq_all=oq_all, it2=it2, wd_mark=wd_mark):
                wait_marks(t, [oq_all, wd_mark])
                if it2 >= 2:
                    wait_marks(t, [res_mark[it2 - 2]])
                for hc in range(HCH):
                    i = t.matmul(bank[:, 0:SWD],
                                 wd_res[:, hc * 1024 + dd * 128: hc * 1024 + (dd + 1) * 128],
                                 oq_sb[:, hc * SWD:(hc + 1) * SWD],
                                 start=(hc == 0), stop=(hc == HCH - 1))
                Tn.inc(i)
            P.on("tensor", fdown)
            dd_mark = Tn.mark
            def fxld(sy, dd=dd, c0=c0, pp=pp, it2=it2, oq_all=oq_all):
                if it2 >= 2:
                    wait_marks(sy, [res_mark[it2 - 2]])
                else:
                    wait_marks(sy, [oq_all])
                xlD[pp].inc(sy.dma_start(out=dxres[pp][:],
                                         in_=x_in[dd * 128:(dd + 1) * 128, c0:c0 + SWD]), 16)
            P.on("sync", fxld)
            xld_mark = xlD[pp].mark
            def fres(v, bank=bank, pp=pp, dd_mark=dd_mark, xld_mark=xld_mark):
                wait_marks(v, [dd_mark, xld_mark])
                v.wait_ge(V.h, V.n)
                V.inc(v.scalar_tensor_tensor(dxres[pp][:], bank[:, 0:SWD], 0.5,
                                             dxres[pp][:], ALU.mult, ALU.add))
            P.on("vector", fres)
            wr_mark = V.mark
            def fxst(g, dd=dd, c0=c0, pp=pp, wr_mark=wr_mark):
                wait_marks(g, [wr_mark])
                xstD[pp].inc(g.dma_start(out=x_out[dd * 128:(dd + 1) * 128, c0:c0 + SWD],
                                         in_=dxres[pp][:]), 16)
            P.on("gpsimd", fxst)
            res_mark[it2] = xstD[pp].mark
        dn_all[st_i] = Tn.mark
    fin = [xstD[0].mark, xstD[1].mark]
    G = P.E["gpsimd"]
    def ffinD(g, fin=fin):
        wait_marks(g, fin)
        G.inc(g.nop())
    P.on("gpsimd", ffinD)
    stD.close()
    return G.mark


def build_l1ffn_launch():
    P = Prog()
    x_in = P.dram_in("x", (D, T_CORE), F32)
    wg = P.dram_in("l1_wg", (HID, D), F32)
    wu = P.dram_in("l1_wu", (HID, D), F32)
    wd = P.dram_in("l1_wd", (D, HID), F32)
    x_out = P.dram_out("x_next", (D, T_CORE), F32)
    with P:
        emit_l1ffn(P, x_in, x_out, wg, wu, wd, [])
    return P.nc




L1A_UPTO = "full"


def _early_out(P, x_out, marks):
    eo = P.sem("earlyout")
    def f(g):
        wait_marks(g, marks)
        for c in range(NCH):
            eo.inc(g.dma_start(out=x_out[c * 128:(c + 1) * 128, 0:512],
                               in_=x_out[c * 128:(c + 1) * 128, 0:512]), 16)
        g.wait_ge(eo.h, eo.n)
    P.on("gpsimd", f)


def emit_l1attn(P, x_in, x_out, qT, kT, vtm, wqkv_d, wo_d,
                bqk_sb, bv_sb, bo_sb, waits):
    V, S, Tn = P.E["vector"], P.E["scalar"], P.E["tensor"]
    stX = contextlib.ExitStack()
    xln_sb = P.sb("xln_sb", (128, NCH * T_CORE), BF16, stX, side="right")
    ln_mark = emit_ln_stream(P, "L1A", x_in, xln_sb, waits)
    waits = waits + [ln_mark]
    gq = P.sem("l1a_g")
    xl = [P.sem(f"l1a_xl{i}") for i in range(2)]
    xst = [P.sem(f"l1a_xs{i}") for i in range(2)]
    scq = P.sem("l1a_sc")

    stON = contextlib.ExitStack()
    onesf = P.sb("l1a_on", (1, 128), F32, stON, side="left")
    ones64 = P.sb("l1a_o64", (64, 1), BF16, stON, side="left")
    gb64 = P.sb("l1a_gb", (64, 1), F32, stON, side="left")
    grow = P.sb("l1a_gr", (1, 4), F32, stON, side="left")
    def finit(v):
        wait_marks(v, waits)
        v.wait_ge(V.h, V.n)
        V.inc(v.memset(onesf[:], 1.0))
        V.inc(v.memset(ones64[:], 1.0))
    P.on("vector", finit)
    ones_mark = V.mark

    def mk_evac(dst, boff):
        def evac(v, ps_t, oc, tt):
            V.inc(v.tensor_scalar(dst[:, oc * T_CORE + tt * 512: oc * T_CORE + tt * 512 + 512],
                                  ps_t[:], bqk_sb[:, boff + oc: boff + oc + 1],
                                  None, ALU.add))
        return evac
    pq_mark = emit_proj_fm(P, "lq", xln_sb, wqkv_d, 0, NCH, mk_evac(qT, 0), waits)
    pk_mark = emit_proj_fm(P, "lk", xln_sb, wqkv_d, D, NCH, mk_evac(kT, NCH),
                           waits + [pq_mark])

    stV = contextlib.ExitStack()
    wv_sb = P.sb("l1a_wv", (128, NCH * 1024), BF16, stV, side="right")
    v_ps = [P.ps(f"l1a_vps{i}", (128, 512), stack=stV) for i in range(2)]
    wv_r = wqkv_d.rearrange("(c p) o -> p c o", p=128)
    def fvw(g):
        wait_marks(g, waits + [pq_mark, pk_mark])
        gq.inc(g.dma_start(out=wv_sb[:].rearrange("p (c o) -> p c o", c=NCH),
                           in_=wv_r[:, :, 2 * D: 3 * D]), 16)
    P.on("gpsimd", fvw)
    vw_mark = gq.mark
    vev = {}
    for tc in range(T_CORE // 128):
        for half in range(2):
            it = tc * 2 + half
            pp = it % 2
            def fvmm(t, tc=tc, half=half, pp=pp, it=it):
                wait_marks(t, waits)
                wait_marks(t, [vw_mark, ones_mark])
                if it >= 2:
                    wait_marks(t, [vev[it - 2]])
                for c in range(NCH):
                    t.matmul(v_ps[pp][:], xln_sb[:, c * T_CORE + tc * 128: c * T_CORE + (tc + 1) * 128],
                             wv_sb[:, c * 1024 + half * 512: c * 1024 + half * 512 + 512],
                             start=(c == 0), stop=False)
                Tn.inc(t.matmul(v_ps[pp][:], onesf[:], bv_sb[0:1, half * 512:(half + 1) * 512],
                                start=False, stop=True))
            P.on("tensor", fvmm)
            vmm_mark = Tn.mark
            def fvev(v, tc=tc, half=half, pp=pp, vmm_mark=vmm_mark):
                wait_marks(v, [vmm_mark])
                v.wait_ge(V.h, V.n)
                V.inc(v.tensor_copy(vtm[:, tc * 1024 + half * 512: tc * 1024 + half * 512 + 512],
                                    v_ps[pp][:]))
            P.on("vector", fvev)
            vev[it] = V.mark
    v_mark = V.mark
    stV.close()
    stX.close()
    if L1A_UPTO == "proj":
        _early_out(P, x_out, [v_mark, pq_mark, pk_mark])
        return None

    stA2 = contextlib.ExitStack()
    attn_sb = P.sb("attn_sb", (128, NCH * T_CORE), BF16, stA2, side="right")
    stP1 = contextlib.ExitStack()
    maxb = P.sb("l1a_mx", (64, 32), F32, stP1, side="right")
    mcol = P.sb("l1a_mc", (64, 1), F32, stP1, side="right")
    mrow = P.sb("l1a_mr", (1, 64), F32, stP1, side="right")
    st_ps = [P.ps(f"l1a_sps{i}", (64, 1024), stack=stP1) for i in range(2)]
    red_mark = {}
    for m in range(32):
        ch = m // 8
        pp = m % 2
        def fs1(t, m=m, pp=pp, pq=pq_mark, pk=pk_mark, v_mark=v_mark):
            wait_marks(t, [pq, pk, v_mark])
            if m >= 2:
                wait_marks(t, [red_mark[m - 2]])
            for grp in (0, 1):
                hb = grp * 64
                for h in range(grp, NH, 2):
                    hc = h // 2
                    i = t.matmul(st_ps[pp][:, h * 64:(h + 1) * 64],
                                 kT[hb:hb + 64, hc * T_CORE + m * 64: hc * T_CORE + m * 64 + 64],
                                 qT[hb:hb + 64, hc * T_CORE + m * 64: hc * T_CORE + m * 64 + 64],
                                 start=True, stop=True)
                Tn.inc(i)
                t.wait_ge(Tn.h, Tn.n)
        P.on("tensor", fs1)
        s1_mark = Tn.mark
        def fr1(v, m=m, pp=pp, s1_mark=s1_mark):
            wait_marks(v, [s1_mark])
            v.wait_ge(V.h, V.n)
            V.inc(v.tensor_reduce(maxb[:, m:m + 1], st_ps[pp][:], mybir.AxisListType.X,
                                  ALU.max, apply_absolute_value=True))
        P.on("vector", fr1)
        red_mark[m] = V.mark
        if L1A_UPTO == "p1score" and m == 0:
            break
    if L1A_UPTO == "p1score":
        m0 = red_mark[0]
        stP1.close()
        _early_out(P, x_out, [m0])
        return None
    def fgm(v):
        wait_marks(v, [red_mark[31]])
        v.wait_ge(V.h, V.n)
        V.inc(v.tensor_reduce(mcol[:], maxb[:], mybir.AxisListType.X, ALU.max))
    P.on("vector", fgm)
    gm1 = V.mark
    def fgd(g, gm1=gm1):
        wait_marks(g, [gm1])
        scq.inc(g.dma_start(out=mrow[0:1, :], in_=mcol[0:64, 0:1]), 16)
    P.on("gpsimd", fgd)
    gm2 = scq.mark
    def fgr(v, gm2=gm2):
        wait_marks(v, [gm2])
        v.wait_ge(V.h, V.n)
        V.inc(v.tensor_reduce(grow[0:1, 0:1], mrow[0:1, :], mybir.AxisListType.X, ALU.max))
        v.wait_ge(V.h, V.n)
        V.inc(v.tensor_scalar(grow[0:1, 1:2], grow[0:1, 0:1], -0.125, None, ALU.mult))
    P.on("vector", fgr)
    gm3 = V.mark
    gb_ps = P.ps("l1a_gbps", (64, 1), stack=stP1)
    def fgb(t, gm3=gm3):
        wait_marks(t, [gm3])
        Tn.inc(t.matmul(gb_ps[:], onesf[0:1, 0:64], grow[0:1, 1:2], start=True, stop=True))
    P.on("tensor", fgb)
    gm4 = Tn.mark
    def fgc(v, gm4=gm4):
        wait_marks(v, [gm4])
        v.wait_ge(V.h, V.n)
        V.inc(v.tensor_copy(gb64[:], gb_ps[:]))
    P.on("vector", fgc)
    g_mark = V.mark
    stP1.close()
    if L1A_UPTO == "phase1":
        _early_out(P, x_out, [g_mark])
        return None

    stP2 = contextlib.ExitStack()
    vblk = P.sb("l1a_vb", (64, 1024), BF16, stP2, side="right")
    ptb = P.sb("l1a_pt", (64, 1024), BF16, stP2, side="right")
    otb = P.sb("l1a_ot", (64, 1024), BF16, stP2, side="right")
    linv = P.sb("l1a_li", (1, 1024), F32, stP2, side="right")
    s2_ps = [P.ps("l1a_s2", (64, 1024), stack=stP2)] * 2
    ot_ps = [P.ps("l1a_otp", (64, 1024), stack=stP2)] * 2
    l_ps = [P.ps(f"l1a_lps{i}", (1, 512), stack=stP2) for i in range(2)]
    rep_ps = P.ps("l1a_rep", (64, 1024), stack=stP2)
    blk_mark = {}
    scat_mark = {}
    for m in range(32):
        pp = m % 2
        def fs2(t, m=m, pp=pp, g_mark=g_mark):
            wait_marks(t, [g_mark])
            if m >= 1:
                wait_marks(t, [blk_mark[m - 1]])
            for grp in (0, 1):
                hb = grp * 64
                for h in range(grp, NH, 2):
                    hc = h // 2
                    i = t.matmul(s2_ps[pp][:, h * 64:(h + 1) * 64],
                                 kT[hb:hb + 64, hc * T_CORE + m * 64: hc * T_CORE + m * 64 + 64],
                                 qT[hb:hb + 64, hc * T_CORE + m * 64: hc * T_CORE + m * 64 + 64],
                                 start=True, stop=True)
                Tn.inc(i)
                t.wait_ge(Tn.h, Tn.n)
        P.on("tensor", fs2)
        s2_mark = Tn.mark
        def fexp(sc, pp=pp, s2_mark=s2_mark, m=m):
            wait_marks(sc, [s2_mark])
            if m >= 1:
                wait_marks(sc, [blk_mark.get(m - 1, (S, 0))])
            S.inc(sc.activation(ptb[:], s2_ps[pp][:], AF.Exp, bias=gb64[:, 0:1], scale=0.125))
        P.on("scalar", fexp)
        exp_mark = S.mark
        def fvb(v, m=m, exp_mark=exp_mark, v_mark=v_mark):
            wait_marks(v, [v_mark])
            if m >= 1:
                wait_marks(v, [blk_mark.get(m - 1, (V, 0))])
            v.wait_ge(V.h, V.n)
            V.inc(v.tensor_copy(vblk[:], vtm[(m % 2) * 64:(m % 2) * 64 + 64,
                                             (m // 2) * 1024:(m // 2) * 1024 + 1024]))
        P.on("vector", fvb)
        vb_mark = V.mark
        def fav(t, pp=pp, exp_mark=exp_mark, vb_mark=vb_mark):
            wait_marks(t, [exp_mark, vb_mark])
            for h in range(NH):
                t.matmul(ot_ps[pp][:, h * 64:(h + 1) * 64], vblk[:, h * 64:(h + 1) * 64],
                         ptb[:, h * 64:(h + 1) * 64], start=True, stop=True)
            t.matmul(l_ps[0][:], ones64[:], ptb[:, 0:512], start=True, stop=True)
            Tn.inc(t.matmul(l_ps[1][:], ones64[:], ptb[:, 512:1024], start=True, stop=True))
        P.on("tensor", fav)
        av_mark = Tn.mark
        def fep(v, pp=pp, av_mark=av_mark):
            wait_marks(v, [av_mark])
            v.wait_ge(V.h, V.n)
            V.inc(v.reciprocal(linv[0:1, 0:512], l_ps[0][:]))
            V.inc(v.reciprocal(linv[0:1, 512:1024], l_ps[1][:]))
        P.on("vector", fep)
        li_mark = V.mark
        def frep(t, li_mark=li_mark):
            wait_marks(t, [li_mark])
            t.matmul(rep_ps[:, 0:512], onesf[0:1, 0:64], linv[0:1, 0:512], start=True, stop=True)
            Tn.inc(t.matmul(rep_ps[:, 512:1024], onesf[0:1, 0:64], linv[0:1, 512:1024],
                            start=True, stop=True))
        P.on("tensor", frep)
        rep_mark = Tn.mark
        def fot(v, pp=pp, rep_mark=rep_mark, m=m):
            wait_marks(v, [rep_mark])
            if m >= 1:
                wait_marks(v, [scat_mark[m - 1]])
            v.wait_ge(V.h, V.n)
            V.inc(v.tensor_copy(otb[:], ot_ps[pp][:]))
            v.wait_ge(V.h, V.n)
            V.inc(v.tensor_mul(otb[:], otb[:], rep_ps[:]))
        P.on("vector", fot)
        blk_mark[m] = V.mark
        def fscat(g, m=m, bm=blk_mark[m]):
            wait_marks(g, [bm])
            src = otb[:].rearrange("d (ch par q) -> d ch par q", ch=8, par=2)
            for par in range(2):
                dst = attn_sb[par * 64:par * 64 + 64, :].rearrange(
                    "d (ch t) -> d ch t", ch=8)[:, :, m * 64:(m + 1) * 64]
                scq.inc(g.dma_start(out=dst, in_=src[:, :, par, :],
                                    allow_slow_non_contiguous=True), 16)
        P.on("gpsimd", fscat)
        scat_mark[m] = scq.mark
    attn_done = [scat_mark[31], blk_mark[31], Tn.mark]
    stP2.close()
    if L1A_UPTO == "phase2":
        _early_out(P, x_out, attn_done)
        return None

    stPJ = contextlib.ExitStack()
    wpj_sb = P.sb("l1a_wpj", (128, NCH * 1024), BF16, stPJ, side="right")
    ptmp = P.sb("l1a_ptmp", (128, 512), F32, stPJ, side="right")
    xres = [P.sb(f"l1a_xr{i}", (128, 512), F32, stPJ, side="right") for i in range(2)]
    pj_ps = [P.ps(f"l1a_pj{i}", (128, 512), stack=stPJ) for i in range(2)]
    wpj_r = wo_d.rearrange("(c p) o -> p c o", p=128)
    def fpw(g):
        wait_marks(g, attn_done)
        gq.inc(g.dma_start(out=wpj_sb[:].rearrange("p (c o) -> p c o", c=NCH), in_=wpj_r), 16)
    P.on("gpsimd", fpw)
    pw_mark = gq.mark
    res_done = {}
    for oc in range(NCH):
        for tt in range(4):
            it = oc * 4 + tt
            pp = it % 2
            def fpm(t, oc=oc, tt=tt, pp=pp, it=it):
                wait_marks(t, [pw_mark] + attn_done)
                if it >= 2:
                    wait_marks(t, [res_done[it - 2]])
                for c in range(NCH):
                    i = t.matmul(pj_ps[pp][:], wpj_sb[:, c * 1024 + oc * 128: c * 1024 + (oc + 1) * 128],
                                 attn_sb[:, c * T_CORE + tt * 512: c * T_CORE + tt * 512 + 512],
                                 start=(c == 0), stop=(c == NCH - 1))
                Tn.inc(i)
            P.on("tensor", fpm)
            pm_mark = Tn.mark
            def fxld(sy, oc=oc, tt=tt, pp=pp, it=it):
                if it >= 2:
                    wait_marks(sy, [res_done[it - 2]])
                else:
                    wait_marks(sy, attn_done)
                xl[pp].inc(sy.dma_start(out=xres[pp][:],
                                        in_=x_in[oc * 128:(oc + 1) * 128, tt * 512:(tt + 1) * 512]), 16)
            P.on("sync", fxld)
            xld_mark = xl[pp].mark
            def fpr(v, oc=oc, pp=pp, pm_mark=pm_mark, xld_mark=xld_mark):
                wait_marks(v, [pm_mark, xld_mark])
                v.wait_ge(V.h, V.n)
                V.inc(v.tensor_scalar(ptmp[:], pj_ps[pp][:], bo_sb[:, oc:oc + 1], None, ALU.add))
                v.wait_ge(V.h, V.n)
                V.inc(v.tensor_add(xres[pp][:], ptmp[:], xres[pp][:]))
            P.on("vector", fpr)
            wr_mark = V.mark
            def fxst(g, oc=oc, tt=tt, pp=pp, wr_mark=wr_mark):
                wait_marks(g, [wr_mark])
                xst[pp].inc(g.dma_start(out=x_out[oc * 128:(oc + 1) * 128, tt * 512:(tt + 1) * 512],
                                        in_=xres[pp][:]), 16)
            P.on("gpsimd", fxst)
            res_done[it] = xst[pp].mark
    fin = [xst[0].mark, xst[1].mark]
    def ffin(g, fin=fin):
        wait_marks(g, fin)
    P.on("gpsimd", ffin)
    stPJ.close()
    stA2.close()
    stON.close()
    return fin[0]


def build_l1attn_launch():
    P = Prog()
    x_in = P.dram_in("x", (D, T_CORE), F32)
    wqkv = P.dram_in("l1_wqkv", (D, 3 * D), F32)
    wo = P.dram_in("l1_wo", (D, D), F32)
    bqk_in = P.dram_in("l1_bqk", (128, 16), F32)
    bv_in = P.dram_in("l1_bv", (1, D), F32)
    bo_in = P.dram_in("l1_bo", (128, 8), F32)
    x_out = P.dram_out("x_next", (D, T_CORE), F32)
    bqk_sb = P.sb("bqk_sb", (128, 16), F32)
    bv_sb = P.sb("bv_sb", (1, D), F32)
    bo_sb = P.sb("bo_sb", (128, 8), F32)
    qT = P.sb("qT", (128, NCH * T_CORE), BF16, side="right")
    kT = P.sb("kT", (128, NCH * T_CORE), BF16, side="right")
    vtm = P.sb("vtm", (128, 16 * 1024), BF16, side="right")
    with P:
        ld = P.sem("smalls")
        def fsmall(sy):
            for src_, dst in ((bqk_in, bqk_sb), (bv_in, bv_sb), (bo_in, bo_sb)):
                ld.inc(sy.dma_start(out=dst[:], in_=src_), 16)
        P.on("sync", fsmall)
        emit_l1attn(P, x_in, x_out, qT, kT, vtm, wqkv, wo,
                    bqk_sb, bv_sb, bo_sb, [ld.mark])
    return P.nc


def _l1_host_prep(inputs):
    Wqkv = np.asarray(inputs["L1_attn_w_qkv"])
    bqkv = np.asarray(inputs["L1_attn_b_qkv"])
    p = np.zeros(3072, dtype=np.int64)
    for h in range(NH):
        for i in range(64):
            p[0 * 1024 + h * 64 + i] = h * 192 + i
            p[1 * 1024 + h * 64 + i] = h * 192 + 64 + i
            p[2 * 1024 + h * 64 + i] = h * 192 + 128 + i
    wqkv = np.ascontiguousarray(Wqkv[:, p])
    bp = bqkv[p]
    bqk = np.ascontiguousarray(bp[:2048].reshape(16, 128).T)
    bv = np.ascontiguousarray(bp[2048:].reshape(1, D))
    bo = np.ascontiguousarray(np.asarray(inputs["L1_attn_b_o"]).reshape(8, 128).T)
    wo = np.ascontiguousarray(np.asarray(inputs["L1_attn_w_o"]))
    return dict(wqkv=wqkv, wo=wo, bqk=bqk, bv=bv, bo=bo)



def build_l0ffn_launch():
    P = Prog()
    x_in = P.dram_in("x", (D, T_CORE), F32)
    wg = P.dram_in("l0_wgT", (D, HID), I32)
    wu = P.dram_in("l0_wuT", (D, HID), I32)
    wd = P.dram_in("l0_wdT", (HID, D), I32)
    x_out = P.dram_out("x_next", (D, T_CORE), F32)
    xln_sb = P.sb("xln_sb", (128, NCH * T_CORE), BF16)
    with P:
        ln_mark = emit_ln_stream(P, "0", x_in, xln_sb, [])
        emit_l0ffn(P, x_in, x_out, xln_sb, wg, wu, wd, [ln_mark])
    return P.nc


def build_l0attn_launch():
    P = Prog()
    x_in = P.dram_in("x", (D, T_CORE), F32)
    wqk = P.dram_in("l0_wqk", (D, 2 * D), F32)
    wv = P.dram_in("l0_wv", (D, D), F32)
    wpj = P.dram_in("l0_wproj", (D, D), F32)
    bqk_in = P.dram_in("l0_bqk", (128, 16), F32)
    bv_in = P.dram_in("l0_bv", (1, D), F32)
    bpj_in = P.dram_in("l0_bproj", (128, 8), F32)
    cos_in = P.dram_in("l0_cos", (128, T_CORE), F32)
    sin_in = P.dram_in("l0_sin", (128, T_CORE), F32)
    mask_in = P.dram_in("l0_mask", (128, 8 * 512), BF16)
    x_out = P.dram_out("x_next", (D, T_CORE), F32)
    qT = P.sb("qT", (128, NCH * T_CORE), BF16)
    attn_sb = P.sb("attn_sb", (128, NCH * T_CORE), BF16)
    cos_sb = P.sb("cos_sb", (128, T_CORE), F32)
    sin_sb = P.sb("sin_sb", (128, T_CORE), F32)
    mask_sb = P.sb("mask_sb", (128, 8 * 512), BF16)
    bqk_sb = P.sb("bqk_sb", (128, 16), F32)
    bv_sb = P.sb("bv_sb", (1, D), F32)
    bpj_sb = P.sb("bpj_sb", (128, 8), F32)
    xln_sb = P.sb("xln_sb", (128, NCH * T_CORE), BF16, side="right")
    with P:
        ld = P.sem("smalls")
        def fsmall(sy):
            for src_, dst in ((bqk_in, bqk_sb), (bv_in, bv_sb), (bpj_in, bpj_sb),
                              (cos_in, cos_sb), (sin_in, sin_sb), (mask_in, mask_sb)):
                ld.inc(sy.dma_start(out=dst[:], in_=src_), 16)
        P.on("sync", fsmall)
        ln_mark = emit_ln_stream(P, "0", x_in, xln_sb, [ld.mark])
        emit_l0attn(P, x_in, x_out, xln_sb, qT, attn_sb, wqk, wv, wpj, bqk_sb,
                    bv_sb, bpj_sb, cos_sb, sin_sb, mask_sb, [ld.mark, ln_mark])
    return P.nc


def _l0_host_prep(inputs):
    Wqkv = np.asarray(inputs["L0_attn_Wqkv"])
    bqkv = np.asarray(inputs["L0_attn_bqkv"])
    perm = np.zeros(2 * D, dtype=np.int64)
    for blk in range(2):
        for m in range(4):
            for hl in range(4):
                h = 4 * m + hl
                for i in range(32):
                    perm[blk * 1024 + (2 * m) * 128 + 32 * hl + i] = blk * 1024 + h * 64 + 2 * i
                    perm[blk * 1024 + (2 * m + 1) * 128 + 32 * hl + i] = blk * 1024 + h * 64 + 2 * i + 1
    wqk = np.ascontiguousarray(Wqkv[:, perm])
    wv = np.ascontiguousarray(Wqkv[:, 2048:])
    bqk_p = bqkv[perm]
    bqk = np.ascontiguousarray(bqk_p.reshape(16, 128).T)
    bv = np.ascontiguousarray(bqkv[2048:].reshape(1, D))
    bpj = np.ascontiguousarray(np.asarray(inputs["L0_attn_bproj"]).reshape(8, 128).T)
    wpj = np.ascontiguousarray(np.asarray(inputs["L0_attn_Wproj"]))
    inv_freq = (1.0 / (10000.0 ** (np.arange(0, DK, 2, dtype=np.float32) / DK))).astype(np.float32)
    tabs = []
    for core in range(NCORES):
        pos = _own_tokens(core).astype(np.float32)
        f = (inv_freq[:, None] * pos[None, :]).astype(np.float32)
        cos = np.tile(np.cos(f).astype(np.float32), (4, 1))
        sin = np.tile(np.sin(f).astype(np.float32), (4, 1))
        tabs.append((np.ascontiguousarray(cos), np.ascontiguousarray(sin)))
    kk = np.arange(128)[:, None]
    qq = np.arange(512)[None, :]
    def diag(dj):
        return np.where(qq >= kk + 128 * dj, np.float32(0.0), np.float32(-1e9))
    allow = np.zeros((128, 512), np.float32)
    blockm = np.full((128, 512), np.float32(-1e9))
    masks = []
    for p in range(2):
        slots = []
        for b_ in range(8):
            dj = b_ - 4 * p
            slots.append(allow if dj < 0 else (diag(dj) if dj < 4 else blockm))
        masks.append(np.ascontiguousarray(
            np.concatenate(slots, axis=1).astype(ml_dtypes.bfloat16)))
    return dict(wqk=wqk, wv=wv, wpj=wpj, bqk=bqk, bv=bv, bpj=bpj, tabs=tabs, masks=masks)



def build_fused(n_sub=6):
    P = Prog()
    x_in = P.dram_in("x", (D, T_CORE), F32)
    l0_wg = P.dram_in("l0_wgT", (D, HID), I32)
    l0_wu = P.dram_in("l0_wuT", (D, HID), I32)
    l0_wd = P.dram_in("l0_wdT", (HID, D), I32)
    wqk = P.dram_in("l0_wqk", (D, 2 * D), F32)
    wv = P.dram_in("l0_wv", (D, D), F32)
    wpj = P.dram_in("l0_wproj", (D, D), F32)
    bqk_in = P.dram_in("l0_bqk", (128, 16), F32)
    bv_in = P.dram_in("l0_bv", (1, D), F32)
    bpj_in = P.dram_in("l0_bproj", (128, 8), F32)
    cos_in = P.dram_in("l0_cos", (128, T_CORE), F32)
    sin_in = P.dram_in("l0_sin", (128, T_CORE), F32)
    mask_in = P.dram_in("l0_mask", (128, 8 * 512), BF16)
    l1_wg = P.dram_in("l1_wg", (HID, D), F32)
    l1_wu = P.dram_in("l1_wu", (HID, D), F32)
    l1_wd = P.dram_in("l1_wd", (D, HID), F32)
    wqkv1 = P.dram_in("l1_wqkv", (D, 3 * D), F32)
    wo1 = P.dram_in("l1_wo", (D, D), F32)
    bqk1_in = P.dram_in("l1_bqk", (128, 16), F32)
    bv1_in = P.dram_in("l1_bv", (1, D), F32)
    bo1_in = P.dram_in("l1_bo", (128, 8), F32)
    x_out = P.dram_out("x_next", (D, T_CORE), F32)
    xp = [P.nc.dram_tensor(f"xping{i}", [D, T_CORE], F32, kind="Internal").ap()
          for i in range(5)]
    dst = [xp[0], xp[1], xp[2], xp[3], xp[4], x_out]
    dst[n_sub - 1] = x_out
    with P:
        waits = []
        with P.scoped("s0"):
            stk = contextlib.ExitStack()
            xln = P.sb("xln", (128, NCH * T_CORE), BF16, stk, side="right")
            lnm = emit_ln_stream(P, "a", x_in, xln, waits)
            fin = emit_l0ffn(P, x_in, dst[0], xln, l0_wg, l0_wu, l0_wd, waits + [lnm])
            if n_sub == 1:
                stk.close()
                return P.nc
            stk.close()
        waits = [fin]
        with P.scoped("s1"):
            stk = contextlib.ExitStack()
            qT = P.sb("qT", (128, NCH * T_CORE), BF16, stk, side="right")
            attn_sb = P.sb("attn", (128, NCH * T_CORE), BF16, stk, side="right")
            cos_sb = P.sb("cos", (128, T_CORE), F32, stk, side="right")
            sin_sb = P.sb("sin", (128, T_CORE), F32, stk, side="right")
            mask_sb = P.sb("mask", (128, 8 * 512), BF16, stk, side="right")
            bqk_sb = P.sb("bqk", (128, 16), F32, stk, side="right")
            bv_sb = P.sb("bv", (1, D), F32, stk, side="right")
            bpj_sb = P.sb("bpj", (128, 8), F32, stk, side="right")
            xln = P.sb("xln", (128, NCH * T_CORE), BF16, stk, side="right")
            ld = P.sem("smalls")
            def fsmall(sy, waits=waits):
                wait_marks(sy, waits)
                for src_, dst in ((bqk_in, bqk_sb), (bv_in, bv_sb), (bpj_in, bpj_sb),
                                  (cos_in, cos_sb), (sin_in, sin_sb), (mask_in, mask_sb)):
                    ld.inc(sy.dma_start(out=dst[:], in_=src_), 16)
            P.on("sync", fsmall)
            waits = waits + [ld.mark]
            lnm = emit_ln_stream(P, "b", xp[0], xln, waits)
            fin = emit_l0attn(P, xp[0], dst[1], xln, qT, attn_sb, wqk, wv, wpj, bqk_sb,
                              bv_sb, bpj_sb, cos_sb, sin_sb, mask_sb, waits + [lnm])
            if n_sub == 2:
                stk.close()
                return P.nc
            stk.close()
        waits = [fin]
        with P.scoped("s2"):
            stk = contextlib.ExitStack()
            xln = P.sb("xln", (128, NCH * T_CORE), BF16, stk, side="right")
            lnm = emit_ln_stream(P, "c", xp[1], xln, waits)
            fin = emit_l0ffn(P, xp[1], dst[2], xln, l0_wg, l0_wu, l0_wd, waits + [lnm])
            if n_sub == 3:
                stk.close()
                return P.nc
            stk.close()
        waits = [fin]
        with P.scoped("s3"):
            fin = emit_l1ffn(P, xp[2], dst[3], l1_wg, l1_wu, l1_wd, waits)
            deq = P._deq_cache
            if n_sub == 4:
                return P.nc
        waits = [fin]
        with P.scoped("s4"):
            stk = contextlib.ExitStack()
            bqk_sb = P.sb("bqk", (128, 16), F32, stk, side="right")
            bv_sb = P.sb("bv", (1, D), F32, stk, side="right")
            bo_sb = P.sb("bo", (128, 8), F32, stk, side="right")
            qT = P.sb("qT", (128, NCH * T_CORE), BF16, stk, side="right")
            kT = P.sb("kT", (128, NCH * T_CORE), BF16, stk, side="right")
            vtm = P.sb("vtm", (128, 16 * 1024), BF16, stk, side="right")
            ld = P.sem("smalls")
            def fsmall1(sy, waits=waits):
                wait_marks(sy, waits)
                for src_, dst in ((bqk1_in, bqk_sb), (bv1_in, bv_sb), (bo1_in, bo_sb)):
                    ld.inc(sy.dma_start(out=dst[:], in_=src_), 16)
            P.on("sync", fsmall1)
            fin = emit_l1attn(P, xp[3], dst[4], qT, kT, vtm, wqkv1, wo1,
                              bqk_sb, bv_sb, bo_sb, waits + [ld.mark])
            if n_sub == 5:
                stk.close()
                return P.nc
            stk.close()
        waits = [fin]
        with P.scoped("s5"):
            emit_l1ffn(P, xp[4], x_out, l1_wg, l1_wu, l1_wd, waits, deq_cache=deq)
    return P.nc


def _in_maps(inputs, shards):
    wgT = np.ascontiguousarray(np.asarray(inputs["L0_ffn_w_gate"]).T)
    wuT = np.ascontiguousarray(np.asarray(inputs["L0_ffn_w_up"]).T)
    wdT = np.ascontiguousarray(np.asarray(inputs["L0_ffn_w_down"]).T)
    hp0 = _l0_host_prep(inputs)
    hp1 = _l1_host_prep(inputs)
    wg1 = np.ascontiguousarray(np.asarray(inputs["L1_ffn_w_gate"]))
    wu1 = np.ascontiguousarray(np.asarray(inputs["L1_ffn_w_up"]))
    wd1 = np.ascontiguousarray(np.asarray(inputs["L1_ffn_w_down"]))
    maps = []
    for c in range(NCORES):
        cos, sin = hp0["tabs"][c]
        maps.append({
            "x": shards[c], "l0_wgT": wgT, "l0_wuT": wuT, "l0_wdT": wdT,
            "l0_wqk": hp0["wqk"], "l0_wv": hp0["wv"], "l0_wproj": hp0["wpj"],
            "l0_bqk": hp0["bqk"], "l0_bv": hp0["bv"], "l0_bproj": hp0["bpj"],
            "l0_cos": cos, "l0_sin": sin, "l0_mask": hp0["masks"][c % 2],
            "l1_wg": wg1, "l1_wu": wu1, "l1_wd": wd1,
            "l1_wqkv": hp1["wqkv"], "l1_wo": hp1["wo"],
            "l1_bqk": hp1["bqk"], "l1_bv": hp1["bv"], "l1_bo": hp1["bo"],
        })
    return maps


def kernel(**inputs):
    x0 = np.asarray(inputs["L0_attn_x"])
    shards = []
    for core in range(NCORES):
        b = core // 2
        toks = _own_tokens(core)
        shards.append(np.ascontiguousarray(x0[b][toks].T))
    nc = build_fused()
    res = run_bass_kernel_spmd(nc, _in_maps(inputs, shards), core_ids=list(range(NCORES)))
    out = np.zeros((B, T, D), dtype=np.float32)
    for core in range(NCORES):
        b = core // 2
        toks = _own_tokens(core)
        out[b][toks] = res.results[core]["x_next"].T
    return out
